# version 02
# baseline (speedup 1.0000x reference)
import contextlib
import numpy as np
import ml_dtypes
import concourse.bass as bass
import concourse.mybir as mybir
from concourse.bass_utils import run_bass_kernel_spmd

F32 = mybir.dt.float32
BF16 = mybir.dt.bfloat16
AF = mybir.ActivationFunctionType
ALU = mybir.AluOpType
AX = mybir.AxisListType
NPBF = ml_dtypes.bfloat16
MAGIC = float(np.float32(1.5 * 2 ** 23))
NEG = float(np.float32(-1e38))

NCORE = 8
TLOC = 2048
NT = TLOC // 128
D = 1024
RG = [[0, 1], [2, 3], [4, 5], [6, 7]]

ENGINES = ("sync", "gpsimd", "vector", "scalar", "tensor")
QSEM = {"sync": "qs", "gpsimd": "qg", "scalar": "qa"}
ESEM = {"vector": "tv", "scalar": "ts", "tensor": "tt", "gpsimd": "tg"}


class Prog:

    def __init__(self, nc, stack):
        self.nc = nc
        self.ops = {e: [] for e in ENGINES}
        self.sems = {}
        self.cnt = {}
        for s in ("qs", "qg", "qa", "tv", "ts", "tt", "tg", "cc"):
            self.sems[s] = stack.enter_context(nc.semaphore(s))
            self.cnt[s] = 0

    def raw(self, eng, fn, waits=(), incs=()):
        w = tuple(x for x in waits if x is not None)
        self.ops[eng].append((fn, w, tuple(incs)))

    def _tick(self, sem, n):
        self.cnt[sem] += n
        return (sem, self.cnt[sem])

    def dma(self, q, out, in_, waits=(), transpose=False):
        sem = QSEM[q]
        t = self._tick(sem, 16)
        if transpose:
            fn = lambda e, out=out, in_=in_: e.dma_start(out=out, in_=in_, transpose=True)
        else:
            fn = lambda e, out=out, in_=in_: e.dma_start(out=out, in_=in_)
        self.raw(q, fn, waits, [(sem, 16)])
        return t

    def ve(self, fn, waits=(), eng="vector"):
        sem = ESEM[eng]
        t = self._tick(sem, 1)
        self.raw(eng, fn, waits, [(sem, 1)])
        return t

    def sve(self, fn, waits=(), eng="vector"):
        sem = ESEM[eng]
        w = tuple(waits) + ((sem, self.cnt[sem]),)
        return self.ve(fn, waits=w, eng=eng)

    def quiet(self, eng, fn, waits=()):
        self.raw(eng, fn, waits, ())

    def mmg(self, fns, waits=()):
        for i, fn in enumerate(fns):
            if i == 0 and i == len(fns) - 1:
                return self.ve(fn, waits=waits, eng="tensor")
            if i == 0:
                self.quiet("tensor", fn, waits=waits)
            elif i == len(fns) - 1:
                return self.ve(fn, eng="tensor")
            else:
                self.quiet("tensor", fn)

    def collective(self, kind, op, ins, outs, waits=()):
        t = self._tick("cc", 1)
        fn = lambda e, kind=kind, op=op, ins=ins, outs=outs: e.collective_compute(
            kind, op, RG, ins=ins, outs=outs)
        self.raw("gpsimd", fn, waits, [("cc", 1)])
        return t

    def now(self, sem):
        return (sem, self.cnt[sem])

    def barrier(self):
        marks = [self.now(s) for s in ("qs", "qg", "qa", "tv", "ts", "tt", "tg", "cc")]
        marks = tuple(m for m in marks if m[1] > 0)
        for e in ENGINES:
            self.raw(e, lambda eng: None, marks, ())

    def check(self):
        vals = {s: 0 for s in self.sems}
        ptr = {e: 0 for e in ENGINES}
        wi = {e: 0 for e in ENGINES}
        total = sum(len(v) for v in self.ops.values())
        done = 0
        progress = True
        while progress:
            progress = False
            for e in ENGINES:
                while ptr[e] < len(self.ops[e]):
                    fn, waits, incs = self.ops[e][ptr[e]]
                    ok = True
                    while wi[e] < len(waits):
                        s, v = waits[wi[e]]
                        if vals[s] >= v:
                            wi[e] += 1
                        else:
                            ok = False
                            break
                    if not ok:
                        break
                    for s, n in incs:
                        vals[s] += n
                    ptr[e] += 1
                    wi[e] = 0
                    done += 1
                    progress = True
        if done != total:
            stuck = {e: (ptr[e], len(self.ops[e]),
                         self.ops[e][ptr[e]][1][wi[e]:] if ptr[e] < len(self.ops[e]) else None)
                     for e in ENGINES}
            raise RuntimeError(f"deadlock: {done}/{total}; stuck={stuck}; sems={vals}")

    def _insert_checkpoints(self):
        qeng = {"qs": "sync", "qg": "gpsimd", "qa": "scalar", "cc": "gpsimd"}
        waited = {}
        for e in ENGINES:
            for fn, waits, incs in self.ops[e]:
                for s, v in waits:
                    if s in qeng:
                        waited.setdefault(s, set()).add(v)
        for s, vals in waited.items():
            e = qeng[s]
            newops = []
            cum = 0
            for fn, waits, incs in self.ops[e]:
                newops.append((fn, waits, incs))
                for ss, n in incs:
                    if ss == s:
                        cum += n
                        if cum in vals:
                            newops.append((lambda eng: None, ((s, cum),), ()))
            self.ops[e] = newops

    def emit(self, block):
        self._insert_checkpoints()
        self.check()
        for ename in ENGINES:
            ops = self.ops[ename]
            sems = self.sems

            def body(eng, ops=ops, sems=sems):
                for fn, waits, incs in ops:
                    for s, v in waits:
                        eng.wait_ge(sems[s], v)
                    inst = fn(eng)
                    for s, n in incs:
                        inst.then_inc(sems[s], n)

            getattr(block, ename)(body)


class Arena:
    def __init__(self, tensor, words):
        self.t = tensor
        self.limit = words
        self.off = 0
        self.marks = []

    def push(self):
        self.marks.append(self.off)

    def pop(self):
        self.off = self.marks.pop()

    def alloc(self, parts, free, dtype):
        esz = 2 if dtype == BF16 else 4
        off_b = (self.off * 4 + 31) // 32 * 32
        words = (free * esz + 3) // 4
        self.off = off_b // 4 + words
        assert self.off <= self.limit, f"arena overflow {self.off} > {self.limit}"
        base = self.t[0:parts, off_b // 4: off_b // 4 + words]
        if dtype == BF16:
            return base.bitcast(BF16)[0:parts, 0:free]
        return base[0:parts, 0:free]


def _f32(x):
    return np.ascontiguousarray(np.asarray(x, np.float32))



def prep_host(inputs):
    sh = {}
    sh["l0w1t"] = _f32(inputs["L0_ffn_w1"]).T.copy()
    sh["l0w2t"] = _f32(inputs["L0_ffn_w2"]).T.copy()
    sh["l0b1"] = _f32(inputs["L0_ffn_b1"]).reshape(1, -1)
    sh["l0b2"] = _f32(inputs["L0_ffn_b2"]).reshape(1, -1)
    sh["a0wq"] = _f32(inputs["L0_attn_Wq"])
    sh["a0wk"] = _f32(inputs["L0_attn_Wk"])
    sh["a0wv"] = _f32(inputs["L0_attn_Wv"])
    sh["a0wo"] = _f32(inputs["L0_attn_Wo"])
    for l in range(4):
        for nm in ("UL", "VR", "UR", "VL"):
            sh[f"a0{nm}{l}t"] = _f32(inputs[f"L0_attn_{nm}{l}"]).T.copy()
    sh["l1wqkvt"] = _f32(inputs["L1_attn_W_qkv"]).T.copy()
    _b = _f32(inputs["L1_attn_b_qkv"])
    sh["l1bqkv"] = _b.reshape(24, 128).T.copy()
    sh["l1bv"] = _b[2048:].reshape(1, 1024).copy()
    for l in range(2):
        sh[f"f1wqkv{l}"] = _f32(inputs["L1_ffn_wqkv"][l])
        sh[f"f1wo{l}"] = _f32(inputs["L1_ffn_wo"][l])
        w1 = _f32(inputs["L1_ffn_w1"][l])
        b1 = _f32(inputs["L1_ffn_b1"][l])
        wp = np.zeros((1024, 5632), np.float32)
        bp = np.zeros((1, 5632), np.float32)
        wp[:, 0:5460:2] = w1[:, 0:2730]
        wp[:, 1:5461:2] = w1[:, 2730:5460]
        bp[0, 0:5460:2] = b1[0:2730]
        bp[0, 1:5461:2] = b1[2730:5460]
        sh[f"f1w1p{l}"] = wp
        sh[f"f1b1p{l}"] = bp
        w2 = _f32(inputs["L1_ffn_w2"][l])
        sh[f"f1w2{l}"] = np.concatenate([w2, np.zeros((86, 1024), np.float32)], 0)
        sh[f"f1b2{l}"] = _f32(inputs["L1_ffn_b2"][l]).reshape(1, -1)
        sh[f"f1lg{l}"] = _f32(inputs["L1_ffn_ln_g"][l]).reshape(8, 128).T.copy()
        sh[f"f1lb{l}"] = _f32(inputs["L1_ffn_ln_b"][l]).reshape(8, 128).T.copy()
        sh[f"f1qs{l}"] = _f32(inputs["L1_ffn_q_scale"][l]).reshape(1, -1)
        sh[f"f1ks{l}"] = _f32(inputs["L1_ffn_k_scale"][l]).reshape(1, -1)

    sh["identity"] = np.eye(128, dtype=np.float32)
    sh["onesrow"] = np.ones((1, 512), np.float32)
    sh["onescol"] = np.ones((128, 1), np.float32)
    idx = np.arange(128)
    sh["cmask"] = (idx[:, None] <= idx[None, :]).astype(np.float32)
    ga = np.zeros((64, 47 * 64), np.float32)
    gb = np.zeros((64, 47 * 64), np.float32)
    gb2 = np.zeros((64, 47 * 64), np.float32)
    for n in range(16):
        blk = slice(4 * n, 4 * n + 4)
        ga[blk, 64 * n + 4 * n: 64 * n + 4 * n + 4] = np.eye(4)
        gb[blk, 64 * n + 4 * n: 64 * n + 4 * n + 4] = np.eye(4)
        gb2[blk, 64 * n + 4 * n: 64 * n + 4 * n + 4] = np.eye(4)
    gb[:, 46 * 64:47 * 64] = np.eye(64)
    gb2[:, 46 * 64:47 * 64] = np.eye(64)
    sh["gat"] = ga
    sh["gbt"] = gb
    sh["gbt2"] = gb2

    n = 4096
    half = np.arange(0, 64, 2, dtype=np.float32)
    inv_freq = 1.0 / 10000.0 ** (half / 64.0)
    tt = np.arange(n, dtype=np.float32)
    freqs = tt[:, None] * inv_freq[None, :]
    freqs = np.concatenate([freqs, freqs], -1)
    base = (half + 0.4 * 64) / (1.4 * 64)
    power = (tt - n // 2) / 256.0
    scale = base[None, :] ** power[:, None]
    scale = np.concatenate([scale, scale], -1).astype(np.float32)
    cos = np.cos(freqs).astype(np.float32)
    sin = np.sin(freqs).astype(np.float32)

    def lmask(qo, first):
        i = 512 + qo * 128 + np.arange(128)
        j = np.arange(1024)
        m = (i[:, None] < j[None, :]) | (i[:, None] > j[None, :] + 512)
        if first:
            m = m | (j[None, :] < 512)
        return np.where(m, np.float32(NEG), np.float32(0.0)).astype(np.float32)

    per_core = []
    for c in range(NCORE):
        h = c % 2
        t0 = h * 2048
        pc = {}
        pc["role0"] = np.full((128, 1), 1.0 if h == 0 else 0.0, np.float32)
        pc["role1"] = np.full((128, 1), 1.0 if h == 1 else 0.0, np.float32)
        pos = np.clip(np.arange(t0 - 512, t0 + TLOC), 0, n - 1)
        pc["cq"] = (cos * scale)[pos]
        pc["sq"] = (sin * scale)[pos]
        pc["ck"] = (cos / scale)[pos]
        pc["sk"] = (sin / scale)[pos]
        mk, mkf = [], []
        for qo in range(4):
            mk.append(lmask(qo, False))
            mkf.append(lmask(qo, h == 0))
        pc["lmask"] = np.concatenate(mk, 0)
        pc["lmaskf"] = np.concatenate(mkf, 0)
        per_core.append(pc)
    return sh, per_core



def emit_l0ffn_prep(p, ar, hb, per):
    ps = per["psum"]
    sw = per["swslots"]
    onescol = per["onescol"]
    onesr = per["onesrow_f"]
    for src, dst, rows, cols, slot in (("l0w1t", "wq1t", 1024, 4096, 0),
                                       ("l0w2t", "wq2t", 4096, 1024, 1)):
        ntile = rows // 128
        ar.push()
        wtile = [ar.alloc(128, cols, F32) for _ in range(2)]
        qtile = ar.alloc(128, cols, BF16)
        acc = ar.alloc(128, 1, F32)
        part = ar.alloc(128, 1, F32)
        mval = ar.alloc(1, 1, F32)
        p.ve(lambda e, a=acc: e.memset(a, 0.0))
        red = []
        for i in range(ntile):
            buf = wtile[i % 2]
            w = (red[i - 2],) if i >= 2 else ()
            tk = p.dma("sync", buf, hb[src][i * 128:(i + 1) * 128, :], waits=w)
            r1 = p.ve(lambda e, b=buf, pa=part: e.reduce_sum(pa, b, axis=AX.X, apply_absolute_value=True),
                      waits=(tk,))
            r2 = p.sve(lambda e, a=acc, pa=part: e.tensor_add(a, a, pa))
            red.append(r2)
        mm = p.ve(lambda e, a=acc, oc=onescol, o=ps[0:1, 0:1]: e.matmul(o, a, oc, start=True, stop=True),
                  waits=(red[-1],), eng="tensor")
        v1 = p.sve(lambda e, m=mval, o=ps[0:1, 0:1], s=1.0 / (rows * cols):
                  e.tensor_scalar(m, o, s, 1e-5, op0=ALU.mult, op1=ALU.max), waits=(mm,))
        mm2 = p.ve(lambda e, o=ps[0:128, 4:5], r=onesr, m=mval:
                   e.matmul(o, r, m, start=True, stop=True), waits=(v1,), eng="tensor")
        swv = sw[:, slot:slot + 1]
        v2 = p.sve(lambda e, s=swv, o=ps[0:128, 4:5]: e.tensor_copy(s, o), waits=(mm2,))
        swr = sw[:, slot + 2:slot + 3]
        v3 = p.sve(lambda e, s=swr, sv=swv: e.reciprocal(s, sv), waits=(v2,))
        last_q = v3
        for i in range(ntile):
            buf = wtile[i % 2]
            tk = p.dma("sync", buf, hb[src][i * 128:(i + 1) * 128, :],
                       waits=(v3,) if i < 2 else (ld_guard[i - 2],))
            a1 = p.ve(lambda e, b=buf, s=swr: e.tensor_scalar(b, b, s, None, op0=ALU.mult),
                      waits=(tk, last_q))
            a2 = p.ve(lambda e, b=buf: e.tensor_scalar(b, b, -1.0, 1.0, op0=ALU.max, op1=ALU.min))
            a3 = p.ve(lambda e, q=qtile, b=buf: e.tensor_scalar(q, b, MAGIC, -MAGIC, op0=ALU.add, op1=ALU.add))
            st = p.dma("sync", hb[dst][i * 128:(i + 1) * 128, :], qtile, waits=(a3,))
            if i == 0:
                ld_guard = []
            ld_guard.append(a2)
            last_q = st
        p.raw("sync", lambda e: None, (last_q,), ())
        ar.pop()


def emit_l0ffn(p, ar, hb, per, xt):
    ps = per["psum"]
    sw = per["swslots"]
    onesr = per["onesrow_f"]
    ar.push()
    stat = ar.alloc(128, 16, F32)
    c1s = ar.alloc(128, 16, F32)
    c2s = ar.alloc(128, 16, F32)

    ar.push()
    W1 = [ar.alloc(128, 4096, BF16) for _ in range(8)]
    xq1T = [ar.alloc(128, 2048, BF16) for _ in range(8)]
    gbuf = ar.alloc(128, 4096, BF16)
    xnq = ar.alloc(128, 2048, F32)
    qt = ar.alloc(128, 2048, BF16)
    b1bc = ar.alloc(128, 4096, BF16)
    brow = ar.alloc(1, 512, F32)

    w1tk = [p.dma("sync", W1[dc], hb["wq1t"][dc * 128:(dc + 1) * 128, :]) for dc in range(8)]

    prev = None
    for j in range(8):
        lt = p.dma("gpsimd", brow, hb["l0b1"][0:1, j * 512:(j + 1) * 512], waits=(prev,) if prev else ())
        mmb = p.ve(lambda e, b=brow: e.matmul(ps[0:128, 0:512], onesr, b, start=True, stop=True),
                   eng="tensor", waits=(lt,) + ((prev,) if prev else ()))
        prev = p.ve(lambda e, j=j: e.tensor_copy(b1bc[:, j * 512:(j + 1) * 512], ps[0:128, 0:512]),
                    waits=(mmb,))
    bias1_done = prev

    xq1_done = []
    prevq = None
    for i in range(NT):
        x = xt[i]
        r1 = p.sve(lambda e, x=x, s=stat: e.reduce_sum(s[:, 0:1], x, axis=AX.X))
        r2 = p.sve(lambda e, x=x, s=stat, o=xnq[:, 0:1024]: e.activation(o, x, AF.Square, accum_out=s[:, 1:2]),
                  eng="scalar", waits=(prevq,) if prevq else ())
        v = p.sve(lambda e, s=stat: e.tensor_scalar(s[:, 0:1], s[:, 0:1], 1.0 / D, None, op0=ALU.mult), waits=(r1, r2))
        v = p.sve(lambda e, s=stat: e.tensor_scalar(s[:, 1:2], s[:, 1:2], 1.0 / D, None, op0=ALU.mult))
        v = p.sve(lambda e, s=stat: e.tensor_tensor(s[:, 2:3], s[:, 0:1], s[:, 0:1], op=ALU.mult))
        v = p.sve(lambda e, s=stat: e.tensor_sub(s[:, 2:3], s[:, 1:2], s[:, 2:3]))
        v = p.sve(lambda e, s=stat: e.tensor_scalar_add(s[:, 3:4], s[:, 2:3], 1e-5))
        v = p.sve(lambda e, s=stat: e.reciprocal(s[:, 3:4], s[:, 3:4]))
        sq = p.sve(lambda e, s=stat: e.activation(s[:, 4:5], s[:, 3:4], AF.Sqrt), eng="scalar", waits=(v,))
        v = p.sve(lambda e, s=stat: e.tensor_tensor(s[:, 5:6], s[:, 2:3], s[:, 3:4], op=ALU.mult), waits=(sq,))
        v = p.sve(lambda e, s=stat: e.tensor_scalar_add(s[:, 5:6], s[:, 5:6], 1e-5))
        v = p.sve(lambda e, s=stat: e.reciprocal(s[:, 5:6], s[:, 5:6]))
        sq = p.sve(lambda e, s=stat: e.activation(s[:, 5:6], s[:, 5:6], AF.Sqrt), eng="scalar", waits=(v,))
        v = p.sve(lambda e, s=stat: e.tensor_tensor(s[:, 6:7], s[:, 4:5], s[:, 5:6], op=ALU.mult), waits=(sq,))
        v = p.sve(lambda e, s=stat: e.tensor_tensor(s[:, 7:8], s[:, 0:1], s[:, 6:7], op=ALU.mult))
        v = p.sve(lambda e, s=stat: e.tensor_scalar(s[:, 7:8], s[:, 7:8], -1.0, None, op0=ALU.mult))
        a1 = p.sve(lambda e, x=x, s=stat: e.tensor_scalar(xnq[:, 0:1024], x, s[:, 6:7], s[:, 7:8],
                  op0=ALU.mult, op1=ALU.add), waits=(v,))
        a2 = p.sve(lambda e, s=stat: e.reduce_max(s[:, 8:9], xnq[:, 0:1024], axis=AX.X,
                  apply_absolute_value=True), waits=(a1,))
        v = p.sve(lambda e, s=stat: e.tensor_scalar(s[:, 8:9], s[:, 8:9], 1e-5, None, op0=ALU.max))
        v = p.sve(lambda e, s=stat: e.reciprocal(s[:, 9:10], s[:, 8:9]))
        v = p.sve(lambda e, s=stat: e.tensor_scalar(s[:, 9:10], s[:, 9:10], 127.0, None, op0=ALU.mult))
        v = p.sve(lambda e, s=stat: e.tensor_scalar(s[:, 10:11], s[:, 8:9], 1.0 / 127.0, None, op0=ALU.mult))
        v = p.sve(lambda e, s=stat, c=c1s, i=i: e.tensor_tensor(c[:, i:i + 1], s[:, 10:11], sw[:, 0:1], op=ALU.mult))
        q1 = p.sve(lambda e, s=stat: e.tensor_scalar(xnq[:, 0:1024], xnq[:, 0:1024], s[:, 9:10], -128.0,
                  op0=ALU.mult, op1=ALU.max), waits=(v,))
        q2 = p.ve(lambda e: e.tensor_scalar(xnq[:, 0:1024], xnq[:, 0:1024], 127.0, MAGIC, op0=ALU.min, op1=ALU.add))
        q3 = p.ve(lambda e, q=qt: e.tensor_scalar(q[:, 0:1024], xnq[:, 0:1024], -MAGIC, None, op0=ALU.add))
        st = p.dma("sync", hb["xq1"][i * 128:(i + 1) * 128, :], qt[:, 0:1024], waits=(q3,))
        xq1_done.append(st)
        prevq = st

    xq1t_tk = [p.dma("sync", xq1T[dc], hb["xq1"][:, dc * 128:(dc + 1) * 128],
                     waits=(xq1_done[-1],), transpose=True) for dc in range(8)]

    xq2_done = []
    prev_ev = (bias1_done,)
    for i in range(NT):
        fns = []
        for dc in range(8):
            lhs = xq1T[dc][:, i * 128:(i + 1) * 128]
            for fc in range(8):
                fns.append(lambda e, o=ps[0:128, fc * 512:(fc + 1) * 512], l=lhs,
                           r=W1[dc][:, fc * 512:(fc + 1) * 512], s=(dc == 0), t=(dc == 7):
                           e.matmul(o, l, r, start=s, stop=t))
        mmtk = p.mmg(fns, waits=tuple(xq1t_tk) + tuple(w1tk) + prev_ev)
        e1 = p.ve(lambda e, c=c1s, i=i: e.scalar_tensor_tensor(gbuf[:, 0:4096], ps[0:128, 0:4096],
                  c[:, i:i + 1], b1bc[:, 0:4096], op0=ALU.mult, op1=ALU.add), waits=(mmtk,))
        g1 = p.ve(lambda e: e.activation(gbuf[:, 0:4096], gbuf[:, 0:4096], AF.Gelu), eng="scalar", waits=(e1,))
        g2 = p.sve(lambda e, s=stat: e.activation(xnq[:, 0:2048].bitcast(BF16)[:, 0:2048], gbuf[:, 0:2048],
                  AF.Square, accum_out=s[:, 1:2]), eng="scalar", waits=(g1,))
        g3 = p.sve(lambda e, s=stat: e.activation(xnq[:, 0:2048].bitcast(BF16)[:, 0:2048], gbuf[:, 2048:4096],
                  AF.Square, accum_out=s[:, 2:3]), eng="scalar")
        r3 = p.sve(lambda e, s=stat: e.reduce_max(s[:, 8:9], gbuf[:, 0:4096], axis=AX.X,
                  apply_absolute_value=True), waits=(g1,))
        v = p.sve(lambda e, s=stat: e.tensor_add(s[:, 1:2], s[:, 1:2], s[:, 2:3]), waits=(g3, r3))
        v = p.sve(lambda e, s=stat: e.tensor_scalar(s[:, 1:2], s[:, 1:2], 1.0 / 4096.0, 1e-5, op0=ALU.mult, op1=ALU.add))
        v = p.sve(lambda e, s=stat: e.reciprocal(s[:, 5:6], s[:, 1:2]))
        sq = p.sve(lambda e, s=stat: e.activation(s[:, 5:6], s[:, 5:6], AF.Sqrt), eng="scalar", waits=(v,))
        v = p.sve(lambda e, s=stat: e.tensor_tensor(s[:, 8:9], s[:, 8:9], s[:, 5:6], op=ALU.mult), waits=(sq,))
        v = p.sve(lambda e, s=stat: e.tensor_scalar(s[:, 8:9], s[:, 8:9], 1e-5, None, op0=ALU.max))
        v = p.sve(lambda e, s=stat: e.reciprocal(s[:, 9:10], s[:, 8:9]))
        v = p.sve(lambda e, s=stat: e.tensor_scalar(s[:, 9:10], s[:, 9:10], 127.0, None, op0=ALU.mult))
        v = p.sve(lambda e, s=stat: e.tensor_tensor(s[:, 9:10], s[:, 9:10], s[:, 5:6], op=ALU.mult))
        v = p.sve(lambda e, s=stat: e.tensor_scalar(s[:, 10:11], s[:, 8:9], 1.0 / 127.0, None, op0=ALU.mult))
        v = p.sve(lambda e, s=stat, c=c2s, i=i: e.tensor_tensor(c[:, i:i + 1], s[:, 10:11], sw[:, 1:2], op=ALU.mult))
        sts = []
        for hf in range(2):
            sl = slice(hf * 2048, (hf + 1) * 2048)
            q1 = p.sve(lambda e, s=stat, sl=sl: e.tensor_scalar(xnq[:, 0:2048], gbuf[:, sl], s[:, 9:10], -128.0,
                      op0=ALU.mult, op1=ALU.max), waits=(v,) if hf == 0 else (sts[-1],))
            q2 = p.ve(lambda e: e.tensor_scalar(xnq[:, 0:2048], xnq[:, 0:2048], 127.0, MAGIC, op0=ALU.min, op1=ALU.add))
            q3 = p.ve(lambda e, q=qt: e.tensor_scalar(q[:, 0:2048], xnq[:, 0:2048], -MAGIC, None, op0=ALU.add))
            st = p.dma("sync", hb["xq2"][i * 128:(i + 1) * 128, hf * 2048:(hf + 1) * 2048], qt[:, 0:2048], waits=(q3,))
            sts.append(st)
        xq2_done.append(sts[-1])
        prev_ev = (q3, sts[-1])
        prev_ev = (e1, q3, sts[-1])
    ar.pop()

    ar.push()
    w2s = [ar.alloc(128, 1024, BF16) for _ in range(2)]
    x2s = [ar.alloc(128, 512, BF16) for _ in range(2)]
    b2bc = ar.alloc(128, 1024, BF16)
    xev = ar.alloc(128, 1024, F32)
    brow2 = ar.alloc(1, 512, F32)

    prev = None
    for j in range(2):
        lt = p.dma("gpsimd", brow2, hb["l0b2"][0:1, j * 512:(j + 1) * 512], waits=(prev,) if prev else (xq2_done[-1],))
        mmb = p.ve(lambda e, b=brow2: e.matmul(ps[0:128, 0:512], onesr, b, start=True, stop=True),
                   eng="tensor", waits=(lt,) + ((prev,) if prev else ()))
        prev = p.ve(lambda e, j=j: e.tensor_copy(b2bc[:, j * 512:(j + 1) * 512], ps[0:128, 0:512]), waits=(mmb,))
    bias2_done = prev

    last_ev = bias2_done
    for tg in range(4):
        tiles = list(range(tg * 4, tg * 4 + 4))
        mm_last = None
        for fc in range(32):
            xbuf = x2s[fc % 2]
            wbuf = w2s[fc % 2]
            guard = (mm_hist[fc - 2],) if fc >= 2 else ((last_ev,) if last_ev else ())
            wt = p.dma("scalar", wbuf, hb["wq2t"][fc * 128:(fc + 1) * 128, :], waits=guard)
            xt2 = p.dma("sync", xbuf, hb["xq2"][tg * 512:(tg + 1) * 512, fc * 128:(fc + 1) * 128],
                        waits=guard + ((xq2_done[-1],) if fc < 2 and tg == 0 else ()), transpose=True)
            fns = []
            for j in range(4):
                lhs = xbuf[:, j * 128:(j + 1) * 128]
                for vc in range(2):
                    fns.append(lambda e, o=ps[0:128, j * 1024 + vc * 512: j * 1024 + (vc + 1) * 512],
                               l=lhs, r=wbuf[:, vc * 512:(vc + 1) * 512], s=(fc == 0), t=(fc == 31):
                               e.matmul(o, l, r, start=s, stop=t))
            mm_last = p.mmg(fns, waits=(wt, xt2) + ((last_ev,) if fc == 0 and last_ev else ()))
            if fc == 0:
                mm_hist = []
            mm_hist.append(mm_last)
        evs = []
        for j, ti in enumerate(tiles):
            e1 = p.ve(lambda e, j=j, ti=ti: e.scalar_tensor_tensor(
                xev, ps[0:128, j * 1024:(j + 1) * 1024], c2s[:, ti:ti + 1],
                b2bc, op0=ALU.mult, op1=ALU.add), waits=(mm_last,))
            e2 = p.ve(lambda e, ti=ti, x=xt[ti]: e.scalar_tensor_tensor(
                x, xev, 0.5, x, op0=ALU.mult, op1=ALU.add))
            evs.append(e2)
        last_ev = evs[-1]
    ar.pop()
    ar.pop()
    return last_ev



A0_IN = [("a0wq", (1024, 64), F32), ("a0wk", (1024, 64), F32),
         ("a0wv", (1024, 64), F32), ("a0wo", (64, 1024), F32),
         ("gat", (64, 3008), F32), ("gbt", (64, 3008), F32), ("gbt2", (64, 3008), F32),
         ("cmask", (128, 128), F32), ("role0", (128, 1), F32), ("role1", (128, 1), F32)]
for _l, _bs, _r in ((0, 4, 4), (1, 8, 8), (2, 16, 8), (3, 32, 8)):
    for _nm in ("UL", "VR", "UR", "VL"):
        A0_IN.append((f"a0{_nm}{_l}t", (_r, _bs), F32))
STAGE_INPUTS = {
    "l0ffn": [("l0w1t", (1024, 4096), F32), ("l0w2t", (4096, 1024), F32),
              ("l0b1", (1, 4096), F32), ("l0b2", (1, 1024), F32)],
    "attn0": A0_IN,
    "l1attn": [("l1wqkvt", (1024, 3072), F32), ("l1bqkv", (128, 24), F32),
               ("l1bv", (1, 1024), F32)],
    "l1ffn": sum([[(f"f1wqkv{l}", (1024, 1536), F32), (f"f1wo{l}", (512, 1024), F32),
                   (f"f1w1p{l}", (1024, 5632), F32), (f"f1b1p{l}", (1, 5632), F32),
                   (f"f1w2{l}", (2816, 1024), F32), (f"f1b2{l}", (1, 1024), F32),
                   (f"f1lg{l}", (128, 8), F32), (f"f1lb{l}", (128, 8), F32),
                   (f"f1qs{l}", (1, 64), F32), (f"f1ks{l}", (1, 64), F32)] for l in range(2)], [])
              + [("cq", (2560, 64), F32), ("sq", (2560, 64), F32),
                 ("ck", (2560, 64), F32), ("sk", (2560, 64), F32),
                 ("lmask", (512, 1024), F32), ("lmaskf", (512, 1024), F32),
                 ("role1", (128, 1), F32)],
}
STAGE_SCRATCH = {
    "l0ffn": [("wq1t", (1024, 4096), BF16), ("wq2t", (4096, 1024), BF16),
              ("xq1", (TLOC, 1024), BF16), ("xq2", (TLOC, 4096), BF16)],
    "attn0": [("gac_h", (64, 3008), F32), ("gbc_h", (64, 3008), F32), ("gb2_h", (64, 3008), F32),
              ("lnx_h", (TLOC, 1024), BF16),
              ("ccin_a0", (64, 80), F32), ("ccout_a0", (64, 80), F32),
              ("d_qt", (64, 2048), BF16), ("d_kt", (64, 2048), BF16), ("d_vt", (64, 2048), BF16),
              ("d_ast", (128, 2944), BF16), ("d_bst", (128, 3008), BF16),
              ("d_qtl", (64, 6016), BF16), ("d_pt", (128, 6016), BF16),
              ("d_yall", (64, 2048), F32), ("d_den", (1, 2048), F32), ("d_mt", (64, 65), F32)],
    "l1attn": [("lnx_h", (TLOC, 1024), BF16),
               ("kin_h", (1024, TLOC), BF16), ("kout_h", (2048, TLOC), BF16),
               ("vin_h", (TLOC, 1024), BF16), ("vout_h", (2 * TLOC, 1024), BF16),
               ("p_h", (512, 4096), BF16)],
    "l1ffn": [("w1p_h0", (1024, 5632), BF16), ("w1p_h1", (1024, 5632), BF16),
              ("w2b_h0", (2816, 1024), BF16), ("w2b_h1", (2816, 1024), BF16),
              ("ucrow_h0", (2, 5632), F32), ("ucrow_h1", (2, 5632), F32),
              ("xsp_h", (TLOC, 1024), F32), ("xi_h", (TLOC, 1024), BF16),
              ("bndout_h", (1026, 1024), BF16), ("bnd1in_h", (1, 1024), BF16),
              ("bnd1out_h", (2, 1024), BF16), ("qh_h", (2560, 512), BF16),
              ("kh_h", (2560, 512), BF16), ("ph_h", (TLOC, 8192), BF16),
              ("hh_h", (TLOC, 2816), BF16), ("rv_h", (1, 2048), F32),
              ("dbgf1_h", (1160, 2048), F32)],
}
DEBUG_SCRATCH = False
COMMON_INPUTS = [("xres_in", (TLOC, 1024), F32),
                 ("onescol", (128, 1), F32), ("onesrow", (1, 512), F32),
                 ("identity", (128, 128), F32)]


def build_program(stages):
    nc = bass.Bass("TRN2", target_bir_lowering=False)
    hb = {}
    names = set()
    specs = list(COMMON_INPUTS)
    for st in stages:
        base = st.split(":")[0]
        for it in STAGE_INPUTS.get(base, ()):
            if it[0] not in names:
                names.add(it[0])
                specs.append(it)
    for name, shape, dt in specs:
        hb[name] = nc.dram_tensor(name, list(shape), dt, kind="ExternalInput").ap()
    hb["xres_out"] = nc.dram_tensor("xres_out", [TLOC, 1024], F32, kind="ExternalOutput").ap()
    snames = set()
    for st in stages:
        base = st.split(":")[0]
        for name, shape, dt in STAGE_SCRATCH.get(base, ()):
            if name not in snames:
                snames.add(name)
                kind = "ExternalOutput" if DEBUG_SCRATCH else "Internal"
                hb[name] = nc.dram_tensor(name, list(shape), dt, kind=kind).ap()

    with contextlib.ExitStack() as stack:
        p = Prog(nc, stack)
        xres = stack.enter_context(nc.sbuf_tensor("xres", [128, NT * 1024], F32))
        arena_t = stack.enter_context(nc.sbuf_tensor("arena", [128, 33000], F32))
        small = stack.enter_context(nc.sbuf_tensor("small", [128, 1024], F32))
        psum = stack.enter_context(nc.psum_tensor("psum", [128, 4096], F32))
        ar = Arena(arena_t, 33000)
        xt = [xres[:, i * 1024:(i + 1) * 1024] for i in range(NT)]

        per = {"psum": psum}
        per["onescol"] = small[:, 0:1]
        per["swslots"] = small[:, 8:16]
        onesrow_full = small[0:1, 16:528]
        per["onesrow_f"] = onesrow_full[0:1, 0:128]

        per["ident_f"] = small[:, 528:656]
        per["ident_b"] = small[:, 656:720].bitcast(BF16)[0:128, 0:128]
        p.dma("sync", per["onescol"], hb["onescol"][:, :])
        p.dma("sync", onesrow_full, hb["onesrow"][:, :])
        p.dma("sync", per["ident_f"], hb["identity"][:, :])
        p.dma("gpsimd", per["ident_b"], hb["identity"][:, :])
        for i in range(NT):
            p.dma("sync", xt[i], hb["xres_in"][i * 128:(i + 1) * 128, :])
        p.barrier()

        did_prep = set()
        for st in stages:
            if st == "l0ffn":
                if "l0" not in did_prep:
                    did_prep.add("l0")
                    emit_l0ffn_prep(p, ar, hb, per)
                    p.barrier()
                emit_l0ffn(p, ar, hb, per, xt)
                p.barrier()
            elif st == "attn0":
                if "a0" not in did_prep:
                    did_prep.add("a0")
                    emit_l0attn_prep(p, ar, hb, per)
                    p.barrier()
                emit_l0attn(p, ar, hb, per, xt)
                p.barrier()
            elif st == "l1attn":
                emit_l1attn(p, ar, hb, per, xt)
                p.barrier()
            elif st == "l1ffn":
                if "f1" not in did_prep:
                    did_prep.add("f1")
                    emit_l1ffn_prep(p, ar, hb, per)
                    p.barrier()
                emit_l1ffn(p, ar, hb, per, xt)
                p.barrier()

        for i in range(NT):
            p.dma("sync", hb["xres_out"][i * 128:(i + 1) * 128, :], xt[i])
        p.raw("sync", lambda e: None, (p.now("qs"),), ())
        with nc.Block() as block:
            p.emit(block)
    return nc, specs


def run_stages(stages, xcores, sh, percore):
    nc, specs = build_program(stages)
    in_maps = []
    for c in range(NCORE):
        m = {}
        for name, shape, dt in specs:
            if name == "xres_in":
                m[name] = np.ascontiguousarray(xcores[c])
            elif name in sh:
                m[name] = sh[name]
            else:
                m[name] = percore[c][name]
        in_maps.append(m)
    res = run_bass_kernel_spmd(nc, in_maps, core_ids=list(range(NCORE)))
    if DEBUG_SCRATCH:
        return res
    return np.stack([res.results[c]["xres_out"] for c in range(NCORE)], 0)


PIPE = [["l0ffn", "attn0", "l0ffn", "l1ffn", "l1attn", "l1ffn"]]


def kernel(**inputs):
    x = _f32(inputs["L0_attn_x"])
    xcores = np.zeros((8, TLOC, 1024), np.float32)
    for b in range(4):
        xcores[2 * b] = x[b, :2048]
        xcores[2 * b + 1] = x[b, 2048:]
    sh, percore = prep_host(inputs)
    for stages in PIPE:
        xcores = run_stages(stages, xcores, sh, percore)
    out = np.zeros_like(x)
    for b in range(4):
        out[b, :2048] = xcores[2 * b]
        out[b, 2048:] = xcores[2 * b + 1]
    return out



A0_TRUNC = 99
A0_DEBUG = False
F1_TRUNC = 99
F1_WO = 1
F1_REFS = {}
A0_REFS = {}
A0_BS = [4, 8, 16, 32]
A0_NN = [8, 4, 2, 1]
A0_TOPS = [(l, m) for l in range(4) for m in range(A0_NN[l])]
A0_NCOMP = 47


def emit_l0attn_prep(p, ar, hb, per):
    ps = per["psum"]
    ar.push()
    tmpl = ar.alloc(64, 3008, F32)
    stg = ar.alloc(32, 32, F32)
    uls = {}
    tks = []
    for l in range(4):
        r, bs = [4, 8, 8, 8][l], A0_BS[l]
        for nm in ("UL", "VR", "UR", "VL"):
            buf = ar.alloc(r, bs, BF16)
            uls[(nm, l)] = buf
            tks.append(p.dma("gpsimd", buf, hb[f"a0{nm}{l}t"][:, :]))
    tdone = []
    for src, dst in (("gat", "gac_h"), ("gbt", "gbc_h"), ("gbt2", "gb2_h")):
        t1 = p.dma("sync", tmpl, hb[src][:, :], waits=tuple(tdone[-1:]))
        t2 = p.dma("sync", hb[dst][:, :], tmpl, waits=(t1,))
        tdone.append(t2)
    last = tdone[-1]
    for l in range(4):
        r, bs = [4, 8, 8, 8][l], A0_BS[l]
        prods = (("PUL", "UL", "UL"), ("PVR", "VR", "VR"), ("ABOT", "VR", "UR"),
                 ("BBOT", "UL", "VL"), ("B2BOT", "VL", "UL"))
        for pn, aop, bop in prods:
            mm = p.ve(lambda e, a=uls[(aop, l)], b=uls[(bop, l)], o=ps[0:bs, 0:bs]:
                      e.matmul(o, a, b, start=True, stop=True),
                      eng="tensor", waits=tuple(tks) + (last,))
            cp = p.ve(lambda e, o=ps[0:bs, 0:bs], s=stg[0:bs, 0:bs]: e.tensor_copy(s, o), waits=(mm,))
            wr = []
            for ti, (ll, m) in enumerate(A0_TOPS):
                if ll != l:
                    continue
                ctop = 16 + ti
                cbot = 31 + ti
                L0r = 2 * bs * m
                R0r = 2 * bs * m + bs
                if pn == "PUL":
                    tgt = [("gac_h", L0r, ctop * 64 + L0r)]
                elif pn == "PVR":
                    tgt = [("gbc_h", R0r, ctop * 64 + R0r), ("gb2_h", R0r, ctop * 64 + R0r)]
                elif pn == "ABOT":
                    tgt = [("gac_h", R0r, cbot * 64 + R0r)]
                elif pn == "BBOT":
                    tgt = [("gbc_h", L0r, cbot * 64 + L0r)]
                else:
                    tgt = [("gb2_h", L0r, cbot * 64 + L0r)]
                for dstn, r0, c0 in tgt:
                    wr.append(p.dma("sync", hb[dstn][r0:r0 + bs, c0:c0 + bs], stg[0:bs, 0:bs], waits=(cp,)))
            last = wr[-1]
    p.raw("sync", lambda e: None, (last,), ())
    ar.pop()


def emit_l0attn(p, ar, hb, per, xt):
    ps = per["psum"]
    onesr = per["onesrow_f"]
    ar.push()
    GA = ar.alloc(64, 3008, BF16)
    GB = ar.alloc(64, 3008, BF16)
    GB2 = ar.alloc(64, 3008, BF16)
    wq_sb = ar.alloc(128, 512, BF16)
    wk_sb = ar.alloc(128, 512, BF16)
    wv_sb = ar.alloc(128, 512, BF16)
    wo_sb = ar.alloc(64, 1024, BF16)
    lnxT = [ar.alloc(128, 2048, BF16) for _ in range(8)]
    QT = ar.alloc(64, 2048, BF16)
    KT = ar.alloc(64, 2048, BF16)
    VT = ar.alloc(64, 2048, BF16)
    Astk = ar.alloc(128, 2944, BF16)
    Bstk = ar.alloc(128, 3008, BF16)
    Qtl = ar.alloc(64, 6016, BF16)
    Pt = ar.alloc(128, 6016, BF16)
    YD = ar.alloc(128, 2048, F32)
    Yall = YD[0:64, :]
    den_row = YD[64:65, :]
    ybf = ar.alloc(64, 2048, BF16)
    MT = ar.alloc(64, 65, F32)
    MTb = ar.alloc(64, 65, BF16)
    Mrecv = ar.alloc(64, 80, F32)
    Mrb = ar.alloc(64, 80, BF16)
    exch = ar.alloc(64, 80, F32)
    lnt = ar.alloc(128, 1024, BF16)
    stat = ar.alloc(128, 16, F32)
    onescol_b = ar.alloc(128, 1, BF16)
    cmask_sb = ar.alloc(128, 128, F32)
    role0v = ar.alloc(128, 1, F32)
    role1v = ar.alloc(128, 1, F32)
    tmpe = ar.alloc(64, 512, F32)
    tmpe2 = ar.alloc(64, 512, F32)
    ztmp = ar.alloc(64, 1, F32)

    global A0_REFS
    A0_REFS = dict(GA=GA, GB=GB, GB2=GB2, QT=QT, KT=KT, VT=VT, Astk=Astk, Bstk=Bstk,
                   Qtl=Qtl, Pt=Pt, Yall=Yall, wq=wq_sb, wk=wk_sb, wv=wv_sb)
    ld0 = []
    ld0.append(p.dma("gpsimd", GA, hb["gac_h"][:, :]))
    ld0.append(p.dma("gpsimd", GB, hb["gbc_h"][:, :]))
    ld0.append(p.dma("gpsimd", GB2, hb["gb2_h"][:, :]))
    for wsb, nm in ((wq_sb, "a0wq"), (wk_sb, "a0wk"), (wv_sb, "a0wv")):
        ld0.append(p.dma("gpsimd", wsb.rearrange("p (c f) -> p c f", f=64),
                         hb[nm].rearrange("(c p) f -> p c f", p=128)))
    ld0.append(p.dma("gpsimd", wo_sb, hb["a0wo"][:, :]))
    ld0.append(p.dma("sync", cmask_sb, hb["cmask"][:, :]))
    ld0.append(p.dma("sync", role0v, hb["role0"][:, :]))
    ld0.append(p.dma("sync", role1v, hb["role1"][:, :]))
    ld0.append(p.ve(lambda e: e.memset(onescol_b, 1.0)))
    ld0.append(p.ve(lambda e: e.memset(MT[:, 0:65], 0.0)))
    ld0.append(p.ve(lambda e: e.memset(MTb[:, 0:65], 0.0)))
    ld0.append(p.ve(lambda e: e.memset(exch[:, 0:80], 0.0)))
    ld0.append(p.ve(lambda e: e.memset(YD[:, 0:2048], 0.0)))
    ld0 = tuple(ld0)

    lnst = []
    sqjunk = Pt.bitcast(F32)
    prevq = None
    for i in range(NT):
        x = xt[i]
        r1 = p.ve(lambda e, x=x, s=stat: e.reduce_sum(s[:, 0:1], x, axis=AX.X))
        r2 = p.ve(lambda e, x=x, s=stat, o=sqjunk: e.activation(o[:, 0:1024], x, AF.Square,
                  accum_out=s[:, 1:2]), eng="scalar", waits=((prevq,) if prevq else ()))
        v = p.sve(lambda e, s=stat: e.tensor_scalar(s[:, 0:1], s[:, 0:1], 1.0 / D, None, op0=ALU.mult), waits=(r1, r2))
        v = p.sve(lambda e, s=stat: e.tensor_scalar(s[:, 1:2], s[:, 1:2], 1.0 / D, None, op0=ALU.mult))
        v = p.sve(lambda e, s=stat: e.tensor_tensor(s[:, 2:3], s[:, 0:1], s[:, 0:1], op=ALU.mult))
        v = p.sve(lambda e, s=stat: e.tensor_sub(s[:, 2:3], s[:, 1:2], s[:, 2:3]))
        v = p.sve(lambda e, s=stat: e.tensor_scalar_add(s[:, 3:4], s[:, 2:3], 1e-5))
        v = p.sve(lambda e, s=stat: e.reciprocal(s[:, 3:4], s[:, 3:4]))
        v = p.sve(lambda e, s=stat: e.activation(s[:, 4:5], s[:, 3:4], AF.Sqrt), eng="scalar", waits=(v,))
        v = p.sve(lambda e, s=stat: e.tensor_tensor(s[:, 5:6], s[:, 0:1], s[:, 4:5], op=ALU.mult), waits=(v,))
        v = p.sve(lambda e, s=stat: e.tensor_scalar(s[:, 5:6], s[:, 5:6], -1.0, None, op0=ALU.mult))
        q1 = p.sve(lambda e, x=x, s=stat: e.tensor_scalar(lnt[:, 0:1024], x, s[:, 4:5], s[:, 5:6],
                   op0=ALU.mult, op1=ALU.add), waits=(v,))
        st = p.dma("sync", hb["lnx_h"][i * 128:(i + 1) * 128, :], lnt[:, 0:1024], waits=(q1,))
        lnst.append(st)
        prevq = st
    if A0_TRUNC < 2:
        ar.pop()
        return lnst[-1]
    lnT_tk = tuple(p.dma("sync", lnxT[dc], hb["lnx_h"][:, dc * 128:(dc + 1) * 128],
                         waits=(lnst[-1],), transpose=True) for dc in range(8))

    projw = ld0 + lnT_tk
    prev_ev = None
    for wsb, OUT, iselu in ((wq_sb, QT, True), (wk_sb, KT, True), (wv_sb, VT, False)):
        for tc in range(4):
            cols = slice(tc * 512, (tc + 1) * 512)
            fns = []
            for dc in range(8):
                fns.append(lambda e, l=wsb[:, dc * 64:(dc + 1) * 64], r=lnxT[dc][:, cols],
                           s=(dc == 0), t=(dc == 7): e.matmul(ps[0:64, 0:512], l, r, start=s, stop=t))
            mm = p.mmg(fns, waits=projw + ((prev_ev,) if prev_ev else ()))
            if iselu:
                e1 = p.ve(lambda e: e.tensor_scalar_min(tmpe, ps[0:64, 0:512], 0.0), waits=(mm,))
                e2 = p.ve(lambda e: e.activation(tmpe2, tmpe, AF.Exp), eng="scalar", waits=(e1,))
                e3 = p.ve(lambda e: e.tensor_scalar_max(tmpe, ps[0:64, 0:512], 0.0), waits=(e2,))
                e4 = p.ve(lambda e, O=OUT, cols=cols: e.tensor_tensor(O[:, cols], tmpe, tmpe2, op=ALU.add),
                          waits=(e3,))
                prev_ev = e4
            else:
                prev_ev = p.ve(lambda e, O=OUT, cols=cols: e.tensor_copy(O[:, cols], ps[0:64, 0:512]),
                               waits=(mm,))

    if A0_TRUNC < 3:
        ar.pop()
        return prev_ev
    if A0_DEBUG:
        p.dma("sync", hb["d_qt"][:, :], QT[:, 0:2048], waits=(prev_ev,))
        p.dma("sync", hb["d_kt"][:, :], KT[:, 0:2048], waits=(prev_ev,))
        p.dma("sync", hb["d_vt"][:, :], VT[:, 0:2048], waits=(prev_ev,))
    proj_done = (prev_ev,)
    ev_guard = proj_done
    for ci in range(NT):
        sl = slice(ci * 128, (ci + 1) * 128)
        fns = [lambda e, c0=c0, c1=c1, sl=sl: e.matmul(ps[0:128, c0:c1], KT[:, sl], GA[:, c0:c1], start=True, stop=True)
               for c0, c1 in ((0, 512), (512, 1024), (1024, 1536), (1536, 1984))]
        fns += [lambda e, c0=c0, c1=c1, sl=sl: e.matmul(ps[0:128, c0:c1], VT[:, sl], GA[:, c0:c1], start=True, stop=True)
                for c0, c1 in ((1984, 2048), (2048, 2560), (2560, 2944))]
        mmA = p.mmg(fns, waits=ev_guard)
        evA = p.ve(lambda e: e.copy(Astk[:, 0:2944], ps[0:128, 0:2944]), eng="scalar", waits=(mmA,))
        fns = [lambda e, c0=c0, c1=c1, sl=sl: e.matmul(ps[0:128, c0:c1], VT[:, sl], GB[:, c0:c1], start=True, stop=True)
               for c0, c1 in ((0, 512), (512, 1024), (1024, 1536), (1536, 1984))]
        fns += [lambda e, c0=c0, c1=c1, sl=sl: e.matmul(ps[0:128, c0:c1], KT[:, sl], GB[:, c0:c1], start=True, stop=True)
                for c0, c1 in ((1984, 2048), (2048, 2560), (2560, 3008))]
        mmB = p.mmg(fns, waits=(evA,))
        evB = p.ve(lambda e: e.copy(Bstk[:, 0:3008], ps[0:128, 0:3008]), eng="scalar", waits=(mmB,))
        for r0, r1 in ((0, 24), (24, 47)):
            fns = [lambda e, c=c, r0=r0, sl=sl: e.matmul(ps[0:64, (c - r0) * 128:(c - r0 + 1) * 128],
                   GB2[:, c * 64:(c + 1) * 64], QT[:, sl], start=True, stop=True)
                   for c in range(r0, r1)]
            mmQ = p.mmg(fns, waits=(evB,) if r0 == 0 else (evQ,))
            evQ = p.ve(lambda e, r0=r0, r1=r1: e.copy(Qtl[:, r0 * 128:r1 * 128],
                       ps[0:64, 0:(r1 - r0) * 128]), eng="scalar", waits=(mmQ,))
        s1ev = []
        for src, cr0, cr1 in ((VT, 0, 31), (KT, 31, 47)):
            ncols = (cr1 - cr0) * 128
            fns = []
            for c0 in range(0, ncols, 512):
                c1 = min(c0 + 512, ncols)
                fns.append(lambda e, c0=c0, c1=c1, src=src, cr0=cr0, sl=sl: e.matmul(
                    ps[0:128, c0:c1], src[:, sl], Qtl[:, cr0 * 128 + c0:cr0 * 128 + c1], start=True, stop=True))
            mmS = p.mmg(fns, waits=(evQ,) if cr0 == 0 else (s1ev[-1],))
            nc_ = cr1 - cr0
            ev = p.ve(lambda e, nc_=nc_, cr0=cr0: e.scalar_tensor_tensor(
                Pt[:, cr0 * 128:(cr0 + nc_) * 128].rearrange("p (c t) -> p c t", t=128),
                ps[0:128, 0:nc_ * 128].rearrange("p (c t) -> p c t", t=128),
                1.0,
                cmask_sb.unsqueeze(1).broadcast_to((128, nc_, 128)),
                op0=ALU.mult, op1=ALU.mult), waits=(mmS,))
            s1ev.append(ev)
        fns = [lambda e, c=c: e.matmul(ps[0:64, 3200:3328], Astk[:, c * 64:(c + 1) * 64],
               Pt[:, c * 128:(c + 1) * 128], start=(c == 0), stop=False) for c in range(46)]
        fns.append(lambda e, sl=sl: e.matmul(ps[0:64, 3200:3328], MTb[:, 0:64], QT[:, sl], start=False, stop=True))
        fns.append(lambda e: e.matmul(ps[0:1, 3328:3456], onescol_b, Pt[:, 46 * 128:47 * 128],
                   start=True, stop=False))
        fns.append(lambda e, sl=sl: e.matmul(ps[0:1, 3328:3456], MTb[:, 64:65], QT[:, sl], start=False, stop=True))
        dbg_tk = ()
        if A0_DEBUG and ci == 0:
            dbg_tk = (p.dma("sync", hb["d_ast"][:, :], Astk[:, 0:2944], waits=(evA,)),
                      p.dma("sync", hb["d_bst"][:, :], Bstk[:, 0:3008], waits=(evB,)),
                      p.dma("sync", hb["d_qtl"][:, :], Qtl[:, 0:6016], waits=(evQ,)),
                      p.dma("sync", hb["d_pt"][:, :], Pt[:, 0:6016], waits=tuple(s1ev)))
        mmY = p.mmg(fns, waits=tuple(s1ev))
        evY = p.ve(lambda e, sl=sl: e.tensor_copy(Yall[:, sl], ps[0:64, 3200:3328]), waits=(mmY,))
        evD = p.ve(lambda e, sl=sl: e.tensor_copy(den_row[0:1, sl], ps[0:1, 3328:3456]), waits=(mmY,))
        fns = [lambda e, c=c: e.matmul(ps[0:64, 3456:3520], Bstk[:, c * 64:(c + 1) * 64],
               Astk[:, c * 64:(c + 1) * 64], start=(c == 0), stop=(c == 45)) for c in range(46)]
        mmM = p.mmg(fns, waits=(evY, evD))
        uM = p.ve(lambda e: e.tensor_add(MT[:, 0:64], MT[:, 0:64], ps[0:64, 3456:3520]), waits=(mmM,))
        uz1 = p.ve(lambda e, sl=sl: e.reduce_sum(ztmp, KT[:, sl], axis=AX.X))
        uz2 = p.sve(lambda e: e.tensor_add(MT[:, 64:65], MT[:, 64:65], ztmp), waits=(uz1,))
        ub = p.sve(lambda e: e.tensor_copy(MTb[:, 0:65], MT[:, 0:65]), waits=(uM, uz2))
        ev_guard = (ub, s1ev[-1], evA, evB) + dbg_tk
    if A0_DEBUG:
        p.dma("sync", hb["d_yall"][:, :], Yall[:, 0:2048], waits=ev_guard)
        p.dma("sync", hb["d_den"][:, :], den_row[0:1, 0:2048], waits=ev_guard)
        p.dma("sync", hb["d_mt"][:, :], MT[:, 0:65], waits=ev_guard)
    if A0_TRUNC < 4:
        ar.pop()
        return ev_guard[0]
    pk = p.sve(lambda e: e.tensor_scalar(exch[:, 0:65], MT[:, 0:65], role0v[0:64, 0:1], None, op0=ALU.mult),
               waits=ev_guard)
    sx = p.dma("sync", hb["ccin_a0"][:, :], exch[:, 0:80], waits=(pk,))
    cc = p.collective("AllReduce", ALU.add, [hb["ccin_a0"][:, :].opt()], [hb["ccout_a0"][:, :].opt()],
                      waits=(sx,))
    lx = p.dma("gpsimd", Mrecv[:, 0:80], hb["ccout_a0"][:, :], waits=(cc,))
    rb = p.ve(lambda e: e.tensor_scalar(Mrb[:, 0:65], Mrecv[:, 0:65], role1v[0:64, 0:1], None, op0=ALU.mult),
              waits=(lx,))
    if A0_TRUNC < 5:
        ar.pop()
        return rb
    corr = []
    for qc in range(4):
        cols = slice(qc * 512, (qc + 1) * 512)
        mmC = p.ve(lambda e, cols=cols: e.matmul(ps[0:65, 0:512], Mrb[:, 0:65], QT[:, cols],
                   start=True, stop=True), eng="tensor", waits=(rb,) + tuple(corr[-2:]))
        c1 = p.ve(lambda e, cols=cols: e.tensor_add(Yall[:, cols], Yall[:, cols], ps[0:64, 0:512]), waits=(mmC,))
        c2 = p.ve(lambda e, cols=cols: e.tensor_add(den_row[0:1, cols], den_row[0:1, cols], ps[64:65, 0:512]),
                  waits=(mmC,))
        corr.append(c2)
        corr.append(c1)
    rrow = Qtl.bitcast(F32)[0:1, 0:2048]
    v = p.sve(lambda e: e.tensor_scalar(rrow[0:1, 0:2048], den_row[0:1, 0:2048], 1e-6, None, op0=ALU.max),
              waits=tuple(corr))
    v = p.sve(lambda e: e.reciprocal(rrow[0:1, 0:2048], rrow[0:1, 0:2048]))
    nrm = []
    for qc in range(4):
        cols = slice(qc * 512, (qc + 1) * 512)
        mmB2 = p.ve(lambda e, cols=cols: e.matmul(ps[0:64, 512:1024], onesr[0:1, 0:64], rrow[0:1, cols],
                    start=True, stop=True), eng="tensor", waits=(v,) + ((nrm[-1],) if nrm else ()))
        n1 = p.ve(lambda e, cols=cols: e.tensor_tensor(ybf[:, cols], Yall[:, cols], ps[0:64, 512:1024],
                  op=ALU.mult), waits=(mmB2,))
        nrm.append(n1)
    last = nrm[-1]
    for ci in range(NT):
        sl = slice(ci * 128, (ci + 1) * 128)
        fns = [lambda e, sl=sl, vc=vc: e.matmul(ps[0:128, 1536 + vc * 512:1536 + (vc + 1) * 512],
               ybf[:, sl], wo_sb[:, vc * 512:(vc + 1) * 512], start=True, stop=True) for vc in range(2)]
        mmW = p.mmg(fns, waits=(last,) + tuple(nrm))
        e1 = p.ve(lambda e, x=xt[ci]: e.tensor_add(x, x, ps[0:128, 1536:2560]), waits=(mmW,))
        last = e1
    ar.pop()
    return last



def emit_ln_tiles(p, ar, hb, per, xt, dst, lnt, stat, sqj):
    prevq = None
    lnst = []
    for i in range(NT):
        x = xt[i]
        r1 = p.ve(lambda e, x=x, s=stat: e.reduce_sum(s[:, 0:1], x, axis=AX.X))
        r2 = p.ve(lambda e, x=x, s=stat, o=sqj: e.activation(o[:, 0:1024], x, AF.Square,
                  accum_out=s[:, 1:2]), eng="scalar", waits=((prevq,) if prevq else ()))
        v = p.sve(lambda e, s=stat: e.tensor_scalar(s[:, 0:1], s[:, 0:1], 1.0 / D, None, op0=ALU.mult), waits=(r1, r2))
        v = p.sve(lambda e, s=stat: e.tensor_scalar(s[:, 1:2], s[:, 1:2], 1.0 / D, None, op0=ALU.mult))
        v = p.sve(lambda e, s=stat: e.tensor_tensor(s[:, 2:3], s[:, 0:1], s[:, 0:1], op=ALU.mult))
        v = p.sve(lambda e, s=stat: e.tensor_sub(s[:, 2:3], s[:, 1:2], s[:, 2:3]))
        v = p.sve(lambda e, s=stat: e.tensor_scalar_add(s[:, 3:4], s[:, 2:3], 1e-5))
        v = p.sve(lambda e, s=stat: e.reciprocal(s[:, 3:4], s[:, 3:4]))
        v = p.sve(lambda e, s=stat: e.activation(s[:, 4:5], s[:, 3:4], AF.Sqrt), eng="scalar", waits=(v,))
        v = p.sve(lambda e, s=stat: e.tensor_tensor(s[:, 5:6], s[:, 0:1], s[:, 4:5], op=ALU.mult), waits=(v,))
        v = p.sve(lambda e, s=stat: e.tensor_scalar(s[:, 5:6], s[:, 5:6], -1.0, None, op0=ALU.mult))
        q1 = p.sve(lambda e, x=x, s=stat: e.tensor_scalar(lnt[:, 0:1024], x, s[:, 4:5], s[:, 5:6],
                   op0=ALU.mult, op1=ALU.add), waits=(v,))
        st = p.dma("sync", dst[i * 128:(i + 1) * 128, :], lnt[:, 0:1024], waits=(q1,))
        lnst.append(st)
        prevq = st
    return lnst


def emit_l1attn(p, ar, hb, per, xt):
    ps = per["psum"]
    ar.push()
    QTf = [ar.alloc(128, 2048, BF16) for _ in range(8)]
    bq_sb = ar.alloc(128, 24, F32)
    lnt = ar.alloc(128, 1024, BF16)
    stat = ar.alloc(128, 16, F32)
    rden = ar.alloc(128, 8, F32)
    negm = ar.alloc(128, 8, F32)
    onesrow_b = ar.alloc(1, 128, BF16)
    ob = p.ve(lambda e: e.memset(onesrow_b[0:1, 0:128], 1.0))
    ldb = p.dma("sync", bq_sb, hb["l1bqkv"][:, :])

    ar.push()
    sqj = ar.alloc(128, 1024, F32)
    lnst = emit_ln_tiles(p, ar, hb, per, xt, hb["lnx_h"], lnt, stat, sqj)
    ar.pop()

    ar.push()
    lnxT = [ar.alloc(128, 2048, BF16) for _ in range(8)]
    wqk = [ar.alloc(128, 1024, BF16) for _ in range(2)]
    wv = [ar.alloc(128, 4096, BF16) for _ in range(2)]
    vbuf = ar.alloc(128, 1024, BF16)
    bvrow = ar.alloc(1, 1024, BF16)
    lnT_tk = tuple(p.dma("sync", lnxT[dc], hb["lnx_h"][:, dc * 128:(dc + 1) * 128],
                         waits=(lnst[-1],), transpose=True) for dc in range(8))
    ldbv = p.dma("gpsimd", bvrow, hb["l1bv"][:, :])
    prev_ev = None
    kst = []
    for et in range(16):
        wbuf = wqk[et % 2]
        wt = p.dma("gpsimd", wbuf.rearrange("p (c e) -> p c e", e=128),
                   hb["l1wqkvt"][:, et * 128:(et + 1) * 128].rearrange("(c p) e -> p c e", p=128),
                   waits=(prev_ev,) if prev_ev else ())
        for tc in range(4):
            cols = slice(tc * 512, (tc + 1) * 512)
            fns = [lambda e, dc=dc, cols=cols, wbuf=wbuf: e.matmul(
                ps[0:128, 0:512], wbuf[:, dc * 128:(dc + 1) * 128], lnxT[dc][:, cols],
                start=(dc == 0), stop=(dc == 7)) for dc in range(8)]
            mm = p.mmg(fns, waits=lnT_tk + (wt, ldb) + ((prev_ev,) if prev_ev else ()))
            if et < 8:
                prev_ev = p.ve(lambda e, et=et, cols=cols: e.tensor_scalar(
                    QTf[et][:, cols], ps[0:128, 0:512], bq_sb[:, et:et + 1], None, op0=ALU.add),
                    waits=(mm,))
            else:
                prev_ev = p.ve(lambda e, et=et: e.tensor_scalar(
                    vbuf[:, 0:512], ps[0:128, 0:512], bq_sb[:, et:et + 1], None, op0=ALU.add),
                    waits=(mm,))
                kst.append(p.dma("sync", hb["kin_h"][(et - 8) * 128:(et - 7) * 128, tc * 512:(tc + 1) * 512],
                                 vbuf[:, 0:512], waits=(prev_ev,)))
                p.raw("vector", lambda e: None, (kst[-1],), ())
    vst = []
    wvt0 = p.dma("gpsimd", wv[0].rearrange("p (c e) -> p c e", e=512),
                 hb["l1wqkvt"][:, 2048:2560].rearrange("(c p) e -> p c e", p=128), waits=(prev_ev,))
    wvt1 = p.dma("gpsimd", wv[1].rearrange("p (c e) -> p c e", e=512),
                 hb["l1wqkvt"][:, 2560:3072].rearrange("(c p) e -> p c e", p=128))
    for i in range(NT):
        fns = []
        for dc in range(8):
            for vc in range(2):
                wb = wv[vc]
                fns.append(lambda e, dc=dc, vc=vc, i=i, wb=wb: e.matmul(
                    ps[0:128, 1024 + vc * 512:1024 + (vc + 1) * 512],
                    lnxT[dc][:, i * 128:(i + 1) * 128], wb[:, dc * 512:(dc + 1) * 512],
                    start=(dc == 0), stop=False))
        fns.append(lambda e: e.matmul(ps[0:128, 1024:1536], onesrow_b[0:1, 0:128], bvrow[0:1, 0:512],
                   start=False, stop=True))
        fns.append(lambda e: e.matmul(ps[0:128, 1536:2048], onesrow_b[0:1, 0:128], bvrow[0:1, 512:1024],
                   start=False, stop=True))
        mm = p.mmg(fns, waits=(wvt0, wvt1, ldbv, ob) + ((vst[-1],) if vst else (prev_ev,)))
        ev = p.ve(lambda e: e.tensor_copy(vbuf[:, 0:1024], ps[0:128, 1024:2048]), waits=(mm,))
        vst.append(p.dma("sync", hb["vin_h"][i * 128:(i + 1) * 128, :], vbuf[:, 0:1024], waits=(ev,)))
        p.raw("vector", lambda e: None, (vst[-1],), ())
    ar.pop()

    cc1 = p.collective("AllGather", ALU.bypass, [hb["kin_h"][:, :].opt()], [hb["kout_h"][:, :].opt()],
                       waits=(kst[-1],))
    cc2 = p.collective("AllGather", ALU.bypass, [hb["vin_h"][:, :].opt()], [hb["vout_h"][:, :].opt()],
                       waits=(vst[-1],))

    ar.push()
    scr = [ar.alloc(128, 4096, BF16) for _ in range(4)]
    pbuf = ar.alloc(128, 4096, BF16)
    ktc = [ar.alloc(128, 4096, BF16) for _ in range(2)]
    vtc = [ar.alloc(128, 1024, BF16) for _ in range(2)]
    ptt = [ar.alloc(128, 512, BF16) for _ in range(2)]
    last_ev = (cc1, cc2)
    for phq in range(4):
        qts = [phq * 4 + j for j in range(4)]
        sc_prev = [None] * 4
        kld_hist = []
        for kc in range(8):
            rank = (kc * 512) // 2048
            kcol = (kc * 512) % 2048
            kb = ktc[kc % 2]
            guard = (kld_hist[kc - 2],) if kc >= 2 else last_ev
            kt_tk = p.dma("sync", kb.rearrange("p (c f) -> p c f", f=512),
                          hb["kout_h"][rank * 1024:(rank + 1) * 1024, kcol:kcol + 512]
                          .rearrange("(c p) f -> p c f", p=128), waits=guard)
            for j in range(4):
                fns = [lambda e, ec=ec, j=j, qt=qts[j], kb=kb: e.matmul(
                    ps[0:128, j * 512:(j + 1) * 512], QTf[ec][:, qt * 128:(qt + 1) * 128],
                    kb[:, ec * 512:(ec + 1) * 512], start=(ec == 0), stop=(ec == 7)) for ec in range(8)]
                mm = p.mmg(fns, waits=(kt_tk,) + ((sc_prev[j],) if sc_prev[j] else (last_ev if kc == 0 else ())))
                ev = p.ve(lambda e, j=j, kc=kc: e.tensor_scalar(
                    scr[j][:, kc * 512:(kc + 1) * 512], ps[0:128, j * 512:(j + 1) * 512],
                    1.0 / 32.0, None, op0=ALU.mult), waits=(mm,))
                sc_prev[j] = ev
            kld_hist.append(sc_prev[3])
        pst = []
        for j in range(4):
            r1 = p.ve(lambda e, j=j: e.reduce_max(negm[:, j:j + 1], scr[j][:, 0:4096], axis=AX.X,
                      negate=True), waits=(sc_prev[3],) + ((pst[-1],) if pst else ()))
            ex = p.ve(lambda e, j=j: e.activation(pbuf[:, 0:4096], scr[j][:, 0:4096], AF.Exp,
                      bias=negm[:, j:j + 1], accum_out=stat[:, 0:1]), eng="scalar", waits=(r1,))
            rc = p.sve(lambda e, j=j: e.reciprocal(rden[:, j:j + 1], stat[:, 0:1]), waits=(ex,))
            pst.append(p.dma("sync", hb["p_h"][j * 128:(j + 1) * 128, :], pbuf[:, 0:4096], waits=(ex, rc)))
        avmm = []
        vld_hist = []
        for kc in range(32):
            ptb = ptt[kc % 2]
            vb = vtc[kc % 2]
            guard = (avmm[(kc - 2) * 4 + 3],) if kc >= 2 else (pst[-1],)
            pt_tk = p.dma("scalar", ptb, hb["p_h"][0:512, kc * 128:(kc + 1) * 128],
                          waits=guard + (pst[-1],), transpose=True)
            v_tk = p.dma("sync", vb, hb["vout_h"][kc * 128:(kc + 1) * 128, :], waits=guard)
            for j in range(4):
                fns = [lambda e, j=j, vc=vc, ptb=ptb, vb=vb, kc=kc: e.matmul(
                    ps[0:128, j * 1024 + vc * 512: j * 1024 + (vc + 1) * 512],
                    ptb[:, j * 128:(j + 1) * 128], vb[:, vc * 512:(vc + 1) * 512],
                    start=(kc == 0), stop=(kc == 31)) for vc in range(2)]
                mm = p.mmg(fns, waits=(pt_tk, v_tk))
                avmm.append(mm)
        evs = []
        for j in range(4):
            e1 = p.ve(lambda e, j=j, qt=qts[j]: e.scalar_tensor_tensor(
                xt[qt], ps[0:128, j * 1024:(j + 1) * 1024], rden[:, j:j + 1], xt[qt],
                op0=ALU.mult, op1=ALU.add), waits=(avmm[-1],))
            evs.append(e1)
        last_ev = tuple(evs) + (avmm[-1],)
    ar.pop()
    ar.pop()
    return evs[-1]



def emit_l1ffn_prep(p, ar, hb, per):
    ps = per["psum"]
    prev_done = ()
    for l in range(2):
        ar.push()
        wt = [ar.alloc(128, 5632, F32) for _ in range(2)]
        wq = ar.alloc(128, 5632, BF16)
        lgc = ar.alloc(128, 8, F32)
        lbc = ar.alloc(128, 8, F32)
        urow = ar.alloc(1, 5632, F32)
        crow = ar.alloc(1, 5632, F32)
        b1r = ar.alloc(1, 5632, F32)
        ocb = ar.alloc(128, 1, BF16)
        t0 = p.dma("sync", lgc, hb[f"f1lg{l}"][:, :])
        t1 = p.dma("sync", lbc, hb[f"f1lb{l}"][:, :])
        t2 = p.dma("sync", b1r, hb[f"f1b1p{l}"][:, :])
        m0 = p.ve(lambda e: e.memset(urow[0:1, 0:5632], 0.0), waits=prev_done)
        m1 = p.ve(lambda e: e.memset(ocb, 1.0))
        cw = p.sve(lambda e: e.tensor_copy(crow[0:1, 0:5632], b1r[0:1, 0:5632]), waits=(t2, m0))
        guard = []
        last = cw
        for dc in range(8):
            buf = wt[dc % 2]
            tk = p.dma("sync", buf, hb[f"f1w1p{l}"][dc * 128:(dc + 1) * 128, :],
                       waits=(guard[dc - 2],) if dc >= 2 else (t0, t1))
            q1 = p.ve(lambda e, b=buf, dc=dc: e.tensor_scalar(wq[:, 0:5632], b, lgc[:, dc:dc + 1], None,
                      op0=ALU.mult), waits=(tk, last))
            st = p.dma("sync", hb[f"w1p_h{l}"][dc * 128:(dc + 1) * 128, :], wq[:, 0:5632], waits=(q1,))
            acc = q1
            for fc in range(11):
                mmu = p.ve(lambda e, fc=fc: e.matmul(ps[0:1, 0:512], ocb,
                           wq[:, fc * 512:(fc + 1) * 512], start=True, stop=True), eng="tensor",
                           waits=(q1, m1, acc))
                ua = p.sve(lambda e, fc=fc: e.tensor_add(urow[0:1, fc * 512:(fc + 1) * 512],
                           urow[0:1, fc * 512:(fc + 1) * 512], ps[0:1, 0:512]), waits=(mmu,))
                mmc = p.ve(lambda e, fc=fc, b=buf, dc=dc: e.matmul(ps[0:1, 512:1024],
                           lbc[:, dc:dc + 1], b[:, fc * 512:(fc + 1) * 512], start=True, stop=True),
                           eng="tensor", waits=(ua,))
                acc = p.sve(lambda e, fc=fc: e.tensor_add(crow[0:1, fc * 512:(fc + 1) * 512],
                            crow[0:1, fc * 512:(fc + 1) * 512], ps[0:1, 512:1024]), waits=(mmc,))
            guard.append(acc)
            last = st
        su = p.dma("sync", hb[f"ucrow_h{l}"][0:1, :], urow[0:1, 0:5632], waits=(guard[-1],))
        sc2 = p.dma("sync", hb[f"ucrow_h{l}"][1:2, :], crow[0:1, 0:5632], waits=(guard[-1],))
        w2cb = [ar.alloc(128, 1024, BF16) for _ in range(2)]
        w2st = []
        for hc2 in range(22):
            b2_ = w2cb[hc2 % 2]
            tkc = p.dma("gpsimd", b2_, hb[f"f1w2{l}"][hc2 * 128:(hc2 + 1) * 128, :],
                        waits=(w2st[hc2 - 2],) if hc2 >= 2 else prev_done)
            w2st.append(p.dma("scalar", hb[f"w2b_h{l}"][hc2 * 128:(hc2 + 1) * 128, :], b2_,
                        waits=(tkc,)))
        p.raw("sync", lambda e: None, (su, sc2, last, w2st[-1], w2st[-2]), ())
        prev_done = (su, sc2, last, w2st[-1], w2st[-2])
        ar.pop()


def emit_l1ffn(p, ar, hb, per, xt):
    ps = per["psum"]
    identb = per["ident_b"]
    identf = per["ident_f"]
    onesr = per["onesrow_f"]
    ar.push()
    stat = ar.alloc(128, 16, F32)
    role1v = ar.alloc(128, 1, F32)
    ocb = ar.alloc(128, 1, BF16)
    orb = ar.alloc(1, 128, BF16)
    ocf = ar.alloc(128, 1, F32)
    lnt = ar.alloc(128, 1024, BF16)
    r0 = p.dma("sync", role1v, hb["role1"][:, :])
    mo = p.ve(lambda e: e.memset(ocb, 1.0))
    mo2 = p.ve(lambda e: e.memset(orb[0:1, 0:128], 1.0))
    mo3 = p.ve(lambda e: e.memset(ocf, 1.0))

    sp = []
    for i in range(NT):
        sp.append(p.dma("sync", hb["xsp_h"][i * 128:(i + 1) * 128, :], xt[i]))
    ar.push()
    sqj = ar.alloc(128, 1024, F32)
    prevq = None
    for i in range(NT):
        x = xt[i]
        r1 = p.ve(lambda e, x=x, s=stat: e.reduce_sum(s[:, 0:1], x, axis=AX.X), waits=(sp[i],))
        r2 = p.ve(lambda e, x=x, s=stat: e.activation(sqj[:, 0:1024], x, AF.Square,
                  accum_out=s[:, 1:2]), eng="scalar", waits=((prevq,) if prevq else ()))
        v = p.sve(lambda e, s=stat: e.tensor_scalar(s[:, 0:1], s[:, 0:1], 1.0 / D, None, op0=ALU.mult), waits=(r1, r2))
        v = p.sve(lambda e, s=stat: e.tensor_scalar(s[:, 1:2], s[:, 1:2], 1.0 / D, None, op0=ALU.mult))
        v = p.sve(lambda e, s=stat: e.tensor_tensor(s[:, 2:3], s[:, 0:1], s[:, 0:1], op=ALU.mult))
        v = p.sve(lambda e, s=stat: e.tensor_sub(s[:, 2:3], s[:, 1:2], s[:, 2:3]))
        v = p.sve(lambda e, s=stat: e.tensor_scalar_add(s[:, 3:4], s[:, 2:3], 1e-5))
        v = p.sve(lambda e, s=stat: e.reciprocal(s[:, 3:4], s[:, 3:4]))
        v = p.sve(lambda e, s=stat: e.activation(s[:, 4:5], s[:, 3:4], AF.Sqrt), eng="scalar", waits=(v,))
        v = p.sve(lambda e, s=stat: e.tensor_tensor(s[:, 5:6], s[:, 0:1], s[:, 4:5], op=ALU.mult), waits=(v,))
        v = p.sve(lambda e, s=stat: e.tensor_scalar(s[:, 5:6], s[:, 5:6], -1.0, None, op0=ALU.mult))
        prevq = p.sve(lambda e, x=x, s=stat: e.tensor_scalar(x, x, s[:, 4:5], s[:, 5:6],
                      op0=ALU.mult, op1=ALU.add), waits=(v,))
    ar.pop()
    x_ready = (prevq,)
    if F1_TRUNC <= 0:
        ar.pop()
        return prevq

    for l in range(2):
        xist = []
        for i in range(NT):
            c1 = p.ve(lambda e, i=i: e.tensor_copy(lnt[:, 0:1024], xt[i]),
                      waits=x_ready + ((xist[-1],) if xist else ()))
            xist.append(p.dma("sync", hb["xi_h"][i * 128:(i + 1) * 128, :], lnt[:, 0:1024], waits=(c1,)))
        ccA = p.collective("AllGather", ALU.bypass, [hb["xi_h"][1535:2048, :].opt()],
                           [hb["bndout_h"][:, :].opt()], waits=(xist[-1],))
        if F1_TRUNC <= 0.5 + 2 * l:
            ar.pop()
            return xist[-1]

        ar.push()
        vsb = [ar.alloc(128, 512, BF16) for _ in range(20)]
        oT = [ar.alloc(128, 2048, BF16) for _ in range(4)]
        ar.push()
        xT = [ar.alloc(128, 2576, BF16) for _ in range(8)]
        wkv = [ar.alloc(128, 1536, BF16) for _ in range(8)]
        qkvt = ar.alloc(128, 1536, F32)
        tbl = {nm: [ar.alloc(128, 64, BF16) for _ in range(20)] for nm in ("cq", "sq", "ck", "sk")}
        qsb = ar.alloc(128, 64, F32)
        ksb = ar.alloc(128, 64, F32)
        qsbb = ar.alloc(128, 64, BF16)
        ksbb = ar.alloc(128, 64, BF16)
        hnrm = ar.alloc(128, 16, F32)
        t1b = ar.alloc(128, 512, BF16)
        t2b = ar.alloc(128, 512, BF16)
        qout = ar.alloc(128, 512, BF16)
        ldw = tuple(p.dma("gpsimd", wkv[dc], hb[f"f1wqkv{l}"][dc * 128:(dc + 1) * 128, :]) for dc in range(8))
        ldt = []
        for nm in ("cq", "sq", "ck", "sk"):
            for ti in range(20):
                ldt.append(p.dma("gpsimd", tbl[nm][ti], hb[nm][ti * 128:(ti + 1) * 128, :]))
        ldq = p.dma("sync", qsb[0:1, 0:64], hb[f"f1qs{l}"][:, :])
        ldk = p.dma("sync", ksb[0:1, 0:64], hb[f"f1ks{l}"][:, :])
        bq = p.ve(lambda e: e.matmul(ps[0:128, 0:64], onesr[0:1, 0:128], qsb[0:1, 0:64],
                  start=True, stop=True), eng="tensor", waits=(ldq, ldk))
        bq2 = p.ve(lambda e: e.tensor_scalar(qsbb[:, 0:64], ps[0:128, 0:64], 8.0, None, op0=ALU.mult), waits=(bq,))
        bk = p.ve(lambda e: e.matmul(ps[0:128, 64:128], onesr[0:1, 0:128], ksb[0:1, 0:64],
                  start=True, stop=True), eng="tensor", waits=(bq2,))
        bk2 = p.ve(lambda e: e.tensor_copy(ksbb[:, 0:64], ps[0:128, 64:128]), waits=(bk,))
        fold = []
        fw = (bq2, bk2) + tuple(ldt)
        for ti in range(20):
            fold.append(p.ve(lambda e, ti=ti: e.tensor_tensor(tbl["cq"][ti], tbl["cq"][ti], qsbb[:, 0:64], op=ALU.mult), waits=fw))
            fold.append(p.ve(lambda e, ti=ti: e.tensor_tensor(tbl["sq"][ti], tbl["sq"][ti], qsbb[:, 0:64], op=ALU.mult)))
            fold.append(p.ve(lambda e, ti=ti: e.tensor_tensor(tbl["ck"][ti], tbl["ck"][ti], ksbb[:, 0:64], op=ALU.mult)))
            fold.append(p.ve(lambda e, ti=ti: e.tensor_tensor(tbl["sk"][ti], tbl["sk"][ti], ksbb[:, 0:64], op=ALU.mult)))
        fold_done = fold[-1]
        xtk = []
        for dc in range(8):
            xtk.append(p.dma("sync", xT[dc][:, 528:2576], hb["xi_h"][:, dc * 128:(dc + 1) * 128],
                             waits=(xist[-1],), transpose=True))
            xtk.append(p.dma("sync", xT[dc][:, 16:528], hb["bndout_h"][0:512, dc * 128:(dc + 1) * 128],
                             waits=(ccA,), transpose=True))
            xtk.append(p.dma("gpsimd", xT[dc][:, 15:16],
                             hb["bndout_h"][512:513, dc * 128:(dc + 1) * 128].rearrange("a b -> b a"),
                             waits=(ccA,)))
        msk = []
        for dc in range(8):
            msk.append(p.ve(lambda e, dc=dc: e.tensor_scalar(xT[dc][:, 15:528], xT[dc][:, 15:528],
                       role1v[:, 0:1], None, op0=ALU.mult), waits=tuple(xtk) + (r0,)))
        msk = tuple(msk)
        qkst = []
        prev_ev = None
        for ti in range(20):
            fns = []
            for dc in range(8):
                off = 16 if dc < 4 else 15
                lhs = xT[dc][:, off + ti * 128: off + ti * 128 + 128]
                for fc in range(3):
                    fns.append(lambda e, lhs=lhs, dc=dc, fc=fc: e.matmul(
                        ps[0:128, fc * 512:(fc + 1) * 512], lhs, wkv[dc][:, fc * 512:(fc + 1) * 512],
                        start=(dc == 0), stop=(dc == 7)))
            mm = p.mmg(fns, waits=msk + ldw + ((prev_ev,) if prev_ev else ()))
            ev = p.ve(lambda e: e.tensor_copy(qkvt[:, 0:1536], ps[0:128, 0:1536]), waits=(mm,))
            n6 = None
            for base, ct, st_, outh in ((0, "cq", "sq", "qh_h"), (512, "ck", "sk", "kh_h")):
                vv = p.ve(lambda e, base=base: e.activation(t1b[:, 0:512], qkvt[:, base:base + 512],
                          AF.Square), eng="scalar", waits=(ev,) + ((n6,) if n6 else ()))
                rr = p.sve(lambda e: e.tensor_reduce(hnrm[:, 0:8],
                           t1b[:, 0:512].rearrange("p (h d) -> p h d", d=64), axis=AX.X, op=ALU.add),
                           waits=(vv,))
                rr = p.sve(lambda e: e.activation(hnrm[:, 0:8], hnrm[:, 0:8], AF.Sqrt), eng="scalar", waits=(rr,))
                rr = p.sve(lambda e: e.tensor_scalar(hnrm[:, 0:8], hnrm[:, 0:8], 1e-12, None, op0=ALU.max), waits=(rr,))
                rr = p.sve(lambda e: e.reciprocal(hnrm[:, 8:16], hnrm[:, 0:8]))
                n1 = p.sve(lambda e, base=base: e.tensor_tensor(
                    t1b[:, 0:512].rearrange("p (h d) -> p h d", d=64),
                    qkvt[:, base:base + 512].rearrange("p (h d) -> p h d", d=64),
                    hnrm[:, 8:16].unsqueeze(2).broadcast_to((128, 8, 64)), op=ALU.mult), waits=(rr,))
                n2 = p.sve(lambda e: e.tensor_scalar(
                    t2b[:, 0:512].rearrange("p (h d) -> p h d", d=64)[:, :, 0:32],
                    t1b[:, 0:512].rearrange("p (h d) -> p h d", d=64)[:, :, 32:64],
                    -1.0, None, op0=ALU.mult), waits=(n1,))
                n3 = p.sve(lambda e: e.tensor_copy(
                    t2b[:, 0:512].rearrange("p (h d) -> p h d", d=64)[:, :, 32:64],
                    t1b[:, 0:512].rearrange("p (h d) -> p h d", d=64)[:, :, 0:32]))
                n4 = p.sve(lambda e, ti=ti, ct=ct: e.tensor_tensor(
                    t1b[:, 0:512].rearrange("p (h d) -> p h d", d=64),
                    t1b[:, 0:512].rearrange("p (h d) -> p h d", d=64),
                    tbl[ct][ti].unsqueeze(1).broadcast_to((128, 8, 64)), op=ALU.mult), waits=(n3, fold_done))
                n5 = p.sve(lambda e, ti=ti, st_=st_: e.tensor_tensor(
                    t2b[:, 0:512].rearrange("p (h d) -> p h d", d=64),
                    t2b[:, 0:512].rearrange("p (h d) -> p h d", d=64),
                    tbl[st_][ti].unsqueeze(1).broadcast_to((128, 8, 64)), op=ALU.mult))
                n6 = p.sve(lambda e: e.tensor_add(qout[:, 0:512], t1b[:, 0:512], t2b[:, 0:512]))
                qkst.append(p.dma("sync", hb[outh][ti * 128:(ti + 1) * 128, :], qout[:, 0:512], waits=(n6,)))
                p.raw("vector", lambda e: None, (qkst[-1],), ())
            prev_ev = p.ve(lambda e, ti=ti: e.tensor_copy(vsb[ti][:, 0:512], qkvt[:, 1024:1536]),
                           waits=(ev,) + (qkst[-1],))
        qkv_done = (prev_ev, qkst[-1])
        ar.pop()
        if F1_TRUNC <= 0.7 + 2 * l:
            ar.pop()
            ar.pop()
            return prev_ev
        ar.push()
        qhT = [ar.alloc(128, 2560, BF16) for _ in range(4)]
        khT = [ar.alloc(128, 2560, BF16) for _ in range(4)]
        lmsk = [ar.alloc(128, 1024, BF16) for _ in range(4)]
        lmskf = [ar.alloc(128, 1024, BF16) for _ in range(4)]
        psb = ar.alloc(128, 8192, BF16)
        ptb = [ar.alloc(128, 2048, BF16) for _ in range(2)]
        rden8 = ar.alloc(128, 8, F32)
        negm8 = ar.alloc(128, 8, F32)
        qtk = tuple(p.dma("sync", qhT[hc], hb["qh_h"][:, hc * 128:(hc + 1) * 128],
                          waits=qkv_done, transpose=True) for hc in range(4))
        ktk = tuple(p.dma("sync", khT[hc], hb["kh_h"][:, hc * 128:(hc + 1) * 128],
                          waits=qkv_done, transpose=True) for hc in range(4))
        mtk = tuple(p.dma("gpsimd", lmsk[q4], hb["lmask"][q4 * 128:(q4 + 1) * 128, :], waits=qkv_done) for q4 in range(4)) + \
              tuple(p.dma("gpsimd", lmskf[q4], hb["lmaskf"][q4 * 128:(q4 + 1) * 128, :], waits=qkv_done) for q4 in range(4))
        pstores = []
        for w in range(4):
            for q4 in range(4):
                exps = []
                for hd in range(8):
                    hc, hr = hd // 2, (hd % 2) * 64
                    qcol = 512 + w * 512 + q4 * 128
                    lm = lmskf if w == 0 else lmsk
                    fns = [lambda e, hc=hc, hr=hr, qcol=qcol, kk=kk, w=w: e.matmul(
                        ps[0:128, kk * 512:(kk + 1) * 512],
                        qhT[hc][hr:hr + 64, qcol:qcol + 128],
                        khT[hc][hr:hr + 64, w * 512 + kk * 512: w * 512 + (kk + 1) * 512],
                        start=True, stop=False) for kk in range(2)]
                    fns += [lambda e, q4=q4, kk=kk, lm=lm: e.matmul(
                        ps[0:128, kk * 512:(kk + 1) * 512], identb[:, 0:128],
                        lm[q4][:, kk * 512:(kk + 1) * 512], start=False, stop=True) for kk in range(2)]
                    mm = p.mmg(fns, waits=qtk + ktk + mtk + (tuple(exps[-2:]) if exps else
                               ((pstores[-1],) if pstores else ())))
                    r1 = p.ve(lambda e, hd=hd: e.reduce_max(negm8[:, hd:hd + 1], ps[0:128, 0:1024],
                              axis=AX.X, negate=True), waits=(mm,))
                    ex = p.ve(lambda e, hd=hd: e.activation(psb[:, hd * 1024:(hd + 1) * 1024],
                              ps[0:128, 0:1024], AF.Exp, bias=negm8[:, hd:hd + 1],
                              accum_out=stat[:, 0:1]), eng="scalar", waits=(r1,))
                    rc = p.sve(lambda e, hd=hd: e.reciprocal(rden8[:, hd:hd + 1], stat[:, 0:1]), waits=(ex,))
                    exps.append(rc)
                    exps.append(ex)
                nrm = p.ve(lambda e: e.tensor_tensor(
                    psb[:, 0:8192].rearrange("p (h k) -> p h k", k=1024),
                    psb[:, 0:8192].rearrange("p (h k) -> p h k", k=1024),
                    rden8[:, 0:8].unsqueeze(2).broadcast_to((128, 8, 1024)), op=ALU.mult),
                    waits=tuple(exps))
                pstores.append(p.dma("sync", hb["ph_h"][(w * 4 + q4) * 128:(w * 4 + q4 + 1) * 128, :],
                               psb[:, 0:8192], waits=(nrm,)))
                p.raw("vector", lambda e: None, (pstores[-1],), ())
        if F1_TRUNC <= 0.8 + 2 * l:
            ar.pop()
            ar.pop()
            ar.pop()
            return pstores[-1]
        avh = []
        prev_oev = None
        for hd in range(8):
            hc, hr = hd // 2, (hd % 2) * 64
            for kk in range(8):
                pb = ptb[kk % 2]
                idx = hd * 8 + kk
                ptk = p.dma("scalar", pb, hb["ph_h"][0:2048, hd * 1024 + kk * 128: hd * 1024 + (kk + 1) * 128],
                            waits=(pstores[-1],) + ((avh[idx - 2],) if idx >= 2 else ()), transpose=True)
                fns = [lambda e, w=w2, kk=kk, hd=hd, pb=pb: e.matmul(
                    ps[0:64, 2048 + w * 512: 2048 + (w + 1) * 512],
                    vsb[w * 4 + kk][:, hd * 64:hd * 64 + 64],
                    pb[:, w * 512:(w + 1) * 512], start=(kk == 0), stop=(kk == 7)) for w2 in range(4)]
                mm = p.mmg(fns, waits=(ptk,) + ((prev_oev,) if prev_oev and kk == 0 else ()))
                avh.append(mm)
            for w in range(4):
                oe = p.ve(lambda e, w=w, hc=hc, hr=hr: e.tensor_copy(
                    oT[hc][hr:hr + 64, w * 512:(w + 1) * 512],
                    ps[0:64, 2048 + w * 512:2048 + (w + 1) * 512]), waits=(avh[-1],))
            prev_oev = oe
        if F1_TRUNC <= 0.9 + 2 * l:
            ar.pop()
            ar.pop()
            ar.pop()
            return prev_oev
        ar.pop()
        ar.push()
        wosb = [ar.alloc(128, 1024, BF16) for _ in range(4)]
        wotk = tuple(p.dma("gpsimd", wosb[ic], hb[f"f1wo{l}"][ic * 128:(ic + 1) * 128, :],
                           waits=(prev_oev,)) for ic in range(4))
        prev = None
        for ti in range(NT):
            fns = []
            for ic in range(4):
                for vc2 in range(2):
                    fns.append(lambda e, ic=ic, vc2=vc2, ti=ti: e.matmul(
                        ps[0:128, vc2 * 512:(vc2 + 1) * 512], oT[ic][:, ti * 128:(ti + 1) * 128],
                        wosb[ic][:, vc2 * 512:(vc2 + 1) * 512], start=(ic == 0), stop=(ic == 3)))
            mm = p.mmg(fns, waits=wotk + (prev_oev,) + ((prev,) if prev else ()))
            prev = p.ve(lambda e, ti=ti: e.tensor_add(xt[ti], xt[ti], ps[0:128, 0:1024]), waits=(mm,))
        ar.pop()
        ar.pop()
        x_ready = (prev,)
        if F1_TRUNC <= 1 + 2 * l:
            ar.pop()
            return prev

        c1 = p.ve(lambda e: e.tensor_copy(lnt[:, 0:1024], xt[NT - 1]), waits=x_ready)
        sx1 = p.dma("gpsimd", hb["bnd1in_h"][0:1, :], lnt[127:128, 0:1024], waits=(c1,))
        ccB = p.collective("AllGather", ALU.bypass, [hb["bnd1in_h"][:, :].opt()],
                           [hb["bnd1out_h"][:, :].opt()], waits=(sx1,))

        xist = []
        for i in range(NT):
            cc = p.ve(lambda e, i=i: e.tensor_copy(lnt[:, 0:1024], xt[i]),
                      waits=x_ready + (c1, sx1) + ((xist[-1],) if xist else ()))
            xist.append(p.dma("sync", hb["xi_h"][i * 128:(i + 1) * 128, :], lnt[:, 0:1024], waits=(cc,)))
        if F1_TRUNC <= 1.5 + 2 * l:
            ar.pop()
            return xist[-1]
        ar.push()
        xT2 = [ar.alloc(128, 2064, BF16) for _ in range(8)]
        statrows = ar.alloc(1, 6144, F32)
        murow = statrows[0:1, 0:2048]
        msqrow = statrows[0:1, 2048:4096]
        rvrow = statrows[0:1, 4096:6144]
        brows = ar.alloc(1, 4096, BF16)
        nmurow = brows[0:1, 0:2048]
        rvirow = brows[0:1, 2048:4096]
        rvsq = ar.alloc(16, 128, F32)
        rvcol = ar.alloc(128, 16, F32)
        sqsc = ar.alloc(128, 512, F32)
        ucrow_t = ar.alloc(1, 11264, BF16)
        ucrow = ucrow_t[0:1, 0:5632]
        ccrow = ucrow_t[0:1, 5632:11264]
        hhtile = [ar.alloc(128, 1024, BF16) for _ in range(2)]
        h1p = ar.alloc(128, 512, F32)
        ggs = ar.alloc(128, 256, F32)
        global F1_REFS
        F1_REFS = dict(xT2=xT2, statrows=statrows, rvcol=rvcol, ucrow=ucrow, ccrow=ccrow,
                       hhtile=hhtile, h1p=h1p, ggs=ggs)
        xtk2 = []
        for dc in range(8):
            xtk2.append(p.dma("sync", xT2[dc][:, 16:2064], hb["xi_h"][:, dc * 128:(dc + 1) * 128],
                              waits=(xist[-1],), transpose=True))
            xtk2.append(p.dma("gpsimd", xT2[dc][:, 15:16],
                              hb["bnd1out_h"][0:1, dc * 128:(dc + 1) * 128].rearrange("a b -> b a"),
                              waits=(ccB,)))
        msk2 = []
        for dc in range(8):
            msk2.append(p.ve(lambda e, dc=dc: e.tensor_scalar(xT2[dc][:, 15:16], xT2[dc][:, 15:16],
                        role1v[:, 0:1], None, op0=ALU.mult), waits=tuple(xtk2)))
        msk2 = tuple(msk2)
        lduc0 = p.dma("gpsimd", ucrow, hb[f"ucrow_h{l}"][0:1, :])
        lduc = p.dma("gpsimd", ccrow, hb[f"ucrow_h{l}"][1:2, :])

        def hsl(dc, c0, c1):
            off = 16 if dc < 4 else 15
            return xT2[dc][:, off + c0: off + c1]

        stev = []
        for tc in range(4):
            fns = [lambda e, dc=dc, tc=tc: e.matmul(ps[0:1, 0:512], ocb,
                   hsl(dc, tc * 512, (tc + 1) * 512), start=(dc == 0), stop=(dc == 7)) for dc in range(8)]
            mm1_ = p.mmg(fns, waits=msk2 + (mo,) + ((stev[-1],) if stev else ()))
            stev.append(p.ve(lambda e, tc=tc: e.tensor_scalar(murow[0:1, tc * 512:(tc + 1) * 512],
                        ps[0:1, 0:512], 1.0 / D, None, op0=ALU.mult), waits=(mm1_,)))
        for tc in range(4):
            sq_p = None
            for dc in range(8):
                sq = p.ve(lambda e, dc=dc, tc=tc: e.activation(sqsc[:, 0:512],
                          hsl(dc, tc * 512, (tc + 1) * 512), AF.Square), eng="scalar",
                          waits=((sq_p,) if sq_p else (stev[-1],)))
                sq_p = p.ve(lambda e, dc=dc: e.matmul(ps[0:1, 512:1024], ocf, sqsc[:, 0:512],
                            start=(dc == 0), stop=(dc == 7)), eng="tensor", waits=(sq, mo3))
            stev.append(p.ve(lambda e, tc=tc: e.tensor_scalar(msqrow[0:1, tc * 512:(tc + 1) * 512],
                        ps[0:1, 512:1024], 1.0 / D, None, op0=ALU.mult), waits=(sq_p,)))
        v = p.sve(lambda e: e.tensor_tensor(rvrow[0:1, 0:2048], murow[0:1, 0:2048], murow[0:1, 0:2048],
                  op=ALU.mult), waits=tuple(stev))
        v = p.sve(lambda e: e.tensor_sub(msqrow[0:1, 0:2048], msqrow[0:1, 0:2048], rvrow[0:1, 0:2048]))
        v = p.sve(lambda e: e.tensor_scalar_add(msqrow[0:1, 0:2048], msqrow[0:1, 0:2048], 1e-5))
        v = p.sve(lambda e: e.activation(rvirow[0:1, 0:2048], msqrow[0:1, 0:2048], AF.Sqrt), eng="scalar", waits=(v,))
        v = p.sve(lambda e: e.reciprocal(rvrow[0:1, 0:2048], msqrow[0:1, 0:2048]), waits=(v,))
        v = p.sve(lambda e: e.activation(rvrow[0:1, 0:2048], rvrow[0:1, 0:2048], AF.Sqrt), eng="scalar", waits=(v,))
        v = p.sve(lambda e: e.tensor_scalar(nmurow[0:1, 0:2048], murow[0:1, 0:2048], -1.0, None, op0=ALU.mult), waits=(v,))
        if A0_DEBUG:
            p.dma("sync", hb["dbgf1_h"][0:1, :], murow[0:1, 0:2048], waits=(v,))
            p.dma("sync", hb["dbgf1_h"][1:2, :], msqrow[0:1, 0:2048], waits=(v,))
        s1 = p.dma("sync", hb["rv_h"][0:1, :], rvrow[0:1, 0:2048], waits=(v,))
        s2 = p.dma("sync", rvsq, hb["rv_h"].rearrange("o (r c) -> (o r) c", c=128), waits=(s1,))
        tr = p.ve(lambda e: e.transpose(ps[0:128, 1024:1040], rvsq[0:16, 0:128], identf[0:16, 0:16]),
                  eng="tensor", waits=(s2,))
        trc = p.ve(lambda e: e.tensor_copy(rvcol[:, 0:16], ps[0:128, 1024:1040]), waits=(tr,))

        ar.push()
        w1h = [ar.alloc(128, 1536, BF16) for _ in range(8)]
        F1_REFS['w1h'] = w1h
        hh_done = []
        e1_last = [None, None]
        st_last = [None, None]
        e3_prev = None
        mm_prev_half = None
        gt = 0
        for half, (f0, nfc) in enumerate(((0, 3), (3, 3), (6, 3), (9, 2))):
            w1tk = tuple(p.dma("sync", w1h[dc][:, 0:nfc * 512],
                               hb[f"w1p_h{l}"][dc * 128:(dc + 1) * 128, f0 * 512:(f0 + nfc) * 512],
                               waits=((mm_prev_half,) if mm_prev_half else (trc,))) for dc in range(8))
            for ti in range(NT):
                par = gt % 2
                pb_ = par * 2048
                fns = []
                for dc in range(8):
                    off = 16 if dc < 4 else 15
                    lhs = xT2[dc][:, off + ti * 128: off + ti * 128 + 128]
                    for fc in range(nfc):
                        fns.append(lambda e, lhs=lhs, dc=dc, fc=fc, pb_=pb_: e.matmul(
                            ps[0:128, pb_ + fc * 512:pb_ + (fc + 1) * 512], lhs, w1h[dc][:, fc * 512:(fc + 1) * 512],
                            start=(dc == 0), stop=False))
                for fc in range(nfc):
                    fns.append(lambda e, fc=fc, f0=f0, ti=ti, pb_=pb_: e.matmul(
                        ps[0:128, pb_ + fc * 512:pb_ + (fc + 1) * 512], nmurow[0:1, ti * 128:(ti + 1) * 128],
                        ucrow[0:1, (f0 + fc) * 512:(f0 + fc + 1) * 512], start=False, stop=False))
                    fns.append(lambda e, fc=fc, f0=f0, ti=ti, pb_=pb_: e.matmul(
                        ps[0:128, pb_ + fc * 512:pb_ + (fc + 1) * 512], rvirow[0:1, ti * 128:(ti + 1) * 128],
                        ccrow[0:1, (f0 + fc) * 512:(f0 + fc + 1) * 512], start=False, stop=(True)))
                mmw = w1tk + (lduc0, lduc, trc)
                if e1_last[par] is not None:
                    mmw += (e1_last[par],)
                mm = p.mmg(fns, waits=mmw)
                hb_ = hhtile[par]
                hev = []
                dbgT = []
                for fc in range(nfc):
                    e1w = (mm,) + (tuple(hev[-1:]) if hev else ()) + tuple(dbgT[-1:])
                    if fc == 0 and st_last[par] is not None:
                        e1w += (st_last[par],)
                    if fc == 0 and e3_prev is not None:
                        e1w += (e3_prev,)
                    e1 = p.ve(lambda e, fc=fc, ti=ti, pb_=pb_: e.tensor_scalar(h1p[:, 0:512],
                              ps[0:128, pb_ + fc * 512:pb_ + (fc + 1) * 512], rvcol[:, ti:ti + 1], None,
                              op0=ALU.mult), waits=e1w)
                    if A0_DEBUG and half == 0 and ti == 0:
                        dbgT.append(p.dma("sync", hb["dbgf1_h"][8 + fc * 128: 8 + (fc + 1) * 128, 0:512],
                                          h1p[:, 0:512], waits=(e1,)))
                    e2w = (e1,)
                    if fc == 0 and e3_prev is not None:
                        e2w += (e3_prev,)
                    e2 = p.ve(lambda e: e.activation(ggs[:, 0:256], h1p[:, 1:512:2],
                              AF.Gelu), eng="scalar", waits=e2w)
                    if A0_DEBUG and half == 0 and ti == 0:
                        dbgT.append(p.dma("sync", hb["dbgf1_h"][520 + fc * 128: 520 + (fc + 1) * 128, 0:256],
                                          ggs[:, 0:256], waits=(e2,)))
                    e3 = p.ve(lambda e, fc=fc, hb_=hb_: e.tensor_tensor(hb_[:, fc * 256:(fc + 1) * 256],
                              h1p[:, 0:512:2], ggs[:, 0:256], op=ALU.mult), waits=(e2,) + tuple(dbgT[-1:]))
                    hev.append(e3)
                    e1_last[par] = e1
                e3_prev = hev[-1]
                if A0_DEBUG and half == 0 and ti == 0:
                    dbgT.append(p.dma("sync", hb["dbgf1_h"][1032:1160, 0:512],
                                      hb_[:, 0:1024].bitcast(F32), waits=tuple(hev)))
                st = p.dma("sync", hb["hh_h"][ti * 128:(ti + 1) * 128, f0 * 256:(f0 + nfc) * 256],
                           hb_[:, 0:nfc * 256], waits=tuple(hev) + tuple(dbgT[-1:]))
                st_last[par] = st
                hh_done.append(st)
                gt += 1
            mm_prev_half = mm
        ar.pop()
        ar.push()
        hhT = [ar.alloc(128, 512, BF16) for _ in range(4)]
        w2s = [ar.alloc(128, 1024, BF16) for _ in range(2)]
        b2r = ar.alloc(1, 1024, BF16)
        F1_REFS['hhT'] = hhT; F1_REFS['w2s'] = w2s; F1_REFS['b2r'] = b2r
        ldb2 = p.dma("gpsimd", b2r, hb[f"f1b2{l}"][:, :], waits=(hh_done[-1],))
        prev2 = None
        for tg in range(4):
            tis = [tg * 4 + j for j in range(4)]
            mm_l = None
            w2g = []
            for hc in range(22):
                wb = w2s[hc % 2]
                hcb = hhT[hc % 4]
                gw = (w2g[hc - 2],) if hc >= 2 else ((prev2,) if prev2 else (hh_done[-1],))
                wtk = p.dma("sync", wb, hb[f"w2b_h{l}"][hc * 128:(hc + 1) * 128, :], waits=gw)
                gh = (w2g[hc - 4],) if hc >= 4 else ((prev2,) if prev2 else (hh_done[-1],))
                htk = p.dma("scalar", hcb, hb["hh_h"][tg * 512:(tg + 1) * 512, hc * 128:(hc + 1) * 128],
                            waits=gh + (hh_done[-1],), transpose=True)
                fns = []
                for j in range(4):
                    for vc2 in range(2):
                        fns.append(lambda e, j=j, vc2=vc2, hc=hc, hcb=hcb, wb=wb: e.matmul(
                            ps[0:128, j * 1024 + vc2 * 512: j * 1024 + (vc2 + 1) * 512],
                            hcb[:, j * 128:(j + 1) * 128], wb[:, vc2 * 512:(vc2 + 1) * 512],
                            start=(hc == 0), stop=False))
                mm_l = p.mmg(fns, waits=(wtk, htk) + ((prev2,) if prev2 and hc == 0 else ()))
                w2g.append(mm_l)
            fns = []
            for j in range(4):
                for vc2 in range(2):
                    fns.append(lambda e, j=j, vc2=vc2: e.matmul(
                        ps[0:128, j * 1024 + vc2 * 512: j * 1024 + (vc2 + 1) * 512],
                        orb[0:1, 0:128], b2r[0:1, vc2 * 512:(vc2 + 1) * 512],
                        start=False, stop=True))
            mm_l = p.mmg(fns, waits=(ldb2, mo2))
            evl = []
            for j in range(4):
                evl.append(p.ve(lambda e, j=j, ti=tis[j]: e.tensor_add(
                    xt[ti], xt[ti], ps[0:128, j * 1024:(j + 1) * 1024]), waits=(mm_l,)))
            prev2 = evl[-1]
        ar.pop()
        ar.pop()
        x_ready = (prev2,)
        if F1_TRUNC <= 2 + 2 * l:
            ar.pop()
            return prev2

    ar.push()
    xo = [ar.alloc(128, 1024, F32) for _ in range(2)]
    fin = None
    for i in range(NT):
        ld = p.dma("sync", xo[i % 2], hb["xsp_h"][i * 128:(i + 1) * 128, :],
                   waits=x_ready + ((fin,) if fin and i >= 2 else ()))
        fin = p.ve(lambda e, i=i, b=xo[i % 2]: e.scalar_tensor_tensor(
            xt[i], xt[i], 0.5, b, op0=ALU.mult, op1=ALU.add), waits=(ld,))
    ar.pop()
    ar.pop()
    return fin


# version 03
# speedup vs baseline: 1.0884x; 1.0884x over previous
import contextlib
import numpy as np
import ml_dtypes
import concourse.bass as bass
import concourse.mybir as mybir
from concourse.bass_utils import run_bass_kernel_spmd

F32 = mybir.dt.float32
BF16 = mybir.dt.bfloat16
AF = mybir.ActivationFunctionType
ALU = mybir.AluOpType
AX = mybir.AxisListType
NPBF = ml_dtypes.bfloat16
MAGIC = float(np.float32(1.5 * 2 ** 23))
NEG = float(np.float32(-1e38))

NCORE = 8
TLOC = 2048
NT = TLOC // 128
D = 1024
RG = [[0, 1], [2, 3], [4, 5], [6, 7]]

ENGINES = ("sync", "gpsimd", "vector", "scalar", "tensor")
QSEM = {"sync": "qs", "gpsimd": "qg", "scalar": "qa"}
ESEM = {"vector": "tv", "scalar": "ts", "tensor": "tt", "gpsimd": "tg"}


class Prog:

    def __init__(self, nc, stack):
        self.nc = nc
        self.ops = {e: [] for e in ENGINES}
        self.sems = {}
        self.cnt = {}
        for s in ("qs", "qg", "qa", "tv", "ts", "tt", "tg", "cc"):
            self.sems[s] = stack.enter_context(nc.semaphore(s))
            self.cnt[s] = 0

    def raw(self, eng, fn, waits=(), incs=()):
        w = tuple(x for x in waits if x is not None)
        self.ops[eng].append((fn, w, tuple(incs)))

    def _tick(self, sem, n):
        self.cnt[sem] += n
        return (sem, self.cnt[sem])

    def dma(self, q, out, in_, waits=(), transpose=False):
        sem = QSEM[q]
        t = self._tick(sem, 16)
        if transpose:
            fn = lambda e, out=out, in_=in_: e.dma_start(out=out, in_=in_, transpose=True)
        else:
            fn = lambda e, out=out, in_=in_: e.dma_start(out=out, in_=in_)
        self.raw(q, fn, waits, [(sem, 16)])
        return t

    def ve(self, fn, waits=(), eng="vector"):
        sem = ESEM[eng]
        t = self._tick(sem, 1)
        self.raw(eng, fn, waits, [(sem, 1)])
        return t

    def sve(self, fn, waits=(), eng="vector"):
        sem = ESEM[eng]
        w = tuple(waits) + ((sem, self.cnt[sem]),)
        return self.ve(fn, waits=w, eng=eng)

    def quiet(self, eng, fn, waits=()):
        self.raw(eng, fn, waits, ())

    def mmg(self, fns, waits=()):
        for i, fn in enumerate(fns):
            if i == 0 and i == len(fns) - 1:
                return self.ve(fn, waits=waits, eng="tensor")
            if i == 0:
                self.quiet("tensor", fn, waits=waits)
            elif i == len(fns) - 1:
                return self.ve(fn, eng="tensor")
            else:
                self.quiet("tensor", fn)

    def collective(self, kind, op, ins, outs, waits=()):
        t = self._tick("cc", 1)
        fn = lambda e, kind=kind, op=op, ins=ins, outs=outs: e.collective_compute(
            kind, op, RG, ins=ins, outs=outs)
        self.raw("gpsimd", fn, waits, [("cc", 1)])
        return t

    def now(self, sem):
        return (sem, self.cnt[sem])

    def barrier(self):
        marks = [self.now(s) for s in ("qs", "qg", "qa", "tv", "ts", "tt", "tg", "cc")]
        marks = tuple(m for m in marks if m[1] > 0)
        for e in ENGINES:
            self.raw(e, lambda eng: None, marks, ())

    def check(self):
        vals = {s: 0 for s in self.sems}
        ptr = {e: 0 for e in ENGINES}
        wi = {e: 0 for e in ENGINES}
        total = sum(len(v) for v in self.ops.values())
        done = 0
        progress = True
        while progress:
            progress = False
            for e in ENGINES:
                while ptr[e] < len(self.ops[e]):
                    fn, waits, incs = self.ops[e][ptr[e]]
                    ok = True
                    while wi[e] < len(waits):
                        s, v = waits[wi[e]]
                        if vals[s] >= v:
                            wi[e] += 1
                        else:
                            ok = False
                            break
                    if not ok:
                        break
                    for s, n in incs:
                        vals[s] += n
                    ptr[e] += 1
                    wi[e] = 0
                    done += 1
                    progress = True
        if done != total:
            stuck = {e: (ptr[e], len(self.ops[e]),
                         self.ops[e][ptr[e]][1][wi[e]:] if ptr[e] < len(self.ops[e]) else None)
                     for e in ENGINES}
            raise RuntimeError(f"deadlock: {done}/{total}; stuck={stuck}; sems={vals}")

    def _insert_checkpoints(self):
        qeng = {"qs": "sync", "qg": "gpsimd", "qa": "scalar", "cc": "gpsimd"}
        waited = {}
        for e in ENGINES:
            for fn, waits, incs in self.ops[e]:
                for s, v in waits:
                    if s in qeng:
                        waited.setdefault(s, set()).add(v)
        for s, vals in waited.items():
            e = qeng[s]
            newops = []
            cum = 0
            for fn, waits, incs in self.ops[e]:
                newops.append((fn, waits, incs))
                for ss, n in incs:
                    if ss == s:
                        cum += n
                        if cum in vals:
                            newops.append((lambda eng: None, ((s, cum),), ()))
            self.ops[e] = newops

    def emit(self, block):
        self._insert_checkpoints()
        self.check()
        for ename in ENGINES:
            ops = self.ops[ename]
            sems = self.sems

            def body(eng, ops=ops, sems=sems):
                for fn, waits, incs in ops:
                    for s, v in waits:
                        eng.wait_ge(sems[s], v)
                    inst = fn(eng)
                    for s, n in incs:
                        inst.then_inc(sems[s], n)

            getattr(block, ename)(body)


class Arena:
    def __init__(self, tensor, words):
        self.t = tensor
        self.limit = words
        self.off = 0
        self.marks = []

    def push(self):
        self.marks.append(self.off)

    def pop(self):
        self.off = self.marks.pop()

    def alloc(self, parts, free, dtype):
        esz = 2 if dtype == BF16 else 4
        off_b = (self.off * 4 + 31) // 32 * 32
        words = (free * esz + 3) // 4
        self.off = off_b // 4 + words
        assert self.off <= self.limit, f"arena overflow {self.off} > {self.limit}"
        base = self.t[0:parts, off_b // 4: off_b // 4 + words]
        if dtype == BF16:
            return base.bitcast(BF16)[0:parts, 0:free]
        return base[0:parts, 0:free]


def _f32(x):
    return np.ascontiguousarray(np.asarray(x, np.float32))



def prep_host(inputs):
    sh = {}
    sh["l0w1t"] = _f32(inputs["L0_ffn_w1"]).T.copy()
    sh["l0w2t"] = _f32(inputs["L0_ffn_w2"]).T.copy()
    sh["l0b1"] = _f32(inputs["L0_ffn_b1"]).reshape(1, -1)
    sh["l0b2"] = _f32(inputs["L0_ffn_b2"]).reshape(1, -1)
    sh["a0wq"] = _f32(inputs["L0_attn_Wq"])
    sh["a0wk"] = _f32(inputs["L0_attn_Wk"])
    sh["a0wv"] = _f32(inputs["L0_attn_Wv"])
    sh["a0wo"] = _f32(inputs["L0_attn_Wo"])
    for l in range(4):
        for nm in ("UL", "VR", "UR", "VL"):
            sh[f"a0{nm}{l}t"] = _f32(inputs[f"L0_attn_{nm}{l}"]).T.copy()
    sh["l1wqkvt"] = _f32(inputs["L1_attn_W_qkv"]).T.copy()
    _b = _f32(inputs["L1_attn_b_qkv"])
    sh["l1bqkv"] = _b.reshape(24, 128).T.copy()
    sh["l1bv"] = _b[2048:].reshape(1, 1024).copy()
    for l in range(2):
        sh[f"f1wqkv{l}"] = _f32(inputs["L1_ffn_wqkv"][l])
        sh[f"f1wo{l}"] = _f32(inputs["L1_ffn_wo"][l])
        w1 = _f32(inputs["L1_ffn_w1"][l])
        b1 = _f32(inputs["L1_ffn_b1"][l])
        wp = np.zeros((1024, 5632), np.float32)
        bp = np.zeros((1, 5632), np.float32)
        wp[:, 0:5460:2] = w1[:, 0:2730]
        wp[:, 1:5461:2] = w1[:, 2730:5460]
        bp[0, 0:5460:2] = b1[0:2730]
        bp[0, 1:5461:2] = b1[2730:5460]
        sh[f"f1w1p{l}"] = wp
        sh[f"f1b1p{l}"] = bp
        w2 = _f32(inputs["L1_ffn_w2"][l])
        sh[f"f1w2{l}"] = np.concatenate([w2, np.zeros((86, 1024), np.float32)], 0)
        sh[f"f1b2{l}"] = _f32(inputs["L1_ffn_b2"][l]).reshape(1, -1)
        sh[f"f1lg{l}"] = _f32(inputs["L1_ffn_ln_g"][l]).reshape(8, 128).T.copy()
        sh[f"f1lb{l}"] = _f32(inputs["L1_ffn_ln_b"][l]).reshape(8, 128).T.copy()
        sh[f"f1qs{l}"] = _f32(inputs["L1_ffn_q_scale"][l]).reshape(1, -1)
        sh[f"f1ks{l}"] = _f32(inputs["L1_ffn_k_scale"][l]).reshape(1, -1)

    sh["identity"] = np.eye(128, dtype=np.float32)
    sh["onesrow"] = np.ones((1, 512), np.float32)
    sh["onescol"] = np.ones((128, 1), np.float32)
    idx = np.arange(128)
    sh["cmask"] = (idx[:, None] <= idx[None, :]).astype(np.float32)
    ga = np.zeros((64, 47 * 64), np.float32)
    gb = np.zeros((64, 47 * 64), np.float32)
    gb2 = np.zeros((64, 47 * 64), np.float32)
    for n in range(16):
        blk = slice(4 * n, 4 * n + 4)
        ga[blk, 64 * n + 4 * n: 64 * n + 4 * n + 4] = np.eye(4)
        gb[blk, 64 * n + 4 * n: 64 * n + 4 * n + 4] = np.eye(4)
        gb2[blk, 64 * n + 4 * n: 64 * n + 4 * n + 4] = np.eye(4)
    gb[:, 46 * 64:47 * 64] = np.eye(64)
    gb2[:, 46 * 64:47 * 64] = np.eye(64)
    sh["gat"] = ga
    sh["gbt"] = gb
    sh["gbt2"] = gb2

    n = 4096
    half = np.arange(0, 64, 2, dtype=np.float32)
    inv_freq = 1.0 / 10000.0 ** (half / 64.0)
    tt = np.arange(n, dtype=np.float32)
    freqs = tt[:, None] * inv_freq[None, :]
    freqs = np.concatenate([freqs, freqs], -1)
    base = (half + 0.4 * 64) / (1.4 * 64)
    power = (tt - n // 2) / 256.0
    scale = base[None, :] ** power[:, None]
    scale = np.concatenate([scale, scale], -1).astype(np.float32)
    cos = np.cos(freqs).astype(np.float32)
    sin = np.sin(freqs).astype(np.float32)

    def lmask(qo, first):
        i = 512 + qo * 128 + np.arange(128)
        j = np.arange(1024)
        m = (i[:, None] < j[None, :]) | (i[:, None] > j[None, :] + 512)
        if first:
            m = m | (j[None, :] < 512)
        return np.where(m, np.float32(NEG), np.float32(0.0)).astype(np.float32)

    per_core = []
    for c in range(NCORE):
        h = c % 2
        t0 = h * 2048
        pc = {}
        pc["role0"] = np.full((128, 1), 1.0 if h == 0 else 0.0, np.float32)
        pc["role1"] = np.full((128, 1), 1.0 if h == 1 else 0.0, np.float32)
        pos = np.clip(np.arange(t0 - 512, t0 + TLOC), 0, n - 1)
        pc["cq"] = (cos * scale)[pos]
        pc["sq"] = (sin * scale)[pos]
        pc["ck"] = (cos / scale)[pos]
        pc["sk"] = (sin / scale)[pos]
        mk, mkf = [], []
        for qo in range(4):
            mk.append(lmask(qo, False))
            mkf.append(lmask(qo, h == 0))
        pc["lmask"] = np.concatenate(mk, 0)
        pc["lmaskf"] = np.concatenate(mkf, 0)
        per_core.append(pc)
    return sh, per_core



def emit_l0ffn_prep(p, ar, hb, per):
    ps = per["psum"]
    sw = per["swslots"]
    onescol = per["onescol"]
    onesr = per["onesrow_f"]
    for src, dst, rows, cols, slot in (("l0w1t", "wq1t", 1024, 4096, 0),
                                       ("l0w2t", "wq2t", 4096, 1024, 1)):
        ntile = rows // 128
        ar.push()
        wtile = [ar.alloc(128, cols, F32) for _ in range(2)]
        qtile = ar.alloc(128, cols, BF16)
        acc = ar.alloc(128, 1, F32)
        part = ar.alloc(128, 1, F32)
        mval = ar.alloc(1, 1, F32)
        p.ve(lambda e, a=acc: e.memset(a, 0.0))
        red = []
        for i in range(ntile):
            buf = wtile[i % 2]
            w = (red[i - 2],) if i >= 2 else ()
            tk = p.dma("sync", buf, hb[src][i * 128:(i + 1) * 128, :], waits=w)
            r1 = p.ve(lambda e, b=buf, pa=part: e.reduce_sum(pa, b, axis=AX.X, apply_absolute_value=True),
                      waits=(tk,))
            r2 = p.sve(lambda e, a=acc, pa=part: e.tensor_add(a, a, pa))
            red.append(r2)
        mm = p.ve(lambda e, a=acc, oc=onescol, o=ps[0:1, 0:1]: e.matmul(o, a, oc, start=True, stop=True),
                  waits=(red[-1],), eng="tensor")
        v1 = p.sve(lambda e, m=mval, o=ps[0:1, 0:1], s=1.0 / (rows * cols):
                  e.tensor_scalar(m, o, s, 1e-5, op0=ALU.mult, op1=ALU.max), waits=(mm,))
        mm2 = p.ve(lambda e, o=ps[0:128, 4:5], r=onesr, m=mval:
                   e.matmul(o, r, m, start=True, stop=True), waits=(v1,), eng="tensor")
        swv = sw[:, slot:slot + 1]
        v2 = p.sve(lambda e, s=swv, o=ps[0:128, 4:5]: e.tensor_copy(s, o), waits=(mm2,))
        swr = sw[:, slot + 2:slot + 3]
        v3 = p.sve(lambda e, s=swr, sv=swv: e.reciprocal(s, sv), waits=(v2,))
        last_q = v3
        for i in range(ntile):
            buf = wtile[i % 2]
            tk = p.dma("sync", buf, hb[src][i * 128:(i + 1) * 128, :],
                       waits=(v3,) if i < 2 else (ld_guard[i - 2],))
            a1 = p.ve(lambda e, b=buf, s=swr: e.tensor_scalar(b, b, s, None, op0=ALU.mult),
                      waits=(tk, last_q))
            a2 = p.ve(lambda e, b=buf: e.tensor_scalar(b, b, -1.0, 1.0, op0=ALU.max, op1=ALU.min))
            a3 = p.ve(lambda e, q=qtile, b=buf: e.tensor_scalar(q, b, MAGIC, -MAGIC, op0=ALU.add, op1=ALU.add))
            st = p.dma("sync", hb[dst][i * 128:(i + 1) * 128, :], qtile, waits=(a3,))
            if i == 0:
                ld_guard = []
            ld_guard.append(a2)
            last_q = st
        p.raw("sync", lambda e: None, (last_q,), ())
        ar.pop()


def emit_l0ffn(p, ar, hb, per, xt):
    ps = per["psum"]
    sw = per["swslots"]
    onesr = per["onesrow_f"]
    ar.push()
    stat = ar.alloc(128, 16, F32)
    c1s = ar.alloc(128, 16, F32)
    c2s = ar.alloc(128, 16, F32)

    ar.push()
    W1 = [ar.alloc(128, 4096, BF16) for _ in range(8)]
    xq1T = [ar.alloc(128, 2048, BF16) for _ in range(8)]
    gbuf = ar.alloc(128, 4096, BF16)
    xnq = ar.alloc(128, 2048, F32)
    qt = ar.alloc(128, 2048, BF16)
    b1bc = ar.alloc(128, 4096, BF16)
    brow = ar.alloc(1, 512, F32)

    w1tk = [p.dma("sync", W1[dc], hb["wq1t"][dc * 128:(dc + 1) * 128, :]) for dc in range(8)]

    prev = None
    for j in range(8):
        lt = p.dma("gpsimd", brow, hb["l0b1"][0:1, j * 512:(j + 1) * 512], waits=(prev,) if prev else ())
        mmb = p.ve(lambda e, b=brow: e.matmul(ps[0:128, 0:512], onesr, b, start=True, stop=True),
                   eng="tensor", waits=(lt,) + ((prev,) if prev else ()))
        prev = p.ve(lambda e, j=j: e.tensor_copy(b1bc[:, j * 512:(j + 1) * 512], ps[0:128, 0:512]),
                    waits=(mmb,))
    bias1_done = prev

    xq1_done = []
    prevq = None
    for i in range(NT):
        x = xt[i]
        r1 = p.sve(lambda e, x=x, s=stat: e.reduce_sum(s[:, 0:1], x, axis=AX.X))
        r2 = p.sve(lambda e, x=x, s=stat, o=xnq[:, 0:1024]: e.activation(o, x, AF.Square, accum_out=s[:, 1:2]),
                  eng="scalar", waits=(prevq,) if prevq else ())
        v = p.sve(lambda e, s=stat: e.tensor_scalar(s[:, 0:1], s[:, 0:1], 1.0 / D, None, op0=ALU.mult), waits=(r1, r2))
        v = p.sve(lambda e, s=stat: e.tensor_scalar(s[:, 1:2], s[:, 1:2], 1.0 / D, None, op0=ALU.mult))
        v = p.sve(lambda e, s=stat: e.tensor_tensor(s[:, 2:3], s[:, 0:1], s[:, 0:1], op=ALU.mult))
        v = p.sve(lambda e, s=stat: e.tensor_sub(s[:, 2:3], s[:, 1:2], s[:, 2:3]))
        v = p.sve(lambda e, s=stat: e.tensor_scalar_add(s[:, 3:4], s[:, 2:3], 1e-5))
        v = p.sve(lambda e, s=stat: e.reciprocal(s[:, 3:4], s[:, 3:4]))
        sq = p.sve(lambda e, s=stat: e.activation(s[:, 4:5], s[:, 3:4], AF.Sqrt), eng="scalar", waits=(v,))
        v = p.sve(lambda e, s=stat: e.tensor_tensor(s[:, 5:6], s[:, 2:3], s[:, 3:4], op=ALU.mult), waits=(sq,))
        v = p.sve(lambda e, s=stat: e.tensor_scalar_add(s[:, 5:6], s[:, 5:6], 1e-5))
        v = p.sve(lambda e, s=stat: e.reciprocal(s[:, 5:6], s[:, 5:6]))
        sq = p.sve(lambda e, s=stat: e.activation(s[:, 5:6], s[:, 5:6], AF.Sqrt), eng="scalar", waits=(v,))
        v = p.sve(lambda e, s=stat: e.tensor_tensor(s[:, 6:7], s[:, 4:5], s[:, 5:6], op=ALU.mult), waits=(sq,))
        v = p.sve(lambda e, s=stat: e.tensor_tensor(s[:, 7:8], s[:, 0:1], s[:, 6:7], op=ALU.mult))
        v = p.sve(lambda e, s=stat: e.tensor_scalar(s[:, 7:8], s[:, 7:8], -1.0, None, op0=ALU.mult))
        a1 = p.sve(lambda e, x=x, s=stat: e.tensor_scalar(xnq[:, 0:1024], x, s[:, 6:7], s[:, 7:8],
                  op0=ALU.mult, op1=ALU.add), waits=(v,))
        a2 = p.sve(lambda e, s=stat: e.reduce_max(s[:, 8:9], xnq[:, 0:1024], axis=AX.X,
                  apply_absolute_value=True), waits=(a1,))
        v = p.sve(lambda e, s=stat: e.tensor_scalar(s[:, 8:9], s[:, 8:9], 1e-5, None, op0=ALU.max))
        v = p.sve(lambda e, s=stat: e.reciprocal(s[:, 9:10], s[:, 8:9]))
        v = p.sve(lambda e, s=stat: e.tensor_scalar(s[:, 9:10], s[:, 9:10], 127.0, None, op0=ALU.mult))
        v = p.sve(lambda e, s=stat: e.tensor_scalar(s[:, 10:11], s[:, 8:9], 1.0 / 127.0, None, op0=ALU.mult))
        v = p.sve(lambda e, s=stat, c=c1s, i=i: e.tensor_tensor(c[:, i:i + 1], s[:, 10:11], sw[:, 0:1], op=ALU.mult))
        q1 = p.sve(lambda e, s=stat: e.tensor_scalar(xnq[:, 0:1024], xnq[:, 0:1024], s[:, 9:10], -128.0,
                  op0=ALU.mult, op1=ALU.max), waits=(v,))
        q2 = p.ve(lambda e: e.tensor_scalar(xnq[:, 0:1024], xnq[:, 0:1024], 127.0, MAGIC, op0=ALU.min, op1=ALU.add))
        q3 = p.ve(lambda e, q=qt: e.tensor_scalar(q[:, 0:1024], xnq[:, 0:1024], -MAGIC, None, op0=ALU.add))
        st = p.dma("sync", hb["xq1"][i * 128:(i + 1) * 128, :], qt[:, 0:1024], waits=(q3,))
        xq1_done.append(st)
        prevq = st

    xq1t_tk = [p.dma("sync", xq1T[dc], hb["xq1"][:, dc * 128:(dc + 1) * 128],
                     waits=(xq1_done[-1],), transpose=True) for dc in range(8)]

    xq2_done = []
    prev_ev = (bias1_done,)
    for i in range(NT):
        fns = []
        for dc in range(8):
            lhs = xq1T[dc][:, i * 128:(i + 1) * 128]
            for fc in range(8):
                fns.append(lambda e, o=ps[0:128, fc * 512:(fc + 1) * 512], l=lhs,
                           r=W1[dc][:, fc * 512:(fc + 1) * 512], s=(dc == 0), t=(dc == 7):
                           e.matmul(o, l, r, start=s, stop=t))
        mmtk = p.mmg(fns, waits=tuple(xq1t_tk) + tuple(w1tk) + prev_ev)
        e1 = p.ve(lambda e, c=c1s, i=i: e.scalar_tensor_tensor(gbuf[:, 0:4096], ps[0:128, 0:4096],
                  c[:, i:i + 1], b1bc[:, 0:4096], op0=ALU.mult, op1=ALU.add), waits=(mmtk,))
        g1 = p.ve(lambda e: e.activation(gbuf[:, 0:4096], gbuf[:, 0:4096], AF.Gelu), eng="scalar", waits=(e1,))
        g2 = p.sve(lambda e, s=stat: e.activation(xnq[:, 0:2048].bitcast(BF16)[:, 0:2048], gbuf[:, 0:2048],
                  AF.Square, accum_out=s[:, 1:2]), eng="scalar", waits=(g1,))
        g3 = p.sve(lambda e, s=stat: e.activation(xnq[:, 0:2048].bitcast(BF16)[:, 0:2048], gbuf[:, 2048:4096],
                  AF.Square, accum_out=s[:, 2:3]), eng="scalar")
        r3 = p.sve(lambda e, s=stat: e.reduce_max(s[:, 8:9], gbuf[:, 0:4096], axis=AX.X,
                  apply_absolute_value=True), waits=(g1,))
        v = p.sve(lambda e, s=stat: e.tensor_add(s[:, 1:2], s[:, 1:2], s[:, 2:3]), waits=(g3, r3))
        v = p.sve(lambda e, s=stat: e.tensor_scalar(s[:, 1:2], s[:, 1:2], 1.0 / 4096.0, 1e-5, op0=ALU.mult, op1=ALU.add))
        v = p.sve(lambda e, s=stat: e.reciprocal(s[:, 5:6], s[:, 1:2]))
        sq = p.sve(lambda e, s=stat: e.activation(s[:, 5:6], s[:, 5:6], AF.Sqrt), eng="scalar", waits=(v,))
        v = p.sve(lambda e, s=stat: e.tensor_tensor(s[:, 8:9], s[:, 8:9], s[:, 5:6], op=ALU.mult), waits=(sq,))
        v = p.sve(lambda e, s=stat: e.tensor_scalar(s[:, 8:9], s[:, 8:9], 1e-5, None, op0=ALU.max))
        v = p.sve(lambda e, s=stat: e.reciprocal(s[:, 9:10], s[:, 8:9]))
        v = p.sve(lambda e, s=stat: e.tensor_scalar(s[:, 9:10], s[:, 9:10], 127.0, None, op0=ALU.mult))
        v = p.sve(lambda e, s=stat: e.tensor_tensor(s[:, 9:10], s[:, 9:10], s[:, 5:6], op=ALU.mult))
        v = p.sve(lambda e, s=stat: e.tensor_scalar(s[:, 10:11], s[:, 8:9], 1.0 / 127.0, None, op0=ALU.mult))
        v = p.sve(lambda e, s=stat, c=c2s, i=i: e.tensor_tensor(c[:, i:i + 1], s[:, 10:11], sw[:, 1:2], op=ALU.mult))
        sts = []
        for hf in range(2):
            sl = slice(hf * 2048, (hf + 1) * 2048)
            q1 = p.sve(lambda e, s=stat, sl=sl: e.tensor_scalar(xnq[:, 0:2048], gbuf[:, sl], s[:, 9:10], -128.0,
                      op0=ALU.mult, op1=ALU.max), waits=(v,) if hf == 0 else (sts[-1],))
            q2 = p.ve(lambda e: e.tensor_scalar(xnq[:, 0:2048], xnq[:, 0:2048], 127.0, MAGIC, op0=ALU.min, op1=ALU.add))
            q3 = p.ve(lambda e, q=qt: e.tensor_scalar(q[:, 0:2048], xnq[:, 0:2048], -MAGIC, None, op0=ALU.add))
            st = p.dma("sync", hb["xq2"][i * 128:(i + 1) * 128, hf * 2048:(hf + 1) * 2048], qt[:, 0:2048], waits=(q3,))
            sts.append(st)
        xq2_done.append(sts[-1])
        prev_ev = (q3, sts[-1])
        prev_ev = (e1, q3, sts[-1])
    ar.pop()

    ar.push()
    w2s = [ar.alloc(128, 1024, BF16) for _ in range(2)]
    x2s = [ar.alloc(128, 512, BF16) for _ in range(2)]
    b2bc = ar.alloc(128, 1024, BF16)
    xev = ar.alloc(128, 1024, F32)
    brow2 = ar.alloc(1, 512, F32)

    prev = None
    for j in range(2):
        lt = p.dma("gpsimd", brow2, hb["l0b2"][0:1, j * 512:(j + 1) * 512], waits=(prev,) if prev else (xq2_done[-1],))
        mmb = p.ve(lambda e, b=brow2: e.matmul(ps[0:128, 0:512], onesr, b, start=True, stop=True),
                   eng="tensor", waits=(lt,) + ((prev,) if prev else ()))
        prev = p.ve(lambda e, j=j: e.tensor_copy(b2bc[:, j * 512:(j + 1) * 512], ps[0:128, 0:512]), waits=(mmb,))
    bias2_done = prev

    last_ev = bias2_done
    for tg in range(4):
        tiles = list(range(tg * 4, tg * 4 + 4))
        mm_last = None
        for fc in range(32):
            xbuf = x2s[fc % 2]
            wbuf = w2s[fc % 2]
            guard = (mm_hist[fc - 2],) if fc >= 2 else ((last_ev,) if last_ev else ())
            wt = p.dma("scalar", wbuf, hb["wq2t"][fc * 128:(fc + 1) * 128, :], waits=guard)
            xt2 = p.dma("sync", xbuf, hb["xq2"][tg * 512:(tg + 1) * 512, fc * 128:(fc + 1) * 128],
                        waits=guard + ((xq2_done[-1],) if fc < 2 and tg == 0 else ()), transpose=True)
            fns = []
            for j in range(4):
                lhs = xbuf[:, j * 128:(j + 1) * 128]
                for vc in range(2):
                    fns.append(lambda e, o=ps[0:128, j * 1024 + vc * 512: j * 1024 + (vc + 1) * 512],
                               l=lhs, r=wbuf[:, vc * 512:(vc + 1) * 512], s=(fc == 0), t=(fc == 31):
                               e.matmul(o, l, r, start=s, stop=t))
            mm_last = p.mmg(fns, waits=(wt, xt2) + ((last_ev,) if fc == 0 and last_ev else ()))
            if fc == 0:
                mm_hist = []
            mm_hist.append(mm_last)
        evs = []
        for j, ti in enumerate(tiles):
            e1 = p.ve(lambda e, j=j, ti=ti: e.scalar_tensor_tensor(
                xev, ps[0:128, j * 1024:(j + 1) * 1024], c2s[:, ti:ti + 1],
                b2bc, op0=ALU.mult, op1=ALU.add), waits=(mm_last,))
            e2 = p.ve(lambda e, ti=ti, x=xt[ti]: e.scalar_tensor_tensor(
                x, xev, 0.5, x, op0=ALU.mult, op1=ALU.add))
            evs.append(e2)
        last_ev = evs[-1]
    ar.pop()
    ar.pop()
    return last_ev



A0_IN = [("a0wq", (1024, 64), F32), ("a0wk", (1024, 64), F32),
         ("a0wv", (1024, 64), F32), ("a0wo", (64, 1024), F32),
         ("gat", (64, 3008), F32), ("gbt", (64, 3008), F32), ("gbt2", (64, 3008), F32),
         ("cmask", (128, 128), F32), ("role0", (128, 1), F32), ("role1", (128, 1), F32)]
for _l, _bs, _r in ((0, 4, 4), (1, 8, 8), (2, 16, 8), (3, 32, 8)):
    for _nm in ("UL", "VR", "UR", "VL"):
        A0_IN.append((f"a0{_nm}{_l}t", (_r, _bs), F32))
STAGE_INPUTS = {
    "l0ffn": [("l0w1t", (1024, 4096), F32), ("l0w2t", (4096, 1024), F32),
              ("l0b1", (1, 4096), F32), ("l0b2", (1, 1024), F32)],
    "attn0": A0_IN,
    "l1attn": [("l1wqkvt", (1024, 3072), F32), ("l1bqkv", (128, 24), F32),
               ("l1bv", (1, 1024), F32)],
    "l1ffn": sum([[(f"f1wqkv{l}", (1024, 1536), F32), (f"f1wo{l}", (512, 1024), F32),
                   (f"f1w1p{l}", (1024, 5632), F32), (f"f1b1p{l}", (1, 5632), F32),
                   (f"f1w2{l}", (2816, 1024), F32), (f"f1b2{l}", (1, 1024), F32),
                   (f"f1lg{l}", (128, 8), F32), (f"f1lb{l}", (128, 8), F32),
                   (f"f1qs{l}", (1, 64), F32), (f"f1ks{l}", (1, 64), F32)] for l in range(2)], [])
              + [("cq", (2560, 64), F32), ("sq", (2560, 64), F32),
                 ("ck", (2560, 64), F32), ("sk", (2560, 64), F32),
                 ("lmask", (512, 1024), F32), ("lmaskf", (512, 1024), F32),
                 ("role1", (128, 1), F32)],
}
STAGE_SCRATCH = {
    "l0ffn": [("wq1t", (1024, 4096), BF16), ("wq2t", (4096, 1024), BF16),
              ("xq1", (TLOC, 1024), BF16), ("xq2", (TLOC, 4096), BF16)],
    "attn0": [("gac_h", (64, 3008), F32), ("gbc_h", (64, 3008), F32), ("gb2_h", (64, 3008), F32),
              ("lnx_h", (TLOC, 1024), BF16),
              ("ccin_a0", (64, 80), F32), ("ccout_a0", (64, 80), F32),
              ("d_qt", (64, 2048), BF16), ("d_kt", (64, 2048), BF16), ("d_vt", (64, 2048), BF16),
              ("d_ast", (128, 2944), BF16), ("d_bst", (128, 3008), BF16),
              ("d_qtl", (64, 6016), BF16), ("d_pt", (128, 6016), BF16),
              ("d_yall", (64, 2048), F32), ("d_den", (1, 2048), F32), ("d_mt", (64, 65), F32)],
    "l1attn": [("lnx_h", (TLOC, 1024), BF16),
               ("kin_h", (1024, TLOC), BF16), ("kout_h", (2048, TLOC), BF16),
               ("vin_h", (TLOC, 1024), BF16), ("vout_h", (2 * TLOC, 1024), BF16),
               ("p_h", (512, 4096), BF16)],
    "l1ffn": [("w1p_h0", (1024, 5632), BF16), ("w1p_h1", (1024, 5632), BF16),
              ("w2b_h0", (2816, 1024), BF16), ("w2b_h1", (2816, 1024), BF16),
              ("ucrow_h0", (2, 5632), F32), ("ucrow_h1", (2, 5632), F32),
              ("xsp_h", (TLOC, 1024), F32), ("xi_h", (TLOC, 1024), BF16),
              ("bndout_h", (1026, 1024), BF16), ("bnd1in_h", (1, 1024), BF16),
              ("bnd1out_h", (2, 1024), BF16), ("qh_h", (2560, 512), BF16),
              ("kh_h", (2560, 512), BF16), ("ph_h", (TLOC, 8192), BF16),
              ("hh_h", (TLOC, 2816), BF16), ("rv_h", (1, 2048), F32),
              ("dbgf1_h", (1160, 2048), F32)],
}
DEBUG_SCRATCH = False
COMMON_INPUTS = [("xres_in", (TLOC, 1024), F32),
                 ("onescol", (128, 1), F32), ("onesrow", (1, 512), F32),
                 ("identity", (128, 128), F32)]


def build_program(stages):
    nc = bass.Bass("TRN2", target_bir_lowering=False)
    hb = {}
    names = set()
    specs = list(COMMON_INPUTS)
    for st in stages:
        base = st.split(":")[0]
        for it in STAGE_INPUTS.get(base, ()):
            if it[0] not in names:
                names.add(it[0])
                specs.append(it)
    for name, shape, dt in specs:
        hb[name] = nc.dram_tensor(name, list(shape), dt, kind="ExternalInput").ap()
    hb["xres_out"] = nc.dram_tensor("xres_out", [TLOC, 1024], F32, kind="ExternalOutput").ap()
    snames = set()
    for st in stages:
        base = st.split(":")[0]
        for name, shape, dt in STAGE_SCRATCH.get(base, ()):
            if name not in snames:
                snames.add(name)
                kind = "ExternalOutput" if DEBUG_SCRATCH else "Internal"
                hb[name] = nc.dram_tensor(name, list(shape), dt, kind=kind).ap()

    with contextlib.ExitStack() as stack:
        p = Prog(nc, stack)
        xres = stack.enter_context(nc.sbuf_tensor("xres", [128, NT * 1024], F32))
        arena_t = stack.enter_context(nc.sbuf_tensor("arena", [128, 33000], F32))
        small = stack.enter_context(nc.sbuf_tensor("small", [128, 1024], F32))
        psum = stack.enter_context(nc.psum_tensor("psum", [128, 4096], F32))
        ar = Arena(arena_t, 33000)
        xt = [xres[:, i * 1024:(i + 1) * 1024] for i in range(NT)]

        per = {"psum": psum}
        per["onescol"] = small[:, 0:1]
        per["swslots"] = small[:, 8:16]
        onesrow_full = small[0:1, 16:528]
        per["onesrow_f"] = onesrow_full[0:1, 0:128]

        per["ident_f"] = small[:, 528:656]
        per["ident_b"] = small[:, 656:720].bitcast(BF16)[0:128, 0:128]
        p.dma("sync", per["onescol"], hb["onescol"][:, :])
        p.dma("sync", onesrow_full, hb["onesrow"][:, :])
        p.dma("sync", per["ident_f"], hb["identity"][:, :])
        p.dma("gpsimd", per["ident_b"], hb["identity"][:, :])
        for i in range(NT):
            p.dma("sync", xt[i], hb["xres_in"][i * 128:(i + 1) * 128, :])
        p.barrier()

        did_prep = set()
        for st in stages:
            if st == "l0ffn":
                if "l0" not in did_prep:
                    did_prep.add("l0")
                    emit_l0ffn_prep(p, ar, hb, per)
                    p.barrier()
                emit_l0ffn(p, ar, hb, per, xt)
                p.barrier()
            elif st == "attn0":
                if "a0" not in did_prep:
                    did_prep.add("a0")
                    emit_l0attn_prep(p, ar, hb, per)
                    p.barrier()
                emit_l0attn(p, ar, hb, per, xt)
                p.barrier()
            elif st == "l1attn":
                emit_l1attn(p, ar, hb, per, xt)
                p.barrier()
            elif st == "l1ffn":
                if "f1" not in did_prep:
                    did_prep.add("f1")
                    emit_l1ffn_prep(p, ar, hb, per)
                    p.barrier()
                emit_l1ffn(p, ar, hb, per, xt)
                p.barrier()

        for i in range(NT):
            p.dma("sync", hb["xres_out"][i * 128:(i + 1) * 128, :], xt[i])
        p.raw("sync", lambda e: None, (p.now("qs"),), ())
        with nc.Block() as block:
            p.emit(block)
    return nc, specs


def run_stages(stages, xcores, sh, percore):
    nc, specs = build_program(stages)
    in_maps = []
    for c in range(NCORE):
        m = {}
        for name, shape, dt in specs:
            if name == "xres_in":
                m[name] = np.ascontiguousarray(xcores[c])
            elif name in sh:
                m[name] = sh[name]
            else:
                m[name] = percore[c][name]
        in_maps.append(m)
    res = run_bass_kernel_spmd(nc, in_maps, core_ids=list(range(NCORE)))
    if DEBUG_SCRATCH:
        return res
    return np.stack([res.results[c]["xres_out"] for c in range(NCORE)], 0)


PIPE = [["l0ffn", "attn0", "l0ffn", "l1ffn", "l1attn", "l1ffn"]]


def kernel(**inputs):
    x = _f32(inputs["L0_attn_x"])
    xcores = np.zeros((8, TLOC, 1024), np.float32)
    for b in range(4):
        xcores[2 * b] = x[b, :2048]
        xcores[2 * b + 1] = x[b, 2048:]
    sh, percore = prep_host(inputs)
    for stages in PIPE:
        xcores = run_stages(stages, xcores, sh, percore)
    out = np.zeros_like(x)
    for b in range(4):
        out[b, :2048] = xcores[2 * b]
        out[b, 2048:] = xcores[2 * b + 1]
    return out



A0_TRUNC = 99
A0_DEBUG = False
F1_TRUNC = 99
F1_WO = 1
F1_REFS = {}
A0_REFS = {}
A0_BS = [4, 8, 16, 32]
A0_NN = [8, 4, 2, 1]
A0_TOPS = [(l, m) for l in range(4) for m in range(A0_NN[l])]
A0_NCOMP = 47


def emit_l0attn_prep(p, ar, hb, per):
    ps = per["psum"]
    ar.push()
    tmpl = ar.alloc(64, 3008, F32)
    stg = ar.alloc(32, 32, F32)
    uls = {}
    tks = []
    for l in range(4):
        r, bs = [4, 8, 8, 8][l], A0_BS[l]
        for nm in ("UL", "VR", "UR", "VL"):
            buf = ar.alloc(r, bs, BF16)
            uls[(nm, l)] = buf
            tks.append(p.dma("gpsimd", buf, hb[f"a0{nm}{l}t"][:, :]))
    tdone = []
    for src, dst in (("gat", "gac_h"), ("gbt", "gbc_h"), ("gbt2", "gb2_h")):
        t1 = p.dma("sync", tmpl, hb[src][:, :], waits=tuple(tdone[-1:]))
        t2 = p.dma("sync", hb[dst][:, :], tmpl, waits=(t1,))
        tdone.append(t2)
    last = tdone[-1]
    for l in range(4):
        r, bs = [4, 8, 8, 8][l], A0_BS[l]
        prods = (("PUL", "UL", "UL"), ("PVR", "VR", "VR"), ("ABOT", "VR", "UR"),
                 ("BBOT", "UL", "VL"), ("B2BOT", "VL", "UL"))
        for pn, aop, bop in prods:
            mm = p.ve(lambda e, a=uls[(aop, l)], b=uls[(bop, l)], o=ps[0:bs, 0:bs]:
                      e.matmul(o, a, b, start=True, stop=True),
                      eng="tensor", waits=tuple(tks) + (last,))
            cp = p.ve(lambda e, o=ps[0:bs, 0:bs], s=stg[0:bs, 0:bs]: e.tensor_copy(s, o), waits=(mm,))
            wr = []
            for ti, (ll, m) in enumerate(A0_TOPS):
                if ll != l:
                    continue
                ctop = 16 + ti
                cbot = 31 + ti
                L0r = 2 * bs * m
                R0r = 2 * bs * m + bs
                if pn == "PUL":
                    tgt = [("gac_h", L0r, ctop * 64 + L0r)]
                elif pn == "PVR":
                    tgt = [("gbc_h", R0r, ctop * 64 + R0r), ("gb2_h", R0r, ctop * 64 + R0r)]
                elif pn == "ABOT":
                    tgt = [("gac_h", R0r, cbot * 64 + R0r)]
                elif pn == "BBOT":
                    tgt = [("gbc_h", L0r, cbot * 64 + L0r)]
                else:
                    tgt = [("gb2_h", L0r, cbot * 64 + L0r)]
                for dstn, r0, c0 in tgt:
                    wr.append(p.dma("sync", hb[dstn][r0:r0 + bs, c0:c0 + bs], stg[0:bs, 0:bs], waits=(cp,)))
            last = wr[-1]
    p.raw("sync", lambda e: None, (last,), ())
    ar.pop()


def emit_l0attn(p, ar, hb, per, xt):
    ps = per["psum"]
    onesr = per["onesrow_f"]
    ar.push()
    GA = ar.alloc(64, 3008, BF16)
    GB = ar.alloc(64, 3008, BF16)
    GB2 = ar.alloc(64, 3008, BF16)
    wq_sb = ar.alloc(128, 512, BF16)
    wk_sb = ar.alloc(128, 512, BF16)
    wv_sb = ar.alloc(128, 512, BF16)
    wo_sb = ar.alloc(64, 1024, BF16)
    lnxT = [ar.alloc(128, 2048, BF16) for _ in range(8)]
    QT = ar.alloc(64, 2048, BF16)
    KT = ar.alloc(64, 2048, BF16)
    VT = ar.alloc(64, 2048, BF16)
    Astk = ar.alloc(128, 2944, BF16)
    Bstk = ar.alloc(128, 3008, BF16)
    Qtl = ar.alloc(64, 6016, BF16)
    Pt = ar.alloc(128, 6016, BF16)
    YD = ar.alloc(128, 2048, F32)
    Yall = YD[0:64, :]
    den_row = YD[64:65, :]
    ybf = ar.alloc(64, 2048, BF16)
    MT = ar.alloc(64, 65, F32)
    MTb = ar.alloc(64, 65, BF16)
    Mrecv = ar.alloc(64, 80, F32)
    Mrb = ar.alloc(64, 80, BF16)
    exch = ar.alloc(64, 80, F32)
    lnt = ar.alloc(128, 1024, BF16)
    stat = ar.alloc(128, 16, F32)
    onescol_b = ar.alloc(128, 1, BF16)
    cmask_sb = ar.alloc(128, 128, F32)
    role0v = ar.alloc(128, 1, F32)
    role1v = ar.alloc(128, 1, F32)
    tmpe = ar.alloc(64, 512, F32)
    tmpe2 = ar.alloc(64, 512, F32)
    ztmp = ar.alloc(64, 1, F32)

    global A0_REFS
    A0_REFS = dict(GA=GA, GB=GB, GB2=GB2, QT=QT, KT=KT, VT=VT, Astk=Astk, Bstk=Bstk,
                   Qtl=Qtl, Pt=Pt, Yall=Yall, wq=wq_sb, wk=wk_sb, wv=wv_sb)
    ld0 = []
    ld0.append(p.dma("gpsimd", GA, hb["gac_h"][:, :]))
    ld0.append(p.dma("gpsimd", GB, hb["gbc_h"][:, :]))
    ld0.append(p.dma("gpsimd", GB2, hb["gb2_h"][:, :]))
    for wsb, nm in ((wq_sb, "a0wq"), (wk_sb, "a0wk"), (wv_sb, "a0wv")):
        ld0.append(p.dma("gpsimd", wsb.rearrange("p (c f) -> p c f", f=64),
                         hb[nm].rearrange("(c p) f -> p c f", p=128)))
    ld0.append(p.dma("gpsimd", wo_sb, hb["a0wo"][:, :]))
    ld0.append(p.dma("sync", cmask_sb, hb["cmask"][:, :]))
    ld0.append(p.dma("sync", role0v, hb["role0"][:, :]))
    ld0.append(p.dma("sync", role1v, hb["role1"][:, :]))
    ld0.append(p.ve(lambda e: e.memset(onescol_b, 1.0)))
    ld0.append(p.ve(lambda e: e.memset(MT[:, 0:65], 0.0)))
    ld0.append(p.ve(lambda e: e.memset(MTb[:, 0:65], 0.0)))
    ld0.append(p.ve(lambda e: e.memset(exch[:, 0:80], 0.0)))
    ld0.append(p.ve(lambda e: e.memset(YD[:, 0:2048], 0.0)))
    ld0 = tuple(ld0)

    lnst = []
    sqjunk = Pt.bitcast(F32)
    prevq = None
    for i in range(NT):
        x = xt[i]
        r1 = p.ve(lambda e, x=x, s=stat: e.reduce_sum(s[:, 0:1], x, axis=AX.X))
        r2 = p.ve(lambda e, x=x, s=stat, o=sqjunk: e.activation(o[:, 0:1024], x, AF.Square,
                  accum_out=s[:, 1:2]), eng="scalar", waits=((prevq,) if prevq else ()))
        v = p.sve(lambda e, s=stat: e.tensor_scalar(s[:, 0:1], s[:, 0:1], 1.0 / D, None, op0=ALU.mult), waits=(r1, r2))
        v = p.sve(lambda e, s=stat: e.tensor_scalar(s[:, 1:2], s[:, 1:2], 1.0 / D, None, op0=ALU.mult))
        v = p.sve(lambda e, s=stat: e.tensor_tensor(s[:, 2:3], s[:, 0:1], s[:, 0:1], op=ALU.mult))
        v = p.sve(lambda e, s=stat: e.tensor_sub(s[:, 2:3], s[:, 1:2], s[:, 2:3]))
        v = p.sve(lambda e, s=stat: e.tensor_scalar_add(s[:, 3:4], s[:, 2:3], 1e-5))
        v = p.sve(lambda e, s=stat: e.reciprocal(s[:, 3:4], s[:, 3:4]))
        v = p.sve(lambda e, s=stat: e.activation(s[:, 4:5], s[:, 3:4], AF.Sqrt), eng="scalar", waits=(v,))
        v = p.sve(lambda e, s=stat: e.tensor_tensor(s[:, 5:6], s[:, 0:1], s[:, 4:5], op=ALU.mult), waits=(v,))
        v = p.sve(lambda e, s=stat: e.tensor_scalar(s[:, 5:6], s[:, 5:6], -1.0, None, op0=ALU.mult))
        q1 = p.sve(lambda e, x=x, s=stat: e.tensor_scalar(lnt[:, 0:1024], x, s[:, 4:5], s[:, 5:6],
                   op0=ALU.mult, op1=ALU.add), waits=(v,))
        st = p.dma("sync", hb["lnx_h"][i * 128:(i + 1) * 128, :], lnt[:, 0:1024], waits=(q1,))
        lnst.append(st)
        prevq = st
    if A0_TRUNC < 2:
        ar.pop()
        return lnst[-1]
    lnT_tk = tuple(p.dma("sync", lnxT[dc], hb["lnx_h"][:, dc * 128:(dc + 1) * 128],
                         waits=(lnst[-1],), transpose=True) for dc in range(8))

    projw = ld0 + lnT_tk
    prev_ev = None
    for wsb, OUT, iselu in ((wq_sb, QT, True), (wk_sb, KT, True), (wv_sb, VT, False)):
        for tc in range(4):
            cols = slice(tc * 512, (tc + 1) * 512)
            fns = []
            for dc in range(8):
                fns.append(lambda e, l=wsb[:, dc * 64:(dc + 1) * 64], r=lnxT[dc][:, cols],
                           s=(dc == 0), t=(dc == 7): e.matmul(ps[0:64, 0:512], l, r, start=s, stop=t))
            mm = p.mmg(fns, waits=projw + ((prev_ev,) if prev_ev else ()))
            if iselu:
                e1 = p.ve(lambda e: e.tensor_scalar_min(tmpe, ps[0:64, 0:512], 0.0), waits=(mm,))
                e2 = p.ve(lambda e: e.activation(tmpe2, tmpe, AF.Exp), eng="scalar", waits=(e1,))
                e3 = p.ve(lambda e: e.tensor_scalar_max(tmpe, ps[0:64, 0:512], 0.0), waits=(e2,))
                e4 = p.ve(lambda e, O=OUT, cols=cols: e.tensor_tensor(O[:, cols], tmpe, tmpe2, op=ALU.add),
                          waits=(e3,))
                prev_ev = e4
            else:
                prev_ev = p.ve(lambda e, O=OUT, cols=cols: e.tensor_copy(O[:, cols], ps[0:64, 0:512]),
                               waits=(mm,))

    if A0_TRUNC < 3:
        ar.pop()
        return prev_ev
    if A0_DEBUG:
        p.dma("sync", hb["d_qt"][:, :], QT[:, 0:2048], waits=(prev_ev,))
        p.dma("sync", hb["d_kt"][:, :], KT[:, 0:2048], waits=(prev_ev,))
        p.dma("sync", hb["d_vt"][:, :], VT[:, 0:2048], waits=(prev_ev,))
    proj_done = (prev_ev,)
    ev_guard = proj_done
    for ci in range(NT):
        sl = slice(ci * 128, (ci + 1) * 128)
        fns = [lambda e, c0=c0, c1=c1, sl=sl: e.matmul(ps[0:128, c0:c1], KT[:, sl], GA[:, c0:c1], start=True, stop=True)
               for c0, c1 in ((0, 512), (512, 1024), (1024, 1536), (1536, 1984))]
        fns += [lambda e, c0=c0, c1=c1, sl=sl: e.matmul(ps[0:128, c0:c1], VT[:, sl], GA[:, c0:c1], start=True, stop=True)
                for c0, c1 in ((1984, 2048), (2048, 2560), (2560, 2944))]
        mmA = p.mmg(fns, waits=ev_guard)
        evA = p.ve(lambda e: e.copy(Astk[:, 0:2944], ps[0:128, 0:2944]), eng="scalar", waits=(mmA,))
        fns = [lambda e, c0=c0, c1=c1, sl=sl: e.matmul(ps[0:128, c0:c1], VT[:, sl], GB[:, c0:c1], start=True, stop=True)
               for c0, c1 in ((0, 512), (512, 1024), (1024, 1536), (1536, 1984))]
        fns += [lambda e, c0=c0, c1=c1, sl=sl: e.matmul(ps[0:128, c0:c1], KT[:, sl], GB[:, c0:c1], start=True, stop=True)
                for c0, c1 in ((1984, 2048), (2048, 2560), (2560, 3008))]
        mmB = p.mmg(fns, waits=(evA,))
        evB = p.ve(lambda e: e.copy(Bstk[:, 0:3008], ps[0:128, 0:3008]), eng="scalar", waits=(mmB,))
        for r0, r1 in ((0, 24), (24, 47)):
            fns = [lambda e, c=c, r0=r0, sl=sl: e.matmul(ps[0:64, (c - r0) * 128:(c - r0 + 1) * 128],
                   GB2[:, c * 64:(c + 1) * 64], QT[:, sl], start=True, stop=True)
                   for c in range(r0, r1)]
            mmQ = p.mmg(fns, waits=(evB,) if r0 == 0 else (evQ,))
            evQ = p.ve(lambda e, r0=r0, r1=r1: e.copy(Qtl[:, r0 * 128:r1 * 128],
                       ps[0:64, 0:(r1 - r0) * 128]), eng="scalar", waits=(mmQ,))
        s1ev = []
        for src, cr0, cr1 in ((VT, 0, 31), (KT, 31, 47)):
            ncols = (cr1 - cr0) * 128
            fns = []
            for c0 in range(0, ncols, 512):
                c1 = min(c0 + 512, ncols)
                fns.append(lambda e, c0=c0, c1=c1, src=src, cr0=cr0, sl=sl: e.matmul(
                    ps[0:128, c0:c1], src[:, sl], Qtl[:, cr0 * 128 + c0:cr0 * 128 + c1], start=True, stop=True))
            mmS = p.mmg(fns, waits=(evQ,) if cr0 == 0 else (s1ev[-1],))
            nc_ = cr1 - cr0
            ev = p.ve(lambda e, nc_=nc_, cr0=cr0: e.scalar_tensor_tensor(
                Pt[:, cr0 * 128:(cr0 + nc_) * 128].rearrange("p (c t) -> p c t", t=128),
                ps[0:128, 0:nc_ * 128].rearrange("p (c t) -> p c t", t=128),
                1.0,
                cmask_sb.unsqueeze(1).broadcast_to((128, nc_, 128)),
                op0=ALU.mult, op1=ALU.mult), waits=(mmS,))
            s1ev.append(ev)
        fns = [lambda e, c=c: e.matmul(ps[0:64, 3200:3328], Astk[:, c * 64:(c + 1) * 64],
               Pt[:, c * 128:(c + 1) * 128], start=(c == 0), stop=False) for c in range(46)]
        fns.append(lambda e, sl=sl: e.matmul(ps[0:64, 3200:3328], MTb[:, 0:64], QT[:, sl], start=False, stop=True))
        fns.append(lambda e: e.matmul(ps[0:1, 3328:3456], onescol_b, Pt[:, 46 * 128:47 * 128],
                   start=True, stop=False))
        fns.append(lambda e, sl=sl: e.matmul(ps[0:1, 3328:3456], MTb[:, 64:65], QT[:, sl], start=False, stop=True))
        dbg_tk = ()
        if A0_DEBUG and ci == 0:
            dbg_tk = (p.dma("sync", hb["d_ast"][:, :], Astk[:, 0:2944], waits=(evA,)),
                      p.dma("sync", hb["d_bst"][:, :], Bstk[:, 0:3008], waits=(evB,)),
                      p.dma("sync", hb["d_qtl"][:, :], Qtl[:, 0:6016], waits=(evQ,)),
                      p.dma("sync", hb["d_pt"][:, :], Pt[:, 0:6016], waits=tuple(s1ev)))
        mmY = p.mmg(fns, waits=tuple(s1ev))
        evY = p.ve(lambda e, sl=sl: e.tensor_copy(Yall[:, sl], ps[0:64, 3200:3328]), waits=(mmY,))
        evD = p.ve(lambda e, sl=sl: e.tensor_copy(den_row[0:1, sl], ps[0:1, 3328:3456]), waits=(mmY,))
        fns = [lambda e, c=c: e.matmul(ps[0:64, 3456:3520], Bstk[:, c * 64:(c + 1) * 64],
               Astk[:, c * 64:(c + 1) * 64], start=(c == 0), stop=(c == 45)) for c in range(46)]
        mmM = p.mmg(fns, waits=(evY, evD))
        uM = p.ve(lambda e: e.tensor_add(MT[:, 0:64], MT[:, 0:64], ps[0:64, 3456:3520]), waits=(mmM,))
        uz1 = p.ve(lambda e, sl=sl: e.reduce_sum(ztmp, KT[:, sl], axis=AX.X))
        uz2 = p.sve(lambda e: e.tensor_add(MT[:, 64:65], MT[:, 64:65], ztmp), waits=(uz1,))
        ub = p.sve(lambda e: e.tensor_copy(MTb[:, 0:65], MT[:, 0:65]), waits=(uM, uz2))
        ev_guard = (ub, s1ev[-1], evA, evB) + dbg_tk
    if A0_DEBUG:
        p.dma("sync", hb["d_yall"][:, :], Yall[:, 0:2048], waits=ev_guard)
        p.dma("sync", hb["d_den"][:, :], den_row[0:1, 0:2048], waits=ev_guard)
        p.dma("sync", hb["d_mt"][:, :], MT[:, 0:65], waits=ev_guard)
    if A0_TRUNC < 4:
        ar.pop()
        return ev_guard[0]
    pk = p.sve(lambda e: e.tensor_scalar(exch[:, 0:65], MT[:, 0:65], role0v[0:64, 0:1], None, op0=ALU.mult),
               waits=ev_guard)
    sx = p.dma("sync", hb["ccin_a0"][:, :], exch[:, 0:80], waits=(pk,))
    cc = p.collective("AllReduce", ALU.add, [hb["ccin_a0"][:, :].opt()], [hb["ccout_a0"][:, :].opt()],
                      waits=(sx,))
    lx = p.dma("gpsimd", Mrecv[:, 0:80], hb["ccout_a0"][:, :], waits=(cc,))
    rb = p.ve(lambda e: e.tensor_scalar(Mrb[:, 0:65], Mrecv[:, 0:65], role1v[0:64, 0:1], None, op0=ALU.mult),
              waits=(lx,))
    if A0_TRUNC < 5:
        ar.pop()
        return rb
    corr = []
    for qc in range(4):
        cols = slice(qc * 512, (qc + 1) * 512)
        mmC = p.ve(lambda e, cols=cols: e.matmul(ps[0:65, 0:512], Mrb[:, 0:65], QT[:, cols],
                   start=True, stop=True), eng="tensor", waits=(rb,) + tuple(corr[-2:]))
        c1 = p.ve(lambda e, cols=cols: e.tensor_add(Yall[:, cols], Yall[:, cols], ps[0:64, 0:512]), waits=(mmC,))
        c2 = p.ve(lambda e, cols=cols: e.tensor_add(den_row[0:1, cols], den_row[0:1, cols], ps[64:65, 0:512]),
                  waits=(mmC,))
        corr.append(c2)
        corr.append(c1)
    rrow = Qtl.bitcast(F32)[0:1, 0:2048]
    v = p.sve(lambda e: e.tensor_scalar(rrow[0:1, 0:2048], den_row[0:1, 0:2048], 1e-6, None, op0=ALU.max),
              waits=tuple(corr))
    v = p.sve(lambda e: e.reciprocal(rrow[0:1, 0:2048], rrow[0:1, 0:2048]))
    nrm = []
    for qc in range(4):
        cols = slice(qc * 512, (qc + 1) * 512)
        mmB2 = p.ve(lambda e, cols=cols: e.matmul(ps[0:64, 512:1024], onesr[0:1, 0:64], rrow[0:1, cols],
                    start=True, stop=True), eng="tensor", waits=(v,) + ((nrm[-1],) if nrm else ()))
        n1 = p.ve(lambda e, cols=cols: e.tensor_tensor(ybf[:, cols], Yall[:, cols], ps[0:64, 512:1024],
                  op=ALU.mult), waits=(mmB2,))
        nrm.append(n1)
    last = nrm[-1]
    for ci in range(NT):
        sl = slice(ci * 128, (ci + 1) * 128)
        fns = [lambda e, sl=sl, vc=vc: e.matmul(ps[0:128, 1536 + vc * 512:1536 + (vc + 1) * 512],
               ybf[:, sl], wo_sb[:, vc * 512:(vc + 1) * 512], start=True, stop=True) for vc in range(2)]
        mmW = p.mmg(fns, waits=(last,) + tuple(nrm))
        e1 = p.ve(lambda e, x=xt[ci]: e.tensor_add(x, x, ps[0:128, 1536:2560]), waits=(mmW,))
        last = e1
    ar.pop()
    return last



def emit_ln_tiles(p, ar, hb, per, xt, dst, lnt, stat, sqj):
    prevq = None
    lnst = []
    for i in range(NT):
        x = xt[i]
        r1 = p.ve(lambda e, x=x, s=stat: e.reduce_sum(s[:, 0:1], x, axis=AX.X))
        r2 = p.ve(lambda e, x=x, s=stat, o=sqj: e.activation(o[:, 0:1024], x, AF.Square,
                  accum_out=s[:, 1:2]), eng="scalar", waits=((prevq,) if prevq else ()))
        v = p.sve(lambda e, s=stat: e.tensor_scalar(s[:, 0:1], s[:, 0:1], 1.0 / D, None, op0=ALU.mult), waits=(r1, r2))
        v = p.sve(lambda e, s=stat: e.tensor_scalar(s[:, 1:2], s[:, 1:2], 1.0 / D, None, op0=ALU.mult))
        v = p.sve(lambda e, s=stat: e.tensor_tensor(s[:, 2:3], s[:, 0:1], s[:, 0:1], op=ALU.mult))
        v = p.sve(lambda e, s=stat: e.tensor_sub(s[:, 2:3], s[:, 1:2], s[:, 2:3]))
        v = p.sve(lambda e, s=stat: e.tensor_scalar_add(s[:, 3:4], s[:, 2:3], 1e-5))
        v = p.sve(lambda e, s=stat: e.reciprocal(s[:, 3:4], s[:, 3:4]))
        v = p.sve(lambda e, s=stat: e.activation(s[:, 4:5], s[:, 3:4], AF.Sqrt), eng="scalar", waits=(v,))
        v = p.sve(lambda e, s=stat: e.tensor_tensor(s[:, 5:6], s[:, 0:1], s[:, 4:5], op=ALU.mult), waits=(v,))
        v = p.sve(lambda e, s=stat: e.tensor_scalar(s[:, 5:6], s[:, 5:6], -1.0, None, op0=ALU.mult))
        q1 = p.sve(lambda e, x=x, s=stat: e.tensor_scalar(lnt[:, 0:1024], x, s[:, 4:5], s[:, 5:6],
                   op0=ALU.mult, op1=ALU.add), waits=(v,))
        st = p.dma("sync", dst[i * 128:(i + 1) * 128, :], lnt[:, 0:1024], waits=(q1,))
        lnst.append(st)
        prevq = st
    return lnst


def emit_l1attn(p, ar, hb, per, xt):
    ps = per["psum"]
    ar.push()
    QTf = [ar.alloc(128, 2048, BF16) for _ in range(8)]
    bq_sb = ar.alloc(128, 24, F32)
    lnt = ar.alloc(128, 1024, BF16)
    stat = ar.alloc(128, 16, F32)
    rden = ar.alloc(128, 8, F32)
    negm = ar.alloc(128, 8, F32)
    onesrow_b = ar.alloc(1, 128, BF16)
    ob = p.ve(lambda e: e.memset(onesrow_b[0:1, 0:128], 1.0))
    ldb = p.dma("sync", bq_sb, hb["l1bqkv"][:, :])

    ar.push()
    sqj = ar.alloc(128, 1024, F32)
    lnst = emit_ln_tiles(p, ar, hb, per, xt, hb["lnx_h"], lnt, stat, sqj)
    ar.pop()

    ar.push()
    lnxT = [ar.alloc(128, 2048, BF16) for _ in range(8)]
    wqk = [ar.alloc(128, 1024, BF16) for _ in range(2)]
    wv = [ar.alloc(128, 4096, BF16) for _ in range(2)]
    vbuf = ar.alloc(128, 1024, BF16)
    bvrow = ar.alloc(1, 1024, BF16)
    lnT_tk = tuple(p.dma("sync", lnxT[dc], hb["lnx_h"][:, dc * 128:(dc + 1) * 128],
                         waits=(lnst[-1],), transpose=True) for dc in range(8))
    ldbv = p.dma("gpsimd", bvrow, hb["l1bv"][:, :])
    prev_ev = None
    kst = []
    for et in range(16):
        wbuf = wqk[et % 2]
        wt = p.dma("gpsimd", wbuf.rearrange("p (c e) -> p c e", e=128),
                   hb["l1wqkvt"][:, et * 128:(et + 1) * 128].rearrange("(c p) e -> p c e", p=128),
                   waits=(prev_ev,) if prev_ev else ())
        for tc in range(4):
            cols = slice(tc * 512, (tc + 1) * 512)
            fns = [lambda e, dc=dc, cols=cols, wbuf=wbuf: e.matmul(
                ps[0:128, 0:512], wbuf[:, dc * 128:(dc + 1) * 128], lnxT[dc][:, cols],
                start=(dc == 0), stop=(dc == 7)) for dc in range(8)]
            mm = p.mmg(fns, waits=lnT_tk + (wt, ldb) + ((prev_ev,) if prev_ev else ()))
            if et < 8:
                prev_ev = p.ve(lambda e, et=et, cols=cols: e.tensor_scalar(
                    QTf[et][:, cols], ps[0:128, 0:512], bq_sb[:, et:et + 1], None, op0=ALU.add),
                    waits=(mm,))
            else:
                prev_ev = p.ve(lambda e, et=et: e.tensor_scalar(
                    vbuf[:, 0:512], ps[0:128, 0:512], bq_sb[:, et:et + 1], None, op0=ALU.add),
                    waits=(mm,))
                kst.append(p.dma("sync", hb["kin_h"][(et - 8) * 128:(et - 7) * 128, tc * 512:(tc + 1) * 512],
                                 vbuf[:, 0:512], waits=(prev_ev,)))
                p.raw("vector", lambda e: None, (kst[-1],), ())
    vst = []
    wvt0 = p.dma("gpsimd", wv[0].rearrange("p (c e) -> p c e", e=512),
                 hb["l1wqkvt"][:, 2048:2560].rearrange("(c p) e -> p c e", p=128), waits=(prev_ev,))
    wvt1 = p.dma("gpsimd", wv[1].rearrange("p (c e) -> p c e", e=512),
                 hb["l1wqkvt"][:, 2560:3072].rearrange("(c p) e -> p c e", p=128))
    for i in range(NT):
        fns = []
        for dc in range(8):
            for vc in range(2):
                wb = wv[vc]
                fns.append(lambda e, dc=dc, vc=vc, i=i, wb=wb: e.matmul(
                    ps[0:128, 1024 + vc * 512:1024 + (vc + 1) * 512],
                    lnxT[dc][:, i * 128:(i + 1) * 128], wb[:, dc * 512:(dc + 1) * 512],
                    start=(dc == 0), stop=False))
        fns.append(lambda e: e.matmul(ps[0:128, 1024:1536], onesrow_b[0:1, 0:128], bvrow[0:1, 0:512],
                   start=False, stop=True))
        fns.append(lambda e: e.matmul(ps[0:128, 1536:2048], onesrow_b[0:1, 0:128], bvrow[0:1, 512:1024],
                   start=False, stop=True))
        mm = p.mmg(fns, waits=(wvt0, wvt1, ldbv, ob) + ((vst[-1],) if vst else (prev_ev,)))
        ev = p.ve(lambda e: e.tensor_copy(vbuf[:, 0:1024], ps[0:128, 1024:2048]), waits=(mm,))
        vst.append(p.dma("sync", hb["vin_h"][i * 128:(i + 1) * 128, :], vbuf[:, 0:1024], waits=(ev,)))
        p.raw("vector", lambda e: None, (vst[-1],), ())
    ar.pop()

    cc1 = p.collective("AllGather", ALU.bypass, [hb["kin_h"][:, :].opt()], [hb["kout_h"][:, :].opt()],
                       waits=(kst[-1],))
    cc2 = p.collective("AllGather", ALU.bypass, [hb["vin_h"][:, :].opt()], [hb["vout_h"][:, :].opt()],
                       waits=(vst[-1],))

    ar.push()
    scr = [ar.alloc(128, 4096, BF16) for _ in range(4)]
    pbuf = ar.alloc(128, 4096, BF16)
    ktc = [ar.alloc(128, 4096, BF16) for _ in range(2)]
    vtc = [ar.alloc(128, 1024, BF16) for _ in range(2)]
    ptt = [ar.alloc(128, 512, BF16) for _ in range(2)]
    last_ev = (cc1, cc2)
    for phq in range(4):
        qts = [phq * 4 + j for j in range(4)]
        sc_prev = [None] * 4
        kld_hist = []
        for kc in range(8):
            rank = (kc * 512) // 2048
            kcol = (kc * 512) % 2048
            kb = ktc[kc % 2]
            guard = (kld_hist[kc - 2],) if kc >= 2 else last_ev
            kt_tk = p.dma("sync", kb.rearrange("p (c f) -> p c f", f=512),
                          hb["kout_h"][rank * 1024:(rank + 1) * 1024, kcol:kcol + 512]
                          .rearrange("(c p) f -> p c f", p=128), waits=guard)
            for j in range(4):
                fns = [lambda e, ec=ec, j=j, qt=qts[j], kb=kb: e.matmul(
                    ps[0:128, j * 512:(j + 1) * 512], QTf[ec][:, qt * 128:(qt + 1) * 128],
                    kb[:, ec * 512:(ec + 1) * 512], start=(ec == 0), stop=(ec == 7)) for ec in range(8)]
                mm = p.mmg(fns, waits=(kt_tk,) + ((sc_prev[j],) if sc_prev[j] else (last_ev if kc == 0 else ())))
                ev = p.ve(lambda e, j=j, kc=kc: e.tensor_scalar(
                    scr[j][:, kc * 512:(kc + 1) * 512], ps[0:128, j * 512:(j + 1) * 512],
                    1.0 / 32.0, None, op0=ALU.mult), waits=(mm,))
                sc_prev[j] = ev
            kld_hist.append(sc_prev[3])
        pst = []
        for j in range(4):
            r1 = p.ve(lambda e, j=j: e.reduce_max(negm[:, j:j + 1], scr[j][:, 0:4096], axis=AX.X,
                      negate=True), waits=(sc_prev[3],) + ((pst[-1],) if pst else ()))
            ex = p.ve(lambda e, j=j: e.activation(pbuf[:, 0:4096], scr[j][:, 0:4096], AF.Exp,
                      bias=negm[:, j:j + 1], accum_out=stat[:, 0:1]), eng="scalar", waits=(r1,))
            rc = p.sve(lambda e, j=j: e.reciprocal(rden[:, j:j + 1], stat[:, 0:1]), waits=(ex,))
            pst.append(p.dma("sync", hb["p_h"][j * 128:(j + 1) * 128, :], pbuf[:, 0:4096], waits=(ex, rc)))
        avmm = []
        vld_hist = []
        for kc in range(32):
            ptb = ptt[kc % 2]
            vb = vtc[kc % 2]
            guard = (avmm[(kc - 2) * 4 + 3],) if kc >= 2 else (pst[-1],)
            pt_tk = p.dma("scalar", ptb, hb["p_h"][0:512, kc * 128:(kc + 1) * 128],
                          waits=guard + (pst[-1],), transpose=True)
            v_tk = p.dma("sync", vb, hb["vout_h"][kc * 128:(kc + 1) * 128, :], waits=guard)
            for j in range(4):
                fns = [lambda e, j=j, vc=vc, ptb=ptb, vb=vb, kc=kc: e.matmul(
                    ps[0:128, j * 1024 + vc * 512: j * 1024 + (vc + 1) * 512],
                    ptb[:, j * 128:(j + 1) * 128], vb[:, vc * 512:(vc + 1) * 512],
                    start=(kc == 0), stop=(kc == 31)) for vc in range(2)]
                mm = p.mmg(fns, waits=(pt_tk, v_tk))
                avmm.append(mm)
        evs = []
        for j in range(4):
            e1 = p.ve(lambda e, j=j, qt=qts[j]: e.scalar_tensor_tensor(
                xt[qt], ps[0:128, j * 1024:(j + 1) * 1024], rden[:, j:j + 1], xt[qt],
                op0=ALU.mult, op1=ALU.add), waits=(avmm[-1],))
            evs.append(e1)
        last_ev = tuple(evs) + (avmm[-1],)
    ar.pop()
    ar.pop()
    return evs[-1]



def emit_l1ffn_prep(p, ar, hb, per):
    ps = per["psum"]
    prev_done = ()
    for l in range(2):
        ar.push()
        wt = [ar.alloc(128, 5632, F32) for _ in range(2)]
        wq = ar.alloc(128, 5632, BF16)
        lgc = ar.alloc(128, 8, F32)
        lbc = ar.alloc(128, 8, F32)
        urow = ar.alloc(1, 5632, F32)
        crow = ar.alloc(1, 5632, F32)
        b1r = ar.alloc(1, 5632, F32)
        ocb = ar.alloc(128, 1, BF16)
        t0 = p.dma("sync", lgc, hb[f"f1lg{l}"][:, :])
        t1 = p.dma("sync", lbc, hb[f"f1lb{l}"][:, :])
        t2 = p.dma("sync", b1r, hb[f"f1b1p{l}"][:, :])
        m0 = p.ve(lambda e: e.memset(urow[0:1, 0:5632], 0.0), waits=prev_done)
        m1 = p.ve(lambda e: e.memset(ocb, 1.0))
        cw = p.sve(lambda e: e.tensor_copy(crow[0:1, 0:5632], b1r[0:1, 0:5632]), waits=(t2, m0))
        guard = []
        last = cw
        for dc in range(8):
            buf = wt[dc % 2]
            tk = p.dma("sync", buf, hb[f"f1w1p{l}"][dc * 128:(dc + 1) * 128, :],
                       waits=(guard[dc - 2],) if dc >= 2 else (t0, t1))
            q1 = p.ve(lambda e, b=buf, dc=dc: e.tensor_scalar(wq[:, 0:5632], b, lgc[:, dc:dc + 1], None,
                      op0=ALU.mult), waits=(tk, last))
            st = p.dma("sync", hb[f"w1p_h{l}"][dc * 128:(dc + 1) * 128, :], wq[:, 0:5632], waits=(q1,))
            acc = q1
            for fc in range(11):
                mmu = p.ve(lambda e, fc=fc: e.matmul(ps[0:1, 0:512], ocb,
                           wq[:, fc * 512:(fc + 1) * 512], start=True, stop=True), eng="tensor",
                           waits=(q1, m1, acc))
                ua = p.sve(lambda e, fc=fc: e.tensor_add(urow[0:1, fc * 512:(fc + 1) * 512],
                           urow[0:1, fc * 512:(fc + 1) * 512], ps[0:1, 0:512]), waits=(mmu,))
                mmc = p.ve(lambda e, fc=fc, b=buf, dc=dc: e.matmul(ps[0:1, 512:1024],
                           lbc[:, dc:dc + 1], b[:, fc * 512:(fc + 1) * 512], start=True, stop=True),
                           eng="tensor", waits=(ua,))
                acc = p.sve(lambda e, fc=fc: e.tensor_add(crow[0:1, fc * 512:(fc + 1) * 512],
                            crow[0:1, fc * 512:(fc + 1) * 512], ps[0:1, 512:1024]), waits=(mmc,))
            guard.append(acc)
            last = st
        su = p.dma("sync", hb[f"ucrow_h{l}"][0:1, :], urow[0:1, 0:5632], waits=(guard[-1],))
        sc2 = p.dma("sync", hb[f"ucrow_h{l}"][1:2, :], crow[0:1, 0:5632], waits=(guard[-1],))
        w2cb = [ar.alloc(128, 1024, BF16) for _ in range(2)]
        w2st = []
        for hc2 in range(22):
            b2_ = w2cb[hc2 % 2]
            tkc = p.dma("gpsimd", b2_, hb[f"f1w2{l}"][hc2 * 128:(hc2 + 1) * 128, :],
                        waits=(w2st[hc2 - 2],) if hc2 >= 2 else prev_done)
            w2st.append(p.dma("scalar", hb[f"w2b_h{l}"][hc2 * 128:(hc2 + 1) * 128, :], b2_,
                        waits=(tkc,)))
        p.raw("sync", lambda e: None, (su, sc2, last, w2st[-1], w2st[-2]), ())
        prev_done = (su, sc2, last, w2st[-1], w2st[-2])
        ar.pop()


def emit_l1ffn(p, ar, hb, per, xt):
    ps = per["psum"]
    identb = per["ident_b"]
    identf = per["ident_f"]
    onesr = per["onesrow_f"]
    ar.push()
    stat = ar.alloc(128, 16, F32)
    role1v = ar.alloc(128, 1, F32)
    ocb = ar.alloc(128, 1, BF16)
    orb = ar.alloc(1, 128, BF16)
    ocf = ar.alloc(128, 1, F32)
    lnt = ar.alloc(128, 1024, BF16)
    r0 = p.dma("sync", role1v, hb["role1"][:, :])
    mo = p.ve(lambda e: e.memset(ocb, 1.0))
    mo2 = p.ve(lambda e: e.memset(orb[0:1, 0:128], 1.0))
    mo3 = p.ve(lambda e: e.memset(ocf, 1.0))

    sp = []
    for i in range(NT):
        sp.append(p.dma("sync", hb["xsp_h"][i * 128:(i + 1) * 128, :], xt[i]))
    ar.push()
    sqj = ar.alloc(128, 1024, F32)
    prevq = None
    for i in range(NT):
        x = xt[i]
        r1 = p.ve(lambda e, x=x, s=stat: e.reduce_sum(s[:, 0:1], x, axis=AX.X), waits=(sp[i],))
        r2 = p.ve(lambda e, x=x, s=stat: e.activation(sqj[:, 0:1024], x, AF.Square,
                  accum_out=s[:, 1:2]), eng="scalar", waits=((prevq,) if prevq else ()))
        v = p.sve(lambda e, s=stat: e.tensor_scalar(s[:, 0:1], s[:, 0:1], 1.0 / D, None, op0=ALU.mult), waits=(r1, r2))
        v = p.sve(lambda e, s=stat: e.tensor_scalar(s[:, 1:2], s[:, 1:2], 1.0 / D, None, op0=ALU.mult))
        v = p.sve(lambda e, s=stat: e.tensor_tensor(s[:, 2:3], s[:, 0:1], s[:, 0:1], op=ALU.mult))
        v = p.sve(lambda e, s=stat: e.tensor_sub(s[:, 2:3], s[:, 1:2], s[:, 2:3]))
        v = p.sve(lambda e, s=stat: e.tensor_scalar_add(s[:, 3:4], s[:, 2:3], 1e-5))
        v = p.sve(lambda e, s=stat: e.reciprocal(s[:, 3:4], s[:, 3:4]))
        v = p.sve(lambda e, s=stat: e.activation(s[:, 4:5], s[:, 3:4], AF.Sqrt), eng="scalar", waits=(v,))
        v = p.sve(lambda e, s=stat: e.tensor_tensor(s[:, 5:6], s[:, 0:1], s[:, 4:5], op=ALU.mult), waits=(v,))
        v = p.sve(lambda e, s=stat: e.tensor_scalar(s[:, 5:6], s[:, 5:6], -1.0, None, op0=ALU.mult))
        prevq = p.sve(lambda e, x=x, s=stat: e.tensor_scalar(x, x, s[:, 4:5], s[:, 5:6],
                      op0=ALU.mult, op1=ALU.add), waits=(v,))
    ar.pop()
    x_ready = (prevq,)
    if F1_TRUNC <= 0:
        ar.pop()
        return prevq

    for l in range(2):
        xist = []
        for i in range(NT):
            c1 = p.ve(lambda e, i=i: e.tensor_copy(lnt[:, 0:1024], xt[i]),
                      waits=x_ready + ((xist[-1],) if xist else ()))
            xist.append(p.dma("sync", hb["xi_h"][i * 128:(i + 1) * 128, :], lnt[:, 0:1024], waits=(c1,)))
        ccA = p.collective("AllGather", ALU.bypass, [hb["xi_h"][1535:2048, :].opt()],
                           [hb["bndout_h"][:, :].opt()], waits=(xist[-1],))
        if F1_TRUNC <= 0.5 + 2 * l:
            ar.pop()
            return xist[-1]

        ar.push()
        vsb = [ar.alloc(128, 512, BF16) for _ in range(20)]
        oT = [ar.alloc(128, 2048, BF16) for _ in range(4)]
        ar.push()
        xT = [ar.alloc(128, 2576, BF16) for _ in range(8)]
        wkv = [ar.alloc(128, 1536, BF16) for _ in range(8)]
        qkvt2 = [ar.alloc(128, 1536, F32) for _ in range(2)]
        tbl = {nm: [ar.alloc(128, 64, BF16) for _ in range(20)] for nm in ("cq", "sq", "ck", "sk")}
        qsb = ar.alloc(128, 64, F32)
        ksb = ar.alloc(128, 64, F32)
        qsbb = ar.alloc(128, 64, BF16)
        ksbb = ar.alloc(128, 64, BF16)
        hnrm = ar.alloc(128, 16, F32)
        t1b = ar.alloc(128, 512, BF16)
        t2b = ar.alloc(128, 512, BF16)
        qout = ar.alloc(128, 512, BF16)
        ldw = tuple(p.dma("gpsimd", wkv[dc], hb[f"f1wqkv{l}"][dc * 128:(dc + 1) * 128, :]) for dc in range(8))
        ldt = []
        for nm in ("cq", "sq", "ck", "sk"):
            for ti in range(20):
                ldt.append(p.dma("gpsimd", tbl[nm][ti], hb[nm][ti * 128:(ti + 1) * 128, :]))
        ldq = p.dma("sync", qsb[0:1, 0:64], hb[f"f1qs{l}"][:, :])
        ldk = p.dma("sync", ksb[0:1, 0:64], hb[f"f1ks{l}"][:, :])
        bq = p.ve(lambda e: e.matmul(ps[0:128, 0:64], onesr[0:1, 0:128], qsb[0:1, 0:64],
                  start=True, stop=True), eng="tensor", waits=(ldq, ldk))
        bq2 = p.ve(lambda e: e.tensor_scalar(qsbb[:, 0:64], ps[0:128, 0:64], 8.0, None, op0=ALU.mult), waits=(bq,))
        bk = p.ve(lambda e: e.matmul(ps[0:128, 64:128], onesr[0:1, 0:128], ksb[0:1, 0:64],
                  start=True, stop=True), eng="tensor", waits=(bq2,))
        bk2 = p.ve(lambda e: e.tensor_copy(ksbb[:, 0:64], ps[0:128, 64:128]), waits=(bk,))
        fold = []
        fw = (bq2, bk2) + tuple(ldt)
        for ti in range(20):
            fold.append(p.ve(lambda e, ti=ti: e.tensor_tensor(tbl["cq"][ti], tbl["cq"][ti], qsbb[:, 0:64], op=ALU.mult), waits=fw))
            fold.append(p.ve(lambda e, ti=ti: e.tensor_tensor(tbl["sq"][ti], tbl["sq"][ti], qsbb[:, 0:64], op=ALU.mult)))
            fold.append(p.ve(lambda e, ti=ti: e.tensor_tensor(tbl["ck"][ti], tbl["ck"][ti], ksbb[:, 0:64], op=ALU.mult)))
            fold.append(p.ve(lambda e, ti=ti: e.tensor_tensor(tbl["sk"][ti], tbl["sk"][ti], ksbb[:, 0:64], op=ALU.mult)))
        fold_done = fold[-1]
        xtk = []
        for dc in range(8):
            xtk.append(p.dma("sync", xT[dc][:, 528:2576], hb["xi_h"][:, dc * 128:(dc + 1) * 128],
                             waits=(xist[-1],), transpose=True))
            xtk.append(p.dma("sync", xT[dc][:, 16:528], hb["bndout_h"][0:512, dc * 128:(dc + 1) * 128],
                             waits=(ccA,), transpose=True))
            xtk.append(p.dma("gpsimd", xT[dc][:, 15:16],
                             hb["bndout_h"][512:513, dc * 128:(dc + 1) * 128].rearrange("a b -> b a"),
                             waits=(ccA,)))
        msk = []
        for dc in range(8):
            msk.append(p.ve(lambda e, dc=dc: e.tensor_scalar(xT[dc][:, 15:528], xT[dc][:, 15:528],
                       role1v[:, 0:1], None, op0=ALU.mult), waits=tuple(xtk) + (r0,)))
        msk = tuple(msk)
        qkst = []
        prev_ev = None
        ev_last = [None, None]
        cons_last = [None, None]
        n6c = None
        for ti in range(20):
            par = ti % 2
            pb_ = par * 2048
            qkvt = qkvt2[par]
            fns = []
            for dc in range(8):
                off = 16 if dc < 4 else 15
                lhs = xT[dc][:, off + ti * 128: off + ti * 128 + 128]
                for fc in range(3):
                    fns.append(lambda e, lhs=lhs, dc=dc, fc=fc, pb_=pb_: e.matmul(
                        ps[0:128, pb_ + fc * 512:pb_ + (fc + 1) * 512], lhs, wkv[dc][:, fc * 512:(fc + 1) * 512],
                        start=(dc == 0), stop=(dc == 7)))
            mmw = msk + ldw
            if ti == 0:
                mmw += (bk2, bq2)
            if ev_last[par] is not None:
                mmw += (ev_last[par],)
            mm = p.mmg(fns, waits=mmw)
            evw = (mm,)
            if cons_last[par] is not None:
                evw += cons_last[par]
            ev = p.ve(lambda e, qkvt=qkvt, pb_=pb_: e.tensor_copy(qkvt[:, 0:1536],
                      ps[0:128, pb_:pb_ + 1536]), waits=evw)
            ev_last[par] = ev
            n6 = None
            for base, ct, st_, outh in ((0, "cq", "sq", "qh_h"), (512, "ck", "sk", "kh_h")):
                vvw = (ev,) + ((n6,) if n6 else ((n6c,) if n6c else ()))
                vv = p.ve(lambda e, base=base, qkvt=qkvt: e.activation(t1b[:, 0:512], qkvt[:, base:base + 512],
                          AF.Square), eng="scalar", waits=vvw)
                rr = p.sve(lambda e: e.tensor_reduce(hnrm[:, 0:8],
                           t1b[:, 0:512].rearrange("p (h d) -> p h d", d=64), axis=AX.X, op=ALU.add),
                           waits=(vv,))
                rr = p.sve(lambda e: e.activation(hnrm[:, 0:8], hnrm[:, 0:8], AF.Sqrt), eng="scalar", waits=(rr,))
                rr = p.sve(lambda e: e.tensor_scalar(hnrm[:, 0:8], hnrm[:, 0:8], 1e-12, None, op0=ALU.max), waits=(rr,))
                rr = p.sve(lambda e: e.reciprocal(hnrm[:, 8:16], hnrm[:, 0:8]))
                n1 = p.sve(lambda e, base=base, qkvt=qkvt: e.tensor_tensor(
                    t1b[:, 0:512].rearrange("p (h d) -> p h d", d=64),
                    qkvt[:, base:base + 512].rearrange("p (h d) -> p h d", d=64),
                    hnrm[:, 8:16].unsqueeze(2).broadcast_to((128, 8, 64)), op=ALU.mult), waits=(rr,))
                n2 = p.sve(lambda e: e.tensor_scalar(
                    t2b[:, 0:512].rearrange("p (h d) -> p h d", d=64)[:, :, 0:32],
                    t1b[:, 0:512].rearrange("p (h d) -> p h d", d=64)[:, :, 32:64],
                    -1.0, None, op0=ALU.mult), waits=(n1,))
                n3 = p.sve(lambda e: e.tensor_copy(
                    t2b[:, 0:512].rearrange("p (h d) -> p h d", d=64)[:, :, 32:64],
                    t1b[:, 0:512].rearrange("p (h d) -> p h d", d=64)[:, :, 0:32]))
                n4 = p.sve(lambda e, ti=ti, ct=ct: e.tensor_tensor(
                    t1b[:, 0:512].rearrange("p (h d) -> p h d", d=64),
                    t1b[:, 0:512].rearrange("p (h d) -> p h d", d=64),
                    tbl[ct][ti].unsqueeze(1).broadcast_to((128, 8, 64)), op=ALU.mult), waits=(n3, fold_done))
                n5 = p.sve(lambda e, ti=ti, st_=st_: e.tensor_tensor(
                    t2b[:, 0:512].rearrange("p (h d) -> p h d", d=64),
                    t2b[:, 0:512].rearrange("p (h d) -> p h d", d=64),
                    tbl[st_][ti].unsqueeze(1).broadcast_to((128, 8, 64)), op=ALU.mult))
                n6 = p.sve(lambda e: e.tensor_add(qout[:, 0:512], t1b[:, 0:512], t2b[:, 0:512]))
                qkst.append(p.dma("sync", hb[outh][ti * 128:(ti + 1) * 128, :], qout[:, 0:512], waits=(n6,)))
                p.raw("vector", lambda e: None, (qkst[-1],), ())
            prev_ev = p.ve(lambda e, ti=ti, qkvt=qkvt: e.tensor_copy(vsb[ti][:, 0:512], qkvt[:, 1024:1536]),
                           waits=(ev,) + (qkst[-1],))
            cons_last[par] = (prev_ev,)
            n6c = n6
        qkv_done = (prev_ev, qkst[-1])
        ar.pop()
        if F1_TRUNC <= 0.7 + 2 * l:
            ar.pop()
            ar.pop()
            return prev_ev
        ar.push()
        qhT = [ar.alloc(128, 2560, BF16) for _ in range(4)]
        khT = [ar.alloc(128, 2560, BF16) for _ in range(4)]
        lmsk = [ar.alloc(128, 1024, BF16) for _ in range(4)]
        lmskf = [ar.alloc(128, 1024, BF16) for _ in range(4)]
        psb = ar.alloc(128, 8192, BF16)
        ptb = [ar.alloc(128, 2048, BF16) for _ in range(2)]
        rden8 = ar.alloc(128, 8, F32)
        negm8 = ar.alloc(128, 8, F32)
        qtk = tuple(p.dma("sync", qhT[hc], hb["qh_h"][:, hc * 128:(hc + 1) * 128],
                          waits=qkv_done, transpose=True) for hc in range(4))
        ktk = tuple(p.dma("sync", khT[hc], hb["kh_h"][:, hc * 128:(hc + 1) * 128],
                          waits=qkv_done, transpose=True) for hc in range(4))
        mtk = tuple(p.dma("gpsimd", lmsk[q4], hb["lmask"][q4 * 128:(q4 + 1) * 128, :], waits=qkv_done) for q4 in range(4)) + \
              tuple(p.dma("gpsimd", lmskf[q4], hb["lmaskf"][q4 * 128:(q4 + 1) * 128, :], waits=qkv_done) for q4 in range(4))
        pstores = []
        for w in range(4):
            for q4 in range(4):
                exps = []
                hdev = {}
                for hd in range(8):
                    hc, hr = hd // 2, (hd % 2) * 64
                    qcol = 512 + w * 512 + q4 * 128
                    lm = lmskf if w == 0 else lmsk
                    pb_ = (hd % 2) * 2048
                    fns = [lambda e, hc=hc, hr=hr, qcol=qcol, kk=kk, w=w, pb_=pb_: e.matmul(
                        ps[0:128, pb_ + kk * 512:pb_ + (kk + 1) * 512],
                        qhT[hc][hr:hr + 64, qcol:qcol + 128],
                        khT[hc][hr:hr + 64, w * 512 + kk * 512: w * 512 + (kk + 1) * 512],
                        start=True, stop=False) for kk in range(2)]
                    fns += [lambda e, q4=q4, kk=kk, lm=lm, pb_=pb_: e.matmul(
                        ps[0:128, pb_ + kk * 512:pb_ + (kk + 1) * 512], identb[:, 0:128],
                        lm[q4][:, kk * 512:(kk + 1) * 512], start=False, stop=True) for kk in range(2)]
                    mmw2 = qtk + ktk + mtk
                    if hd >= 2:
                        mmw2 += hdev[hd - 2]
                    elif pstores:
                        mmw2 += (pstores[-1],)
                    mm = p.mmg(fns, waits=mmw2)
                    r1 = p.ve(lambda e, hd=hd, pb_=pb_: e.reduce_max(negm8[:, hd:hd + 1],
                              ps[0:128, pb_:pb_ + 1024],
                              axis=AX.X, negate=True), waits=(mm,))
                    ex = p.ve(lambda e, hd=hd, pb_=pb_: e.activation(psb[:, hd * 1024:(hd + 1) * 1024],
                              ps[0:128, pb_:pb_ + 1024], AF.Exp, bias=negm8[:, hd:hd + 1],
                              accum_out=stat[:, hd:hd + 1]), eng="scalar", waits=(r1,))
                    rc = p.ve(lambda e, hd=hd: e.reciprocal(rden8[:, hd:hd + 1], stat[:, hd:hd + 1]),
                              waits=(ex,))
                    hdev[hd] = (r1, ex)
                    exps.append(rc)
                    exps.append(ex)
                nrm = p.ve(lambda e: e.tensor_tensor(
                    psb[:, 0:8192].rearrange("p (h k) -> p h k", k=1024),
                    psb[:, 0:8192].rearrange("p (h k) -> p h k", k=1024),
                    rden8[:, 0:8].unsqueeze(2).broadcast_to((128, 8, 1024)), op=ALU.mult),
                    waits=tuple(exps))
                pstores.append(p.dma("sync", hb["ph_h"][(w * 4 + q4) * 128:(w * 4 + q4 + 1) * 128, :],
                               psb[:, 0:8192], waits=(nrm,)))
                p.raw("vector", lambda e: None, (pstores[-1],), ())
        if F1_TRUNC <= 0.8 + 2 * l:
            ar.pop()
            ar.pop()
            ar.pop()
            return pstores[-1]
        avh = []
        prev_oev = None
        for hd in range(8):
            hc, hr = hd // 2, (hd % 2) * 64
            for kk in range(8):
                pb = ptb[kk % 2]
                idx = hd * 8 + kk
                ptk = p.dma("scalar", pb, hb["ph_h"][0:2048, hd * 1024 + kk * 128: hd * 1024 + (kk + 1) * 128],
                            waits=(pstores[-1],) + ((avh[idx - 2],) if idx >= 2 else ()), transpose=True)
                fns = [lambda e, w=w2, kk=kk, hd=hd, pb=pb: e.matmul(
                    ps[0:64, 2048 + w * 512: 2048 + (w + 1) * 512],
                    vsb[w * 4 + kk][:, hd * 64:hd * 64 + 64],
                    pb[:, w * 512:(w + 1) * 512], start=(kk == 0), stop=(kk == 7)) for w2 in range(4)]
                mm = p.mmg(fns, waits=(ptk,) + ((prev_oev,) if prev_oev and kk == 0 else ()))
                avh.append(mm)
            for w in range(4):
                oe = p.ve(lambda e, w=w, hc=hc, hr=hr: e.tensor_copy(
                    oT[hc][hr:hr + 64, w * 512:(w + 1) * 512],
                    ps[0:64, 2048 + w * 512:2048 + (w + 1) * 512]), waits=(avh[-1],))
            prev_oev = oe
        if F1_TRUNC <= 0.9 + 2 * l:
            ar.pop()
            ar.pop()
            ar.pop()
            return prev_oev
        ar.pop()
        ar.push()
        wosb = [ar.alloc(128, 1024, BF16) for _ in range(4)]
        wotk = tuple(p.dma("gpsimd", wosb[ic], hb[f"f1wo{l}"][ic * 128:(ic + 1) * 128, :],
                           waits=(prev_oev,)) for ic in range(4))
        prev = None
        for ti in range(NT):
            fns = []
            for ic in range(4):
                for vc2 in range(2):
                    fns.append(lambda e, ic=ic, vc2=vc2, ti=ti: e.matmul(
                        ps[0:128, vc2 * 512:(vc2 + 1) * 512], oT[ic][:, ti * 128:(ti + 1) * 128],
                        wosb[ic][:, vc2 * 512:(vc2 + 1) * 512], start=(ic == 0), stop=(ic == 3)))
            mm = p.mmg(fns, waits=wotk + (prev_oev,) + ((prev,) if prev else ()))
            prev = p.ve(lambda e, ti=ti: e.tensor_add(xt[ti], xt[ti], ps[0:128, 0:1024]), waits=(mm,))
        ar.pop()
        ar.pop()
        x_ready = (prev,)
        if F1_TRUNC <= 1 + 2 * l:
            ar.pop()
            return prev

        c1 = p.ve(lambda e: e.tensor_copy(lnt[:, 0:1024], xt[NT - 1]), waits=x_ready)
        sx1 = p.dma("gpsimd", hb["bnd1in_h"][0:1, :], lnt[127:128, 0:1024], waits=(c1,))
        ccB = p.collective("AllGather", ALU.bypass, [hb["bnd1in_h"][:, :].opt()],
                           [hb["bnd1out_h"][:, :].opt()], waits=(sx1,))

        xist = []
        for i in range(NT):
            cc = p.ve(lambda e, i=i: e.tensor_copy(lnt[:, 0:1024], xt[i]),
                      waits=x_ready + (c1, sx1) + ((xist[-1],) if xist else ()))
            xist.append(p.dma("sync", hb["xi_h"][i * 128:(i + 1) * 128, :], lnt[:, 0:1024], waits=(cc,)))
        if F1_TRUNC <= 1.5 + 2 * l:
            ar.pop()
            return xist[-1]
        ar.push()
        xT2 = [ar.alloc(128, 2064, BF16) for _ in range(8)]
        statrows = ar.alloc(1, 6144, F32)
        murow = statrows[0:1, 0:2048]
        msqrow = statrows[0:1, 2048:4096]
        rvrow = statrows[0:1, 4096:6144]
        brows = ar.alloc(1, 4096, BF16)
        nmurow = brows[0:1, 0:2048]
        rvirow = brows[0:1, 2048:4096]
        rvsq = ar.alloc(16, 128, F32)
        rvcol = ar.alloc(128, 16, F32)
        sqsc = ar.alloc(128, 512, F32)
        ucrow_t = ar.alloc(1, 11264, BF16)
        ucrow = ucrow_t[0:1, 0:5632]
        ccrow = ucrow_t[0:1, 5632:11264]
        hhtile = [ar.alloc(128, 1024, BF16) for _ in range(2)]
        h1p = ar.alloc(128, 512, F32)
        ggs = ar.alloc(128, 256, F32)
        global F1_REFS
        F1_REFS = dict(xT2=xT2, statrows=statrows, rvcol=rvcol, ucrow=ucrow, ccrow=ccrow,
                       hhtile=hhtile, h1p=h1p, ggs=ggs)
        xtk2 = []
        for dc in range(8):
            xtk2.append(p.dma("sync", xT2[dc][:, 16:2064], hb["xi_h"][:, dc * 128:(dc + 1) * 128],
                              waits=(xist[-1],), transpose=True))
            xtk2.append(p.dma("gpsimd", xT2[dc][:, 15:16],
                              hb["bnd1out_h"][0:1, dc * 128:(dc + 1) * 128].rearrange("a b -> b a"),
                              waits=(ccB,)))
        msk2 = []
        for dc in range(8):
            msk2.append(p.ve(lambda e, dc=dc: e.tensor_scalar(xT2[dc][:, 15:16], xT2[dc][:, 15:16],
                        role1v[:, 0:1], None, op0=ALU.mult), waits=tuple(xtk2)))
        msk2 = tuple(msk2)
        lduc0 = p.dma("gpsimd", ucrow, hb[f"ucrow_h{l}"][0:1, :])
        lduc = p.dma("gpsimd", ccrow, hb[f"ucrow_h{l}"][1:2, :])

        def hsl(dc, c0, c1):
            off = 16 if dc < 4 else 15
            return xT2[dc][:, off + c0: off + c1]

        stev = []
        for tc in range(4):
            fns = [lambda e, dc=dc, tc=tc: e.matmul(ps[0:1, 0:512], ocb,
                   hsl(dc, tc * 512, (tc + 1) * 512), start=(dc == 0), stop=(dc == 7)) for dc in range(8)]
            mm1_ = p.mmg(fns, waits=msk2 + (mo,) + ((stev[-1],) if stev else ()))
            stev.append(p.ve(lambda e, tc=tc: e.tensor_scalar(murow[0:1, tc * 512:(tc + 1) * 512],
                        ps[0:1, 0:512], 1.0 / D, None, op0=ALU.mult), waits=(mm1_,)))
        for tc in range(4):
            sq_p = None
            for dc in range(8):
                sq = p.ve(lambda e, dc=dc, tc=tc: e.activation(sqsc[:, 0:512],
                          hsl(dc, tc * 512, (tc + 1) * 512), AF.Square), eng="scalar",
                          waits=((sq_p,) if sq_p else (stev[-1],)))
                sq_p = p.ve(lambda e, dc=dc: e.matmul(ps[0:1, 512:1024], ocf, sqsc[:, 0:512],
                            start=(dc == 0), stop=(dc == 7)), eng="tensor", waits=(sq, mo3))
            stev.append(p.ve(lambda e, tc=tc: e.tensor_scalar(msqrow[0:1, tc * 512:(tc + 1) * 512],
                        ps[0:1, 512:1024], 1.0 / D, None, op0=ALU.mult), waits=(sq_p,)))
        v = p.sve(lambda e: e.tensor_tensor(rvrow[0:1, 0:2048], murow[0:1, 0:2048], murow[0:1, 0:2048],
                  op=ALU.mult), waits=tuple(stev))
        v = p.sve(lambda e: e.tensor_sub(msqrow[0:1, 0:2048], msqrow[0:1, 0:2048], rvrow[0:1, 0:2048]))
        v = p.sve(lambda e: e.tensor_scalar_add(msqrow[0:1, 0:2048], msqrow[0:1, 0:2048], 1e-5))
        v = p.sve(lambda e: e.activation(rvirow[0:1, 0:2048], msqrow[0:1, 0:2048], AF.Sqrt), eng="scalar", waits=(v,))
        v = p.sve(lambda e: e.reciprocal(rvrow[0:1, 0:2048], msqrow[0:1, 0:2048]), waits=(v,))
        v = p.sve(lambda e: e.activation(rvrow[0:1, 0:2048], rvrow[0:1, 0:2048], AF.Sqrt), eng="scalar", waits=(v,))
        v = p.sve(lambda e: e.tensor_scalar(nmurow[0:1, 0:2048], murow[0:1, 0:2048], -1.0, None, op0=ALU.mult), waits=(v,))
        if A0_DEBUG:
            p.dma("sync", hb["dbgf1_h"][0:1, :], murow[0:1, 0:2048], waits=(v,))
            p.dma("sync", hb["dbgf1_h"][1:2, :], msqrow[0:1, 0:2048], waits=(v,))
        s1 = p.dma("sync", hb["rv_h"][0:1, :], rvrow[0:1, 0:2048], waits=(v,))
        s2 = p.dma("sync", rvsq, hb["rv_h"].rearrange("o (r c) -> (o r) c", c=128), waits=(s1,))
        tr = p.ve(lambda e: e.transpose(ps[0:128, 1024:1040], rvsq[0:16, 0:128], identf[0:16, 0:16]),
                  eng="tensor", waits=(s2,))
        trc = p.ve(lambda e: e.tensor_copy(rvcol[:, 0:16], ps[0:128, 1024:1040]), waits=(tr,))

        ar.push()
        w1h = [ar.alloc(128, 1536, BF16) for _ in range(8)]
        F1_REFS['w1h'] = w1h
        hh_done = []
        e1_last = [None, None]
        st_last = [None, None]
        e3_prev = None
        mm_prev_half = None
        gt = 0
        for half, (f0, nfc) in enumerate(((0, 3), (3, 3), (6, 3), (9, 2))):
            w1tk = tuple(p.dma("sync", w1h[dc][:, 0:nfc * 512],
                               hb[f"w1p_h{l}"][dc * 128:(dc + 1) * 128, f0 * 512:(f0 + nfc) * 512],
                               waits=((mm_prev_half,) if mm_prev_half else (trc,))) for dc in range(8))
            for ti in range(NT):
                par = gt % 2
                pb_ = par * 2048
                fns = []
                for dc in range(8):
                    off = 16 if dc < 4 else 15
                    lhs = xT2[dc][:, off + ti * 128: off + ti * 128 + 128]
                    for fc in range(nfc):
                        fns.append(lambda e, lhs=lhs, dc=dc, fc=fc, pb_=pb_: e.matmul(
                            ps[0:128, pb_ + fc * 512:pb_ + (fc + 1) * 512], lhs, w1h[dc][:, fc * 512:(fc + 1) * 512],
                            start=(dc == 0), stop=False))
                for fc in range(nfc):
                    fns.append(lambda e, fc=fc, f0=f0, ti=ti, pb_=pb_: e.matmul(
                        ps[0:128, pb_ + fc * 512:pb_ + (fc + 1) * 512], nmurow[0:1, ti * 128:(ti + 1) * 128],
                        ucrow[0:1, (f0 + fc) * 512:(f0 + fc + 1) * 512], start=False, stop=False))
                    fns.append(lambda e, fc=fc, f0=f0, ti=ti, pb_=pb_: e.matmul(
                        ps[0:128, pb_ + fc * 512:pb_ + (fc + 1) * 512], rvirow[0:1, ti * 128:(ti + 1) * 128],
                        ccrow[0:1, (f0 + fc) * 512:(f0 + fc + 1) * 512], start=False, stop=(True)))
                mmw = w1tk + (lduc0, lduc, trc)
                if e1_last[par] is not None:
                    mmw += (e1_last[par],)
                mm = p.mmg(fns, waits=mmw)
                hb_ = hhtile[par]
                hev = []
                dbgT = []
                for fc in range(nfc):
                    e1w = (mm,) + (tuple(hev[-1:]) if hev else ()) + tuple(dbgT[-1:])
                    if fc == 0 and st_last[par] is not None:
                        e1w += (st_last[par],)
                    if fc == 0 and e3_prev is not None:
                        e1w += (e3_prev,)
                    e1 = p.ve(lambda e, fc=fc, ti=ti, pb_=pb_: e.tensor_scalar(h1p[:, 0:512],
                              ps[0:128, pb_ + fc * 512:pb_ + (fc + 1) * 512], rvcol[:, ti:ti + 1], None,
                              op0=ALU.mult), waits=e1w)
                    if A0_DEBUG and half == 0 and ti == 0:
                        dbgT.append(p.dma("sync", hb["dbgf1_h"][8 + fc * 128: 8 + (fc + 1) * 128, 0:512],
                                          h1p[:, 0:512], waits=(e1,)))
                    e2w = (e1,)
                    if fc == 0 and e3_prev is not None:
                        e2w += (e3_prev,)
                    e2 = p.ve(lambda e: e.activation(ggs[:, 0:256], h1p[:, 1:512:2],
                              AF.Gelu), eng="scalar", waits=e2w)
                    if A0_DEBUG and half == 0 and ti == 0:
                        dbgT.append(p.dma("sync", hb["dbgf1_h"][520 + fc * 128: 520 + (fc + 1) * 128, 0:256],
                                          ggs[:, 0:256], waits=(e2,)))
                    e3 = p.ve(lambda e, fc=fc, hb_=hb_: e.tensor_tensor(hb_[:, fc * 256:(fc + 1) * 256],
                              h1p[:, 0:512:2], ggs[:, 0:256], op=ALU.mult), waits=(e2,) + tuple(dbgT[-1:]))
                    hev.append(e3)
                    e1_last[par] = e1
                e3_prev = hev[-1]
                if A0_DEBUG and half == 0 and ti == 0:
                    dbgT.append(p.dma("sync", hb["dbgf1_h"][1032:1160, 0:512],
                                      hb_[:, 0:1024].bitcast(F32), waits=tuple(hev)))
                st = p.dma("sync", hb["hh_h"][ti * 128:(ti + 1) * 128, f0 * 256:(f0 + nfc) * 256],
                           hb_[:, 0:nfc * 256], waits=tuple(hev) + tuple(dbgT[-1:]))
                st_last[par] = st
                hh_done.append(st)
                gt += 1
            mm_prev_half = mm
        ar.pop()
        ar.push()
        hhT = [ar.alloc(128, 512, BF16) for _ in range(4)]
        w2s = [ar.alloc(128, 1024, BF16) for _ in range(2)]
        b2r = ar.alloc(1, 1024, BF16)
        F1_REFS['hhT'] = hhT; F1_REFS['w2s'] = w2s; F1_REFS['b2r'] = b2r
        ldb2 = p.dma("gpsimd", b2r, hb[f"f1b2{l}"][:, :], waits=(hh_done[-1],))
        prev2 = None
        for tg in range(4):
            tis = [tg * 4 + j for j in range(4)]
            mm_l = None
            w2g = []
            for hc in range(22):
                wb = w2s[hc % 2]
                hcb = hhT[hc % 4]
                gw = (w2g[hc - 2],) if hc >= 2 else ((prev2,) if prev2 else (hh_done[-1],))
                wtk = p.dma("sync", wb, hb[f"w2b_h{l}"][hc * 128:(hc + 1) * 128, :], waits=gw)
                gh = (w2g[hc - 4],) if hc >= 4 else ((prev2,) if prev2 else (hh_done[-1],))
                htk = p.dma("scalar", hcb, hb["hh_h"][tg * 512:(tg + 1) * 512, hc * 128:(hc + 1) * 128],
                            waits=gh + (hh_done[-1],), transpose=True)
                fns = []
                for j in range(4):
                    for vc2 in range(2):
                        fns.append(lambda e, j=j, vc2=vc2, hc=hc, hcb=hcb, wb=wb: e.matmul(
                            ps[0:128, j * 1024 + vc2 * 512: j * 1024 + (vc2 + 1) * 512],
                            hcb[:, j * 128:(j + 1) * 128], wb[:, vc2 * 512:(vc2 + 1) * 512],
                            start=(hc == 0), stop=False))
                mm_l = p.mmg(fns, waits=(wtk, htk) + ((prev2,) if prev2 and hc == 0 else ()))
                w2g.append(mm_l)
            fns = []
            for j in range(4):
                for vc2 in range(2):
                    fns.append(lambda e, j=j, vc2=vc2: e.matmul(
                        ps[0:128, j * 1024 + vc2 * 512: j * 1024 + (vc2 + 1) * 512],
                        orb[0:1, 0:128], b2r[0:1, vc2 * 512:(vc2 + 1) * 512],
                        start=False, stop=True))
            mm_l = p.mmg(fns, waits=(ldb2, mo2))
            evl = []
            for j in range(4):
                evl.append(p.ve(lambda e, j=j, ti=tis[j]: e.tensor_add(
                    xt[ti], xt[ti], ps[0:128, j * 1024:(j + 1) * 1024]), waits=(mm_l,)))
            prev2 = evl[-1]
        ar.pop()
        ar.pop()
        x_ready = (prev2,)
        if F1_TRUNC <= 2 + 2 * l:
            ar.pop()
            return prev2

    ar.push()
    xo = [ar.alloc(128, 1024, F32) for _ in range(2)]
    fin = None
    for i in range(NT):
        ld = p.dma("sync", xo[i % 2], hb["xsp_h"][i * 128:(i + 1) * 128, :],
                   waits=x_ready + ((fin,) if fin and i >= 2 else ()))
        fin = p.ve(lambda e, i=i, b=xo[i % 2]: e.scalar_tensor_tensor(
            xt[i], xt[i], 0.5, b, op0=ALU.mult, op1=ALU.add), waits=(ld,))
    ar.pop()
    ar.pop()
    return fin


# version 05
# speedup vs baseline: 1.2240x; 1.1246x over previous
import contextlib
import numpy as np
import ml_dtypes
import concourse.bass as bass
import concourse.mybir as mybir
from concourse.bass_utils import run_bass_kernel_spmd

F32 = mybir.dt.float32
BF16 = mybir.dt.bfloat16
AF = mybir.ActivationFunctionType
ALU = mybir.AluOpType
AX = mybir.AxisListType
NPBF = ml_dtypes.bfloat16
MAGIC = float(np.float32(1.5 * 2 ** 23))
NEG = float(np.float32(-1e38))

NCORE = 8
TLOC = 2048
NT = TLOC // 128
D = 1024
RG = [[0, 1], [2, 3], [4, 5], [6, 7]]

ENGINES = ("sync", "gpsimd", "vector", "scalar", "tensor")
QSEM = {"sync": "qs", "gpsimd": "qg", "scalar": "qa"}
ESEM = {"vector": "tv", "scalar": "ts", "tensor": "tt", "gpsimd": "tg"}


class Prog:

    def __init__(self, nc, stack):
        self.nc = nc
        self.ops = {e: [] for e in ENGINES}
        self.sems = {}
        self.cnt = {}
        for s in ("qs", "qg", "qa", "tv", "ts", "tt", "tg", "cc"):
            self.sems[s] = stack.enter_context(nc.semaphore(s))
            self.cnt[s] = 0

    def raw(self, eng, fn, waits=(), incs=()):
        w = tuple(x for x in waits if x is not None)
        self.ops[eng].append((fn, w, tuple(incs)))

    def _tick(self, sem, n):
        self.cnt[sem] += n
        return (sem, self.cnt[sem])

    def dma(self, q, out, in_, waits=(), transpose=False):
        sem = QSEM[q]
        t = self._tick(sem, 16)
        if transpose:
            fn = lambda e, out=out, in_=in_: e.dma_start(out=out, in_=in_, transpose=True)
        else:
            fn = lambda e, out=out, in_=in_: e.dma_start(out=out, in_=in_)
        self.raw(q, fn, waits, [(sem, 16)])
        return t

    def ve(self, fn, waits=(), eng="vector"):
        sem = ESEM[eng]
        t = self._tick(sem, 1)
        self.raw(eng, fn, waits, [(sem, 1)])
        return t

    def sve(self, fn, waits=(), eng="vector"):
        sem = ESEM[eng]
        w = tuple(waits) + ((sem, self.cnt[sem]),)
        return self.ve(fn, waits=w, eng=eng)

    def quiet(self, eng, fn, waits=()):
        self.raw(eng, fn, waits, ())

    def mmg(self, fns, waits=()):
        for i, fn in enumerate(fns):
            if i == 0 and i == len(fns) - 1:
                return self.ve(fn, waits=waits, eng="tensor")
            if i == 0:
                self.quiet("tensor", fn, waits=waits)
            elif i == len(fns) - 1:
                return self.ve(fn, eng="tensor")
            else:
                self.quiet("tensor", fn)

    def collective(self, kind, op, ins, outs, waits=()):
        t = self._tick("cc", 1)
        fn = lambda e, kind=kind, op=op, ins=ins, outs=outs: e.collective_compute(
            kind, op, RG, ins=ins, outs=outs)
        self.raw("gpsimd", fn, waits, [("cc", 1)])
        return t

    def now(self, sem):
        return (sem, self.cnt[sem])

    def barrier(self):
        marks = [self.now(s) for s in ("qs", "qg", "qa", "tv", "ts", "tt", "tg", "cc")]
        marks = tuple(m for m in marks if m[1] > 0)
        for e in ENGINES:
            self.raw(e, lambda eng: None, marks, ())

    def check(self):
        vals = {s: 0 for s in self.sems}
        ptr = {e: 0 for e in ENGINES}
        wi = {e: 0 for e in ENGINES}
        total = sum(len(v) for v in self.ops.values())
        done = 0
        progress = True
        while progress:
            progress = False
            for e in ENGINES:
                while ptr[e] < len(self.ops[e]):
                    fn, waits, incs = self.ops[e][ptr[e]]
                    ok = True
                    while wi[e] < len(waits):
                        s, v = waits[wi[e]]
                        if vals[s] >= v:
                            wi[e] += 1
                        else:
                            ok = False
                            break
                    if not ok:
                        break
                    for s, n in incs:
                        vals[s] += n
                    ptr[e] += 1
                    wi[e] = 0
                    done += 1
                    progress = True
        if done != total:
            stuck = {e: (ptr[e], len(self.ops[e]),
                         self.ops[e][ptr[e]][1][wi[e]:] if ptr[e] < len(self.ops[e]) else None)
                     for e in ENGINES}
            raise RuntimeError(f"deadlock: {done}/{total}; stuck={stuck}; sems={vals}")

    def _insert_checkpoints(self):
        qeng = {"qs": "sync", "qg": "gpsimd", "qa": "scalar", "cc": "gpsimd"}
        waited = {}
        for e in ENGINES:
            for fn, waits, incs in self.ops[e]:
                for s, v in waits:
                    if s in qeng:
                        waited.setdefault(s, set()).add(v)
        for s, vals in waited.items():
            e = qeng[s]
            newops = []
            cum = 0
            for fn, waits, incs in self.ops[e]:
                newops.append((fn, waits, incs))
                for ss, n in incs:
                    if ss == s:
                        cum += n
                        if cum in vals:
                            newops.append((lambda eng: None, ((s, cum),), ()))
            self.ops[e] = newops

    def _reduce_waits(self):
        for e in ENGINES:
            newops = []
            for fn, waits, incs in self.ops[e]:
                best = {}
                for s, v in waits:
                    if v > best.get(s, -1):
                        best[s] = v
                newops.append((fn, tuple(best.items()), incs))
            self.ops[e] = newops

    def emit(self, block):
        self._reduce_waits()
        self._insert_checkpoints()
        self.check()
        for ename in ENGINES:
            ops = self.ops[ename]
            sems = self.sems

            def body(eng, ops=ops, sems=sems):
                for fn, waits, incs in ops:
                    for s, v in waits:
                        eng.wait_ge(sems[s], v)
                    inst = fn(eng)
                    for s, n in incs:
                        inst.then_inc(sems[s], n)

            getattr(block, ename)(body)


class Arena:
    def __init__(self, tensor, words):
        self.t = tensor
        self.limit = words
        self.off = 0
        self.marks = []

    def push(self):
        self.marks.append(self.off)

    def pop(self):
        self.off = self.marks.pop()

    def alloc(self, parts, free, dtype):
        esz = 2 if dtype == BF16 else 4
        off_b = (self.off * 4 + 31) // 32 * 32
        words = (free * esz + 3) // 4
        self.off = off_b // 4 + words
        assert self.off <= self.limit, f"arena overflow {self.off} > {self.limit}"
        base = self.t[0:parts, off_b // 4: off_b // 4 + words]
        if dtype == BF16:
            return base.bitcast(BF16)[0:parts, 0:free]
        return base[0:parts, 0:free]


def _f32(x):
    return np.ascontiguousarray(np.asarray(x, np.float32))



def prep_host(inputs):
    sh = {}
    sh["l0w1t"] = _f32(inputs["L0_ffn_w1"]).T.copy()
    sh["l0w2t"] = _f32(inputs["L0_ffn_w2"]).T.copy()
    sh["l0b1"] = _f32(inputs["L0_ffn_b1"]).reshape(1, -1)
    sh["l0b2"] = _f32(inputs["L0_ffn_b2"]).reshape(1, -1)
    sh["a0wq"] = _f32(inputs["L0_attn_Wq"])
    sh["a0wk"] = _f32(inputs["L0_attn_Wk"])
    sh["a0wv"] = _f32(inputs["L0_attn_Wv"])
    sh["a0wo"] = _f32(inputs["L0_attn_Wo"])
    for l in range(4):
        for nm in ("UL", "VR", "UR", "VL"):
            sh[f"a0{nm}{l}t"] = _f32(inputs[f"L0_attn_{nm}{l}"]).T.copy()
    sh["l1wqkvt"] = _f32(inputs["L1_attn_W_qkv"]).T.copy()
    _b = _f32(inputs["L1_attn_b_qkv"])
    sh["l1bqkv"] = _b.reshape(24, 128).T.copy()
    sh["l1bv"] = _b[2048:].reshape(1, 1024).copy()
    for l in range(2):
        sh[f"f1wqkv{l}"] = _f32(inputs["L1_ffn_wqkv"][l])
        sh[f"f1wo{l}"] = _f32(inputs["L1_ffn_wo"][l])
        w1 = _f32(inputs["L1_ffn_w1"][l])
        b1 = _f32(inputs["L1_ffn_b1"][l])
        wp = np.zeros((1024, 5632), np.float32)
        bp = np.zeros((1, 5632), np.float32)
        wp[:, 0:5460:2] = w1[:, 0:2730]
        wp[:, 1:5461:2] = w1[:, 2730:5460]
        bp[0, 0:5460:2] = b1[0:2730]
        bp[0, 1:5461:2] = b1[2730:5460]
        sh[f"f1w1p{l}"] = wp
        sh[f"f1b1p{l}"] = bp
        w2 = _f32(inputs["L1_ffn_w2"][l])
        sh[f"f1w2{l}"] = np.concatenate([w2, np.zeros((86, 1024), np.float32)], 0)
        sh[f"f1b2{l}"] = _f32(inputs["L1_ffn_b2"][l]).reshape(1, -1)
        sh[f"f1lg{l}"] = _f32(inputs["L1_ffn_ln_g"][l]).reshape(8, 128).T.copy()
        sh[f"f1lb{l}"] = _f32(inputs["L1_ffn_ln_b"][l]).reshape(8, 128).T.copy()
        sh[f"f1qs{l}"] = _f32(inputs["L1_ffn_q_scale"][l]).reshape(1, -1)
        sh[f"f1ks{l}"] = _f32(inputs["L1_ffn_k_scale"][l]).reshape(1, -1)

    sh["identity"] = np.eye(128, dtype=np.float32)
    sh["onesrow"] = np.ones((1, 512), np.float32)
    sh["onescol"] = np.ones((128, 1), np.float32)
    idx = np.arange(128)
    sh["cmask"] = (idx[:, None] <= idx[None, :]).astype(np.float32)
    ga = np.zeros((64, 47 * 64), np.float32)
    gb = np.zeros((64, 47 * 64), np.float32)
    gb2 = np.zeros((64, 47 * 64), np.float32)
    for n in range(16):
        blk = slice(4 * n, 4 * n + 4)
        ga[blk, 64 * n + 4 * n: 64 * n + 4 * n + 4] = np.eye(4)
        gb[blk, 64 * n + 4 * n: 64 * n + 4 * n + 4] = np.eye(4)
        gb2[blk, 64 * n + 4 * n: 64 * n + 4 * n + 4] = np.eye(4)
    gb[:, 46 * 64:47 * 64] = np.eye(64)
    gb2[:, 46 * 64:47 * 64] = np.eye(64)
    sh["gat"] = ga
    sh["gbt"] = gb
    sh["gbt2"] = gb2

    n = 4096
    half = np.arange(0, 64, 2, dtype=np.float32)
    inv_freq = 1.0 / 10000.0 ** (half / 64.0)
    tt = np.arange(n, dtype=np.float32)
    freqs = tt[:, None] * inv_freq[None, :]
    freqs = np.concatenate([freqs, freqs], -1)
    base = (half + 0.4 * 64) / (1.4 * 64)
    power = (tt - n // 2) / 256.0
    scale = base[None, :] ** power[:, None]
    scale = np.concatenate([scale, scale], -1).astype(np.float32)
    cos = np.cos(freqs).astype(np.float32)
    sin = np.sin(freqs).astype(np.float32)

    def lmask(qo, first):
        i = 512 + qo * 128 + np.arange(128)
        j = np.arange(1024)
        m = (i[:, None] < j[None, :]) | (i[:, None] > j[None, :] + 512)
        if first:
            m = m | (j[None, :] < 512)
        return np.where(m, np.float32(NEG), np.float32(0.0)).astype(np.float32)

    per_core = []
    for c in range(NCORE):
        h = c % 2
        t0 = h * 2048
        pc = {}
        pc["role0"] = np.full((128, 1), 1.0 if h == 0 else 0.0, np.float32)
        pc["role1"] = np.full((128, 1), 1.0 if h == 1 else 0.0, np.float32)
        pos = np.clip(np.arange(t0 - 512, t0 + TLOC), 0, n - 1)
        pc["cq"] = (cos * scale)[pos]
        pc["sq"] = (sin * scale)[pos]
        pc["ck"] = (cos / scale)[pos]
        pc["sk"] = (sin / scale)[pos]
        mk, mkf = [], []
        for qo in range(4):
            mk.append(lmask(qo, False))
            mkf.append(lmask(qo, h == 0))
        pc["lmask"] = np.concatenate(mk, 0)
        pc["lmaskf"] = np.concatenate(mkf, 0)
        per_core.append(pc)
    return sh, per_core



def emit_l0ffn_prep(p, ar, hb, per):
    ps = per["psum"]
    sw = per["swslots"]
    onescol = per["onescol"]
    onesr = per["onesrow_f"]
    for src, dst, rows, cols, slot in (("l0w1t", "wq1t", 1024, 4096, 0),
                                       ("l0w2t", "wq2t", 4096, 1024, 1)):
        ntile = rows // 128
        ar.push()
        wtile = [ar.alloc(128, cols, F32) for _ in range(2)]
        qtile = ar.alloc(128, cols, BF16)
        acc = ar.alloc(128, 1, F32)
        part = ar.alloc(128, 1, F32)
        mval = ar.alloc(1, 1, F32)
        p.ve(lambda e, a=acc: e.memset(a, 0.0))
        red = []
        for i in range(ntile):
            buf = wtile[i % 2]
            w = (red[i - 2],) if i >= 2 else ()
            tk = p.dma("sync", buf, hb[src][i * 128:(i + 1) * 128, :], waits=w)
            r1 = p.ve(lambda e, b=buf, pa=part: e.reduce_sum(pa, b, axis=AX.X, apply_absolute_value=True),
                      waits=(tk,))
            r2 = p.sve(lambda e, a=acc, pa=part: e.tensor_add(a, a, pa))
            red.append(r2)
        mm = p.ve(lambda e, a=acc, oc=onescol, o=ps[0:1, 0:1]: e.matmul(o, a, oc, start=True, stop=True),
                  waits=(red[-1],), eng="tensor")
        v1 = p.sve(lambda e, m=mval, o=ps[0:1, 0:1], s=1.0 / (rows * cols):
                  e.tensor_scalar(m, o, s, 1e-5, op0=ALU.mult, op1=ALU.max), waits=(mm,))
        mm2 = p.ve(lambda e, o=ps[0:128, 4:5], r=onesr, m=mval:
                   e.matmul(o, r, m, start=True, stop=True), waits=(v1,), eng="tensor")
        swv = sw[:, slot:slot + 1]
        v2 = p.sve(lambda e, s=swv, o=ps[0:128, 4:5]: e.tensor_copy(s, o), waits=(mm2,))
        swr = sw[:, slot + 2:slot + 3]
        v3 = p.sve(lambda e, s=swr, sv=swv: e.reciprocal(s, sv), waits=(v2,))
        last_q = v3
        for i in range(ntile):
            buf = wtile[i % 2]
            tk = p.dma("sync", buf, hb[src][i * 128:(i + 1) * 128, :],
                       waits=(v3,) if i < 2 else (ld_guard[i - 2],))
            a1 = p.ve(lambda e, b=buf, s=swr: e.tensor_scalar(b, b, s, None, op0=ALU.mult),
                      waits=(tk, last_q))
            a2 = p.ve(lambda e, b=buf: e.tensor_scalar(b, b, -1.0, 1.0, op0=ALU.max, op1=ALU.min))
            a3 = p.ve(lambda e, q=qtile, b=buf: e.tensor_scalar(q, b, MAGIC, -MAGIC, op0=ALU.add, op1=ALU.add))
            st = p.dma("sync", hb[dst][i * 128:(i + 1) * 128, :], qtile, waits=(a3,))
            if i == 0:
                ld_guard = []
            ld_guard.append(a2)
            last_q = st
        p.raw("sync", lambda e: None, (last_q,), ())
        ar.pop()


def emit_l0ffn(p, ar, hb, per, xt):
    ps = per["psum"]
    sw = per["swslots"]
    onesr = per["onesrow_f"]
    ar.push()
    stat = ar.alloc(128, 16, F32)
    c1s = ar.alloc(128, 16, F32)
    c2s = ar.alloc(128, 16, F32)

    ar.push()
    W1 = [ar.alloc(128, 4096, BF16) for _ in range(8)]
    xq1T = [ar.alloc(128, 2048, BF16) for _ in range(8)]
    gbuf = ar.alloc(128, 4096, BF16)
    xnq = ar.alloc(128, 2048, F32)
    qt = ar.alloc(128, 2048, BF16)
    b1bc = ar.alloc(128, 4096, BF16)
    brow = ar.alloc(1, 512, F32)

    w1tk = [p.dma("sync", W1[dc], hb["wq1t"][dc * 128:(dc + 1) * 128, :]) for dc in range(8)]

    prev = None
    for j in range(8):
        lt = p.dma("gpsimd", brow, hb["l0b1"][0:1, j * 512:(j + 1) * 512], waits=(prev,) if prev else ())
        mmb = p.ve(lambda e, b=brow: e.matmul(ps[0:128, 0:512], onesr, b, start=True, stop=True),
                   eng="tensor", waits=(lt,) + ((prev,) if prev else ()))
        prev = p.ve(lambda e, j=j: e.tensor_copy(b1bc[:, j * 512:(j + 1) * 512], ps[0:128, 0:512]),
                    waits=(mmb,))
    bias1_done = prev

    xq1_done = []
    prevq = None
    for i in range(NT):
        x = xt[i]
        r1 = p.sve(lambda e, x=x, s=stat: e.reduce_sum(s[:, 0:1], x, axis=AX.X))
        r2 = p.sve(lambda e, x=x, s=stat, o=xnq[:, 0:1024]: e.activation(o, x, AF.Square, accum_out=s[:, 1:2]),
                  eng="scalar", waits=(prevq,) if prevq else ())
        v = p.sve(lambda e, s=stat: e.tensor_scalar(s[:, 0:1], s[:, 0:1], 1.0 / D, None, op0=ALU.mult), waits=(r1, r2))
        v = p.sve(lambda e, s=stat: e.tensor_scalar(s[:, 1:2], s[:, 1:2], 1.0 / D, None, op0=ALU.mult))
        v = p.sve(lambda e, s=stat: e.tensor_tensor(s[:, 2:3], s[:, 0:1], s[:, 0:1], op=ALU.mult))
        v = p.sve(lambda e, s=stat: e.tensor_sub(s[:, 2:3], s[:, 1:2], s[:, 2:3]))
        v = p.sve(lambda e, s=stat: e.tensor_scalar_add(s[:, 3:4], s[:, 2:3], 1e-5))
        v = p.sve(lambda e, s=stat: e.reciprocal(s[:, 3:4], s[:, 3:4]))
        sq = p.sve(lambda e, s=stat: e.activation(s[:, 4:5], s[:, 3:4], AF.Sqrt), eng="scalar", waits=(v,))
        v = p.sve(lambda e, s=stat: e.tensor_tensor(s[:, 5:6], s[:, 2:3], s[:, 3:4], op=ALU.mult), waits=(sq,))
        v = p.sve(lambda e, s=stat: e.tensor_scalar_add(s[:, 5:6], s[:, 5:6], 1e-5))
        v = p.sve(lambda e, s=stat: e.reciprocal(s[:, 5:6], s[:, 5:6]))
        sq = p.sve(lambda e, s=stat: e.activation(s[:, 5:6], s[:, 5:6], AF.Sqrt), eng="scalar", waits=(v,))
        v = p.sve(lambda e, s=stat: e.tensor_tensor(s[:, 6:7], s[:, 4:5], s[:, 5:6], op=ALU.mult), waits=(sq,))
        v = p.sve(lambda e, s=stat: e.tensor_tensor(s[:, 7:8], s[:, 0:1], s[:, 6:7], op=ALU.mult))
        v = p.sve(lambda e, s=stat: e.tensor_scalar(s[:, 7:8], s[:, 7:8], -1.0, None, op0=ALU.mult))
        a1 = p.sve(lambda e, x=x, s=stat: e.tensor_scalar(xnq[:, 0:1024], x, s[:, 6:7], s[:, 7:8],
                  op0=ALU.mult, op1=ALU.add), waits=(v,))
        a2 = p.sve(lambda e, s=stat: e.reduce_max(s[:, 8:9], xnq[:, 0:1024], axis=AX.X,
                  apply_absolute_value=True), waits=(a1,))
        v = p.sve(lambda e, s=stat: e.tensor_scalar(s[:, 8:9], s[:, 8:9], 1e-5, None, op0=ALU.max))
        v = p.sve(lambda e, s=stat: e.reciprocal(s[:, 9:10], s[:, 8:9]))
        v = p.sve(lambda e, s=stat: e.tensor_scalar(s[:, 9:10], s[:, 9:10], 127.0, None, op0=ALU.mult))
        v = p.sve(lambda e, s=stat: e.tensor_scalar(s[:, 10:11], s[:, 8:9], 1.0 / 127.0, None, op0=ALU.mult))
        v = p.sve(lambda e, s=stat, c=c1s, i=i: e.tensor_tensor(c[:, i:i + 1], s[:, 10:11], sw[:, 0:1], op=ALU.mult))
        q1 = p.sve(lambda e, s=stat: e.tensor_scalar(xnq[:, 0:1024], xnq[:, 0:1024], s[:, 9:10], -128.0,
                  op0=ALU.mult, op1=ALU.max), waits=(v,))
        q2 = p.ve(lambda e: e.tensor_scalar(xnq[:, 0:1024], xnq[:, 0:1024], 127.0, MAGIC, op0=ALU.min, op1=ALU.add))
        q3 = p.ve(lambda e, q=qt: e.tensor_scalar(q[:, 0:1024], xnq[:, 0:1024], -MAGIC, None, op0=ALU.add))
        st = p.dma("sync", hb["xq1"][i * 128:(i + 1) * 128, :], qt[:, 0:1024], waits=(q3,))
        xq1_done.append(st)
        prevq = st

    xq1t_tk = [p.dma("sync", xq1T[dc], hb["xq1"][:, dc * 128:(dc + 1) * 128],
                     waits=(xq1_done[-1],), transpose=True) for dc in range(8)]

    xq2_done = []
    prev_ev = (bias1_done,)
    for i in range(NT):
        fns = []
        for dc in range(8):
            lhs = xq1T[dc][:, i * 128:(i + 1) * 128]
            for fc in range(8):
                fns.append(lambda e, o=ps[0:128, fc * 512:(fc + 1) * 512], l=lhs,
                           r=W1[dc][:, fc * 512:(fc + 1) * 512], s=(dc == 0), t=(dc == 7):
                           e.matmul(o, l, r, start=s, stop=t))
        mmtk = p.mmg(fns, waits=tuple(xq1t_tk) + tuple(w1tk) + prev_ev)
        e1 = p.ve(lambda e, c=c1s, i=i: e.scalar_tensor_tensor(gbuf[:, 0:4096], ps[0:128, 0:4096],
                  c[:, i:i + 1], b1bc[:, 0:4096], op0=ALU.mult, op1=ALU.add), waits=(mmtk,))
        g1 = p.ve(lambda e: e.activation(gbuf[:, 0:4096], gbuf[:, 0:4096], AF.Gelu), eng="scalar", waits=(e1,))
        g2 = p.sve(lambda e, s=stat: e.activation(xnq[:, 0:2048].bitcast(BF16)[:, 0:2048], gbuf[:, 0:2048],
                  AF.Square, accum_out=s[:, 1:2]), eng="scalar", waits=(g1,))
        g3 = p.sve(lambda e, s=stat: e.activation(xnq[:, 0:2048].bitcast(BF16)[:, 0:2048], gbuf[:, 2048:4096],
                  AF.Square, accum_out=s[:, 2:3]), eng="scalar")
        r3 = p.sve(lambda e, s=stat: e.reduce_max(s[:, 8:9], gbuf[:, 0:4096], axis=AX.X,
                  apply_absolute_value=True), waits=(g1,))
        v = p.sve(lambda e, s=stat: e.tensor_add(s[:, 1:2], s[:, 1:2], s[:, 2:3]), waits=(g3, r3))
        v = p.sve(lambda e, s=stat: e.tensor_scalar(s[:, 1:2], s[:, 1:2], 1.0 / 4096.0, 1e-5, op0=ALU.mult, op1=ALU.add))
        v = p.sve(lambda e, s=stat: e.reciprocal(s[:, 5:6], s[:, 1:2]))
        sq = p.sve(lambda e, s=stat: e.activation(s[:, 5:6], s[:, 5:6], AF.Sqrt), eng="scalar", waits=(v,))
        v = p.sve(lambda e, s=stat: e.tensor_tensor(s[:, 8:9], s[:, 8:9], s[:, 5:6], op=ALU.mult), waits=(sq,))
        v = p.sve(lambda e, s=stat: e.tensor_scalar(s[:, 8:9], s[:, 8:9], 1e-5, None, op0=ALU.max))
        v = p.sve(lambda e, s=stat: e.reciprocal(s[:, 9:10], s[:, 8:9]))
        v = p.sve(lambda e, s=stat: e.tensor_scalar(s[:, 9:10], s[:, 9:10], 127.0, None, op0=ALU.mult))
        v = p.sve(lambda e, s=stat: e.tensor_tensor(s[:, 9:10], s[:, 9:10], s[:, 5:6], op=ALU.mult))
        v = p.sve(lambda e, s=stat: e.tensor_scalar(s[:, 10:11], s[:, 8:9], 1.0 / 127.0, None, op0=ALU.mult))
        v = p.sve(lambda e, s=stat, c=c2s, i=i: e.tensor_tensor(c[:, i:i + 1], s[:, 10:11], sw[:, 1:2], op=ALU.mult))
        sts = []
        for hf in range(2):
            sl = slice(hf * 2048, (hf + 1) * 2048)
            q1 = p.sve(lambda e, s=stat, sl=sl: e.tensor_scalar(xnq[:, 0:2048], gbuf[:, sl], s[:, 9:10], -128.0,
                      op0=ALU.mult, op1=ALU.max), waits=(v,) if hf == 0 else (sts[-1],))
            q2 = p.ve(lambda e: e.tensor_scalar(xnq[:, 0:2048], xnq[:, 0:2048], 127.0, MAGIC, op0=ALU.min, op1=ALU.add))
            q3 = p.ve(lambda e, q=qt: e.tensor_scalar(q[:, 0:2048], xnq[:, 0:2048], -MAGIC, None, op0=ALU.add))
            st = p.dma("sync", hb["xq2"][i * 128:(i + 1) * 128, hf * 2048:(hf + 1) * 2048], qt[:, 0:2048], waits=(q3,))
            sts.append(st)
        xq2_done.append(sts[-1])
        prev_ev = (q3, sts[-1])
        prev_ev = (e1, q3, sts[-1])
    ar.pop()

    ar.push()
    w2s = [ar.alloc(128, 1024, BF16) for _ in range(2)]
    x2s = [ar.alloc(128, 512, BF16) for _ in range(2)]
    b2bc = ar.alloc(128, 1024, BF16)
    xev = ar.alloc(128, 1024, F32)
    brow2 = ar.alloc(1, 512, F32)

    prev = None
    for j in range(2):
        lt = p.dma("gpsimd", brow2, hb["l0b2"][0:1, j * 512:(j + 1) * 512], waits=(prev,) if prev else (xq2_done[-1],))
        mmb = p.ve(lambda e, b=brow2: e.matmul(ps[0:128, 0:512], onesr, b, start=True, stop=True),
                   eng="tensor", waits=(lt,) + ((prev,) if prev else ()))
        prev = p.ve(lambda e, j=j: e.tensor_copy(b2bc[:, j * 512:(j + 1) * 512], ps[0:128, 0:512]), waits=(mmb,))
    bias2_done = prev

    last_ev = bias2_done
    for tg in range(4):
        tiles = list(range(tg * 4, tg * 4 + 4))
        mm_last = None
        for fc in range(32):
            xbuf = x2s[fc % 2]
            wbuf = w2s[fc % 2]
            guard = (mm_hist[fc - 2],) if fc >= 2 else ((last_ev,) if last_ev else ())
            wt = p.dma("scalar", wbuf, hb["wq2t"][fc * 128:(fc + 1) * 128, :], waits=guard)
            xt2 = p.dma("sync", xbuf, hb["xq2"][tg * 512:(tg + 1) * 512, fc * 128:(fc + 1) * 128],
                        waits=guard + ((xq2_done[-1],) if fc < 2 and tg == 0 else ()), transpose=True)
            fns = []
            for j in range(4):
                lhs = xbuf[:, j * 128:(j + 1) * 128]
                for vc in range(2):
                    fns.append(lambda e, o=ps[0:128, j * 1024 + vc * 512: j * 1024 + (vc + 1) * 512],
                               l=lhs, r=wbuf[:, vc * 512:(vc + 1) * 512], s=(fc == 0), t=(fc == 31):
                               e.matmul(o, l, r, start=s, stop=t))
            mm_last = p.mmg(fns, waits=(wt, xt2) + ((last_ev,) if fc == 0 and last_ev else ()))
            if fc == 0:
                mm_hist = []
            mm_hist.append(mm_last)
        evs = []
        for j, ti in enumerate(tiles):
            e1 = p.ve(lambda e, j=j, ti=ti: e.scalar_tensor_tensor(
                xev, ps[0:128, j * 1024:(j + 1) * 1024], c2s[:, ti:ti + 1],
                b2bc, op0=ALU.mult, op1=ALU.add), waits=(mm_last,))
            e2 = p.ve(lambda e, ti=ti, x=xt[ti]: e.scalar_tensor_tensor(
                x, xev, 0.5, x, op0=ALU.mult, op1=ALU.add))
            evs.append(e2)
        last_ev = evs[-1]
    ar.pop()
    ar.pop()
    return last_ev



A0_IN = [("a0wq", (1024, 64), F32), ("a0wk", (1024, 64), F32),
         ("a0wv", (1024, 64), F32), ("a0wo", (64, 1024), F32),
         ("gat", (64, 3008), F32), ("gbt", (64, 3008), F32), ("gbt2", (64, 3008), F32),
         ("cmask", (128, 128), F32), ("role0", (128, 1), F32), ("role1", (128, 1), F32)]
for _l, _bs, _r in ((0, 4, 4), (1, 8, 8), (2, 16, 8), (3, 32, 8)):
    for _nm in ("UL", "VR", "UR", "VL"):
        A0_IN.append((f"a0{_nm}{_l}t", (_r, _bs), F32))
STAGE_INPUTS = {
    "l0ffn": [("l0w1t", (1024, 4096), F32), ("l0w2t", (4096, 1024), F32),
              ("l0b1", (1, 4096), F32), ("l0b2", (1, 1024), F32)],
    "attn0": A0_IN,
    "l1attn": [("l1wqkvt", (1024, 3072), F32), ("l1bqkv", (128, 24), F32),
               ("l1bv", (1, 1024), F32)],
    "l1ffn": sum([[(f"f1wqkv{l}", (1024, 1536), F32), (f"f1wo{l}", (512, 1024), F32),
                   (f"f1w1p{l}", (1024, 5632), F32), (f"f1b1p{l}", (1, 5632), F32),
                   (f"f1w2{l}", (2816, 1024), F32), (f"f1b2{l}", (1, 1024), F32),
                   (f"f1lg{l}", (128, 8), F32), (f"f1lb{l}", (128, 8), F32),
                   (f"f1qs{l}", (1, 64), F32), (f"f1ks{l}", (1, 64), F32)] for l in range(2)], [])
              + [("cq", (2560, 64), F32), ("sq", (2560, 64), F32),
                 ("ck", (2560, 64), F32), ("sk", (2560, 64), F32),
                 ("lmask", (512, 1024), F32), ("lmaskf", (512, 1024), F32),
                 ("role1", (128, 1), F32)],
}
STAGE_SCRATCH = {
    "l0ffn": [("wq1t", (1024, 4096), BF16), ("wq2t", (4096, 1024), BF16),
              ("xq1", (TLOC, 1024), BF16), ("xq2", (TLOC, 4096), BF16)],
    "attn0": [("gac_h", (64, 3008), F32), ("gbc_h", (64, 3008), F32), ("gb2_h", (64, 3008), F32),
              ("lnx_h", (TLOC, 1024), BF16),
              ("ccin_a0", (64, 80), F32), ("ccout_a0", (64, 80), F32),
              ("d_qt", (64, 2048), BF16), ("d_kt", (64, 2048), BF16), ("d_vt", (64, 2048), BF16),
              ("d_ast", (128, 2944), BF16), ("d_bst", (128, 3008), BF16),
              ("d_qtl", (64, 6016), BF16), ("d_pt", (128, 6016), BF16),
              ("d_yall", (64, 2048), F32), ("d_den", (1, 2048), F32), ("d_mt", (64, 65), F32)],
    "l1attn": [("lnx_h", (TLOC, 1024), BF16),
               ("kin_h", (1024, TLOC), BF16), ("kout_h", (2048, TLOC), BF16),
               ("vin_h", (TLOC, 1024), BF16), ("vout_h", (2 * TLOC, 1024), BF16),
               ("p_h", (512, 4096), BF16)],
    "l1ffn": [("w1p_h0", (1024, 5632), BF16), ("w1p_h1", (1024, 5632), BF16),
              ("w2b_h0", (2816, 1024), BF16), ("w2b_h1", (2816, 1024), BF16),
              ("ucrow_h0", (2, 5632), F32), ("ucrow_h1", (2, 5632), F32),
              ("xsp_h", (TLOC, 1024), F32), ("xi_h", (TLOC, 1024), BF16),
              ("bndout_h", (1026, 1024), BF16), ("bnd1in_h", (1, 1024), BF16),
              ("bnd1out_h", (2, 1024), BF16), ("qh_h", (2560, 512), BF16),
              ("kh_h", (2560, 512), BF16), ("ph_h", (TLOC, 8192), BF16),
              ("hh_h", (TLOC, 2816), BF16), ("rv_h", (1, 2048), F32),
              ("dbgf1_h", (1160, 2048), F32)],
}
DEBUG_SCRATCH = False
COMMON_INPUTS = [("xres_in", (TLOC, 1024), F32),
                 ("onescol", (128, 1), F32), ("onesrow", (1, 512), F32),
                 ("identity", (128, 128), F32)]


def build_program(stages):
    nc = bass.Bass("TRN2", target_bir_lowering=False)
    hb = {}
    names = set()
    specs = list(COMMON_INPUTS)
    for st in stages:
        base = st.split(":")[0]
        for it in STAGE_INPUTS.get(base, ()):
            if it[0] not in names:
                names.add(it[0])
                specs.append(it)
    for name, shape, dt in specs:
        hb[name] = nc.dram_tensor(name, list(shape), dt, kind="ExternalInput").ap()
    hb["xres_out"] = nc.dram_tensor("xres_out", [TLOC, 1024], F32, kind="ExternalOutput").ap()
    snames = set()
    for st in stages:
        base = st.split(":")[0]
        for name, shape, dt in STAGE_SCRATCH.get(base, ()):
            if name not in snames:
                snames.add(name)
                kind = "ExternalOutput" if DEBUG_SCRATCH else "Internal"
                hb[name] = nc.dram_tensor(name, list(shape), dt, kind=kind).ap()

    with contextlib.ExitStack() as stack:
        p = Prog(nc, stack)
        xres = stack.enter_context(nc.sbuf_tensor("xres", [128, NT * 1024], F32))
        arena_t = stack.enter_context(nc.sbuf_tensor("arena", [128, 33000], F32))
        small = stack.enter_context(nc.sbuf_tensor("small", [128, 1024], F32))
        psum = stack.enter_context(nc.psum_tensor("psum", [128, 4096], F32))
        ar = Arena(arena_t, 33000)
        xt = [xres[:, i * 1024:(i + 1) * 1024] for i in range(NT)]

        per = {"psum": psum}
        per["onescol"] = small[:, 0:1]
        per["swslots"] = small[:, 8:16]
        onesrow_full = small[0:1, 16:528]
        per["onesrow_f"] = onesrow_full[0:1, 0:128]

        per["ident_f"] = small[:, 528:656]
        per["ident_b"] = small[:, 656:720].bitcast(BF16)[0:128, 0:128]
        p.dma("sync", per["onescol"], hb["onescol"][:, :])
        p.dma("sync", onesrow_full, hb["onesrow"][:, :])
        p.dma("sync", per["ident_f"], hb["identity"][:, :])
        p.dma("gpsimd", per["ident_b"], hb["identity"][:, :])
        for i in range(NT):
            p.dma("sync", xt[i], hb["xres_in"][i * 128:(i + 1) * 128, :])
        p.barrier()

        did_prep = set()
        for st in stages:
            if st == "l0ffn":
                if "l0" not in did_prep:
                    did_prep.add("l0")
                    emit_l0ffn_prep(p, ar, hb, per)
                    p.barrier()
                emit_l0ffn(p, ar, hb, per, xt)
                p.barrier()
            elif st == "attn0":
                if "a0" not in did_prep:
                    did_prep.add("a0")
                    emit_l0attn_prep(p, ar, hb, per)
                    p.barrier()
                emit_l0attn(p, ar, hb, per, xt)
                p.barrier()
            elif st == "l1attn":
                emit_l1attn(p, ar, hb, per, xt)
                p.barrier()
            elif st == "l1ffn":
                if "f1" not in did_prep:
                    did_prep.add("f1")
                    emit_l1ffn_prep(p, ar, hb, per)
                    p.barrier()
                emit_l1ffn(p, ar, hb, per, xt)
                p.barrier()

        for i in range(NT):
            p.dma("sync", hb["xres_out"][i * 128:(i + 1) * 128, :], xt[i])
        p.raw("sync", lambda e: None, (p.now("qs"),), ())
        with nc.Block() as block:
            p.emit(block)
    return nc, specs


def run_stages(stages, xcores, sh, percore):
    nc, specs = build_program(stages)
    in_maps = []
    for c in range(NCORE):
        m = {}
        for name, shape, dt in specs:
            if name == "xres_in":
                m[name] = np.ascontiguousarray(xcores[c])
            elif name in sh:
                m[name] = sh[name]
            else:
                m[name] = percore[c][name]
        in_maps.append(m)
    res = run_bass_kernel_spmd(nc, in_maps, core_ids=list(range(NCORE)))
    if DEBUG_SCRATCH:
        return res
    return np.stack([res.results[c]["xres_out"] for c in range(NCORE)], 0)


PIPE = [["l0ffn", "attn0", "l0ffn", "l1ffn", "l1attn", "l1ffn"]]


def kernel(**inputs):
    x = _f32(inputs["L0_attn_x"])
    xcores = np.zeros((8, TLOC, 1024), np.float32)
    for b in range(4):
        xcores[2 * b] = x[b, :2048]
        xcores[2 * b + 1] = x[b, 2048:]
    sh, percore = prep_host(inputs)
    for stages in PIPE:
        xcores = run_stages(stages, xcores, sh, percore)
    out = np.zeros_like(x)
    for b in range(4):
        out[b, :2048] = xcores[2 * b]
        out[b, 2048:] = xcores[2 * b + 1]
    return out



A0_TRUNC = 99
A0_DEBUG = False
F1_TRUNC = 99
F1_WO = 1
F1_REFS = {}
A0_REFS = {}
A0_BS = [4, 8, 16, 32]
A0_NN = [8, 4, 2, 1]
A0_TOPS = [(l, m) for l in range(4) for m in range(A0_NN[l])]
A0_NCOMP = 47


def emit_l0attn_prep(p, ar, hb, per):
    ps = per["psum"]
    ar.push()
    tmpl = ar.alloc(64, 3008, F32)
    stg = ar.alloc(32, 32, F32)
    uls = {}
    tks = []
    for l in range(4):
        r, bs = [4, 8, 8, 8][l], A0_BS[l]
        for nm in ("UL", "VR", "UR", "VL"):
            buf = ar.alloc(r, bs, BF16)
            uls[(nm, l)] = buf
            tks.append(p.dma("gpsimd", buf, hb[f"a0{nm}{l}t"][:, :]))
    tdone = []
    for src, dst in (("gat", "gac_h"), ("gbt", "gbc_h"), ("gbt2", "gb2_h")):
        t1 = p.dma("sync", tmpl, hb[src][:, :], waits=tuple(tdone[-1:]))
        t2 = p.dma("sync", hb[dst][:, :], tmpl, waits=(t1,))
        tdone.append(t2)
    last = tdone[-1]
    for l in range(4):
        r, bs = [4, 8, 8, 8][l], A0_BS[l]
        prods = (("PUL", "UL", "UL"), ("PVR", "VR", "VR"), ("ABOT", "VR", "UR"),
                 ("BBOT", "UL", "VL"), ("B2BOT", "VL", "UL"))
        for pn, aop, bop in prods:
            mm = p.ve(lambda e, a=uls[(aop, l)], b=uls[(bop, l)], o=ps[0:bs, 0:bs]:
                      e.matmul(o, a, b, start=True, stop=True),
                      eng="tensor", waits=tuple(tks) + (last,))
            cp = p.ve(lambda e, o=ps[0:bs, 0:bs], s=stg[0:bs, 0:bs]: e.tensor_copy(s, o), waits=(mm,))
            wr = []
            for ti, (ll, m) in enumerate(A0_TOPS):
                if ll != l:
                    continue
                ctop = 16 + ti
                cbot = 31 + ti
                L0r = 2 * bs * m
                R0r = 2 * bs * m + bs
                if pn == "PUL":
                    tgt = [("gac_h", L0r, ctop * 64 + L0r)]
                elif pn == "PVR":
                    tgt = [("gbc_h", R0r, ctop * 64 + R0r), ("gb2_h", R0r, ctop * 64 + R0r)]
                elif pn == "ABOT":
                    tgt = [("gac_h", R0r, cbot * 64 + R0r)]
                elif pn == "BBOT":
                    tgt = [("gbc_h", L0r, cbot * 64 + L0r)]
                else:
                    tgt = [("gb2_h", L0r, cbot * 64 + L0r)]
                for dstn, r0, c0 in tgt:
                    wr.append(p.dma("sync", hb[dstn][r0:r0 + bs, c0:c0 + bs], stg[0:bs, 0:bs], waits=(cp,)))
            last = wr[-1]
    p.raw("sync", lambda e: None, (last,), ())
    ar.pop()


def emit_l0attn(p, ar, hb, per, xt):
    ps = per["psum"]
    onesr = per["onesrow_f"]
    ar.push()
    GA = ar.alloc(64, 3008, BF16)
    GB = ar.alloc(64, 3008, BF16)
    GB2 = ar.alloc(64, 3008, BF16)
    wq_sb = ar.alloc(128, 512, BF16)
    wk_sb = ar.alloc(128, 512, BF16)
    wv_sb = ar.alloc(128, 512, BF16)
    wo_sb = ar.alloc(64, 1024, BF16)
    lnxT = [ar.alloc(128, 2048, BF16) for _ in range(8)]
    QT = ar.alloc(64, 2048, BF16)
    KT = ar.alloc(64, 2048, BF16)
    VT = ar.alloc(64, 2048, BF16)
    Astk = ar.alloc(128, 2944, BF16)
    Bstk = ar.alloc(128, 3008, BF16)
    Qtl = ar.alloc(64, 6016, BF16)
    Pt = ar.alloc(128, 6016, BF16)
    YD = ar.alloc(128, 2048, F32)
    Yall = YD[0:64, :]
    den_row = YD[64:65, :]
    ybf = ar.alloc(64, 2048, BF16)
    MT = ar.alloc(64, 65, F32)
    MTb = ar.alloc(64, 65, BF16)
    Mrecv = ar.alloc(64, 80, F32)
    Mrb = ar.alloc(64, 80, BF16)
    exch = ar.alloc(64, 80, F32)
    lnt = ar.alloc(128, 1024, BF16)
    stat = ar.alloc(128, 16, F32)
    onescol_b = ar.alloc(128, 1, BF16)
    cmask_sb = ar.alloc(128, 128, F32)
    role0v = ar.alloc(128, 1, F32)
    role1v = ar.alloc(128, 1, F32)
    tmpe = ar.alloc(64, 512, F32)
    tmpe2 = ar.alloc(64, 512, F32)
    ztmp = ar.alloc(64, 1, F32)

    global A0_REFS
    A0_REFS = dict(GA=GA, GB=GB, GB2=GB2, QT=QT, KT=KT, VT=VT, Astk=Astk, Bstk=Bstk,
                   Qtl=Qtl, Pt=Pt, Yall=Yall, wq=wq_sb, wk=wk_sb, wv=wv_sb)
    ld0 = []
    ld0.append(p.dma("gpsimd", GA, hb["gac_h"][:, :]))
    ld0.append(p.dma("gpsimd", GB, hb["gbc_h"][:, :]))
    ld0.append(p.dma("gpsimd", GB2, hb["gb2_h"][:, :]))
    for wsb, nm in ((wq_sb, "a0wq"), (wk_sb, "a0wk"), (wv_sb, "a0wv")):
        ld0.append(p.dma("gpsimd", wsb.rearrange("p (c f) -> p c f", f=64),
                         hb[nm].rearrange("(c p) f -> p c f", p=128)))
    ld0.append(p.dma("gpsimd", wo_sb, hb["a0wo"][:, :]))
    ld0.append(p.dma("sync", cmask_sb, hb["cmask"][:, :]))
    ld0.append(p.dma("sync", role0v, hb["role0"][:, :]))
    ld0.append(p.dma("sync", role1v, hb["role1"][:, :]))
    ld0.append(p.ve(lambda e: e.memset(onescol_b, 1.0)))
    ld0.append(p.ve(lambda e: e.memset(MT[:, 0:65], 0.0)))
    ld0.append(p.ve(lambda e: e.memset(MTb[:, 0:65], 0.0)))
    ld0.append(p.ve(lambda e: e.memset(exch[:, 0:80], 0.0)))
    ld0.append(p.ve(lambda e: e.memset(YD[:, 0:2048], 0.0)))
    ld0 = tuple(ld0)

    lnst = []
    sqjunk = Pt.bitcast(F32)
    prevq = None
    for i in range(NT):
        x = xt[i]
        r1 = p.ve(lambda e, x=x, s=stat: e.reduce_sum(s[:, 0:1], x, axis=AX.X))
        r2 = p.ve(lambda e, x=x, s=stat, o=sqjunk: e.activation(o[:, 0:1024], x, AF.Square,
                  accum_out=s[:, 1:2]), eng="scalar", waits=((prevq,) if prevq else ()))
        v = p.sve(lambda e, s=stat: e.tensor_scalar(s[:, 0:1], s[:, 0:1], 1.0 / D, None, op0=ALU.mult), waits=(r1, r2))
        v = p.sve(lambda e, s=stat: e.tensor_scalar(s[:, 1:2], s[:, 1:2], 1.0 / D, None, op0=ALU.mult))
        v = p.sve(lambda e, s=stat: e.tensor_tensor(s[:, 2:3], s[:, 0:1], s[:, 0:1], op=ALU.mult))
        v = p.sve(lambda e, s=stat: e.tensor_sub(s[:, 2:3], s[:, 1:2], s[:, 2:3]))
        v = p.sve(lambda e, s=stat: e.tensor_scalar_add(s[:, 3:4], s[:, 2:3], 1e-5))
        v = p.sve(lambda e, s=stat: e.reciprocal(s[:, 3:4], s[:, 3:4]))
        v = p.sve(lambda e, s=stat: e.activation(s[:, 4:5], s[:, 3:4], AF.Sqrt), eng="scalar", waits=(v,))
        v = p.sve(lambda e, s=stat: e.tensor_tensor(s[:, 5:6], s[:, 0:1], s[:, 4:5], op=ALU.mult), waits=(v,))
        v = p.sve(lambda e, s=stat: e.tensor_scalar(s[:, 5:6], s[:, 5:6], -1.0, None, op0=ALU.mult))
        q1 = p.sve(lambda e, x=x, s=stat: e.tensor_scalar(lnt[:, 0:1024], x, s[:, 4:5], s[:, 5:6],
                   op0=ALU.mult, op1=ALU.add), waits=(v,))
        st = p.dma("sync", hb["lnx_h"][i * 128:(i + 1) * 128, :], lnt[:, 0:1024], waits=(q1,))
        lnst.append(st)
        prevq = st
    if A0_TRUNC < 2:
        ar.pop()
        return lnst[-1]
    lnT_tk = tuple(p.dma("sync", lnxT[dc], hb["lnx_h"][:, dc * 128:(dc + 1) * 128],
                         waits=(lnst[-1],), transpose=True) for dc in range(8))

    projw = ld0 + lnT_tk
    prev_ev = None
    for wsb, OUT, iselu in ((wq_sb, QT, True), (wk_sb, KT, True), (wv_sb, VT, False)):
        for tc in range(4):
            cols = slice(tc * 512, (tc + 1) * 512)
            fns = []
            for dc in range(8):
                fns.append(lambda e, l=wsb[:, dc * 64:(dc + 1) * 64], r=lnxT[dc][:, cols],
                           s=(dc == 0), t=(dc == 7): e.matmul(ps[0:64, 0:512], l, r, start=s, stop=t))
            mm = p.mmg(fns, waits=projw + ((prev_ev,) if prev_ev else ()))
            if iselu:
                e1 = p.ve(lambda e: e.tensor_scalar_min(tmpe, ps[0:64, 0:512], 0.0), waits=(mm,))
                e2 = p.ve(lambda e: e.activation(tmpe2, tmpe, AF.Exp), eng="scalar", waits=(e1,))
                e3 = p.ve(lambda e: e.tensor_scalar_max(tmpe, ps[0:64, 0:512], 0.0), waits=(e2,))
                e4 = p.ve(lambda e, O=OUT, cols=cols: e.tensor_tensor(O[:, cols], tmpe, tmpe2, op=ALU.add),
                          waits=(e3,))
                prev_ev = e4
            else:
                prev_ev = p.ve(lambda e, O=OUT, cols=cols: e.tensor_copy(O[:, cols], ps[0:64, 0:512]),
                               waits=(mm,))

    if A0_TRUNC < 3:
        ar.pop()
        return prev_ev
    if A0_DEBUG:
        p.dma("sync", hb["d_qt"][:, :], QT[:, 0:2048], waits=(prev_ev,))
        p.dma("sync", hb["d_kt"][:, :], KT[:, 0:2048], waits=(prev_ev,))
        p.dma("sync", hb["d_vt"][:, :], VT[:, 0:2048], waits=(prev_ev,))
    proj_done = (prev_ev,)
    ev_guard = proj_done
    for ci in range(NT):
        sl = slice(ci * 128, (ci + 1) * 128)
        fns = [lambda e, c0=c0, c1=c1, sl=sl: e.matmul(ps[0:128, c0:c1], KT[:, sl], GA[:, c0:c1], start=True, stop=True)
               for c0, c1 in ((0, 512), (512, 1024), (1024, 1536), (1536, 1984))]
        fns += [lambda e, c0=c0, c1=c1, sl=sl: e.matmul(ps[0:128, c0:c1], VT[:, sl], GA[:, c0:c1], start=True, stop=True)
                for c0, c1 in ((1984, 2048), (2048, 2560), (2560, 2944))]
        mmA = p.mmg(fns, waits=ev_guard)
        evA = p.ve(lambda e: e.copy(Astk[:, 0:2944], ps[0:128, 0:2944]), eng="scalar", waits=(mmA,))
        fns = [lambda e, c0=c0, c1=c1, sl=sl: e.matmul(ps[0:128, c0:c1], VT[:, sl], GB[:, c0:c1], start=True, stop=True)
               for c0, c1 in ((0, 512), (512, 1024), (1024, 1536), (1536, 1984))]
        fns += [lambda e, c0=c0, c1=c1, sl=sl: e.matmul(ps[0:128, c0:c1], KT[:, sl], GB[:, c0:c1], start=True, stop=True)
                for c0, c1 in ((1984, 2048), (2048, 2560), (2560, 3008))]
        mmB = p.mmg(fns, waits=(evA,))
        evB = p.ve(lambda e: e.copy(Bstk[:, 0:3008], ps[0:128, 0:3008]), eng="scalar", waits=(mmB,))
        for r0, r1 in ((0, 24), (24, 47)):
            fns = [lambda e, c=c, r0=r0, sl=sl: e.matmul(ps[0:64, (c - r0) * 128:(c - r0 + 1) * 128],
                   GB2[:, c * 64:(c + 1) * 64], QT[:, sl], start=True, stop=True)
                   for c in range(r0, r1)]
            mmQ = p.mmg(fns, waits=(evB,) if r0 == 0 else (evQ,))
            evQ = p.ve(lambda e, r0=r0, r1=r1: e.copy(Qtl[:, r0 * 128:r1 * 128],
                       ps[0:64, 0:(r1 - r0) * 128]), eng="scalar", waits=(mmQ,))
        s1ev = []
        for src, cr0, cr1 in ((VT, 0, 31), (KT, 31, 47)):
            ncols = (cr1 - cr0) * 128
            fns = []
            for c0 in range(0, ncols, 512):
                c1 = min(c0 + 512, ncols)
                fns.append(lambda e, c0=c0, c1=c1, src=src, cr0=cr0, sl=sl: e.matmul(
                    ps[0:128, c0:c1], src[:, sl], Qtl[:, cr0 * 128 + c0:cr0 * 128 + c1], start=True, stop=True))
            mmS = p.mmg(fns, waits=(evQ,) if cr0 == 0 else (s1ev[-1],))
            nc_ = cr1 - cr0
            ev = p.ve(lambda e, nc_=nc_, cr0=cr0: e.scalar_tensor_tensor(
                Pt[:, cr0 * 128:(cr0 + nc_) * 128].rearrange("p (c t) -> p c t", t=128),
                ps[0:128, 0:nc_ * 128].rearrange("p (c t) -> p c t", t=128),
                1.0,
                cmask_sb.unsqueeze(1).broadcast_to((128, nc_, 128)),
                op0=ALU.mult, op1=ALU.mult), waits=(mmS,))
            s1ev.append(ev)
        fns = [lambda e, c=c: e.matmul(ps[0:64, 3200:3328], Astk[:, c * 64:(c + 1) * 64],
               Pt[:, c * 128:(c + 1) * 128], start=(c == 0), stop=False) for c in range(46)]
        fns.append(lambda e, sl=sl: e.matmul(ps[0:64, 3200:3328], MTb[:, 0:64], QT[:, sl], start=False, stop=True))
        fns.append(lambda e: e.matmul(ps[0:1, 3328:3456], onescol_b, Pt[:, 46 * 128:47 * 128],
                   start=True, stop=False))
        fns.append(lambda e, sl=sl: e.matmul(ps[0:1, 3328:3456], MTb[:, 64:65], QT[:, sl], start=False, stop=True))
        dbg_tk = ()
        if A0_DEBUG and ci == 0:
            dbg_tk = (p.dma("sync", hb["d_ast"][:, :], Astk[:, 0:2944], waits=(evA,)),
                      p.dma("sync", hb["d_bst"][:, :], Bstk[:, 0:3008], waits=(evB,)),
                      p.dma("sync", hb["d_qtl"][:, :], Qtl[:, 0:6016], waits=(evQ,)),
                      p.dma("sync", hb["d_pt"][:, :], Pt[:, 0:6016], waits=tuple(s1ev)))
        mmY = p.mmg(fns, waits=tuple(s1ev))
        evY = p.ve(lambda e, sl=sl: e.tensor_copy(Yall[:, sl], ps[0:64, 3200:3328]), waits=(mmY,))
        evD = p.ve(lambda e, sl=sl: e.tensor_copy(den_row[0:1, sl], ps[0:1, 3328:3456]), waits=(mmY,))
        fns = [lambda e, c=c: e.matmul(ps[0:64, 3456:3520], Bstk[:, c * 64:(c + 1) * 64],
               Astk[:, c * 64:(c + 1) * 64], start=(c == 0), stop=(c == 45)) for c in range(46)]
        mmM = p.mmg(fns, waits=(evY, evD))
        uM = p.ve(lambda e: e.tensor_add(MT[:, 0:64], MT[:, 0:64], ps[0:64, 3456:3520]), waits=(mmM,))
        uz1 = p.ve(lambda e, sl=sl: e.reduce_sum(ztmp, KT[:, sl], axis=AX.X))
        uz2 = p.sve(lambda e: e.tensor_add(MT[:, 64:65], MT[:, 64:65], ztmp), waits=(uz1,))
        ub = p.sve(lambda e: e.tensor_copy(MTb[:, 0:65], MT[:, 0:65]), waits=(uM, uz2))
        ev_guard = (ub, s1ev[-1], evA, evB) + dbg_tk
    if A0_DEBUG:
        p.dma("sync", hb["d_yall"][:, :], Yall[:, 0:2048], waits=ev_guard)
        p.dma("sync", hb["d_den"][:, :], den_row[0:1, 0:2048], waits=ev_guard)
        p.dma("sync", hb["d_mt"][:, :], MT[:, 0:65], waits=ev_guard)
    if A0_TRUNC < 4:
        ar.pop()
        return ev_guard[0]
    pk = p.sve(lambda e: e.tensor_scalar(exch[:, 0:65], MT[:, 0:65], role0v[0:64, 0:1], None, op0=ALU.mult),
               waits=ev_guard)
    sx = p.dma("sync", hb["ccin_a0"][:, :], exch[:, 0:80], waits=(pk,))
    cc = p.collective("AllReduce", ALU.add, [hb["ccin_a0"][:, :].opt()], [hb["ccout_a0"][:, :].opt()],
                      waits=(sx,))
    lx = p.dma("gpsimd", Mrecv[:, 0:80], hb["ccout_a0"][:, :], waits=(cc,))
    rb = p.ve(lambda e: e.tensor_scalar(Mrb[:, 0:65], Mrecv[:, 0:65], role1v[0:64, 0:1], None, op0=ALU.mult),
              waits=(lx,))
    if A0_TRUNC < 5:
        ar.pop()
        return rb
    corr = []
    for qc in range(4):
        cols = slice(qc * 512, (qc + 1) * 512)
        mmC = p.ve(lambda e, cols=cols: e.matmul(ps[0:65, 0:512], Mrb[:, 0:65], QT[:, cols],
                   start=True, stop=True), eng="tensor", waits=(rb,) + tuple(corr[-2:]))
        c1 = p.ve(lambda e, cols=cols: e.tensor_add(Yall[:, cols], Yall[:, cols], ps[0:64, 0:512]), waits=(mmC,))
        c2 = p.ve(lambda e, cols=cols: e.tensor_add(den_row[0:1, cols], den_row[0:1, cols], ps[64:65, 0:512]),
                  waits=(mmC,))
        corr.append(c2)
        corr.append(c1)
    rrow = Qtl.bitcast(F32)[0:1, 0:2048]
    v = p.sve(lambda e: e.tensor_scalar(rrow[0:1, 0:2048], den_row[0:1, 0:2048], 1e-6, None, op0=ALU.max),
              waits=tuple(corr))
    v = p.sve(lambda e: e.reciprocal(rrow[0:1, 0:2048], rrow[0:1, 0:2048]))
    nrm = []
    for qc in range(4):
        cols = slice(qc * 512, (qc + 1) * 512)
        mmB2 = p.ve(lambda e, cols=cols: e.matmul(ps[0:64, 512:1024], onesr[0:1, 0:64], rrow[0:1, cols],
                    start=True, stop=True), eng="tensor", waits=(v,) + ((nrm[-1],) if nrm else ()))
        n1 = p.ve(lambda e, cols=cols: e.tensor_tensor(ybf[:, cols], Yall[:, cols], ps[0:64, 512:1024],
                  op=ALU.mult), waits=(mmB2,))
        nrm.append(n1)
    last = nrm[-1]
    for ci in range(NT):
        sl = slice(ci * 128, (ci + 1) * 128)
        fns = [lambda e, sl=sl, vc=vc: e.matmul(ps[0:128, 1536 + vc * 512:1536 + (vc + 1) * 512],
               ybf[:, sl], wo_sb[:, vc * 512:(vc + 1) * 512], start=True, stop=True) for vc in range(2)]
        mmW = p.mmg(fns, waits=(last,) + tuple(nrm))
        e1 = p.ve(lambda e, x=xt[ci]: e.tensor_add(x, x, ps[0:128, 1536:2560]), waits=(mmW,))
        last = e1
    ar.pop()
    return last



def emit_ln_tiles(p, ar, hb, per, xt, dst, lnt, stat, sqj):
    prevq = None
    lnst = []
    for i in range(NT):
        x = xt[i]
        r1 = p.ve(lambda e, x=x, s=stat: e.reduce_sum(s[:, 0:1], x, axis=AX.X))
        r2 = p.ve(lambda e, x=x, s=stat, o=sqj: e.activation(o[:, 0:1024], x, AF.Square,
                  accum_out=s[:, 1:2]), eng="scalar", waits=((prevq,) if prevq else ()))
        v = p.sve(lambda e, s=stat: e.tensor_scalar(s[:, 0:1], s[:, 0:1], 1.0 / D, None, op0=ALU.mult), waits=(r1, r2))
        v = p.sve(lambda e, s=stat: e.tensor_scalar(s[:, 1:2], s[:, 1:2], 1.0 / D, None, op0=ALU.mult))
        v = p.sve(lambda e, s=stat: e.tensor_tensor(s[:, 2:3], s[:, 0:1], s[:, 0:1], op=ALU.mult))
        v = p.sve(lambda e, s=stat: e.tensor_sub(s[:, 2:3], s[:, 1:2], s[:, 2:3]))
        v = p.sve(lambda e, s=stat: e.tensor_scalar_add(s[:, 3:4], s[:, 2:3], 1e-5))
        v = p.sve(lambda e, s=stat: e.reciprocal(s[:, 3:4], s[:, 3:4]))
        v = p.sve(lambda e, s=stat: e.activation(s[:, 4:5], s[:, 3:4], AF.Sqrt), eng="scalar", waits=(v,))
        v = p.sve(lambda e, s=stat: e.tensor_tensor(s[:, 5:6], s[:, 0:1], s[:, 4:5], op=ALU.mult), waits=(v,))
        v = p.sve(lambda e, s=stat: e.tensor_scalar(s[:, 5:6], s[:, 5:6], -1.0, None, op0=ALU.mult))
        q1 = p.sve(lambda e, x=x, s=stat: e.tensor_scalar(lnt[:, 0:1024], x, s[:, 4:5], s[:, 5:6],
                   op0=ALU.mult, op1=ALU.add), waits=(v,))
        st = p.dma("sync", dst[i * 128:(i + 1) * 128, :], lnt[:, 0:1024], waits=(q1,))
        lnst.append(st)
        prevq = st
    return lnst


def emit_l1attn(p, ar, hb, per, xt):
    ps = per["psum"]
    ar.push()
    QTf = [ar.alloc(128, 2048, BF16) for _ in range(8)]
    bq_sb = ar.alloc(128, 24, F32)
    lnt = ar.alloc(128, 1024, BF16)
    stat = ar.alloc(128, 16, F32)
    rden = ar.alloc(128, 8, F32)
    negm = ar.alloc(128, 8, F32)
    onesrow_b = ar.alloc(1, 128, BF16)
    ob = p.ve(lambda e: e.memset(onesrow_b[0:1, 0:128], 1.0))
    ldb = p.dma("sync", bq_sb, hb["l1bqkv"][:, :])

    ar.push()
    sqj = ar.alloc(128, 1024, F32)
    lnst = emit_ln_tiles(p, ar, hb, per, xt, hb["lnx_h"], lnt, stat, sqj)
    ar.pop()

    ar.push()
    lnxT = [ar.alloc(128, 2048, BF16) for _ in range(8)]
    wqk = [ar.alloc(128, 1024, BF16) for _ in range(2)]
    wv = [ar.alloc(128, 4096, BF16) for _ in range(2)]
    vbuf = ar.alloc(128, 1024, BF16)
    bvrow = ar.alloc(1, 1024, BF16)
    lnT_tk = tuple(p.dma("sync", lnxT[dc], hb["lnx_h"][:, dc * 128:(dc + 1) * 128],
                         waits=(lnst[-1],), transpose=True) for dc in range(8))
    ldbv = p.dma("gpsimd", bvrow, hb["l1bv"][:, :])
    prev_ev = None
    kst = []
    for et in range(16):
        wbuf = wqk[et % 2]
        wt = p.dma("gpsimd", wbuf.rearrange("p (c e) -> p c e", e=128),
                   hb["l1wqkvt"][:, et * 128:(et + 1) * 128].rearrange("(c p) e -> p c e", p=128),
                   waits=(prev_ev,) if prev_ev else ())
        for tc in range(4):
            cols = slice(tc * 512, (tc + 1) * 512)
            fns = [lambda e, dc=dc, cols=cols, wbuf=wbuf: e.matmul(
                ps[0:128, 0:512], wbuf[:, dc * 128:(dc + 1) * 128], lnxT[dc][:, cols],
                start=(dc == 0), stop=(dc == 7)) for dc in range(8)]
            mm = p.mmg(fns, waits=lnT_tk + (wt, ldb) + ((prev_ev,) if prev_ev else ()))
            if et < 8:
                prev_ev = p.ve(lambda e, et=et, cols=cols: e.tensor_scalar(
                    QTf[et][:, cols], ps[0:128, 0:512], bq_sb[:, et:et + 1], None, op0=ALU.add),
                    waits=(mm,))
            else:
                prev_ev = p.ve(lambda e, et=et: e.tensor_scalar(
                    vbuf[:, 0:512], ps[0:128, 0:512], bq_sb[:, et:et + 1], None, op0=ALU.add),
                    waits=(mm,))
                kst.append(p.dma("sync", hb["kin_h"][(et - 8) * 128:(et - 7) * 128, tc * 512:(tc + 1) * 512],
                                 vbuf[:, 0:512], waits=(prev_ev,)))
                p.raw("vector", lambda e: None, (kst[-1],), ())
    vst = []
    wvt0 = p.dma("gpsimd", wv[0].rearrange("p (c e) -> p c e", e=512),
                 hb["l1wqkvt"][:, 2048:2560].rearrange("(c p) e -> p c e", p=128), waits=(prev_ev,))
    wvt1 = p.dma("gpsimd", wv[1].rearrange("p (c e) -> p c e", e=512),
                 hb["l1wqkvt"][:, 2560:3072].rearrange("(c p) e -> p c e", p=128))
    for i in range(NT):
        fns = []
        for dc in range(8):
            for vc in range(2):
                wb = wv[vc]
                fns.append(lambda e, dc=dc, vc=vc, i=i, wb=wb: e.matmul(
                    ps[0:128, 1024 + vc * 512:1024 + (vc + 1) * 512],
                    lnxT[dc][:, i * 128:(i + 1) * 128], wb[:, dc * 512:(dc + 1) * 512],
                    start=(dc == 0), stop=False))
        fns.append(lambda e: e.matmul(ps[0:128, 1024:1536], onesrow_b[0:1, 0:128], bvrow[0:1, 0:512],
                   start=False, stop=True))
        fns.append(lambda e: e.matmul(ps[0:128, 1536:2048], onesrow_b[0:1, 0:128], bvrow[0:1, 512:1024],
                   start=False, stop=True))
        mm = p.mmg(fns, waits=(wvt0, wvt1, ldbv, ob) + ((vst[-1],) if vst else (prev_ev,)))
        ev = p.ve(lambda e: e.tensor_copy(vbuf[:, 0:1024], ps[0:128, 1024:2048]), waits=(mm,))
        vst.append(p.dma("sync", hb["vin_h"][i * 128:(i + 1) * 128, :], vbuf[:, 0:1024], waits=(ev,)))
        p.raw("vector", lambda e: None, (vst[-1],), ())
    ar.pop()

    cc1 = p.collective("AllGather", ALU.bypass, [hb["kin_h"][:, :].opt()], [hb["kout_h"][:, :].opt()],
                       waits=(kst[-1],))
    cc2 = p.collective("AllGather", ALU.bypass, [hb["vin_h"][:, :].opt()], [hb["vout_h"][:, :].opt()],
                       waits=(vst[-1],))

    ar.push()
    scr = [ar.alloc(128, 4096, BF16) for _ in range(4)]
    pbuf = ar.alloc(128, 4096, BF16)
    ktc = [ar.alloc(128, 4096, BF16) for _ in range(2)]
    vtc = [ar.alloc(128, 1024, BF16) for _ in range(2)]
    ptt = [ar.alloc(128, 512, BF16) for _ in range(2)]
    last_ev = (cc1, cc2)
    for phq in range(4):
        qts = [phq * 4 + j for j in range(4)]
        sc_prev = [None] * 4
        kld_hist = []
        for kc in range(8):
            rank = (kc * 512) // 2048
            kcol = (kc * 512) % 2048
            kb = ktc[kc % 2]
            guard = (kld_hist[kc - 2],) if kc >= 2 else last_ev
            kt_tk = p.dma("sync", kb.rearrange("p (c f) -> p c f", f=512),
                          hb["kout_h"][rank * 1024:(rank + 1) * 1024, kcol:kcol + 512]
                          .rearrange("(c p) f -> p c f", p=128), waits=guard)
            for j in range(4):
                fns = [lambda e, ec=ec, j=j, qt=qts[j], kb=kb: e.matmul(
                    ps[0:128, j * 512:(j + 1) * 512], QTf[ec][:, qt * 128:(qt + 1) * 128],
                    kb[:, ec * 512:(ec + 1) * 512], start=(ec == 0), stop=(ec == 7)) for ec in range(8)]
                mm = p.mmg(fns, waits=(kt_tk,) + ((sc_prev[j],) if sc_prev[j] else (last_ev if kc == 0 else ())))
                ev = p.ve(lambda e, j=j, kc=kc: e.tensor_scalar(
                    scr[j][:, kc * 512:(kc + 1) * 512], ps[0:128, j * 512:(j + 1) * 512],
                    1.0 / 32.0, None, op0=ALU.mult), waits=(mm,))
                sc_prev[j] = ev
            kld_hist.append(sc_prev[3])
        pst = []
        for j in range(4):
            r1 = p.ve(lambda e, j=j: e.reduce_max(negm[:, j:j + 1], scr[j][:, 0:4096], axis=AX.X,
                      negate=True), waits=(sc_prev[3],) + ((pst[-1],) if pst else ()))
            ex = p.ve(lambda e, j=j: e.activation(pbuf[:, 0:4096], scr[j][:, 0:4096], AF.Exp,
                      bias=negm[:, j:j + 1], accum_out=stat[:, 0:1]), eng="scalar", waits=(r1,))
            rc = p.sve(lambda e, j=j: e.reciprocal(rden[:, j:j + 1], stat[:, 0:1]), waits=(ex,))
            pst.append(p.dma("sync", hb["p_h"][j * 128:(j + 1) * 128, :], pbuf[:, 0:4096], waits=(ex, rc)))
        avmm = []
        vld_hist = []
        for kc in range(32):
            ptb = ptt[kc % 2]
            vb = vtc[kc % 2]
            guard = (avmm[(kc - 2) * 4 + 3],) if kc >= 2 else (pst[-1],)
            pt_tk = p.dma("scalar", ptb, hb["p_h"][0:512, kc * 128:(kc + 1) * 128],
                          waits=guard + (pst[-1],), transpose=True)
            v_tk = p.dma("sync", vb, hb["vout_h"][kc * 128:(kc + 1) * 128, :], waits=guard)
            for j in range(4):
                fns = [lambda e, j=j, vc=vc, ptb=ptb, vb=vb, kc=kc: e.matmul(
                    ps[0:128, j * 1024 + vc * 512: j * 1024 + (vc + 1) * 512],
                    ptb[:, j * 128:(j + 1) * 128], vb[:, vc * 512:(vc + 1) * 512],
                    start=(kc == 0), stop=(kc == 31)) for vc in range(2)]
                mm = p.mmg(fns, waits=(pt_tk, v_tk))
                avmm.append(mm)
        evs = []
        for j in range(4):
            e1 = p.ve(lambda e, j=j, qt=qts[j]: e.scalar_tensor_tensor(
                xt[qt], ps[0:128, j * 1024:(j + 1) * 1024], rden[:, j:j + 1], xt[qt],
                op0=ALU.mult, op1=ALU.add), waits=(avmm[-1],))
            evs.append(e1)
        last_ev = tuple(evs) + (avmm[-1],)
    ar.pop()
    ar.pop()
    return evs[-1]



def emit_l1ffn_prep(p, ar, hb, per):
    ps = per["psum"]
    prev_done = ()
    for l in range(2):
        ar.push()
        wt = [ar.alloc(128, 5632, F32) for _ in range(2)]
        wq = ar.alloc(128, 5632, BF16)
        lgc = ar.alloc(128, 8, F32)
        lbc = ar.alloc(128, 8, F32)
        urow = ar.alloc(1, 5632, F32)
        crow = ar.alloc(1, 5632, F32)
        b1r = ar.alloc(1, 5632, F32)
        ocb = ar.alloc(128, 1, BF16)
        t0 = p.dma("sync", lgc, hb[f"f1lg{l}"][:, :])
        t1 = p.dma("sync", lbc, hb[f"f1lb{l}"][:, :])
        t2 = p.dma("sync", b1r, hb[f"f1b1p{l}"][:, :])
        m0 = p.ve(lambda e: e.memset(urow[0:1, 0:5632], 0.0), waits=prev_done)
        m1 = p.ve(lambda e: e.memset(ocb, 1.0))
        cw = p.sve(lambda e: e.tensor_copy(crow[0:1, 0:5632], b1r[0:1, 0:5632]), waits=(t2, m0))
        guard = []
        last = cw
        for dc in range(8):
            buf = wt[dc % 2]
            tk = p.dma("sync", buf, hb[f"f1w1p{l}"][dc * 128:(dc + 1) * 128, :],
                       waits=(guard[dc - 2],) if dc >= 2 else (t0, t1))
            q1 = p.ve(lambda e, b=buf, dc=dc: e.tensor_scalar(wq[:, 0:5632], b, lgc[:, dc:dc + 1], None,
                      op0=ALU.mult), waits=(tk, last))
            st = p.dma("sync", hb[f"w1p_h{l}"][dc * 128:(dc + 1) * 128, :], wq[:, 0:5632], waits=(q1,))
            acc = q1
            for fc in range(11):
                mmu = p.ve(lambda e, fc=fc: e.matmul(ps[0:1, 0:512], ocb,
                           wq[:, fc * 512:(fc + 1) * 512], start=True, stop=True), eng="tensor",
                           waits=(q1, m1, acc))
                ua = p.sve(lambda e, fc=fc: e.tensor_add(urow[0:1, fc * 512:(fc + 1) * 512],
                           urow[0:1, fc * 512:(fc + 1) * 512], ps[0:1, 0:512]), waits=(mmu,))
                mmc = p.ve(lambda e, fc=fc, b=buf, dc=dc: e.matmul(ps[0:1, 512:1024],
                           lbc[:, dc:dc + 1], b[:, fc * 512:(fc + 1) * 512], start=True, stop=True),
                           eng="tensor", waits=(ua,))
                acc = p.sve(lambda e, fc=fc: e.tensor_add(crow[0:1, fc * 512:(fc + 1) * 512],
                            crow[0:1, fc * 512:(fc + 1) * 512], ps[0:1, 512:1024]), waits=(mmc,))
            guard.append(acc)
            last = st
        su = p.dma("sync", hb[f"ucrow_h{l}"][0:1, :], urow[0:1, 0:5632], waits=(guard[-1],))
        sc2 = p.dma("sync", hb[f"ucrow_h{l}"][1:2, :], crow[0:1, 0:5632], waits=(guard[-1],))
        w2cb = [ar.alloc(128, 1024, BF16) for _ in range(2)]
        w2st = []
        for hc2 in range(22):
            b2_ = w2cb[hc2 % 2]
            tkc = p.dma("gpsimd", b2_, hb[f"f1w2{l}"][hc2 * 128:(hc2 + 1) * 128, :],
                        waits=(w2st[hc2 - 2],) if hc2 >= 2 else prev_done)
            w2st.append(p.dma("scalar", hb[f"w2b_h{l}"][hc2 * 128:(hc2 + 1) * 128, :], b2_,
                        waits=(tkc,)))
        p.raw("sync", lambda e: None, (su, sc2, last, w2st[-1], w2st[-2]), ())
        prev_done = (su, sc2, last, w2st[-1], w2st[-2])
        ar.pop()


def emit_l1ffn(p, ar, hb, per, xt):
    ps = per["psum"]
    identb = per["ident_b"]
    identf = per["ident_f"]
    onesr = per["onesrow_f"]
    ar.push()
    stat = ar.alloc(128, 16, F32)
    st2 = ar.alloc(128, 96, F32)
    role1v = ar.alloc(128, 1, F32)
    ocb = ar.alloc(128, 1, BF16)
    orb = ar.alloc(1, 128, BF16)
    ocf = ar.alloc(128, 1, F32)
    lnt2 = [ar.alloc(128, 1024, BF16) for _ in range(2)]
    lnt = lnt2[0]
    r0 = p.dma("sync", role1v, hb["role1"][:, :])
    mo = p.ve(lambda e: e.memset(ocb, 1.0))
    mo2 = p.ve(lambda e: e.memset(orb[0:1, 0:128], 1.0))
    mo3 = p.ve(lambda e: e.memset(ocf, 1.0))

    sp = []
    for i in range(NT):
        sp.append(p.dma("sync", hb["xsp_h"][i * 128:(i + 1) * 128, :], xt[i]))
    ar.push()
    sqj2 = [ar.alloc(128, 1024, F32) for _ in range(2)]
    rs = []
    r2s = []
    for i in range(NT):
        x = xt[i]
        r1 = p.ve(lambda e, x=x, i=i: e.reduce_sum(st2[:, i:i + 1], x, axis=AX.X), waits=(sp[i],))
        r2 = p.ve(lambda e, x=x, i=i, b=sqj2[i % 2]: e.activation(b[:, 0:1024], x, AF.Square,
                  accum_out=st2[:, 16 + i:17 + i]), eng="scalar",
                  waits=((r2s[i - 2],) if i >= 2 else ()) + (sp[i],))
        rs.append(r1)
        r2s.append(r2)
    v = p.ve(lambda e: e.tensor_scalar(st2[:, 0:32], st2[:, 0:32], 1.0 / D, None, op0=ALU.mult),
             waits=tuple(rs) + tuple(r2s))
    v = p.sve(lambda e: e.tensor_tensor(st2[:, 80:96], st2[:, 0:16], st2[:, 0:16], op=ALU.mult), waits=(v,))
    v = p.sve(lambda e: e.tensor_sub(st2[:, 32:48], st2[:, 16:32], st2[:, 80:96]))
    v = p.sve(lambda e: e.tensor_scalar_add(st2[:, 32:48], st2[:, 32:48], 1e-5))
    v = p.sve(lambda e: e.reciprocal(st2[:, 48:64], st2[:, 32:48]))
    v = p.sve(lambda e: e.activation(st2[:, 48:64], st2[:, 48:64], AF.Sqrt), eng="scalar", waits=(v,))
    v = p.sve(lambda e: e.tensor_tensor(st2[:, 64:80], st2[:, 0:16], st2[:, 48:64], op=ALU.mult), waits=(v,))
    v = p.sve(lambda e: e.tensor_scalar(st2[:, 64:80], st2[:, 64:80], -1.0, None, op0=ALU.mult))
    prevq_l = []
    for i in range(NT):
        prevq_l.append(p.ve(lambda e, x=xt[i], i=i: e.tensor_scalar(x, x, st2[:, 48 + i:49 + i],
                       st2[:, 64 + i:65 + i], op0=ALU.mult, op1=ALU.add), waits=(v,)))
    prevq = prevq_l[-1]
    ar.pop()
    x_ready = tuple(prevq_l)
    if F1_TRUNC <= 0:
        ar.pop()
        return prevq

    xi_order = [11, 12, 13, 14, 15] + list(range(11))
    for l in range(2):
        xist = []
        for k, i in enumerate(xi_order):
            b = lnt2[k % 2]
            xw = (prevq_l[i],) if l == 0 else x_ready
            c1 = p.ve(lambda e, i=i, b=b: e.tensor_copy(b[:, 0:1024], xt[i]),
                      waits=xw + ((xist[k - 2],) if k >= 2 else ()))
            xist.append(p.dma("sync", hb["xi_h"][i * 128:(i + 1) * 128, :], b[:, 0:1024], waits=(c1,)))
        ccA = p.collective("AllGather", ALU.bypass, [hb["xi_h"][1535:2048, :].opt()],
                           [hb["bndout_h"][:, :].opt()], waits=tuple(xist[0:5]))
        if F1_TRUNC <= 0.5 + 2 * l:
            ar.pop()
            return xist[-1]

        ar.push()
        vsb = [ar.alloc(128, 512, BF16) for _ in range(20)]
        oT = [ar.alloc(128, 2048, BF16) for _ in range(4)]
        ar.push()
        xT = [ar.alloc(128, 2576, BF16) for _ in range(8)]
        wkv = [ar.alloc(128, 1536, BF16) for _ in range(8)]
        qkvt2 = [ar.alloc(128, 1024, F32) for _ in range(2)]
        tbl = {nm: [ar.alloc(128, 64, BF16) for _ in range(20)] for nm in ("cq", "sq", "ck", "sk")}
        qsb = ar.alloc(128, 64, F32)
        ksb = ar.alloc(128, 64, F32)
        qsbb = ar.alloc(128, 64, BF16)
        ksbb = ar.alloc(128, 64, BF16)
        hnrm = ar.alloc(128, 16, F32)
        t1b = ar.alloc(128, 512, BF16)
        t2b = ar.alloc(128, 512, BF16)
        qout = ar.alloc(128, 512, BF16)
        ldw = tuple(p.dma("gpsimd", wkv[dc], hb[f"f1wqkv{l}"][dc * 128:(dc + 1) * 128, :]) for dc in range(8))
        ldt = []
        for nm in ("cq", "sq", "ck", "sk"):
            for ti in range(20):
                ldt.append(p.dma("gpsimd", tbl[nm][ti], hb[nm][ti * 128:(ti + 1) * 128, :]))
        ldq = p.dma("sync", qsb[0:1, 0:64], hb[f"f1qs{l}"][:, :])
        ldk = p.dma("sync", ksb[0:1, 0:64], hb[f"f1ks{l}"][:, :])
        bq = p.ve(lambda e: e.matmul(ps[0:128, 0:64], onesr[0:1, 0:128], qsb[0:1, 0:64],
                  start=True, stop=True), eng="tensor", waits=(ldq, ldk))
        bq2 = p.ve(lambda e: e.tensor_scalar(qsbb[:, 0:64], ps[0:128, 0:64], 8.0, None, op0=ALU.mult), waits=(bq,))
        bk = p.ve(lambda e: e.matmul(ps[0:128, 64:128], onesr[0:1, 0:128], ksb[0:1, 0:64],
                  start=True, stop=True), eng="tensor", waits=(bq2,))
        bk2 = p.ve(lambda e: e.tensor_copy(ksbb[:, 0:64], ps[0:128, 64:128]), waits=(bk,))
        fold = []
        fw = (bq2, bk2) + tuple(ldt)
        for ti in range(20):
            fold.append(p.ve(lambda e, ti=ti: e.tensor_tensor(tbl["cq"][ti], tbl["cq"][ti], qsbb[:, 0:64], op=ALU.mult), waits=fw))
            fold.append(p.ve(lambda e, ti=ti: e.tensor_tensor(tbl["sq"][ti], tbl["sq"][ti], qsbb[:, 0:64], op=ALU.mult)))
            fold.append(p.ve(lambda e, ti=ti: e.tensor_tensor(tbl["ck"][ti], tbl["ck"][ti], ksbb[:, 0:64], op=ALU.mult)))
            fold.append(p.ve(lambda e, ti=ti: e.tensor_tensor(tbl["sk"][ti], tbl["sk"][ti], ksbb[:, 0:64], op=ALU.mult)))
        fold_done = fold[-1]
        xtk = []
        for dc in range(8):
            xtk.append(p.dma("sync", xT[dc][:, 528:2576], hb["xi_h"][:, dc * 128:(dc + 1) * 128],
                             waits=(xist[-1],), transpose=True))
            xtk.append(p.dma("sync", xT[dc][:, 16:528], hb["bndout_h"][0:512, dc * 128:(dc + 1) * 128],
                             waits=(ccA,), transpose=True))
            xtk.append(p.dma("gpsimd", xT[dc][:, 15:16],
                             hb["bndout_h"][512:513, dc * 128:(dc + 1) * 128].rearrange("a b -> b a"),
                             waits=(ccA,)))
        msk = []
        for dc in range(8):
            msk.append(p.ve(lambda e, dc=dc: e.tensor_scalar(xT[dc][:, 15:528], xT[dc][:, 15:528],
                       role1v[:, 0:1], None, op0=ALU.mult), waits=tuple(xtk) + (r0,)))
        msk = tuple(msk)
        qkst = []
        prev_ev = None
        ev_last = [None, None]
        cons_last = [None, None]
        n6c = None
        for ti in range(20):
            par = ti % 2
            pb_ = par * 2048
            qkvt = qkvt2[par]
            fns = []
            for dc in range(8):
                off = 16 if dc < 4 else 15
                lhs = xT[dc][:, off + ti * 128: off + ti * 128 + 128]
                for fc in range(3):
                    fns.append(lambda e, lhs=lhs, dc=dc, fc=fc, pb_=pb_: e.matmul(
                        ps[0:128, pb_ + fc * 512:pb_ + (fc + 1) * 512], lhs, wkv[dc][:, fc * 512:(fc + 1) * 512],
                        start=(dc == 0), stop=(dc == 7)))
            mmw = msk + ldw
            if ti == 0:
                mmw += (bk2, bq2)
            if ev_last[par] is not None:
                mmw += (ev_last[par],)
            mm = p.mmg(fns, waits=mmw)
            evw = (mm,)
            if cons_last[par] is not None:
                evw += cons_last[par]
            ev = p.ve(lambda e, qkvt=qkvt, pb_=pb_: e.tensor_copy(qkvt[:, 0:1024],
                      ps[0:128, pb_:pb_ + 1024]), waits=evw)
            n6 = None
            for base, ct, st_, outh in ((0, "cq", "sq", "qh_h"), (512, "ck", "sk", "kh_h")):
                vvw = (ev,) + ((n6,) if n6 else ((n6c,) if n6c else ()))
                vv = p.ve(lambda e, base=base, qkvt=qkvt: e.activation(t1b[:, 0:512], qkvt[:, base:base + 512],
                          AF.Square), eng="scalar", waits=vvw)
                rr = p.sve(lambda e: e.tensor_reduce(hnrm[:, 0:8],
                           t1b[:, 0:512].rearrange("p (h d) -> p h d", d=64), axis=AX.X, op=ALU.add),
                           waits=(vv,))
                rr = p.sve(lambda e: e.activation(hnrm[:, 0:8], hnrm[:, 0:8], AF.Sqrt), eng="scalar", waits=(rr,))
                rr = p.sve(lambda e: e.tensor_scalar(hnrm[:, 0:8], hnrm[:, 0:8], 1e-12, None, op0=ALU.max), waits=(rr,))
                rr = p.sve(lambda e: e.reciprocal(hnrm[:, 8:16], hnrm[:, 0:8]))
                n1 = p.sve(lambda e, base=base, qkvt=qkvt: e.tensor_tensor(
                    t1b[:, 0:512].rearrange("p (h d) -> p h d", d=64),
                    qkvt[:, base:base + 512].rearrange("p (h d) -> p h d", d=64),
                    hnrm[:, 8:16].unsqueeze(2).broadcast_to((128, 8, 64)), op=ALU.mult), waits=(rr,))
                last_n1 = n1
                n2 = p.sve(lambda e: e.tensor_scalar(
                    t2b[:, 0:512].rearrange("p (h d) -> p h d", d=64)[:, :, 0:32],
                    t1b[:, 0:512].rearrange("p (h d) -> p h d", d=64)[:, :, 32:64],
                    -1.0, None, op0=ALU.mult), waits=(n1,))
                n3 = p.sve(lambda e: e.tensor_copy(
                    t2b[:, 0:512].rearrange("p (h d) -> p h d", d=64)[:, :, 32:64],
                    t1b[:, 0:512].rearrange("p (h d) -> p h d", d=64)[:, :, 0:32]))
                n4 = p.sve(lambda e, ti=ti, ct=ct: e.tensor_tensor(
                    t1b[:, 0:512].rearrange("p (h d) -> p h d", d=64),
                    t1b[:, 0:512].rearrange("p (h d) -> p h d", d=64),
                    tbl[ct][ti].unsqueeze(1).broadcast_to((128, 8, 64)), op=ALU.mult), waits=(n3, fold_done))
                n5 = p.sve(lambda e, ti=ti, st_=st_: e.tensor_tensor(
                    t2b[:, 0:512].rearrange("p (h d) -> p h d", d=64),
                    t2b[:, 0:512].rearrange("p (h d) -> p h d", d=64),
                    tbl[st_][ti].unsqueeze(1).broadcast_to((128, 8, 64)), op=ALU.mult))
                n6 = p.sve(lambda e: e.tensor_add(qout[:, 0:512], t1b[:, 0:512], t2b[:, 0:512]))
                qkst.append(p.dma("sync", hb[outh][ti * 128:(ti + 1) * 128, :], qout[:, 0:512], waits=(n6,)))
                p.raw("vector", lambda e: None, (qkst[-1],), ())
            prev_ev = p.ve(lambda e, ti=ti, pb_=pb_: e.tensor_copy(vsb[ti][:, 0:512],
                           ps[0:128, pb_ + 1024:pb_ + 1536]), waits=(mm, ev))
            ev_last[par] = prev_ev
            cons_last[par] = (last_n1,)
            n6c = n6
        qkv_done = (prev_ev, qkst[-1])
        ar.pop()
        if F1_TRUNC <= 0.7 + 2 * l:
            ar.pop()
            ar.pop()
            return prev_ev
        ar.push()
        qhT = [ar.alloc(128, 2560, BF16) for _ in range(4)]
        khT = [ar.alloc(128, 2560, BF16) for _ in range(4)]
        lmsk = [ar.alloc(128, 1024, BF16) for _ in range(4)]
        lmskf = [ar.alloc(128, 1024, BF16) for _ in range(4)]
        psb = ar.alloc(128, 8192, BF16)
        ptb = [ar.alloc(128, 2048, BF16) for _ in range(2)]
        rden8 = ar.alloc(128, 8, F32)
        negm8 = ar.alloc(128, 8, F32)
        qtk = tuple(p.dma("sync", qhT[hc], hb["qh_h"][:, hc * 128:(hc + 1) * 128],
                          waits=qkv_done, transpose=True) for hc in range(4))
        ktk = tuple(p.dma("sync", khT[hc], hb["kh_h"][:, hc * 128:(hc + 1) * 128],
                          waits=qkv_done, transpose=True) for hc in range(4))
        mtk = tuple(p.dma("gpsimd", lmsk[q4], hb["lmask"][q4 * 128:(q4 + 1) * 128, :], waits=qkv_done) for q4 in range(4)) + \
              tuple(p.dma("gpsimd", lmskf[q4], hb["lmaskf"][q4 * 128:(q4 + 1) * 128, :], waits=qkv_done) for q4 in range(4))
        pstores = []
        for w in range(4):
            for q4 in range(4):
                exps = []
                hdev = {}
                for hd in range(8):
                    hc, hr = hd // 2, (hd % 2) * 64
                    qcol = 512 + w * 512 + q4 * 128
                    lm = lmskf if w == 0 else lmsk
                    pb_ = (hd % 4) * 1024
                    fns = [lambda e, hc=hc, hr=hr, qcol=qcol, kk=kk, w=w, pb_=pb_: e.matmul(
                        ps[0:128, pb_ + kk * 512:pb_ + (kk + 1) * 512],
                        qhT[hc][hr:hr + 64, qcol:qcol + 128],
                        khT[hc][hr:hr + 64, w * 512 + kk * 512: w * 512 + (kk + 1) * 512],
                        start=True, stop=False) for kk in range(2)]
                    fns += [lambda e, q4=q4, kk=kk, lm=lm, pb_=pb_: e.matmul(
                        ps[0:128, pb_ + kk * 512:pb_ + (kk + 1) * 512], identb[:, 0:128],
                        lm[q4][:, kk * 512:(kk + 1) * 512], start=False, stop=True) for kk in range(2)]
                    mmw2 = qtk + ktk + mtk
                    if hd >= 4:
                        mmw2 += hdev[hd - 4]
                    elif pstores:
                        mmw2 += (pstores[-1],)
                    mm = p.mmg(fns, waits=mmw2)
                    r1 = p.ve(lambda e, hd=hd, pb_=pb_: e.reduce_max(negm8[:, hd:hd + 1],
                              ps[0:128, pb_:pb_ + 1024],
                              axis=AX.X, negate=True), waits=(mm,))
                    ex = p.ve(lambda e, hd=hd, pb_=pb_: e.activation(psb[:, hd * 1024:(hd + 1) * 1024],
                              ps[0:128, pb_:pb_ + 1024], AF.Exp, bias=negm8[:, hd:hd + 1],
                              accum_out=stat[:, hd:hd + 1]), eng="scalar", waits=(r1,))
                    rc = p.ve(lambda e, hd=hd: e.reciprocal(rden8[:, hd:hd + 1], stat[:, hd:hd + 1]),
                              waits=(ex,))
                    hdev[hd] = (r1, ex)
                    exps.append(rc)
                    exps.append(ex)
                nrm = p.ve(lambda e: e.tensor_tensor(
                    psb[:, 0:8192].rearrange("p (h k) -> p h k", k=1024),
                    psb[:, 0:8192].rearrange("p (h k) -> p h k", k=1024),
                    rden8[:, 0:8].unsqueeze(2).broadcast_to((128, 8, 1024)), op=ALU.mult),
                    waits=tuple(exps))
                pstores.append(p.dma("sync", hb["ph_h"][(w * 4 + q4) * 128:(w * 4 + q4 + 1) * 128, :],
                               psb[:, 0:8192], waits=(nrm,)))
                p.raw("vector", lambda e: None, (pstores[-1],), ())
        if F1_TRUNC <= 0.8 + 2 * l:
            ar.pop()
            ar.pop()
            ar.pop()
            return pstores[-1]
        avh = []
        prev_oev = None
        for hd in range(8):
            hc, hr = hd // 2, (hd % 2) * 64
            for kk in range(8):
                pb = ptb[kk % 2]
                idx = hd * 8 + kk
                ptk = p.dma("scalar", pb, hb["ph_h"][0:2048, hd * 1024 + kk * 128: hd * 1024 + (kk + 1) * 128],
                            waits=(pstores[-1],) + ((avh[idx - 2],) if idx >= 2 else ()), transpose=True)
                fns = [lambda e, w=w2, kk=kk, hd=hd, pb=pb: e.matmul(
                    ps[0:64, 2048 + w * 512: 2048 + (w + 1) * 512],
                    vsb[w * 4 + kk][:, hd * 64:hd * 64 + 64],
                    pb[:, w * 512:(w + 1) * 512], start=(kk == 0), stop=(kk == 7)) for w2 in range(4)]
                mm = p.mmg(fns, waits=(ptk,) + ((prev_oev,) if prev_oev and kk == 0 else ()))
                avh.append(mm)
            for w in range(4):
                oe = p.ve(lambda e, w=w, hc=hc, hr=hr: e.tensor_copy(
                    oT[hc][hr:hr + 64, w * 512:(w + 1) * 512],
                    ps[0:64, 2048 + w * 512:2048 + (w + 1) * 512]), waits=(avh[-1],))
            prev_oev = oe
        if F1_TRUNC <= 0.9 + 2 * l:
            ar.pop()
            ar.pop()
            ar.pop()
            return prev_oev
        ar.pop()
        ar.push()
        wosb = [ar.alloc(128, 1024, BF16) for _ in range(4)]
        wotk = tuple(p.dma("gpsimd", wosb[ic], hb[f"f1wo{l}"][ic * 128:(ic + 1) * 128, :],
                           waits=(prev_oev,)) for ic in range(4))
        prev = None
        for ti in range(NT):
            fns = []
            for ic in range(4):
                for vc2 in range(2):
                    fns.append(lambda e, ic=ic, vc2=vc2, ti=ti: e.matmul(
                        ps[0:128, vc2 * 512:(vc2 + 1) * 512], oT[ic][:, ti * 128:(ti + 1) * 128],
                        wosb[ic][:, vc2 * 512:(vc2 + 1) * 512], start=(ic == 0), stop=(ic == 3)))
            mm = p.mmg(fns, waits=wotk + (prev_oev,) + ((prev,) if prev else ()))
            prev = p.ve(lambda e, ti=ti: e.tensor_add(xt[ti], xt[ti], ps[0:128, 0:1024]), waits=(mm,))
        ar.pop()
        ar.pop()
        x_ready = (prev,)
        if F1_TRUNC <= 1 + 2 * l:
            ar.pop()
            return prev

        c1 = p.ve(lambda e: e.tensor_copy(lnt[:, 0:1024], xt[NT - 1]), waits=x_ready)
        sx1 = p.dma("gpsimd", hb["bnd1in_h"][0:1, :], lnt[127:128, 0:1024], waits=(c1,))
        ccB = p.collective("AllGather", ALU.bypass, [hb["bnd1in_h"][:, :].opt()],
                           [hb["bnd1out_h"][:, :].opt()], waits=(sx1,))

        xist = []
        for i in range(NT):
            b = lnt2[i % 2]
            cc = p.ve(lambda e, i=i, b=b: e.tensor_copy(b[:, 0:1024], xt[i]),
                      waits=x_ready + (c1, sx1) + ((xist[i - 2],) if i >= 2 else ()))
            xist.append(p.dma("sync", hb["xi_h"][i * 128:(i + 1) * 128, :], b[:, 0:1024], waits=(cc,)))
        if F1_TRUNC <= 1.5 + 2 * l:
            ar.pop()
            return xist[-1]
        ar.push()
        xT2 = [ar.alloc(128, 2064, BF16) for _ in range(8)]
        statrows = ar.alloc(1, 6144, F32)
        murow = statrows[0:1, 0:2048]
        msqrow = statrows[0:1, 2048:4096]
        rvrow = statrows[0:1, 4096:6144]
        brows = ar.alloc(1, 4096, BF16)
        nmurow = brows[0:1, 0:2048]
        rvirow = brows[0:1, 2048:4096]
        rvsq = ar.alloc(16, 128, F32)
        rvcol = ar.alloc(128, 16, F32)
        sqsc = ar.alloc(128, 512, F32)
        ucrow_t = ar.alloc(1, 11264, BF16)
        ucrow = ucrow_t[0:1, 0:5632]
        ccrow = ucrow_t[0:1, 5632:11264]
        hhtile = [ar.alloc(128, 1024, BF16) for _ in range(2)]
        h1p = ar.alloc(128, 512, F32)
        ggs = ar.alloc(128, 256, F32)
        global F1_REFS
        F1_REFS = dict(xT2=xT2, statrows=statrows, rvcol=rvcol, ucrow=ucrow, ccrow=ccrow,
                       hhtile=hhtile, h1p=h1p, ggs=ggs)
        xtk2 = []
        for dc in range(8):
            xtk2.append(p.dma("sync", xT2[dc][:, 16:2064], hb["xi_h"][:, dc * 128:(dc + 1) * 128],
                              waits=(xist[-1],), transpose=True))
            xtk2.append(p.dma("gpsimd", xT2[dc][:, 15:16],
                              hb["bnd1out_h"][0:1, dc * 128:(dc + 1) * 128].rearrange("a b -> b a"),
                              waits=(ccB,)))
        msk2 = []
        for dc in range(8):
            msk2.append(p.ve(lambda e, dc=dc: e.tensor_scalar(xT2[dc][:, 15:16], xT2[dc][:, 15:16],
                        role1v[:, 0:1], None, op0=ALU.mult), waits=tuple(xtk2)))
        msk2 = tuple(msk2)
        lduc0 = p.dma("gpsimd", ucrow, hb[f"ucrow_h{l}"][0:1, :])
        lduc = p.dma("gpsimd", ccrow, hb[f"ucrow_h{l}"][1:2, :])

        def hsl(dc, c0, c1):
            off = 16 if dc < 4 else 15
            return xT2[dc][:, off + c0: off + c1]

        stev = []
        for tc in range(4):
            fns = [lambda e, dc=dc, tc=tc: e.matmul(ps[0:1, 0:512], ocb,
                   hsl(dc, tc * 512, (tc + 1) * 512), start=(dc == 0), stop=(dc == 7)) for dc in range(8)]
            mm1_ = p.mmg(fns, waits=msk2 + (mo,) + ((stev[-1],) if stev else ()))
            stev.append(p.ve(lambda e, tc=tc: e.tensor_scalar(murow[0:1, tc * 512:(tc + 1) * 512],
                        ps[0:1, 0:512], 1.0 / D, None, op0=ALU.mult), waits=(mm1_,)))
        for tc in range(4):
            sq_p = None
            for dc in range(8):
                sq = p.ve(lambda e, dc=dc, tc=tc: e.activation(sqsc[:, 0:512],
                          hsl(dc, tc * 512, (tc + 1) * 512), AF.Square), eng="scalar",
                          waits=((sq_p,) if sq_p else (stev[-1],)))
                sq_p = p.ve(lambda e, dc=dc: e.matmul(ps[0:1, 512:1024], ocf, sqsc[:, 0:512],
                            start=(dc == 0), stop=(dc == 7)), eng="tensor", waits=(sq, mo3))
            stev.append(p.ve(lambda e, tc=tc: e.tensor_scalar(msqrow[0:1, tc * 512:(tc + 1) * 512],
                        ps[0:1, 512:1024], 1.0 / D, None, op0=ALU.mult), waits=(sq_p,)))
        v = p.sve(lambda e: e.tensor_tensor(rvrow[0:1, 0:2048], murow[0:1, 0:2048], murow[0:1, 0:2048],
                  op=ALU.mult), waits=tuple(stev))
        v = p.sve(lambda e: e.tensor_sub(msqrow[0:1, 0:2048], msqrow[0:1, 0:2048], rvrow[0:1, 0:2048]))
        v = p.sve(lambda e: e.tensor_scalar_add(msqrow[0:1, 0:2048], msqrow[0:1, 0:2048], 1e-5))
        v = p.sve(lambda e: e.activation(rvirow[0:1, 0:2048], msqrow[0:1, 0:2048], AF.Sqrt), eng="scalar", waits=(v,))
        v = p.sve(lambda e: e.reciprocal(rvrow[0:1, 0:2048], msqrow[0:1, 0:2048]), waits=(v,))
        v = p.sve(lambda e: e.activation(rvrow[0:1, 0:2048], rvrow[0:1, 0:2048], AF.Sqrt), eng="scalar", waits=(v,))
        v = p.sve(lambda e: e.tensor_scalar(nmurow[0:1, 0:2048], murow[0:1, 0:2048], -1.0, None, op0=ALU.mult), waits=(v,))
        if A0_DEBUG:
            p.dma("sync", hb["dbgf1_h"][0:1, :], murow[0:1, 0:2048], waits=(v,))
            p.dma("sync", hb["dbgf1_h"][1:2, :], msqrow[0:1, 0:2048], waits=(v,))
        s1 = p.dma("sync", hb["rv_h"][0:1, :], rvrow[0:1, 0:2048], waits=(v,))
        s2 = p.dma("sync", rvsq, hb["rv_h"].rearrange("o (r c) -> (o r) c", c=128), waits=(s1,))
        tr = p.ve(lambda e: e.transpose(ps[0:128, 1024:1040], rvsq[0:16, 0:128], identf[0:16, 0:16]),
                  eng="tensor", waits=(s2,))
        trc = p.ve(lambda e: e.tensor_copy(rvcol[:, 0:16], ps[0:128, 1024:1040]), waits=(tr,))

        ar.push()
        w1h = [ar.alloc(128, 1024, BF16) for _ in range(8)]
        F1_REFS['w1h'] = w1h
        hh_done = []
        e1_last = [None, None]
        st_last = [None, None]
        e3_prev = None
        mm_prev_half = None
        gt = 0
        for half, (f0, nfc) in enumerate(((0, 2), (2, 2), (4, 2), (6, 2), (8, 2), (10, 1))):
            w1tk = tuple(p.dma("sync", w1h[dc][:, 0:nfc * 512],
                               hb[f"w1p_h{l}"][dc * 128:(dc + 1) * 128, f0 * 512:(f0 + nfc) * 512],
                               waits=((mm_prev_half,) if mm_prev_half else (trc,))) for dc in range(8))
            for ti in range(NT):
                par = gt % 2
                pb_ = par * 2048
                fns = []
                for dc in range(8):
                    off = 16 if dc < 4 else 15
                    lhs = xT2[dc][:, off + ti * 128: off + ti * 128 + 128]
                    for fc in range(nfc):
                        fns.append(lambda e, lhs=lhs, dc=dc, fc=fc, pb_=pb_: e.matmul(
                            ps[0:128, pb_ + fc * 512:pb_ + (fc + 1) * 512], lhs, w1h[dc][:, fc * 512:(fc + 1) * 512],
                            start=(dc == 0), stop=False))
                for fc in range(nfc):
                    fns.append(lambda e, fc=fc, f0=f0, ti=ti, pb_=pb_: e.matmul(
                        ps[0:128, pb_ + fc * 512:pb_ + (fc + 1) * 512], nmurow[0:1, ti * 128:(ti + 1) * 128],
                        ucrow[0:1, (f0 + fc) * 512:(f0 + fc + 1) * 512], start=False, stop=False))
                    fns.append(lambda e, fc=fc, f0=f0, ti=ti, pb_=pb_: e.matmul(
                        ps[0:128, pb_ + fc * 512:pb_ + (fc + 1) * 512], rvirow[0:1, ti * 128:(ti + 1) * 128],
                        ccrow[0:1, (f0 + fc) * 512:(f0 + fc + 1) * 512], start=False, stop=(True)))
                mmw = w1tk + (lduc0, lduc, trc)
                if e1_last[par] is not None:
                    mmw += (e1_last[par],)
                mm = p.mmg(fns, waits=mmw)
                hb_ = hhtile[par]
                hev = []
                dbgT = []
                for fc in range(nfc):
                    e1w = (mm,) + (tuple(hev[-1:]) if hev else ()) + tuple(dbgT[-1:])
                    if fc == 0 and st_last[par] is not None:
                        e1w += (st_last[par],)
                    if fc == 0 and e3_prev is not None:
                        e1w += (e3_prev,)
                    e1 = p.ve(lambda e, fc=fc, ti=ti, pb_=pb_: e.tensor_scalar(h1p[:, 0:512],
                              ps[0:128, pb_ + fc * 512:pb_ + (fc + 1) * 512], rvcol[:, ti:ti + 1], None,
                              op0=ALU.mult), waits=e1w)
                    if A0_DEBUG and half == 0 and ti == 0:
                        dbgT.append(p.dma("sync", hb["dbgf1_h"][8 + fc * 128: 8 + (fc + 1) * 128, 0:512],
                                          h1p[:, 0:512], waits=(e1,)))
                    e2w = (e1,)
                    if fc == 0 and e3_prev is not None:
                        e2w += (e3_prev,)
                    e2 = p.ve(lambda e: e.activation(ggs[:, 0:256], h1p[:, 1:512:2],
                              AF.Gelu), eng="scalar", waits=e2w)
                    if A0_DEBUG and half == 0 and ti == 0:
                        dbgT.append(p.dma("sync", hb["dbgf1_h"][520 + fc * 128: 520 + (fc + 1) * 128, 0:256],
                                          ggs[:, 0:256], waits=(e2,)))
                    e3 = p.ve(lambda e, fc=fc, hb_=hb_: e.tensor_tensor(hb_[:, fc * 256:(fc + 1) * 256],
                              h1p[:, 0:512:2], ggs[:, 0:256], op=ALU.mult), waits=(e2,) + tuple(dbgT[-1:]))
                    hev.append(e3)
                    e1_last[par] = e1
                e3_prev = hev[-1]
                if A0_DEBUG and half == 0 and ti == 0:
                    dbgT.append(p.dma("sync", hb["dbgf1_h"][1032:1160, 0:512],
                                      hb_[:, 0:1024].bitcast(F32), waits=tuple(hev)))
                st = p.dma("sync", hb["hh_h"][ti * 128:(ti + 1) * 128, f0 * 256:(f0 + nfc) * 256],
                           hb_[:, 0:nfc * 256], waits=tuple(hev) + tuple(dbgT[-1:]))
                st_last[par] = st
                hh_done.append(st)
                gt += 1
            mm_prev_half = mm
        ar.pop()
        ar.push()
        hhT = [ar.alloc(128, 512, BF16) for _ in range(4)]
        w2s = [ar.alloc(128, 1024, BF16) for _ in range(2)]
        b2r = ar.alloc(1, 1024, BF16)
        F1_REFS['hhT'] = hhT; F1_REFS['w2s'] = w2s; F1_REFS['b2r'] = b2r
        ldb2 = p.dma("gpsimd", b2r, hb[f"f1b2{l}"][:, :], waits=(hh_done[-1],))
        prev2 = None
        for tg in range(4):
            tis = [tg * 4 + j for j in range(4)]
            mm_l = None
            w2g = []
            for hc in range(22):
                wb = w2s[hc % 2]
                hcb = hhT[hc % 4]
                gw = (w2g[hc - 2],) if hc >= 2 else ((prev2,) if prev2 else (hh_done[-1],))
                wtk = p.dma("sync", wb, hb[f"w2b_h{l}"][hc * 128:(hc + 1) * 128, :], waits=gw)
                gh = (w2g[hc - 4],) if hc >= 4 else ((prev2,) if prev2 else (hh_done[-1],))
                htk = p.dma("scalar", hcb, hb["hh_h"][tg * 512:(tg + 1) * 512, hc * 128:(hc + 1) * 128],
                            waits=gh + (hh_done[-1],), transpose=True)
                fns = []
                for j in range(4):
                    for vc2 in range(2):
                        fns.append(lambda e, j=j, vc2=vc2, hc=hc, hcb=hcb, wb=wb: e.matmul(
                            ps[0:128, j * 1024 + vc2 * 512: j * 1024 + (vc2 + 1) * 512],
                            hcb[:, j * 128:(j + 1) * 128], wb[:, vc2 * 512:(vc2 + 1) * 512],
                            start=(hc == 0), stop=False))
                mm_l = p.mmg(fns, waits=(wtk, htk) + ((prev2,) if prev2 and hc == 0 else ()))
                w2g.append(mm_l)
            fns = []
            for j in range(4):
                for vc2 in range(2):
                    fns.append(lambda e, j=j, vc2=vc2: e.matmul(
                        ps[0:128, j * 1024 + vc2 * 512: j * 1024 + (vc2 + 1) * 512],
                        orb[0:1, 0:128], b2r[0:1, vc2 * 512:(vc2 + 1) * 512],
                        start=False, stop=True))
            mm_l = p.mmg(fns, waits=(ldb2, mo2))
            evl = []
            for j in range(4):
                evl.append(p.ve(lambda e, j=j, ti=tis[j]: e.tensor_add(
                    xt[ti], xt[ti], ps[0:128, j * 1024:(j + 1) * 1024]), waits=(mm_l,)))
            prev2 = evl[-1]
        ar.pop()
        ar.pop()
        x_ready = (prev2,)
        if F1_TRUNC <= 2 + 2 * l:
            ar.pop()
            return prev2

    ar.push()
    xo = [ar.alloc(128, 1024, F32) for _ in range(2)]
    fin = None
    for i in range(NT):
        ld = p.dma("sync", xo[i % 2], hb["xsp_h"][i * 128:(i + 1) * 128, :],
                   waits=x_ready + ((fin,) if fin and i >= 2 else ()))
        fin = p.ve(lambda e, i=i, b=xo[i % 2]: e.scalar_tensor_tensor(
            xt[i], xt[i], 0.5, b, op0=ALU.mult, op1=ALU.add), waits=(ld,))
    ar.pop()
    ar.pop()
    return fin


# version 06
# speedup vs baseline: 1.2245x; 1.0005x over previous
import contextlib
import numpy as np
import ml_dtypes
import concourse.bass as bass
import concourse.mybir as mybir
from concourse.bass_utils import run_bass_kernel_spmd

F32 = mybir.dt.float32
BF16 = mybir.dt.bfloat16
AF = mybir.ActivationFunctionType
ALU = mybir.AluOpType
AX = mybir.AxisListType
NPBF = ml_dtypes.bfloat16
MAGIC = float(np.float32(1.5 * 2 ** 23))
NEG = float(np.float32(-1e38))

NCORE = 8
TLOC = 2048
NT = TLOC // 128
D = 1024
RG = [[0, 1], [2, 3], [4, 5], [6, 7]]

ENGINES = ("sync", "gpsimd", "vector", "scalar", "tensor")
QSEM = {"sync": "qs", "gpsimd": "qg", "scalar": "qa"}
ESEM = {"vector": "tv", "scalar": "ts", "tensor": "tt", "gpsimd": "tg"}


class Prog:

    def __init__(self, nc, stack):
        self.nc = nc
        self.ops = {e: [] for e in ENGINES}
        self.sems = {}
        self.cnt = {}
        for s in ("qs", "qg", "qa", "tv", "ts", "tt", "tg", "cc"):
            self.sems[s] = stack.enter_context(nc.semaphore(s))
            self.cnt[s] = 0

    def raw(self, eng, fn, waits=(), incs=()):
        w = tuple(x for x in waits if x is not None)
        self.ops[eng].append((fn, w, tuple(incs)))

    def _tick(self, sem, n):
        self.cnt[sem] += n
        return (sem, self.cnt[sem])

    def dma(self, q, out, in_, waits=(), transpose=False):
        sem = QSEM[q]
        t = self._tick(sem, 16)
        if transpose:
            fn = lambda e, out=out, in_=in_: e.dma_start(out=out, in_=in_, transpose=True)
        else:
            fn = lambda e, out=out, in_=in_: e.dma_start(out=out, in_=in_)
        self.raw(q, fn, waits, [(sem, 16)])
        return t

    def ve(self, fn, waits=(), eng="vector"):
        sem = ESEM[eng]
        t = self._tick(sem, 1)
        self.raw(eng, fn, waits, [(sem, 1)])
        return t

    def sve(self, fn, waits=(), eng="vector"):
        sem = ESEM[eng]
        w = tuple(waits) + ((sem, self.cnt[sem]),)
        return self.ve(fn, waits=w, eng=eng)

    def quiet(self, eng, fn, waits=()):
        self.raw(eng, fn, waits, ())

    def mmg(self, fns, waits=()):
        for i, fn in enumerate(fns):
            if i == 0 and i == len(fns) - 1:
                return self.ve(fn, waits=waits, eng="tensor")
            if i == 0:
                self.quiet("tensor", fn, waits=waits)
            elif i == len(fns) - 1:
                return self.ve(fn, eng="tensor")
            else:
                self.quiet("tensor", fn)

    def collective(self, kind, op, ins, outs, waits=()):
        t = self._tick("cc", 1)
        fn = lambda e, kind=kind, op=op, ins=ins, outs=outs: e.collective_compute(
            kind, op, RG, ins=ins, outs=outs)
        self.raw("gpsimd", fn, waits, [("cc", 1)])
        return t

    def now(self, sem):
        return (sem, self.cnt[sem])

    def barrier(self):
        marks = [self.now(s) for s in ("qs", "qg", "qa", "tv", "ts", "tt", "tg", "cc")]
        marks = tuple(m for m in marks if m[1] > 0)
        for e in ENGINES:
            self.raw(e, lambda eng: None, marks, ())

    def check(self):
        vals = {s: 0 for s in self.sems}
        ptr = {e: 0 for e in ENGINES}
        wi = {e: 0 for e in ENGINES}
        total = sum(len(v) for v in self.ops.values())
        done = 0
        progress = True
        while progress:
            progress = False
            for e in ENGINES:
                while ptr[e] < len(self.ops[e]):
                    fn, waits, incs = self.ops[e][ptr[e]]
                    ok = True
                    while wi[e] < len(waits):
                        s, v = waits[wi[e]]
                        if vals[s] >= v:
                            wi[e] += 1
                        else:
                            ok = False
                            break
                    if not ok:
                        break
                    for s, n in incs:
                        vals[s] += n
                    ptr[e] += 1
                    wi[e] = 0
                    done += 1
                    progress = True
        if done != total:
            stuck = {e: (ptr[e], len(self.ops[e]),
                         self.ops[e][ptr[e]][1][wi[e]:] if ptr[e] < len(self.ops[e]) else None)
                     for e in ENGINES}
            raise RuntimeError(f"deadlock: {done}/{total}; stuck={stuck}; sems={vals}")

    def _insert_checkpoints(self):
        qeng = {"qs": "sync", "qg": "gpsimd", "qa": "scalar", "cc": "gpsimd"}
        waited = {}
        for e in ENGINES:
            for fn, waits, incs in self.ops[e]:
                for s, v in waits:
                    if s in qeng:
                        waited.setdefault(s, set()).add(v)
        for s, vals in waited.items():
            e = qeng[s]
            newops = []
            cum = 0
            for fn, waits, incs in self.ops[e]:
                newops.append((fn, waits, incs))
                for ss, n in incs:
                    if ss == s:
                        cum += n
                        if cum in vals:
                            newops.append((lambda eng: None, ((s, cum),), ()))
            self.ops[e] = newops

    def _reduce_waits(self):
        for e in ENGINES:
            newops = []
            for fn, waits, incs in self.ops[e]:
                best = {}
                for s, v in waits:
                    if v > best.get(s, -1):
                        best[s] = v
                newops.append((fn, tuple(best.items()), incs))
            self.ops[e] = newops

    def emit(self, block):
        self._reduce_waits()
        self._insert_checkpoints()
        self.check()
        for ename in ENGINES:
            ops = self.ops[ename]
            sems = self.sems

            def body(eng, ops=ops, sems=sems):
                for fn, waits, incs in ops:
                    for s, v in waits:
                        eng.wait_ge(sems[s], v)
                    inst = fn(eng)
                    for s, n in incs:
                        inst.then_inc(sems[s], n)

            getattr(block, ename)(body)


class Arena:
    def __init__(self, tensor, words):
        self.t = tensor
        self.limit = words
        self.off = 0
        self.marks = []

    def push(self):
        self.marks.append(self.off)

    def pop(self):
        self.off = self.marks.pop()

    def alloc(self, parts, free, dtype):
        esz = 2 if dtype == BF16 else 4
        off_b = (self.off * 4 + 31) // 32 * 32
        words = (free * esz + 3) // 4
        self.off = off_b // 4 + words
        assert self.off <= self.limit, f"arena overflow {self.off} > {self.limit}"
        base = self.t[0:parts, off_b // 4: off_b // 4 + words]
        if dtype == BF16:
            return base.bitcast(BF16)[0:parts, 0:free]
        return base[0:parts, 0:free]


def _f32(x):
    return np.ascontiguousarray(np.asarray(x, np.float32))



def prep_host(inputs):
    sh = {}
    sh["l0w1t"] = _f32(inputs["L0_ffn_w1"]).T.copy()
    sh["l0w2t"] = _f32(inputs["L0_ffn_w2"]).T.copy()
    sh["l0b1"] = _f32(inputs["L0_ffn_b1"]).reshape(1, -1)
    sh["l0b2"] = _f32(inputs["L0_ffn_b2"]).reshape(1, -1)
    sh["a0wq"] = _f32(inputs["L0_attn_Wq"])
    sh["a0wk"] = _f32(inputs["L0_attn_Wk"])
    sh["a0wv"] = _f32(inputs["L0_attn_Wv"])
    sh["a0wo"] = _f32(inputs["L0_attn_Wo"])
    for l in range(4):
        for nm in ("UL", "VR", "UR", "VL"):
            sh[f"a0{nm}{l}t"] = _f32(inputs[f"L0_attn_{nm}{l}"]).T.copy()
    sh["l1wqkvt"] = _f32(inputs["L1_attn_W_qkv"]).T.copy()
    _b = _f32(inputs["L1_attn_b_qkv"])
    sh["l1bqkv"] = _b.reshape(24, 128).T.copy()
    sh["l1bv"] = _b[2048:].reshape(1, 1024).copy()
    for l in range(2):
        sh[f"f1wqkv{l}"] = _f32(inputs["L1_ffn_wqkv"][l])
        sh[f"f1wo{l}"] = _f32(inputs["L1_ffn_wo"][l])
        w1 = _f32(inputs["L1_ffn_w1"][l])
        b1 = _f32(inputs["L1_ffn_b1"][l])
        wp = np.zeros((1024, 5632), np.float32)
        bp = np.zeros((1, 5632), np.float32)
        wp[:, 0:5460:2] = w1[:, 0:2730]
        wp[:, 1:5461:2] = w1[:, 2730:5460]
        bp[0, 0:5460:2] = b1[0:2730]
        bp[0, 1:5461:2] = b1[2730:5460]
        sh[f"f1w1p{l}"] = wp
        sh[f"f1b1p{l}"] = bp
        w2 = _f32(inputs["L1_ffn_w2"][l])
        sh[f"f1w2{l}"] = np.concatenate([w2, np.zeros((86, 1024), np.float32)], 0)
        sh[f"f1b2{l}"] = _f32(inputs["L1_ffn_b2"][l]).reshape(1, -1)
        sh[f"f1lg{l}"] = _f32(inputs["L1_ffn_ln_g"][l]).reshape(8, 128).T.copy()
        sh[f"f1lb{l}"] = _f32(inputs["L1_ffn_ln_b"][l]).reshape(8, 128).T.copy()
        sh[f"f1qs{l}"] = _f32(inputs["L1_ffn_q_scale"][l]).reshape(1, -1)
        sh[f"f1ks{l}"] = _f32(inputs["L1_ffn_k_scale"][l]).reshape(1, -1)

    sh["identity"] = np.eye(128, dtype=np.float32)
    sh["onesrow"] = np.ones((1, 512), np.float32)
    sh["onescol"] = np.ones((128, 1), np.float32)
    idx = np.arange(128)
    sh["cmask"] = (idx[:, None] <= idx[None, :]).astype(np.float32)
    ga = np.zeros((64, 47 * 64), np.float32)
    gb = np.zeros((64, 47 * 64), np.float32)
    gb2 = np.zeros((64, 47 * 64), np.float32)
    for n in range(16):
        blk = slice(4 * n, 4 * n + 4)
        ga[blk, 64 * n + 4 * n: 64 * n + 4 * n + 4] = np.eye(4)
        gb[blk, 64 * n + 4 * n: 64 * n + 4 * n + 4] = np.eye(4)
        gb2[blk, 64 * n + 4 * n: 64 * n + 4 * n + 4] = np.eye(4)
    gb[:, 46 * 64:47 * 64] = np.eye(64)
    gb2[:, 46 * 64:47 * 64] = np.eye(64)
    sh["gat"] = ga
    sh["gbt"] = gb
    sh["gbt2"] = gb2

    n = 4096
    half = np.arange(0, 64, 2, dtype=np.float32)
    inv_freq = 1.0 / 10000.0 ** (half / 64.0)
    tt = np.arange(n, dtype=np.float32)
    freqs = tt[:, None] * inv_freq[None, :]
    freqs = np.concatenate([freqs, freqs], -1)
    base = (half + 0.4 * 64) / (1.4 * 64)
    power = (tt - n // 2) / 256.0
    scale = base[None, :] ** power[:, None]
    scale = np.concatenate([scale, scale], -1).astype(np.float32)
    cos = np.cos(freqs).astype(np.float32)
    sin = np.sin(freqs).astype(np.float32)

    def lmask(qo, first):
        i = 512 + qo * 128 + np.arange(128)
        j = np.arange(1024)
        m = (i[:, None] < j[None, :]) | (i[:, None] > j[None, :] + 512)
        if first:
            m = m | (j[None, :] < 512)
        return np.where(m, np.float32(NEG), np.float32(0.0)).astype(np.float32)

    per_core = []
    for c in range(NCORE):
        h = c % 2
        t0 = h * 2048
        pc = {}
        pc["role0"] = np.full((128, 1), 1.0 if h == 0 else 0.0, np.float32)
        pc["role1"] = np.full((128, 1), 1.0 if h == 1 else 0.0, np.float32)
        pos = np.clip(np.arange(t0 - 512, t0 + TLOC), 0, n - 1)
        pc["cq"] = (cos * scale)[pos]
        pc["sq"] = (sin * scale)[pos]
        pc["ck"] = (cos / scale)[pos]
        pc["sk"] = (sin / scale)[pos]
        mk, mkf = [], []
        for qo in range(4):
            mk.append(lmask(qo, False))
            mkf.append(lmask(qo, h == 0))
        pc["lmask"] = np.concatenate(mk, 0)
        pc["lmaskf"] = np.concatenate(mkf, 0)
        per_core.append(pc)
    return sh, per_core



def emit_l0ffn_prep(p, ar, hb, per):
    ps = per["psum"]
    sw = per["swslots"]
    onescol = per["onescol"]
    onesr = per["onesrow_f"]
    for src, dst, rows, cols, slot in (("l0w1t", "wq1t", 1024, 4096, 0),
                                       ("l0w2t", "wq2t", 4096, 1024, 1)):
        ntile = rows // 128
        ar.push()
        wtile = [ar.alloc(128, cols, F32) for _ in range(2)]
        qtile = ar.alloc(128, cols, BF16)
        acc = ar.alloc(128, 1, F32)
        part = ar.alloc(128, 1, F32)
        mval = ar.alloc(1, 1, F32)
        p.ve(lambda e, a=acc: e.memset(a, 0.0))
        red = []
        for i in range(ntile):
            buf = wtile[i % 2]
            w = (red[i - 2],) if i >= 2 else ()
            tk = p.dma("sync", buf, hb[src][i * 128:(i + 1) * 128, :], waits=w)
            r1 = p.ve(lambda e, b=buf, pa=part: e.reduce_sum(pa, b, axis=AX.X, apply_absolute_value=True),
                      waits=(tk,))
            r2 = p.sve(lambda e, a=acc, pa=part: e.tensor_add(a, a, pa))
            red.append(r2)
        mm = p.ve(lambda e, a=acc, oc=onescol, o=ps[0:1, 0:1]: e.matmul(o, a, oc, start=True, stop=True),
                  waits=(red[-1],), eng="tensor")
        v1 = p.sve(lambda e, m=mval, o=ps[0:1, 0:1], s=1.0 / (rows * cols):
                  e.tensor_scalar(m, o, s, 1e-5, op0=ALU.mult, op1=ALU.max), waits=(mm,))
        mm2 = p.ve(lambda e, o=ps[0:128, 4:5], r=onesr, m=mval:
                   e.matmul(o, r, m, start=True, stop=True), waits=(v1,), eng="tensor")
        swv = sw[:, slot:slot + 1]
        v2 = p.sve(lambda e, s=swv, o=ps[0:128, 4:5]: e.tensor_copy(s, o), waits=(mm2,))
        swr = sw[:, slot + 2:slot + 3]
        v3 = p.sve(lambda e, s=swr, sv=swv: e.reciprocal(s, sv), waits=(v2,))
        last_q = v3
        for i in range(ntile):
            buf = wtile[i % 2]
            tk = p.dma("sync", buf, hb[src][i * 128:(i + 1) * 128, :],
                       waits=(v3,) if i < 2 else (ld_guard[i - 2],))
            a1 = p.ve(lambda e, b=buf, s=swr: e.tensor_scalar(b, b, s, None, op0=ALU.mult),
                      waits=(tk, last_q))
            a2 = p.ve(lambda e, b=buf: e.tensor_scalar(b, b, -1.0, 1.0, op0=ALU.max, op1=ALU.min))
            a3 = p.ve(lambda e, q=qtile, b=buf: e.tensor_scalar(q, b, MAGIC, -MAGIC, op0=ALU.add, op1=ALU.add))
            st = p.dma("sync", hb[dst][i * 128:(i + 1) * 128, :], qtile, waits=(a3,))
            if i == 0:
                ld_guard = []
            ld_guard.append(a2)
            last_q = st
        p.raw("sync", lambda e: None, (last_q,), ())
        ar.pop()


def emit_l0ffn(p, ar, hb, per, xt):
    ps = per["psum"]
    sw = per["swslots"]
    onesr = per["onesrow_f"]
    ar.push()
    stat = ar.alloc(128, 16, F32)
    c1s = ar.alloc(128, 16, F32)
    c2s = ar.alloc(128, 16, F32)

    ar.push()
    W1 = [ar.alloc(128, 4096, BF16) for _ in range(8)]
    xq1T = [ar.alloc(128, 2048, BF16) for _ in range(8)]
    gbuf = ar.alloc(128, 4096, BF16)
    xnq = ar.alloc(128, 2048, F32)
    qt = ar.alloc(128, 2048, BF16)
    b1bc = ar.alloc(128, 4096, BF16)
    brow = ar.alloc(1, 512, F32)

    w1tk = [p.dma("sync", W1[dc], hb["wq1t"][dc * 128:(dc + 1) * 128, :]) for dc in range(8)]

    prev = None
    for j in range(8):
        lt = p.dma("gpsimd", brow, hb["l0b1"][0:1, j * 512:(j + 1) * 512], waits=(prev,) if prev else ())
        mmb = p.ve(lambda e, b=brow: e.matmul(ps[0:128, 0:512], onesr, b, start=True, stop=True),
                   eng="tensor", waits=(lt,) + ((prev,) if prev else ()))
        prev = p.ve(lambda e, j=j: e.tensor_copy(b1bc[:, j * 512:(j + 1) * 512], ps[0:128, 0:512]),
                    waits=(mmb,))
    bias1_done = prev

    xq1_done = []
    prevq = None
    for i in range(NT):
        x = xt[i]
        r1 = p.sve(lambda e, x=x, s=stat: e.reduce_sum(s[:, 0:1], x, axis=AX.X))
        r2 = p.sve(lambda e, x=x, s=stat, o=xnq[:, 0:1024]: e.activation(o, x, AF.Square, accum_out=s[:, 1:2]),
                  eng="scalar", waits=(prevq,) if prevq else ())
        v = p.sve(lambda e, s=stat: e.tensor_scalar(s[:, 0:1], s[:, 0:1], 1.0 / D, None, op0=ALU.mult), waits=(r1, r2))
        v = p.sve(lambda e, s=stat: e.tensor_scalar(s[:, 1:2], s[:, 1:2], 1.0 / D, None, op0=ALU.mult))
        v = p.sve(lambda e, s=stat: e.tensor_tensor(s[:, 2:3], s[:, 0:1], s[:, 0:1], op=ALU.mult))
        v = p.sve(lambda e, s=stat: e.tensor_sub(s[:, 2:3], s[:, 1:2], s[:, 2:3]))
        v = p.sve(lambda e, s=stat: e.tensor_scalar_add(s[:, 3:4], s[:, 2:3], 1e-5))
        v = p.sve(lambda e, s=stat: e.reciprocal(s[:, 3:4], s[:, 3:4]))
        sq = p.sve(lambda e, s=stat: e.activation(s[:, 4:5], s[:, 3:4], AF.Sqrt), eng="scalar", waits=(v,))
        v = p.sve(lambda e, s=stat: e.tensor_tensor(s[:, 5:6], s[:, 2:3], s[:, 3:4], op=ALU.mult), waits=(sq,))
        v = p.sve(lambda e, s=stat: e.tensor_scalar_add(s[:, 5:6], s[:, 5:6], 1e-5))
        v = p.sve(lambda e, s=stat: e.reciprocal(s[:, 5:6], s[:, 5:6]))
        sq = p.sve(lambda e, s=stat: e.activation(s[:, 5:6], s[:, 5:6], AF.Sqrt), eng="scalar", waits=(v,))
        v = p.sve(lambda e, s=stat: e.tensor_tensor(s[:, 6:7], s[:, 4:5], s[:, 5:6], op=ALU.mult), waits=(sq,))
        v = p.sve(lambda e, s=stat: e.tensor_tensor(s[:, 7:8], s[:, 0:1], s[:, 6:7], op=ALU.mult))
        v = p.sve(lambda e, s=stat: e.tensor_scalar(s[:, 7:8], s[:, 7:8], -1.0, None, op0=ALU.mult))
        a1 = p.sve(lambda e, x=x, s=stat: e.tensor_scalar(xnq[:, 0:1024], x, s[:, 6:7], s[:, 7:8],
                  op0=ALU.mult, op1=ALU.add), waits=(v,))
        a2 = p.sve(lambda e, s=stat: e.reduce_max(s[:, 8:9], xnq[:, 0:1024], axis=AX.X,
                  apply_absolute_value=True), waits=(a1,))
        v = p.sve(lambda e, s=stat: e.tensor_scalar(s[:, 8:9], s[:, 8:9], 1e-5, None, op0=ALU.max))
        v = p.sve(lambda e, s=stat: e.reciprocal(s[:, 9:10], s[:, 8:9]))
        v = p.sve(lambda e, s=stat: e.tensor_scalar(s[:, 9:10], s[:, 9:10], 127.0, None, op0=ALU.mult))
        v = p.sve(lambda e, s=stat: e.tensor_scalar(s[:, 10:11], s[:, 8:9], 1.0 / 127.0, None, op0=ALU.mult))
        v = p.sve(lambda e, s=stat, c=c1s, i=i: e.tensor_tensor(c[:, i:i + 1], s[:, 10:11], sw[:, 0:1], op=ALU.mult))
        q1 = p.sve(lambda e, s=stat: e.tensor_scalar(xnq[:, 0:1024], xnq[:, 0:1024], s[:, 9:10], -128.0,
                  op0=ALU.mult, op1=ALU.max), waits=(v,))
        q2 = p.ve(lambda e: e.tensor_scalar(xnq[:, 0:1024], xnq[:, 0:1024], 127.0, MAGIC, op0=ALU.min, op1=ALU.add))
        q3 = p.ve(lambda e, q=qt: e.tensor_scalar(q[:, 0:1024], xnq[:, 0:1024], -MAGIC, None, op0=ALU.add))
        st = p.dma("sync", hb["xq1"][i * 128:(i + 1) * 128, :], qt[:, 0:1024], waits=(q3,))
        xq1_done.append(st)
        prevq = st

    xq1t_tk = [p.dma("sync", xq1T[dc], hb["xq1"][:, dc * 128:(dc + 1) * 128],
                     waits=(xq1_done[-1],), transpose=True) for dc in range(8)]

    xq2_done = []
    prev_ev = (bias1_done,)
    for i in range(NT):
        fns = []
        for dc in range(8):
            lhs = xq1T[dc][:, i * 128:(i + 1) * 128]
            for fc in range(8):
                fns.append(lambda e, o=ps[0:128, fc * 512:(fc + 1) * 512], l=lhs,
                           r=W1[dc][:, fc * 512:(fc + 1) * 512], s=(dc == 0), t=(dc == 7):
                           e.matmul(o, l, r, start=s, stop=t))
        mmtk = p.mmg(fns, waits=tuple(xq1t_tk) + tuple(w1tk) + prev_ev)
        e1 = p.ve(lambda e, c=c1s, i=i: e.scalar_tensor_tensor(gbuf[:, 0:4096], ps[0:128, 0:4096],
                  c[:, i:i + 1], b1bc[:, 0:4096], op0=ALU.mult, op1=ALU.add), waits=(mmtk,))
        g1 = p.ve(lambda e: e.activation(gbuf[:, 0:4096], gbuf[:, 0:4096], AF.Gelu), eng="scalar", waits=(e1,))
        g2 = p.sve(lambda e, s=stat: e.activation(xnq[:, 0:2048].bitcast(BF16)[:, 0:2048], gbuf[:, 0:2048],
                  AF.Square, accum_out=s[:, 1:2]), eng="scalar", waits=(g1,))
        g3 = p.sve(lambda e, s=stat: e.activation(xnq[:, 0:2048].bitcast(BF16)[:, 0:2048], gbuf[:, 2048:4096],
                  AF.Square, accum_out=s[:, 2:3]), eng="scalar")
        r3 = p.sve(lambda e, s=stat: e.reduce_max(s[:, 8:9], gbuf[:, 0:4096], axis=AX.X,
                  apply_absolute_value=True), waits=(g1,))
        v = p.sve(lambda e, s=stat: e.tensor_add(s[:, 1:2], s[:, 1:2], s[:, 2:3]), waits=(g3, r3))
        v = p.sve(lambda e, s=stat: e.tensor_scalar(s[:, 1:2], s[:, 1:2], 1.0 / 4096.0, 1e-5, op0=ALU.mult, op1=ALU.add))
        v = p.sve(lambda e, s=stat: e.reciprocal(s[:, 5:6], s[:, 1:2]))
        sq = p.sve(lambda e, s=stat: e.activation(s[:, 5:6], s[:, 5:6], AF.Sqrt), eng="scalar", waits=(v,))
        v = p.sve(lambda e, s=stat: e.tensor_tensor(s[:, 8:9], s[:, 8:9], s[:, 5:6], op=ALU.mult), waits=(sq,))
        v = p.sve(lambda e, s=stat: e.tensor_scalar(s[:, 8:9], s[:, 8:9], 1e-5, None, op0=ALU.max))
        v = p.sve(lambda e, s=stat: e.reciprocal(s[:, 9:10], s[:, 8:9]))
        v = p.sve(lambda e, s=stat: e.tensor_scalar(s[:, 9:10], s[:, 9:10], 127.0, None, op0=ALU.mult))
        v = p.sve(lambda e, s=stat: e.tensor_tensor(s[:, 9:10], s[:, 9:10], s[:, 5:6], op=ALU.mult))
        v = p.sve(lambda e, s=stat: e.tensor_scalar(s[:, 10:11], s[:, 8:9], 1.0 / 127.0, None, op0=ALU.mult))
        v = p.sve(lambda e, s=stat, c=c2s, i=i: e.tensor_tensor(c[:, i:i + 1], s[:, 10:11], sw[:, 1:2], op=ALU.mult))
        sts = []
        for hf in range(2):
            sl = slice(hf * 2048, (hf + 1) * 2048)
            q1 = p.sve(lambda e, s=stat, sl=sl: e.tensor_scalar(xnq[:, 0:2048], gbuf[:, sl], s[:, 9:10], -128.0,
                      op0=ALU.mult, op1=ALU.max), waits=(v,) if hf == 0 else (sts[-1],))
            q2 = p.ve(lambda e: e.tensor_scalar(xnq[:, 0:2048], xnq[:, 0:2048], 127.0, MAGIC, op0=ALU.min, op1=ALU.add))
            q3 = p.ve(lambda e, q=qt: e.tensor_scalar(q[:, 0:2048], xnq[:, 0:2048], -MAGIC, None, op0=ALU.add))
            st = p.dma("sync", hb["xq2"][i * 128:(i + 1) * 128, hf * 2048:(hf + 1) * 2048], qt[:, 0:2048], waits=(q3,))
            sts.append(st)
        xq2_done.append(sts[-1])
        prev_ev = (q3, sts[-1])
        prev_ev = (e1, q3, sts[-1])
    ar.pop()

    ar.push()
    w2s = [ar.alloc(128, 1024, BF16) for _ in range(2)]
    x2s = [ar.alloc(128, 512, BF16) for _ in range(2)]
    b2bc = ar.alloc(128, 1024, BF16)
    xev = ar.alloc(128, 1024, F32)
    brow2 = ar.alloc(1, 512, F32)

    prev = None
    for j in range(2):
        lt = p.dma("gpsimd", brow2, hb["l0b2"][0:1, j * 512:(j + 1) * 512], waits=(prev,) if prev else (xq2_done[-1],))
        mmb = p.ve(lambda e, b=brow2: e.matmul(ps[0:128, 0:512], onesr, b, start=True, stop=True),
                   eng="tensor", waits=(lt,) + ((prev,) if prev else ()))
        prev = p.ve(lambda e, j=j: e.tensor_copy(b2bc[:, j * 512:(j + 1) * 512], ps[0:128, 0:512]), waits=(mmb,))
    bias2_done = prev

    last_ev = bias2_done
    for tg in range(4):
        tiles = list(range(tg * 4, tg * 4 + 4))
        mm_last = None
        for fc in range(32):
            xbuf = x2s[fc % 2]
            wbuf = w2s[fc % 2]
            guard = (mm_hist[fc - 2],) if fc >= 2 else ((last_ev,) if last_ev else ())
            wt = p.dma("scalar", wbuf, hb["wq2t"][fc * 128:(fc + 1) * 128, :], waits=guard)
            xt2 = p.dma("sync", xbuf, hb["xq2"][tg * 512:(tg + 1) * 512, fc * 128:(fc + 1) * 128],
                        waits=guard + ((xq2_done[-1],) if fc < 2 and tg == 0 else ()), transpose=True)
            fns = []
            for j in range(4):
                lhs = xbuf[:, j * 128:(j + 1) * 128]
                for vc in range(2):
                    fns.append(lambda e, o=ps[0:128, j * 1024 + vc * 512: j * 1024 + (vc + 1) * 512],
                               l=lhs, r=wbuf[:, vc * 512:(vc + 1) * 512], s=(fc == 0), t=(fc == 31):
                               e.matmul(o, l, r, start=s, stop=t))
            mm_last = p.mmg(fns, waits=(wt, xt2) + ((last_ev,) if fc == 0 and last_ev else ()))
            if fc == 0:
                mm_hist = []
            mm_hist.append(mm_last)
        evs = []
        for j, ti in enumerate(tiles):
            e1 = p.ve(lambda e, j=j, ti=ti: e.scalar_tensor_tensor(
                xev, ps[0:128, j * 1024:(j + 1) * 1024], c2s[:, ti:ti + 1],
                b2bc, op0=ALU.mult, op1=ALU.add), waits=(mm_last,))
            e2 = p.ve(lambda e, ti=ti, x=xt[ti]: e.scalar_tensor_tensor(
                x, xev, 0.5, x, op0=ALU.mult, op1=ALU.add))
            evs.append(e2)
        last_ev = evs[-1]
    ar.pop()
    ar.pop()
    return last_ev



A0_IN = [("a0wq", (1024, 64), F32), ("a0wk", (1024, 64), F32),
         ("a0wv", (1024, 64), F32), ("a0wo", (64, 1024), F32),
         ("gat", (64, 3008), F32), ("gbt", (64, 3008), F32), ("gbt2", (64, 3008), F32),
         ("cmask", (128, 128), F32), ("role0", (128, 1), F32), ("role1", (128, 1), F32)]
for _l, _bs, _r in ((0, 4, 4), (1, 8, 8), (2, 16, 8), (3, 32, 8)):
    for _nm in ("UL", "VR", "UR", "VL"):
        A0_IN.append((f"a0{_nm}{_l}t", (_r, _bs), F32))
STAGE_INPUTS = {
    "l0ffn": [("l0w1t", (1024, 4096), F32), ("l0w2t", (4096, 1024), F32),
              ("l0b1", (1, 4096), F32), ("l0b2", (1, 1024), F32)],
    "attn0": A0_IN,
    "l1attn": [("l1wqkvt", (1024, 3072), F32), ("l1bqkv", (128, 24), F32),
               ("l1bv", (1, 1024), F32)],
    "l1ffn": sum([[(f"f1wqkv{l}", (1024, 1536), F32), (f"f1wo{l}", (512, 1024), F32),
                   (f"f1w1p{l}", (1024, 5632), F32), (f"f1b1p{l}", (1, 5632), F32),
                   (f"f1w2{l}", (2816, 1024), F32), (f"f1b2{l}", (1, 1024), F32),
                   (f"f1lg{l}", (128, 8), F32), (f"f1lb{l}", (128, 8), F32),
                   (f"f1qs{l}", (1, 64), F32), (f"f1ks{l}", (1, 64), F32)] for l in range(2)], [])
              + [("cq", (2560, 64), F32), ("sq", (2560, 64), F32),
                 ("ck", (2560, 64), F32), ("sk", (2560, 64), F32),
                 ("lmask", (512, 1024), F32), ("lmaskf", (512, 1024), F32),
                 ("role1", (128, 1), F32)],
}
STAGE_SCRATCH = {
    "l0ffn": [("wq1t", (1024, 4096), BF16), ("wq2t", (4096, 1024), BF16),
              ("xq1", (TLOC, 1024), BF16), ("xq2", (TLOC, 4096), BF16)],
    "attn0": [("gac_h", (64, 3008), F32), ("gbc_h", (64, 3008), F32), ("gb2_h", (64, 3008), F32),
              ("lnx_h", (TLOC, 1024), BF16),
              ("ccin_a0", (64, 80), F32), ("ccout_a0", (64, 80), F32),
              ("d_qt", (64, 2048), BF16), ("d_kt", (64, 2048), BF16), ("d_vt", (64, 2048), BF16),
              ("d_ast", (128, 2944), BF16), ("d_bst", (128, 3008), BF16),
              ("d_qtl", (64, 6016), BF16), ("d_pt", (128, 6016), BF16),
              ("d_yall", (64, 2048), F32), ("d_den", (1, 2048), F32), ("d_mt", (64, 65), F32)],
    "l1attn": [("lnx_h", (TLOC, 1024), BF16),
               ("kin_h", (1024, TLOC), BF16), ("kout_h", (2048, TLOC), BF16),
               ("vin_h", (TLOC, 1024), BF16), ("vout_h", (2 * TLOC, 1024), BF16),
               ("p_h", (512, 4096), BF16)],
    "l1ffn": [("w1p_h0", (1024, 5632), BF16), ("w1p_h1", (1024, 5632), BF16),
              ("w2b_h0", (2816, 1024), BF16), ("w2b_h1", (2816, 1024), BF16),
              ("ucrow_h0", (2, 5632), F32), ("ucrow_h1", (2, 5632), F32),
              ("xsp_h", (TLOC, 1024), F32), ("xi_h", (TLOC, 1024), BF16),
              ("bndout_h", (1026, 1024), BF16), ("bnd1in_h", (1, 1024), BF16),
              ("bnd1out_h", (2, 1024), BF16), ("qh_h", (2560, 512), BF16),
              ("kh_h", (2560, 512), BF16), ("ph_h", (TLOC, 8192), BF16),
              ("hh_h", (TLOC, 2816), BF16), ("rv_h", (1, 2048), F32),
              ("dbgf1_h", (1160, 2048), F32)],
}
DEBUG_SCRATCH = False
COMMON_INPUTS = [("xres_in", (TLOC, 1024), F32),
                 ("onescol", (128, 1), F32), ("onesrow", (1, 512), F32),
                 ("identity", (128, 128), F32)]


def build_program(stages):
    nc = bass.Bass("TRN2", target_bir_lowering=False)
    hb = {}
    names = set()
    specs = list(COMMON_INPUTS)
    for st in stages:
        base = st.split(":")[0]
        for it in STAGE_INPUTS.get(base, ()):
            if it[0] not in names:
                names.add(it[0])
                specs.append(it)
    for name, shape, dt in specs:
        hb[name] = nc.dram_tensor(name, list(shape), dt, kind="ExternalInput").ap()
    hb["xres_out"] = nc.dram_tensor("xres_out", [TLOC, 1024], F32, kind="ExternalOutput").ap()
    snames = set()
    for st in stages:
        base = st.split(":")[0]
        for name, shape, dt in STAGE_SCRATCH.get(base, ()):
            if name not in snames:
                snames.add(name)
                kind = "ExternalOutput" if DEBUG_SCRATCH else "Internal"
                hb[name] = nc.dram_tensor(name, list(shape), dt, kind=kind).ap()

    with contextlib.ExitStack() as stack:
        p = Prog(nc, stack)
        xres = stack.enter_context(nc.sbuf_tensor("xres", [128, NT * 1024], F32))
        arena_t = stack.enter_context(nc.sbuf_tensor("arena", [128, 33000], F32))
        small = stack.enter_context(nc.sbuf_tensor("small", [128, 1024], F32))
        psum = stack.enter_context(nc.psum_tensor("psum", [128, 4096], F32))
        ar = Arena(arena_t, 33000)
        xt = [xres[:, i * 1024:(i + 1) * 1024] for i in range(NT)]

        per = {"psum": psum}
        per["onescol"] = small[:, 0:1]
        per["swslots"] = small[:, 8:16]
        onesrow_full = small[0:1, 16:528]
        per["onesrow_f"] = onesrow_full[0:1, 0:128]

        per["ident_f"] = small[:, 528:656]
        per["ident_b"] = small[:, 656:720].bitcast(BF16)[0:128, 0:128]
        p.dma("sync", per["onescol"], hb["onescol"][:, :])
        p.dma("sync", onesrow_full, hb["onesrow"][:, :])
        p.dma("sync", per["ident_f"], hb["identity"][:, :])
        p.dma("gpsimd", per["ident_b"], hb["identity"][:, :])
        for i in range(NT):
            p.dma("sync", xt[i], hb["xres_in"][i * 128:(i + 1) * 128, :])
        p.barrier()

        did_prep = set()
        for st in stages:
            if st == "l0ffn":
                if "l0" not in did_prep:
                    did_prep.add("l0")
                    emit_l0ffn_prep(p, ar, hb, per)
                    p.barrier()
                emit_l0ffn(p, ar, hb, per, xt)
                p.barrier()
            elif st == "attn0":
                if "a0" not in did_prep:
                    did_prep.add("a0")
                    emit_l0attn_prep(p, ar, hb, per)
                    p.barrier()
                emit_l0attn(p, ar, hb, per, xt)
                p.barrier()
            elif st == "l1attn":
                emit_l1attn(p, ar, hb, per, xt)
                p.barrier()
            elif st == "l1ffn":
                if "f1" not in did_prep:
                    did_prep.add("f1")
                    emit_l1ffn_prep(p, ar, hb, per)
                    p.barrier()
                emit_l1ffn(p, ar, hb, per, xt)
                p.barrier()

        for i in range(NT):
            p.dma("sync", hb["xres_out"][i * 128:(i + 1) * 128, :], xt[i])
        p.raw("sync", lambda e: None, (p.now("qs"),), ())
        with nc.Block() as block:
            p.emit(block)
    return nc, specs


def run_stages(stages, xcores, sh, percore):
    nc, specs = build_program(stages)
    in_maps = []
    for c in range(NCORE):
        m = {}
        for name, shape, dt in specs:
            if name == "xres_in":
                m[name] = np.ascontiguousarray(xcores[c])
            elif name in sh:
                m[name] = sh[name]
            else:
                m[name] = percore[c][name]
        in_maps.append(m)
    res = run_bass_kernel_spmd(nc, in_maps, core_ids=list(range(NCORE)))
    if DEBUG_SCRATCH:
        return res
    return np.stack([res.results[c]["xres_out"] for c in range(NCORE)], 0)


PIPE = [["l0ffn", "attn0", "l0ffn", "l1ffn", "l1attn", "l1ffn"]]


def kernel(**inputs):
    x = _f32(inputs["L0_attn_x"])
    xcores = np.zeros((8, TLOC, 1024), np.float32)
    for b in range(4):
        xcores[2 * b] = x[b, :2048]
        xcores[2 * b + 1] = x[b, 2048:]
    sh, percore = prep_host(inputs)
    for stages in PIPE:
        xcores = run_stages(stages, xcores, sh, percore)
    out = np.zeros_like(x)
    for b in range(4):
        out[b, :2048] = xcores[2 * b]
        out[b, 2048:] = xcores[2 * b + 1]
    return out



A0_TRUNC = 99
A0_DEBUG = False
F1_TRUNC = 99
F1_WO = 1
F1_REFS = {}
A0_REFS = {}
A0_BS = [4, 8, 16, 32]
A0_NN = [8, 4, 2, 1]
A0_TOPS = [(l, m) for l in range(4) for m in range(A0_NN[l])]
A0_NCOMP = 47


def emit_l0attn_prep(p, ar, hb, per):
    ps = per["psum"]
    ar.push()
    tmpl = ar.alloc(64, 3008, F32)
    stg = ar.alloc(32, 32, F32)
    uls = {}
    tks = []
    for l in range(4):
        r, bs = [4, 8, 8, 8][l], A0_BS[l]
        for nm in ("UL", "VR", "UR", "VL"):
            buf = ar.alloc(r, bs, BF16)
            uls[(nm, l)] = buf
            tks.append(p.dma("gpsimd", buf, hb[f"a0{nm}{l}t"][:, :]))
    tdone = []
    for src, dst in (("gat", "gac_h"), ("gbt", "gbc_h"), ("gbt2", "gb2_h")):
        t1 = p.dma("sync", tmpl, hb[src][:, :], waits=tuple(tdone[-1:]))
        t2 = p.dma("sync", hb[dst][:, :], tmpl, waits=(t1,))
        tdone.append(t2)
    last = tdone[-1]
    for l in range(4):
        r, bs = [4, 8, 8, 8][l], A0_BS[l]
        prods = (("PUL", "UL", "UL"), ("PVR", "VR", "VR"), ("ABOT", "VR", "UR"),
                 ("BBOT", "UL", "VL"), ("B2BOT", "VL", "UL"))
        for pn, aop, bop in prods:
            mm = p.ve(lambda e, a=uls[(aop, l)], b=uls[(bop, l)], o=ps[0:bs, 0:bs]:
                      e.matmul(o, a, b, start=True, stop=True),
                      eng="tensor", waits=tuple(tks) + (last,))
            cp = p.ve(lambda e, o=ps[0:bs, 0:bs], s=stg[0:bs, 0:bs]: e.tensor_copy(s, o), waits=(mm,))
            wr = []
            for ti, (ll, m) in enumerate(A0_TOPS):
                if ll != l:
                    continue
                ctop = 16 + ti
                cbot = 31 + ti
                L0r = 2 * bs * m
                R0r = 2 * bs * m + bs
                if pn == "PUL":
                    tgt = [("gac_h", L0r, ctop * 64 + L0r)]
                elif pn == "PVR":
                    tgt = [("gbc_h", R0r, ctop * 64 + R0r), ("gb2_h", R0r, ctop * 64 + R0r)]
                elif pn == "ABOT":
                    tgt = [("gac_h", R0r, cbot * 64 + R0r)]
                elif pn == "BBOT":
                    tgt = [("gbc_h", L0r, cbot * 64 + L0r)]
                else:
                    tgt = [("gb2_h", L0r, cbot * 64 + L0r)]
                for dstn, r0, c0 in tgt:
                    wr.append(p.dma("sync", hb[dstn][r0:r0 + bs, c0:c0 + bs], stg[0:bs, 0:bs], waits=(cp,)))
            last = wr[-1]
    p.raw("sync", lambda e: None, (last,), ())
    ar.pop()


def emit_l0attn(p, ar, hb, per, xt):
    ps = per["psum"]
    onesr = per["onesrow_f"]
    ar.push()
    GA = ar.alloc(64, 3008, BF16)
    GB = ar.alloc(64, 3008, BF16)
    GB2 = ar.alloc(64, 3008, BF16)
    wq_sb = ar.alloc(128, 512, BF16)
    wk_sb = ar.alloc(128, 512, BF16)
    wv_sb = ar.alloc(128, 512, BF16)
    wo_sb = ar.alloc(64, 1024, BF16)
    lnxT = [ar.alloc(128, 2048, BF16) for _ in range(8)]
    QT = ar.alloc(64, 2048, BF16)
    KT = ar.alloc(64, 2048, BF16)
    VT = ar.alloc(64, 2048, BF16)
    Astk = ar.alloc(128, 2944, BF16)
    Bstk = ar.alloc(128, 3008, BF16)
    Qtl = ar.alloc(64, 6016, BF16)
    Pt = ar.alloc(128, 6016, BF16)
    YD = ar.alloc(128, 2048, F32)
    Yall = YD[0:64, :]
    den_row = YD[64:65, :]
    ybf = ar.alloc(64, 2048, BF16)
    MT = ar.alloc(64, 65, F32)
    MTb = ar.alloc(64, 65, BF16)
    Mrecv = ar.alloc(64, 80, F32)
    Mrb = ar.alloc(64, 80, BF16)
    exch = ar.alloc(64, 80, F32)
    lnt = ar.alloc(128, 1024, BF16)
    stat = ar.alloc(128, 16, F32)
    onescol_b = ar.alloc(128, 1, BF16)
    cmask_sb = ar.alloc(128, 128, F32)
    role0v = ar.alloc(128, 1, F32)
    role1v = ar.alloc(128, 1, F32)
    tmpe = ar.alloc(64, 512, F32)
    tmpe2 = ar.alloc(64, 512, F32)
    ztmp = ar.alloc(64, 1, F32)

    global A0_REFS
    A0_REFS = dict(GA=GA, GB=GB, GB2=GB2, QT=QT, KT=KT, VT=VT, Astk=Astk, Bstk=Bstk,
                   Qtl=Qtl, Pt=Pt, Yall=Yall, wq=wq_sb, wk=wk_sb, wv=wv_sb)
    ld0 = []
    ld0.append(p.dma("gpsimd", GA, hb["gac_h"][:, :]))
    ld0.append(p.dma("gpsimd", GB, hb["gbc_h"][:, :]))
    ld0.append(p.dma("gpsimd", GB2, hb["gb2_h"][:, :]))
    for wsb, nm in ((wq_sb, "a0wq"), (wk_sb, "a0wk"), (wv_sb, "a0wv")):
        ld0.append(p.dma("gpsimd", wsb.rearrange("p (c f) -> p c f", f=64),
                         hb[nm].rearrange("(c p) f -> p c f", p=128)))
    ld0.append(p.dma("gpsimd", wo_sb, hb["a0wo"][:, :]))
    ld0.append(p.dma("sync", cmask_sb, hb["cmask"][:, :]))
    ld0.append(p.dma("sync", role0v, hb["role0"][:, :]))
    ld0.append(p.dma("sync", role1v, hb["role1"][:, :]))
    ld0.append(p.ve(lambda e: e.memset(onescol_b, 1.0)))
    ld0.append(p.ve(lambda e: e.memset(MT[:, 0:65], 0.0)))
    ld0.append(p.ve(lambda e: e.memset(MTb[:, 0:65], 0.0)))
    ld0.append(p.ve(lambda e: e.memset(exch[:, 0:80], 0.0)))
    ld0.append(p.ve(lambda e: e.memset(YD[:, 0:2048], 0.0)))
    ld0 = tuple(ld0)

    lnst = []
    sqjunk = Pt.bitcast(F32)
    prevq = None
    for i in range(NT):
        x = xt[i]
        r1 = p.ve(lambda e, x=x, s=stat: e.reduce_sum(s[:, 0:1], x, axis=AX.X))
        r2 = p.ve(lambda e, x=x, s=stat, o=sqjunk: e.activation(o[:, 0:1024], x, AF.Square,
                  accum_out=s[:, 1:2]), eng="scalar", waits=((prevq,) if prevq else ()))
        v = p.sve(lambda e, s=stat: e.tensor_scalar(s[:, 0:1], s[:, 0:1], 1.0 / D, None, op0=ALU.mult), waits=(r1, r2))
        v = p.sve(lambda e, s=stat: e.tensor_scalar(s[:, 1:2], s[:, 1:2], 1.0 / D, None, op0=ALU.mult))
        v = p.sve(lambda e, s=stat: e.tensor_tensor(s[:, 2:3], s[:, 0:1], s[:, 0:1], op=ALU.mult))
        v = p.sve(lambda e, s=stat: e.tensor_sub(s[:, 2:3], s[:, 1:2], s[:, 2:3]))
        v = p.sve(lambda e, s=stat: e.tensor_scalar_add(s[:, 3:4], s[:, 2:3], 1e-5))
        v = p.sve(lambda e, s=stat: e.reciprocal(s[:, 3:4], s[:, 3:4]))
        v = p.sve(lambda e, s=stat: e.activation(s[:, 4:5], s[:, 3:4], AF.Sqrt), eng="scalar", waits=(v,))
        v = p.sve(lambda e, s=stat: e.tensor_tensor(s[:, 5:6], s[:, 0:1], s[:, 4:5], op=ALU.mult), waits=(v,))
        v = p.sve(lambda e, s=stat: e.tensor_scalar(s[:, 5:6], s[:, 5:6], -1.0, None, op0=ALU.mult))
        q1 = p.sve(lambda e, x=x, s=stat: e.tensor_scalar(lnt[:, 0:1024], x, s[:, 4:5], s[:, 5:6],
                   op0=ALU.mult, op1=ALU.add), waits=(v,))
        st = p.dma("sync", hb["lnx_h"][i * 128:(i + 1) * 128, :], lnt[:, 0:1024], waits=(q1,))
        lnst.append(st)
        prevq = st
    if A0_TRUNC < 2:
        ar.pop()
        return lnst[-1]
    lnT_tk = tuple(p.dma("sync", lnxT[dc], hb["lnx_h"][:, dc * 128:(dc + 1) * 128],
                         waits=(lnst[-1],), transpose=True) for dc in range(8))

    projw = ld0 + lnT_tk
    prev_ev = None
    for wsb, OUT, iselu in ((wq_sb, QT, True), (wk_sb, KT, True), (wv_sb, VT, False)):
        for tc in range(4):
            cols = slice(tc * 512, (tc + 1) * 512)
            fns = []
            for dc in range(8):
                fns.append(lambda e, l=wsb[:, dc * 64:(dc + 1) * 64], r=lnxT[dc][:, cols],
                           s=(dc == 0), t=(dc == 7): e.matmul(ps[0:64, 0:512], l, r, start=s, stop=t))
            mm = p.mmg(fns, waits=projw + ((prev_ev,) if prev_ev else ()))
            if iselu:
                e1 = p.ve(lambda e: e.tensor_scalar_min(tmpe, ps[0:64, 0:512], 0.0), waits=(mm,))
                e2 = p.ve(lambda e: e.activation(tmpe2, tmpe, AF.Exp), eng="scalar", waits=(e1,))
                e3 = p.ve(lambda e: e.tensor_scalar_max(tmpe, ps[0:64, 0:512], 0.0), waits=(e2,))
                e4 = p.ve(lambda e, O=OUT, cols=cols: e.tensor_tensor(O[:, cols], tmpe, tmpe2, op=ALU.add),
                          waits=(e3,))
                prev_ev = e4
            else:
                prev_ev = p.ve(lambda e, O=OUT, cols=cols: e.tensor_copy(O[:, cols], ps[0:64, 0:512]),
                               waits=(mm,))

    if A0_TRUNC < 3:
        ar.pop()
        return prev_ev
    if A0_DEBUG:
        p.dma("sync", hb["d_qt"][:, :], QT[:, 0:2048], waits=(prev_ev,))
        p.dma("sync", hb["d_kt"][:, :], KT[:, 0:2048], waits=(prev_ev,))
        p.dma("sync", hb["d_vt"][:, :], VT[:, 0:2048], waits=(prev_ev,))
    proj_done = (prev_ev,)
    ev_guard = proj_done
    for ci in range(NT):
        sl = slice(ci * 128, (ci + 1) * 128)
        fns = [lambda e, c0=c0, c1=c1, sl=sl: e.matmul(ps[0:128, c0:c1], KT[:, sl], GA[:, c0:c1], start=True, stop=True)
               for c0, c1 in ((0, 512), (512, 1024), (1024, 1536), (1536, 1984))]
        fns += [lambda e, c0=c0, c1=c1, sl=sl: e.matmul(ps[0:128, c0:c1], VT[:, sl], GA[:, c0:c1], start=True, stop=True)
                for c0, c1 in ((1984, 2048), (2048, 2560), (2560, 2944))]
        mmA = p.mmg(fns, waits=ev_guard)
        evA = p.ve(lambda e: e.copy(Astk[:, 0:2944], ps[0:128, 0:2944]), eng="scalar", waits=(mmA,))
        fns = [lambda e, c0=c0, c1=c1, sl=sl: e.matmul(ps[0:128, c0:c1], VT[:, sl], GB[:, c0:c1], start=True, stop=True)
               for c0, c1 in ((0, 512), (512, 1024), (1024, 1536), (1536, 1984))]
        fns += [lambda e, c0=c0, c1=c1, sl=sl: e.matmul(ps[0:128, c0:c1], KT[:, sl], GB[:, c0:c1], start=True, stop=True)
                for c0, c1 in ((1984, 2048), (2048, 2560), (2560, 3008))]
        mmB = p.mmg(fns, waits=(evA,))
        evB = p.ve(lambda e: e.copy(Bstk[:, 0:3008], ps[0:128, 0:3008]), eng="scalar", waits=(mmB,))
        for r0, r1 in ((0, 24), (24, 47)):
            fns = [lambda e, c=c, r0=r0, sl=sl: e.matmul(ps[0:64, (c - r0) * 128:(c - r0 + 1) * 128],
                   GB2[:, c * 64:(c + 1) * 64], QT[:, sl], start=True, stop=True)
                   for c in range(r0, r1)]
            mmQ = p.mmg(fns, waits=(evB,) if r0 == 0 else (evQ,))
            evQ = p.ve(lambda e, r0=r0, r1=r1: e.copy(Qtl[:, r0 * 128:r1 * 128],
                       ps[0:64, 0:(r1 - r0) * 128]), eng="scalar", waits=(mmQ,))
        s1ev = []
        for src, cr0, cr1 in ((VT, 0, 31), (KT, 31, 47)):
            ncols = (cr1 - cr0) * 128
            fns = []
            for c0 in range(0, ncols, 512):
                c1 = min(c0 + 512, ncols)
                fns.append(lambda e, c0=c0, c1=c1, src=src, cr0=cr0, sl=sl: e.matmul(
                    ps[0:128, c0:c1], src[:, sl], Qtl[:, cr0 * 128 + c0:cr0 * 128 + c1], start=True, stop=True))
            mmS = p.mmg(fns, waits=(evQ,) if cr0 == 0 else (s1ev[-1],))
            nc_ = cr1 - cr0
            ev = p.ve(lambda e, nc_=nc_, cr0=cr0: e.scalar_tensor_tensor(
                Pt[:, cr0 * 128:(cr0 + nc_) * 128].rearrange("p (c t) -> p c t", t=128),
                ps[0:128, 0:nc_ * 128].rearrange("p (c t) -> p c t", t=128),
                1.0,
                cmask_sb.unsqueeze(1).broadcast_to((128, nc_, 128)),
                op0=ALU.mult, op1=ALU.mult), waits=(mmS,))
            s1ev.append(ev)
        fns = [lambda e, c=c: e.matmul(ps[0:64, 3200:3328], Astk[:, c * 64:(c + 1) * 64],
               Pt[:, c * 128:(c + 1) * 128], start=(c == 0), stop=False) for c in range(46)]
        fns.append(lambda e, sl=sl: e.matmul(ps[0:64, 3200:3328], MTb[:, 0:64], QT[:, sl], start=False, stop=True))
        fns.append(lambda e: e.matmul(ps[0:1, 3328:3456], onescol_b, Pt[:, 46 * 128:47 * 128],
                   start=True, stop=False))
        fns.append(lambda e, sl=sl: e.matmul(ps[0:1, 3328:3456], MTb[:, 64:65], QT[:, sl], start=False, stop=True))
        dbg_tk = ()
        if A0_DEBUG and ci == 0:
            dbg_tk = (p.dma("sync", hb["d_ast"][:, :], Astk[:, 0:2944], waits=(evA,)),
                      p.dma("sync", hb["d_bst"][:, :], Bstk[:, 0:3008], waits=(evB,)),
                      p.dma("sync", hb["d_qtl"][:, :], Qtl[:, 0:6016], waits=(evQ,)),
                      p.dma("sync", hb["d_pt"][:, :], Pt[:, 0:6016], waits=tuple(s1ev)))
        mmY = p.mmg(fns, waits=tuple(s1ev))
        evY = p.ve(lambda e, sl=sl: e.tensor_copy(Yall[:, sl], ps[0:64, 3200:3328]), waits=(mmY,))
        evD = p.ve(lambda e, sl=sl: e.tensor_copy(den_row[0:1, sl], ps[0:1, 3328:3456]), waits=(mmY,))
        fns = [lambda e, c=c: e.matmul(ps[0:64, 3456:3520], Bstk[:, c * 64:(c + 1) * 64],
               Astk[:, c * 64:(c + 1) * 64], start=(c == 0), stop=(c == 45)) for c in range(46)]
        mmM = p.mmg(fns, waits=(evY, evD))
        uM = p.ve(lambda e: e.tensor_add(MT[:, 0:64], MT[:, 0:64], ps[0:64, 3456:3520]), waits=(mmM,))
        uz1 = p.ve(lambda e, sl=sl: e.reduce_sum(ztmp, KT[:, sl], axis=AX.X))
        uz2 = p.sve(lambda e: e.tensor_add(MT[:, 64:65], MT[:, 64:65], ztmp), waits=(uz1,))
        ub = p.sve(lambda e: e.tensor_copy(MTb[:, 0:65], MT[:, 0:65]), waits=(uM, uz2))
        ev_guard = (ub, s1ev[-1], evA, evB) + dbg_tk
    if A0_DEBUG:
        p.dma("sync", hb["d_yall"][:, :], Yall[:, 0:2048], waits=ev_guard)
        p.dma("sync", hb["d_den"][:, :], den_row[0:1, 0:2048], waits=ev_guard)
        p.dma("sync", hb["d_mt"][:, :], MT[:, 0:65], waits=ev_guard)
    if A0_TRUNC < 4:
        ar.pop()
        return ev_guard[0]
    pk = p.sve(lambda e: e.tensor_scalar(exch[:, 0:65], MT[:, 0:65], role0v[0:64, 0:1], None, op0=ALU.mult),
               waits=ev_guard)
    sx = p.dma("sync", hb["ccin_a0"][:, :], exch[:, 0:80], waits=(pk,))
    cc = p.collective("AllReduce", ALU.add, [hb["ccin_a0"][:, :].opt()], [hb["ccout_a0"][:, :].opt()],
                      waits=(sx,))
    lx = p.dma("gpsimd", Mrecv[:, 0:80], hb["ccout_a0"][:, :], waits=(cc,))
    rb = p.ve(lambda e: e.tensor_scalar(Mrb[:, 0:65], Mrecv[:, 0:65], role1v[0:64, 0:1], None, op0=ALU.mult),
              waits=(lx,))
    if A0_TRUNC < 5:
        ar.pop()
        return rb
    corr = []
    for qc in range(4):
        cols = slice(qc * 512, (qc + 1) * 512)
        mmC = p.ve(lambda e, cols=cols: e.matmul(ps[0:65, 0:512], Mrb[:, 0:65], QT[:, cols],
                   start=True, stop=True), eng="tensor", waits=(rb,) + tuple(corr[-2:]))
        c1 = p.ve(lambda e, cols=cols: e.tensor_add(Yall[:, cols], Yall[:, cols], ps[0:64, 0:512]), waits=(mmC,))
        c2 = p.ve(lambda e, cols=cols: e.tensor_add(den_row[0:1, cols], den_row[0:1, cols], ps[64:65, 0:512]),
                  waits=(mmC,))
        corr.append(c2)
        corr.append(c1)
    rrow = Qtl.bitcast(F32)[0:1, 0:2048]
    v = p.sve(lambda e: e.tensor_scalar(rrow[0:1, 0:2048], den_row[0:1, 0:2048], 1e-6, None, op0=ALU.max),
              waits=tuple(corr))
    v = p.sve(lambda e: e.reciprocal(rrow[0:1, 0:2048], rrow[0:1, 0:2048]))
    nrm = []
    for qc in range(4):
        cols = slice(qc * 512, (qc + 1) * 512)
        mmB2 = p.ve(lambda e, cols=cols: e.matmul(ps[0:64, 512:1024], onesr[0:1, 0:64], rrow[0:1, cols],
                    start=True, stop=True), eng="tensor", waits=(v,) + ((nrm[-1],) if nrm else ()))
        n1 = p.ve(lambda e, cols=cols: e.tensor_tensor(ybf[:, cols], Yall[:, cols], ps[0:64, 512:1024],
                  op=ALU.mult), waits=(mmB2,))
        nrm.append(n1)
    last = nrm[-1]
    for ci in range(NT):
        sl = slice(ci * 128, (ci + 1) * 128)
        fns = [lambda e, sl=sl, vc=vc: e.matmul(ps[0:128, 1536 + vc * 512:1536 + (vc + 1) * 512],
               ybf[:, sl], wo_sb[:, vc * 512:(vc + 1) * 512], start=True, stop=True) for vc in range(2)]
        mmW = p.mmg(fns, waits=(last,) + tuple(nrm))
        e1 = p.ve(lambda e, x=xt[ci]: e.tensor_add(x, x, ps[0:128, 1536:2560]), waits=(mmW,))
        last = e1
    ar.pop()
    return last



def emit_ln_tiles(p, ar, hb, per, xt, dst, lnt, stat, sqj):
    prevq = None
    lnst = []
    for i in range(NT):
        x = xt[i]
        r1 = p.ve(lambda e, x=x, s=stat: e.reduce_sum(s[:, 0:1], x, axis=AX.X))
        r2 = p.ve(lambda e, x=x, s=stat, o=sqj: e.activation(o[:, 0:1024], x, AF.Square,
                  accum_out=s[:, 1:2]), eng="scalar", waits=((prevq,) if prevq else ()))
        v = p.sve(lambda e, s=stat: e.tensor_scalar(s[:, 0:1], s[:, 0:1], 1.0 / D, None, op0=ALU.mult), waits=(r1, r2))
        v = p.sve(lambda e, s=stat: e.tensor_scalar(s[:, 1:2], s[:, 1:2], 1.0 / D, None, op0=ALU.mult))
        v = p.sve(lambda e, s=stat: e.tensor_tensor(s[:, 2:3], s[:, 0:1], s[:, 0:1], op=ALU.mult))
        v = p.sve(lambda e, s=stat: e.tensor_sub(s[:, 2:3], s[:, 1:2], s[:, 2:3]))
        v = p.sve(lambda e, s=stat: e.tensor_scalar_add(s[:, 3:4], s[:, 2:3], 1e-5))
        v = p.sve(lambda e, s=stat: e.reciprocal(s[:, 3:4], s[:, 3:4]))
        v = p.sve(lambda e, s=stat: e.activation(s[:, 4:5], s[:, 3:4], AF.Sqrt), eng="scalar", waits=(v,))
        v = p.sve(lambda e, s=stat: e.tensor_tensor(s[:, 5:6], s[:, 0:1], s[:, 4:5], op=ALU.mult), waits=(v,))
        v = p.sve(lambda e, s=stat: e.tensor_scalar(s[:, 5:6], s[:, 5:6], -1.0, None, op0=ALU.mult))
        q1 = p.sve(lambda e, x=x, s=stat: e.tensor_scalar(lnt[:, 0:1024], x, s[:, 4:5], s[:, 5:6],
                   op0=ALU.mult, op1=ALU.add), waits=(v,))
        st = p.dma("sync", dst[i * 128:(i + 1) * 128, :], lnt[:, 0:1024], waits=(q1,))
        lnst.append(st)
        prevq = st
    return lnst


def emit_l1attn(p, ar, hb, per, xt):
    ps = per["psum"]
    ar.push()
    QTf = [ar.alloc(128, 2048, BF16) for _ in range(8)]
    bq_sb = ar.alloc(128, 24, F32)
    lnt = ar.alloc(128, 1024, BF16)
    stat = ar.alloc(128, 16, F32)
    rden = ar.alloc(128, 8, F32)
    negm = ar.alloc(128, 8, F32)
    onesrow_b = ar.alloc(1, 128, BF16)
    ob = p.ve(lambda e: e.memset(onesrow_b[0:1, 0:128], 1.0))
    ldb = p.dma("sync", bq_sb, hb["l1bqkv"][:, :])

    ar.push()
    sqj = ar.alloc(128, 1024, F32)
    lnst = emit_ln_tiles(p, ar, hb, per, xt, hb["lnx_h"], lnt, stat, sqj)
    ar.pop()

    ar.push()
    lnxT = [ar.alloc(128, 2048, BF16) for _ in range(8)]
    wqk = [ar.alloc(128, 1024, BF16) for _ in range(2)]
    wv = [ar.alloc(128, 4096, BF16) for _ in range(2)]
    vbuf = ar.alloc(128, 1024, BF16)
    bvrow = ar.alloc(1, 1024, BF16)
    lnT_tk = tuple(p.dma("sync", lnxT[dc], hb["lnx_h"][:, dc * 128:(dc + 1) * 128],
                         waits=(lnst[-1],), transpose=True) for dc in range(8))
    ldbv = p.dma("gpsimd", bvrow, hb["l1bv"][:, :])
    prev_ev = None
    kst = []
    for et in range(16):
        wbuf = wqk[et % 2]
        wt = p.dma("gpsimd", wbuf.rearrange("p (c e) -> p c e", e=128),
                   hb["l1wqkvt"][:, et * 128:(et + 1) * 128].rearrange("(c p) e -> p c e", p=128),
                   waits=(prev_ev,) if prev_ev else ())
        for tc in range(4):
            cols = slice(tc * 512, (tc + 1) * 512)
            fns = [lambda e, dc=dc, cols=cols, wbuf=wbuf: e.matmul(
                ps[0:128, 0:512], wbuf[:, dc * 128:(dc + 1) * 128], lnxT[dc][:, cols],
                start=(dc == 0), stop=(dc == 7)) for dc in range(8)]
            mm = p.mmg(fns, waits=lnT_tk + (wt, ldb) + ((prev_ev,) if prev_ev else ()))
            if et < 8:
                prev_ev = p.ve(lambda e, et=et, cols=cols: e.tensor_scalar(
                    QTf[et][:, cols], ps[0:128, 0:512], bq_sb[:, et:et + 1], None, op0=ALU.add),
                    waits=(mm,))
            else:
                prev_ev = p.ve(lambda e, et=et: e.tensor_scalar(
                    vbuf[:, 0:512], ps[0:128, 0:512], bq_sb[:, et:et + 1], None, op0=ALU.add),
                    waits=(mm,))
                kst.append(p.dma("sync", hb["kin_h"][(et - 8) * 128:(et - 7) * 128, tc * 512:(tc + 1) * 512],
                                 vbuf[:, 0:512], waits=(prev_ev,)))
                p.raw("vector", lambda e: None, (kst[-1],), ())
    vst = []
    wvt0 = p.dma("gpsimd", wv[0].rearrange("p (c e) -> p c e", e=512),
                 hb["l1wqkvt"][:, 2048:2560].rearrange("(c p) e -> p c e", p=128), waits=(prev_ev,))
    wvt1 = p.dma("gpsimd", wv[1].rearrange("p (c e) -> p c e", e=512),
                 hb["l1wqkvt"][:, 2560:3072].rearrange("(c p) e -> p c e", p=128))
    for i in range(NT):
        fns = []
        for dc in range(8):
            for vc in range(2):
                wb = wv[vc]
                fns.append(lambda e, dc=dc, vc=vc, i=i, wb=wb: e.matmul(
                    ps[0:128, 1024 + vc * 512:1024 + (vc + 1) * 512],
                    lnxT[dc][:, i * 128:(i + 1) * 128], wb[:, dc * 512:(dc + 1) * 512],
                    start=(dc == 0), stop=False))
        fns.append(lambda e: e.matmul(ps[0:128, 1024:1536], onesrow_b[0:1, 0:128], bvrow[0:1, 0:512],
                   start=False, stop=True))
        fns.append(lambda e: e.matmul(ps[0:128, 1536:2048], onesrow_b[0:1, 0:128], bvrow[0:1, 512:1024],
                   start=False, stop=True))
        mm = p.mmg(fns, waits=(wvt0, wvt1, ldbv, ob) + ((vst[-1],) if vst else (prev_ev,)))
        ev = p.ve(lambda e: e.tensor_copy(vbuf[:, 0:1024], ps[0:128, 1024:2048]), waits=(mm,))
        vst.append(p.dma("sync", hb["vin_h"][i * 128:(i + 1) * 128, :], vbuf[:, 0:1024], waits=(ev,)))
        p.raw("vector", lambda e: None, (vst[-1],), ())
    ar.pop()

    cc1 = p.collective("AllGather", ALU.bypass, [hb["kin_h"][:, :].opt()], [hb["kout_h"][:, :].opt()],
                       waits=(kst[-1],))
    cc2 = p.collective("AllGather", ALU.bypass, [hb["vin_h"][:, :].opt()], [hb["vout_h"][:, :].opt()],
                       waits=(vst[-1],))

    ar.push()
    scr = [ar.alloc(128, 4096, BF16) for _ in range(4)]
    pbuf = ar.alloc(128, 4096, BF16)
    ktc = [ar.alloc(128, 4096, BF16) for _ in range(2)]
    vtc = [ar.alloc(128, 1024, BF16) for _ in range(2)]
    ptt = [ar.alloc(128, 512, BF16) for _ in range(2)]
    last_ev = (cc1, cc2)
    for phq in range(4):
        qts = [phq * 4 + j for j in range(4)]
        sc_prev = [None] * 4
        kld_hist = []
        for kc in range(8):
            rank = (kc * 512) // 2048
            kcol = (kc * 512) % 2048
            kb = ktc[kc % 2]
            guard = (kld_hist[kc - 2],) if kc >= 2 else last_ev
            kt_tk = p.dma("sync", kb.rearrange("p (c f) -> p c f", f=512),
                          hb["kout_h"][rank * 1024:(rank + 1) * 1024, kcol:kcol + 512]
                          .rearrange("(c p) f -> p c f", p=128), waits=guard)
            for j in range(4):
                fns = [lambda e, ec=ec, j=j, qt=qts[j], kb=kb: e.matmul(
                    ps[0:128, j * 512:(j + 1) * 512], QTf[ec][:, qt * 128:(qt + 1) * 128],
                    kb[:, ec * 512:(ec + 1) * 512], start=(ec == 0), stop=(ec == 7)) for ec in range(8)]
                mm = p.mmg(fns, waits=(kt_tk,) + ((sc_prev[j],) if sc_prev[j] else (last_ev if kc == 0 else ())))
                ev = p.ve(lambda e, j=j, kc=kc: e.tensor_scalar(
                    scr[j][:, kc * 512:(kc + 1) * 512], ps[0:128, j * 512:(j + 1) * 512],
                    1.0 / 32.0, None, op0=ALU.mult), waits=(mm,))
                sc_prev[j] = ev
            kld_hist.append(sc_prev[3])
        pst = []
        for j in range(4):
            r1 = p.ve(lambda e, j=j: e.reduce_max(negm[:, j:j + 1], scr[j][:, 0:4096], axis=AX.X,
                      negate=True), waits=(sc_prev[3],) + ((pst[-1],) if pst else ()))
            ex = p.ve(lambda e, j=j: e.activation(pbuf[:, 0:4096], scr[j][:, 0:4096], AF.Exp,
                      bias=negm[:, j:j + 1], accum_out=stat[:, 0:1]), eng="scalar", waits=(r1,))
            rc = p.sve(lambda e, j=j: e.reciprocal(rden[:, j:j + 1], stat[:, 0:1]), waits=(ex,))
            pst.append(p.dma("sync", hb["p_h"][j * 128:(j + 1) * 128, :], pbuf[:, 0:4096], waits=(ex, rc)))
        avmm = []
        vld_hist = []
        for kc in range(32):
            ptb = ptt[kc % 2]
            vb = vtc[kc % 2]
            guard = (avmm[(kc - 2) * 4 + 3],) if kc >= 2 else (pst[-1],)
            pt_tk = p.dma("scalar", ptb, hb["p_h"][0:512, kc * 128:(kc + 1) * 128],
                          waits=guard + (pst[-1],), transpose=True)
            v_tk = p.dma("sync", vb, hb["vout_h"][kc * 128:(kc + 1) * 128, :], waits=guard)
            for j in range(4):
                fns = [lambda e, j=j, vc=vc, ptb=ptb, vb=vb, kc=kc: e.matmul(
                    ps[0:128, j * 1024 + vc * 512: j * 1024 + (vc + 1) * 512],
                    ptb[:, j * 128:(j + 1) * 128], vb[:, vc * 512:(vc + 1) * 512],
                    start=(kc == 0), stop=(kc == 31)) for vc in range(2)]
                mm = p.mmg(fns, waits=(pt_tk, v_tk))
                avmm.append(mm)
        evs = []
        for j in range(4):
            e1 = p.ve(lambda e, j=j, qt=qts[j]: e.scalar_tensor_tensor(
                xt[qt], ps[0:128, j * 1024:(j + 1) * 1024], rden[:, j:j + 1], xt[qt],
                op0=ALU.mult, op1=ALU.add), waits=(avmm[-1],))
            evs.append(e1)
        last_ev = tuple(evs) + (avmm[-1],)
    ar.pop()
    ar.pop()
    return evs[-1]



def emit_l1ffn_prep(p, ar, hb, per):
    ps = per["psum"]
    prev_done = ()
    for l in range(2):
        ar.push()
        wt = [ar.alloc(128, 5632, F32) for _ in range(2)]
        wq = ar.alloc(128, 5632, BF16)
        lgc = ar.alloc(128, 8, F32)
        lbc = ar.alloc(128, 8, F32)
        urow = ar.alloc(1, 5632, F32)
        crow = ar.alloc(1, 5632, F32)
        b1r = ar.alloc(1, 5632, F32)
        ocb = ar.alloc(128, 1, BF16)
        t0 = p.dma("sync", lgc, hb[f"f1lg{l}"][:, :])
        t1 = p.dma("sync", lbc, hb[f"f1lb{l}"][:, :])
        t2 = p.dma("sync", b1r, hb[f"f1b1p{l}"][:, :])
        m0 = p.ve(lambda e: e.memset(urow[0:1, 0:5632], 0.0), waits=prev_done)
        m1 = p.ve(lambda e: e.memset(ocb, 1.0))
        cw = p.sve(lambda e: e.tensor_copy(crow[0:1, 0:5632], b1r[0:1, 0:5632]), waits=(t2, m0))
        guard = []
        last = cw
        for dc in range(8):
            buf = wt[dc % 2]
            tk = p.dma("sync", buf, hb[f"f1w1p{l}"][dc * 128:(dc + 1) * 128, :],
                       waits=(guard[dc - 2],) if dc >= 2 else (t0, t1))
            q1 = p.ve(lambda e, b=buf, dc=dc: e.tensor_scalar(wq[:, 0:5632], b, lgc[:, dc:dc + 1], None,
                      op0=ALU.mult), waits=(tk, last))
            st = p.dma("sync", hb[f"w1p_h{l}"][dc * 128:(dc + 1) * 128, :], wq[:, 0:5632], waits=(q1,))
            acc = q1
            for fc in range(11):
                mmu = p.ve(lambda e, fc=fc: e.matmul(ps[0:1, 0:512], ocb,
                           wq[:, fc * 512:(fc + 1) * 512], start=True, stop=True), eng="tensor",
                           waits=(q1, m1, acc))
                ua = p.sve(lambda e, fc=fc: e.tensor_add(urow[0:1, fc * 512:(fc + 1) * 512],
                           urow[0:1, fc * 512:(fc + 1) * 512], ps[0:1, 0:512]), waits=(mmu,))
                mmc = p.ve(lambda e, fc=fc, b=buf, dc=dc: e.matmul(ps[0:1, 512:1024],
                           lbc[:, dc:dc + 1], b[:, fc * 512:(fc + 1) * 512], start=True, stop=True),
                           eng="tensor", waits=(ua,))
                acc = p.sve(lambda e, fc=fc: e.tensor_add(crow[0:1, fc * 512:(fc + 1) * 512],
                            crow[0:1, fc * 512:(fc + 1) * 512], ps[0:1, 512:1024]), waits=(mmc,))
            guard.append(acc)
            last = st
        su = p.dma("sync", hb[f"ucrow_h{l}"][0:1, :], urow[0:1, 0:5632], waits=(guard[-1],))
        sc2 = p.dma("sync", hb[f"ucrow_h{l}"][1:2, :], crow[0:1, 0:5632], waits=(guard[-1],))
        w2cb = [ar.alloc(128, 1024, BF16) for _ in range(2)]
        w2st = []
        for hc2 in range(22):
            b2_ = w2cb[hc2 % 2]
            tkc = p.dma("gpsimd", b2_, hb[f"f1w2{l}"][hc2 * 128:(hc2 + 1) * 128, :],
                        waits=(w2st[hc2 - 2],) if hc2 >= 2 else prev_done)
            w2st.append(p.dma("scalar", hb[f"w2b_h{l}"][hc2 * 128:(hc2 + 1) * 128, :], b2_,
                        waits=(tkc,)))
        p.raw("sync", lambda e: None, (su, sc2, last, w2st[-1], w2st[-2]), ())
        prev_done = (su, sc2, last, w2st[-1], w2st[-2])
        ar.pop()


def emit_l1ffn(p, ar, hb, per, xt):
    ps = per["psum"]
    identb = per["ident_b"]
    identf = per["ident_f"]
    onesr = per["onesrow_f"]
    ar.push()
    stat = ar.alloc(128, 16, F32)
    st2 = ar.alloc(128, 96, F32)
    role1v = ar.alloc(128, 1, F32)
    ocb = ar.alloc(128, 1, BF16)
    orb = ar.alloc(1, 128, BF16)
    ocf = ar.alloc(128, 1, F32)
    lnt2 = [ar.alloc(128, 1024, BF16) for _ in range(2)]
    lnt = lnt2[0]
    r0 = p.dma("sync", role1v, hb["role1"][:, :])
    mo = p.ve(lambda e: e.memset(ocb, 1.0))
    mo2 = p.ve(lambda e: e.memset(orb[0:1, 0:128], 1.0))
    mo3 = p.ve(lambda e: e.memset(ocf, 1.0))

    sp = []
    for i in range(NT):
        sp.append(p.dma("sync", hb["xsp_h"][i * 128:(i + 1) * 128, :], xt[i]))
    spall = (sp[-1],)
    ar.push()
    sqj2 = [ar.alloc(128, 1024, F32) for _ in range(2)]
    rs = []
    r2s = []
    for i in range(NT):
        x = xt[i]
        r1 = p.ve(lambda e, x=x, i=i: e.reduce_sum(st2[:, i:i + 1], x, axis=AX.X), waits=spall)
        r2 = p.ve(lambda e, x=x, i=i, b=sqj2[i % 2]: e.activation(b[:, 0:1024], x, AF.Square,
                  accum_out=st2[:, 16 + i:17 + i]), eng="scalar",
                  waits=((r2s[i - 2],) if i >= 2 else ()) + spall)
        rs.append(r1)
        r2s.append(r2)
    v = p.ve(lambda e: e.tensor_scalar(st2[:, 0:32], st2[:, 0:32], 1.0 / D, None, op0=ALU.mult),
             waits=tuple(rs) + tuple(r2s))
    v = p.sve(lambda e: e.tensor_tensor(st2[:, 80:96], st2[:, 0:16], st2[:, 0:16], op=ALU.mult), waits=(v,))
    v = p.sve(lambda e: e.tensor_sub(st2[:, 32:48], st2[:, 16:32], st2[:, 80:96]))
    v = p.sve(lambda e: e.tensor_scalar_add(st2[:, 32:48], st2[:, 32:48], 1e-5))
    v = p.sve(lambda e: e.reciprocal(st2[:, 48:64], st2[:, 32:48]))
    v = p.sve(lambda e: e.activation(st2[:, 48:64], st2[:, 48:64], AF.Sqrt), eng="scalar", waits=(v,))
    v = p.sve(lambda e: e.tensor_tensor(st2[:, 64:80], st2[:, 0:16], st2[:, 48:64], op=ALU.mult), waits=(v,))
    v = p.sve(lambda e: e.tensor_scalar(st2[:, 64:80], st2[:, 64:80], -1.0, None, op0=ALU.mult))
    prevq_l = []
    for i in range(NT):
        prevq_l.append(p.ve(lambda e, x=xt[i], i=i: e.tensor_scalar(x, x, st2[:, 48 + i:49 + i],
                       st2[:, 64 + i:65 + i], op0=ALU.mult, op1=ALU.add), waits=(v,)))
    prevq = prevq_l[-1]
    ar.pop()
    x_ready = tuple(prevq_l)
    if F1_TRUNC <= 0:
        ar.pop()
        return prevq

    xi_order = [11, 12, 13, 14, 15] + list(range(11))
    for l in range(2):
        xist = []
        for k, i in enumerate(xi_order):
            xw = (prevq_l[i],) if l == 0 else x_ready
            xist.append(p.dma("gpsimd", hb["xi_h"][i * 128:(i + 1) * 128, :], xt[i], waits=xw))
        ccA = p.collective("AllGather", ALU.bypass, [hb["xi_h"][1535:2048, :].opt()],
                           [hb["bndout_h"][:, :].opt()], waits=tuple(xist[0:5]))
        if F1_TRUNC <= 0.5 + 2 * l:
            ar.pop()
            return xist[-1]

        ar.push()
        vsb = [ar.alloc(128, 512, BF16) for _ in range(20)]
        oT = [ar.alloc(128, 2048, BF16) for _ in range(4)]
        ar.push()
        xT = [ar.alloc(128, 2576, BF16) for _ in range(8)]
        wkv = [ar.alloc(128, 1536, BF16) for _ in range(8)]
        qkvt2 = [ar.alloc(128, 1024, F32) for _ in range(2)]
        tbl = {nm: [ar.alloc(128, 64, BF16) for _ in range(20)] for nm in ("cq", "sq", "ck", "sk")}
        qsb = ar.alloc(128, 64, F32)
        ksb = ar.alloc(128, 64, F32)
        qsbb = ar.alloc(128, 64, BF16)
        ksbb = ar.alloc(128, 64, BF16)
        hnrm = ar.alloc(128, 16, F32)
        t1b = ar.alloc(128, 512, BF16)
        t2b = ar.alloc(128, 512, BF16)
        qout = ar.alloc(128, 512, BF16)
        ldw = tuple(p.dma("gpsimd", wkv[dc], hb[f"f1wqkv{l}"][dc * 128:(dc + 1) * 128, :]) for dc in range(8))
        ldt = []
        for nm in ("cq", "sq", "ck", "sk"):
            for ti in range(20):
                ldt.append(p.dma("gpsimd", tbl[nm][ti], hb[nm][ti * 128:(ti + 1) * 128, :]))
        ldq = p.dma("sync", qsb[0:1, 0:64], hb[f"f1qs{l}"][:, :])
        ldk = p.dma("sync", ksb[0:1, 0:64], hb[f"f1ks{l}"][:, :])
        bq = p.ve(lambda e: e.matmul(ps[0:128, 0:64], onesr[0:1, 0:128], qsb[0:1, 0:64],
                  start=True, stop=True), eng="tensor", waits=(ldq, ldk) + x_ready)
        bq2 = p.ve(lambda e: e.tensor_scalar(qsbb[:, 0:64], ps[0:128, 0:64], 8.0, None, op0=ALU.mult), waits=(bq,))
        bk = p.ve(lambda e: e.matmul(ps[0:128, 64:128], onesr[0:1, 0:128], ksb[0:1, 0:64],
                  start=True, stop=True), eng="tensor", waits=(bq2,))
        bk2 = p.ve(lambda e: e.tensor_copy(ksbb[:, 0:64], ps[0:128, 64:128]), waits=(bk,))
        fold = []
        fw = (bq2, bk2) + tuple(ldt)
        for ti in range(20):
            fold.append(p.ve(lambda e, ti=ti: e.tensor_tensor(tbl["cq"][ti], tbl["cq"][ti], qsbb[:, 0:64], op=ALU.mult), waits=fw))
            fold.append(p.ve(lambda e, ti=ti: e.tensor_tensor(tbl["sq"][ti], tbl["sq"][ti], qsbb[:, 0:64], op=ALU.mult)))
            fold.append(p.ve(lambda e, ti=ti: e.tensor_tensor(tbl["ck"][ti], tbl["ck"][ti], ksbb[:, 0:64], op=ALU.mult)))
            fold.append(p.ve(lambda e, ti=ti: e.tensor_tensor(tbl["sk"][ti], tbl["sk"][ti], ksbb[:, 0:64], op=ALU.mult)))
        fold_done = fold[-1]
        xtk = []
        for dc in range(8):
            xtk.append(p.dma("sync", xT[dc][:, 528:2576], hb["xi_h"][:, dc * 128:(dc + 1) * 128],
                             waits=(xist[-1],), transpose=True))
            xtk.append(p.dma("sync", xT[dc][:, 16:528], hb["bndout_h"][0:512, dc * 128:(dc + 1) * 128],
                             waits=(ccA,), transpose=True))
            xtk.append(p.dma("gpsimd", xT[dc][:, 15:16],
                             hb["bndout_h"][512:513, dc * 128:(dc + 1) * 128].rearrange("a b -> b a"),
                             waits=(ccA,)))
        msk = []
        for dc in range(8):
            msk.append(p.ve(lambda e, dc=dc: e.tensor_scalar(xT[dc][:, 15:528], xT[dc][:, 15:528],
                       role1v[:, 0:1], None, op0=ALU.mult), waits=tuple(xtk) + (r0,)))
        msk = tuple(msk)
        qkst = []
        prev_ev = None
        ev_last = [None, None]
        cons_last = [None, None]
        n6c = None
        for ti in range(20):
            par = ti % 2
            pb_ = par * 2048
            qkvt = qkvt2[par]
            fns = []
            for dc in range(8):
                off = 16 if dc < 4 else 15
                lhs = xT[dc][:, off + ti * 128: off + ti * 128 + 128]
                for fc in range(3):
                    fns.append(lambda e, lhs=lhs, dc=dc, fc=fc, pb_=pb_: e.matmul(
                        ps[0:128, pb_ + fc * 512:pb_ + (fc + 1) * 512], lhs, wkv[dc][:, fc * 512:(fc + 1) * 512],
                        start=(dc == 0), stop=(dc == 7)))
            mmw = msk + ldw
            if ti == 0:
                mmw += (bk2, bq2)
            if ev_last[par] is not None:
                mmw += (ev_last[par],)
            mm = p.mmg(fns, waits=mmw)
            evw = (mm,)
            if cons_last[par] is not None:
                evw += cons_last[par]
            ev = p.ve(lambda e, qkvt=qkvt, pb_=pb_: e.tensor_copy(qkvt[:, 0:1024],
                      ps[0:128, pb_:pb_ + 1024]), waits=evw)
            n6 = None
            for base, ct, st_, outh in ((0, "cq", "sq", "qh_h"), (512, "ck", "sk", "kh_h")):
                vvw = (ev,) + ((n6,) if n6 else ((n6c,) if n6c else ()))
                vv = p.ve(lambda e, base=base, qkvt=qkvt: e.activation(t1b[:, 0:512], qkvt[:, base:base + 512],
                          AF.Square), eng="scalar", waits=vvw)
                rr = p.sve(lambda e: e.tensor_reduce(hnrm[:, 0:8],
                           t1b[:, 0:512].rearrange("p (h d) -> p h d", d=64), axis=AX.X, op=ALU.add),
                           waits=(vv,))
                rr = p.sve(lambda e: e.activation(hnrm[:, 0:8], hnrm[:, 0:8], AF.Sqrt), eng="scalar", waits=(rr,))
                rr = p.sve(lambda e: e.tensor_scalar(hnrm[:, 0:8], hnrm[:, 0:8], 1e-12, None, op0=ALU.max), waits=(rr,))
                rr = p.sve(lambda e: e.reciprocal(hnrm[:, 8:16], hnrm[:, 0:8]))
                n1 = p.sve(lambda e, base=base, qkvt=qkvt: e.tensor_tensor(
                    t1b[:, 0:512].rearrange("p (h d) -> p h d", d=64),
                    qkvt[:, base:base + 512].rearrange("p (h d) -> p h d", d=64),
                    hnrm[:, 8:16].unsqueeze(2).broadcast_to((128, 8, 64)), op=ALU.mult), waits=(rr,))
                last_n1 = n1
                n2 = p.sve(lambda e: e.tensor_scalar(
                    t2b[:, 0:512].rearrange("p (h d) -> p h d", d=64)[:, :, 0:32],
                    t1b[:, 0:512].rearrange("p (h d) -> p h d", d=64)[:, :, 32:64],
                    -1.0, None, op0=ALU.mult), waits=(n1,))
                n3 = p.sve(lambda e: e.tensor_copy(
                    t2b[:, 0:512].rearrange("p (h d) -> p h d", d=64)[:, :, 32:64],
                    t1b[:, 0:512].rearrange("p (h d) -> p h d", d=64)[:, :, 0:32]))
                n4 = p.sve(lambda e, ti=ti, ct=ct: e.tensor_tensor(
                    t1b[:, 0:512].rearrange("p (h d) -> p h d", d=64),
                    t1b[:, 0:512].rearrange("p (h d) -> p h d", d=64),
                    tbl[ct][ti].unsqueeze(1).broadcast_to((128, 8, 64)), op=ALU.mult), waits=(n3, fold_done))
                n5 = p.sve(lambda e, ti=ti, st_=st_: e.tensor_tensor(
                    t2b[:, 0:512].rearrange("p (h d) -> p h d", d=64),
                    t2b[:, 0:512].rearrange("p (h d) -> p h d", d=64),
                    tbl[st_][ti].unsqueeze(1).broadcast_to((128, 8, 64)), op=ALU.mult))
                n6 = p.sve(lambda e: e.tensor_add(qout[:, 0:512], t1b[:, 0:512], t2b[:, 0:512]))
                qkst.append(p.dma("sync", hb[outh][ti * 128:(ti + 1) * 128, :], qout[:, 0:512], waits=(n6,)))
                p.raw("vector", lambda e: None, (qkst[-1],), ())
            prev_ev = p.ve(lambda e, ti=ti, pb_=pb_: e.tensor_copy(vsb[ti][:, 0:512],
                           ps[0:128, pb_ + 1024:pb_ + 1536]), waits=(mm, ev))
            ev_last[par] = prev_ev
            cons_last[par] = (last_n1,)
            n6c = n6
        qkv_done = (prev_ev, qkst[-1])
        ar.pop()
        if F1_TRUNC <= 0.7 + 2 * l:
            ar.pop()
            ar.pop()
            return prev_ev
        ar.push()
        qhT = [ar.alloc(128, 2560, BF16) for _ in range(4)]
        khT = [ar.alloc(128, 2560, BF16) for _ in range(4)]
        lmsk = [ar.alloc(128, 1024, BF16) for _ in range(4)]
        lmskf = [ar.alloc(128, 1024, BF16) for _ in range(4)]
        psb = ar.alloc(128, 8192, BF16)
        ptb = [ar.alloc(128, 2048, BF16) for _ in range(2)]
        rden8 = ar.alloc(128, 8, F32)
        negm8 = ar.alloc(128, 8, F32)
        qtk = tuple(p.dma("sync", qhT[hc], hb["qh_h"][:, hc * 128:(hc + 1) * 128],
                          waits=qkv_done, transpose=True) for hc in range(4))
        ktk = tuple(p.dma("sync", khT[hc], hb["kh_h"][:, hc * 128:(hc + 1) * 128],
                          waits=qkv_done, transpose=True) for hc in range(4))
        mtk = tuple(p.dma("gpsimd", lmsk[q4], hb["lmask"][q4 * 128:(q4 + 1) * 128, :], waits=qkv_done) for q4 in range(4)) + \
              tuple(p.dma("gpsimd", lmskf[q4], hb["lmaskf"][q4 * 128:(q4 + 1) * 128, :], waits=qkv_done) for q4 in range(4))
        pstores = []
        for w in range(4):
            for q4 in range(4):
                exps = []
                hdev = {}
                for hd in range(8):
                    hc, hr = hd // 2, (hd % 2) * 64
                    qcol = 512 + w * 512 + q4 * 128
                    lm = lmskf if w == 0 else lmsk
                    pb_ = (hd % 4) * 1024
                    fns = [lambda e, hc=hc, hr=hr, qcol=qcol, kk=kk, w=w, pb_=pb_: e.matmul(
                        ps[0:128, pb_ + kk * 512:pb_ + (kk + 1) * 512],
                        qhT[hc][hr:hr + 64, qcol:qcol + 128],
                        khT[hc][hr:hr + 64, w * 512 + kk * 512: w * 512 + (kk + 1) * 512],
                        start=True, stop=False) for kk in range(2)]
                    fns += [lambda e, q4=q4, kk=kk, lm=lm, pb_=pb_: e.matmul(
                        ps[0:128, pb_ + kk * 512:pb_ + (kk + 1) * 512], identb[:, 0:128],
                        lm[q4][:, kk * 512:(kk + 1) * 512], start=False, stop=True) for kk in range(2)]
                    mmw2 = qtk + ktk + mtk
                    if hd >= 4:
                        mmw2 += hdev[hd - 4]
                    elif pstores:
                        mmw2 += (pstores[-1],)
                    mm = p.mmg(fns, waits=mmw2)
                    r1 = p.ve(lambda e, hd=hd, pb_=pb_: e.reduce_max(negm8[:, hd:hd + 1],
                              ps[0:128, pb_:pb_ + 1024],
                              axis=AX.X, negate=True), waits=(mm,))
                    ex = p.ve(lambda e, hd=hd, pb_=pb_: e.activation(psb[:, hd * 1024:(hd + 1) * 1024],
                              ps[0:128, pb_:pb_ + 1024], AF.Exp, bias=negm8[:, hd:hd + 1],
                              accum_out=stat[:, hd:hd + 1]), eng="scalar", waits=(r1,))
                    rc = p.ve(lambda e, hd=hd: e.reciprocal(rden8[:, hd:hd + 1], stat[:, hd:hd + 1]),
                              waits=(ex,))
                    hdev[hd] = (r1, ex)
                    exps.append(rc)
                    exps.append(ex)
                nrm = p.ve(lambda e: e.tensor_tensor(
                    psb[:, 0:8192].rearrange("p (h k) -> p h k", k=1024),
                    psb[:, 0:8192].rearrange("p (h k) -> p h k", k=1024),
                    rden8[:, 0:8].unsqueeze(2).broadcast_to((128, 8, 1024)), op=ALU.mult),
                    waits=tuple(exps))
                pstores.append(p.dma("sync", hb["ph_h"][(w * 4 + q4) * 128:(w * 4 + q4 + 1) * 128, :],
                               psb[:, 0:8192], waits=(nrm,)))
                p.raw("vector", lambda e: None, (pstores[-1],), ())
        if F1_TRUNC <= 0.8 + 2 * l:
            ar.pop()
            ar.pop()
            ar.pop()
            return pstores[-1]
        avh = []
        prev_oev = None
        for hd in range(8):
            hc, hr = hd // 2, (hd % 2) * 64
            for kk in range(8):
                pb = ptb[kk % 2]
                idx = hd * 8 + kk
                ptk = p.dma("scalar", pb, hb["ph_h"][0:2048, hd * 1024 + kk * 128: hd * 1024 + (kk + 1) * 128],
                            waits=(pstores[-1],) + ((avh[idx - 2],) if idx >= 2 else ()), transpose=True)
                fns = [lambda e, w=w2, kk=kk, hd=hd, pb=pb: e.matmul(
                    ps[0:64, 2048 + w * 512: 2048 + (w + 1) * 512],
                    vsb[w * 4 + kk][:, hd * 64:hd * 64 + 64],
                    pb[:, w * 512:(w + 1) * 512], start=(kk == 0), stop=(kk == 7)) for w2 in range(4)]
                mm = p.mmg(fns, waits=(ptk,) + ((prev_oev,) if prev_oev and kk == 0 else ()))
                avh.append(mm)
            for w in range(4):
                oe = p.ve(lambda e, w=w, hc=hc, hr=hr: e.tensor_copy(
                    oT[hc][hr:hr + 64, w * 512:(w + 1) * 512],
                    ps[0:64, 2048 + w * 512:2048 + (w + 1) * 512]), waits=(avh[-1],))
            prev_oev = oe
        if F1_TRUNC <= 0.9 + 2 * l:
            ar.pop()
            ar.pop()
            ar.pop()
            return prev_oev
        ar.pop()
        ar.push()
        wosb = [ar.alloc(128, 1024, BF16) for _ in range(4)]
        wotk = tuple(p.dma("gpsimd", wosb[ic], hb[f"f1wo{l}"][ic * 128:(ic + 1) * 128, :],
                           waits=(prev_oev,)) for ic in range(4))
        prev = None
        prev_list = []
        for ti in range(NT):
            fns = []
            for ic in range(4):
                for vc2 in range(2):
                    fns.append(lambda e, ic=ic, vc2=vc2, ti=ti, pb_=(ti % 2) * 2048: e.matmul(
                        ps[0:128, pb_ + vc2 * 512:pb_ + (vc2 + 1) * 512], oT[ic][:, ti * 128:(ti + 1) * 128],
                        wosb[ic][:, vc2 * 512:(vc2 + 1) * 512], start=(ic == 0), stop=(ic == 3)))
            mm = p.mmg(fns, waits=wotk + (prev_oev,) + ((prev_list[ti - 2],) if ti >= 2 else ()))
            prev = p.ve(lambda e, ti=ti, pb_=(ti % 2) * 2048: e.tensor_add(xt[ti], xt[ti],
                        ps[0:128, pb_:pb_ + 1024]), waits=(mm,))
            prev_list.append(prev)
        ar.pop()
        ar.pop()
        x_ready = (prev,)
        if F1_TRUNC <= 1 + 2 * l:
            ar.pop()
            return prev

        c1 = p.ve(lambda e: e.tensor_copy(lnt[:, 0:1024], xt[NT - 1]), waits=x_ready)
        sx1 = p.dma("gpsimd", hb["bnd1in_h"][0:1, :], lnt[127:128, 0:1024], waits=(c1,))
        ccB = p.collective("AllGather", ALU.bypass, [hb["bnd1in_h"][:, :].opt()],
                           [hb["bnd1out_h"][:, :].opt()], waits=(sx1,))

        xist = []
        for i in range(NT):
            xist.append(p.dma("gpsimd", hb["xi_h"][i * 128:(i + 1) * 128, :], xt[i],
                        waits=(prev_list[i], c1, sx1)))
        if F1_TRUNC <= 1.5 + 2 * l:
            ar.pop()
            return xist[-1]
        ar.push()
        xT2 = [ar.alloc(128, 2064, BF16) for _ in range(8)]
        statrows = ar.alloc(1, 6144, F32)
        murow = statrows[0:1, 0:2048]
        msqrow = statrows[0:1, 2048:4096]
        rvrow = statrows[0:1, 4096:6144]
        brows = ar.alloc(1, 4096, BF16)
        nmurow = brows[0:1, 0:2048]
        rvirow = brows[0:1, 2048:4096]
        rvsq = ar.alloc(16, 128, F32)
        rvcol = ar.alloc(128, 16, F32)
        sqsc = ar.alloc(128, 512, F32)
        ucrow_t = ar.alloc(1, 11264, BF16)
        ucrow = ucrow_t[0:1, 0:5632]
        ccrow = ucrow_t[0:1, 5632:11264]
        hhtile = [ar.alloc(128, 1024, BF16) for _ in range(2)]
        h1p = ar.alloc(128, 512, F32)
        ggs = ar.alloc(128, 256, F32)
        global F1_REFS
        F1_REFS = dict(xT2=xT2, statrows=statrows, rvcol=rvcol, ucrow=ucrow, ccrow=ccrow,
                       hhtile=hhtile, h1p=h1p, ggs=ggs)
        xtk2 = []
        for dc in range(8):
            xtk2.append(p.dma("sync", xT2[dc][:, 16:2064], hb["xi_h"][:, dc * 128:(dc + 1) * 128],
                              waits=(xist[-1],), transpose=True))
            xtk2.append(p.dma("gpsimd", xT2[dc][:, 15:16],
                              hb["bnd1out_h"][0:1, dc * 128:(dc + 1) * 128].rearrange("a b -> b a"),
                              waits=(ccB,)))
        msk2 = []
        for dc in range(8):
            msk2.append(p.ve(lambda e, dc=dc: e.tensor_scalar(xT2[dc][:, 15:16], xT2[dc][:, 15:16],
                        role1v[:, 0:1], None, op0=ALU.mult), waits=tuple(xtk2)))
        msk2 = tuple(msk2)
        lduc0 = p.dma("gpsimd", ucrow, hb[f"ucrow_h{l}"][0:1, :])
        lduc = p.dma("gpsimd", ccrow, hb[f"ucrow_h{l}"][1:2, :])

        def hsl(dc, c0, c1):
            off = 16 if dc < 4 else 15
            return xT2[dc][:, off + c0: off + c1]

        stev = []
        for tc in range(4):
            fns = [lambda e, dc=dc, tc=tc: e.matmul(ps[0:1, 0:512], ocb,
                   hsl(dc, tc * 512, (tc + 1) * 512), start=(dc == 0), stop=(dc == 7)) for dc in range(8)]
            mm1_ = p.mmg(fns, waits=msk2 + (mo,) + ((stev[-1],) if stev else ()))
            stev.append(p.ve(lambda e, tc=tc: e.tensor_scalar(murow[0:1, tc * 512:(tc + 1) * 512],
                        ps[0:1, 0:512], 1.0 / D, None, op0=ALU.mult), waits=(mm1_,)))
        for tc in range(4):
            sq_p = None
            for dc in range(8):
                sq = p.ve(lambda e, dc=dc, tc=tc: e.activation(sqsc[:, 0:512],
                          hsl(dc, tc * 512, (tc + 1) * 512), AF.Square), eng="scalar",
                          waits=((sq_p,) if sq_p else (stev[-1],)))
                sq_p = p.ve(lambda e, dc=dc: e.matmul(ps[0:1, 512:1024], ocf, sqsc[:, 0:512],
                            start=(dc == 0), stop=(dc == 7)), eng="tensor", waits=(sq, mo3))
            stev.append(p.ve(lambda e, tc=tc: e.tensor_scalar(msqrow[0:1, tc * 512:(tc + 1) * 512],
                        ps[0:1, 512:1024], 1.0 / D, None, op0=ALU.mult), waits=(sq_p,)))
        v = p.sve(lambda e: e.tensor_tensor(rvrow[0:1, 0:2048], murow[0:1, 0:2048], murow[0:1, 0:2048],
                  op=ALU.mult), waits=tuple(stev))
        v = p.sve(lambda e: e.tensor_sub(msqrow[0:1, 0:2048], msqrow[0:1, 0:2048], rvrow[0:1, 0:2048]))
        v = p.sve(lambda e: e.tensor_scalar_add(msqrow[0:1, 0:2048], msqrow[0:1, 0:2048], 1e-5))
        v = p.sve(lambda e: e.activation(rvirow[0:1, 0:2048], msqrow[0:1, 0:2048], AF.Sqrt), eng="scalar", waits=(v,))
        v = p.sve(lambda e: e.reciprocal(rvrow[0:1, 0:2048], msqrow[0:1, 0:2048]), waits=(v,))
        v = p.sve(lambda e: e.activation(rvrow[0:1, 0:2048], rvrow[0:1, 0:2048], AF.Sqrt), eng="scalar", waits=(v,))
        v = p.sve(lambda e: e.tensor_scalar(nmurow[0:1, 0:2048], murow[0:1, 0:2048], -1.0, None, op0=ALU.mult), waits=(v,))
        if A0_DEBUG:
            p.dma("sync", hb["dbgf1_h"][0:1, :], murow[0:1, 0:2048], waits=(v,))
            p.dma("sync", hb["dbgf1_h"][1:2, :], msqrow[0:1, 0:2048], waits=(v,))
        s1 = p.dma("sync", hb["rv_h"][0:1, :], rvrow[0:1, 0:2048], waits=(v,))
        s2 = p.dma("sync", rvsq, hb["rv_h"].rearrange("o (r c) -> (o r) c", c=128), waits=(s1,))
        tr = p.ve(lambda e: e.transpose(ps[0:128, 1024:1040], rvsq[0:16, 0:128], identf[0:16, 0:16]),
                  eng="tensor", waits=(s2,))
        trc = p.ve(lambda e: e.tensor_copy(rvcol[:, 0:16], ps[0:128, 1024:1040]), waits=(tr,))

        ar.push()
        w1h = [ar.alloc(128, 1024, BF16) for _ in range(8)]
        F1_REFS['w1h'] = w1h
        hh_done = []
        e1_last = [None, None]
        st_last = [None, None]
        e3_prev = None
        mm_prev_half = None
        gt = 0
        for half, (f0, nfc) in enumerate(((0, 2), (2, 2), (4, 2), (6, 2), (8, 2), (10, 1))):
            w1tk = tuple(p.dma("sync", w1h[dc][:, 0:nfc * 512],
                               hb[f"w1p_h{l}"][dc * 128:(dc + 1) * 128, f0 * 512:(f0 + nfc) * 512],
                               waits=((mm_prev_half,) if mm_prev_half else (trc,))) for dc in range(8))
            for ti in range(NT):
                par = gt % 2
                pb_ = par * 2048
                fns = []
                for dc in range(8):
                    off = 16 if dc < 4 else 15
                    lhs = xT2[dc][:, off + ti * 128: off + ti * 128 + 128]
                    for fc in range(nfc):
                        fns.append(lambda e, lhs=lhs, dc=dc, fc=fc, pb_=pb_: e.matmul(
                            ps[0:128, pb_ + fc * 512:pb_ + (fc + 1) * 512], lhs, w1h[dc][:, fc * 512:(fc + 1) * 512],
                            start=(dc == 0), stop=False))
                for fc in range(nfc):
                    fns.append(lambda e, fc=fc, f0=f0, ti=ti, pb_=pb_: e.matmul(
                        ps[0:128, pb_ + fc * 512:pb_ + (fc + 1) * 512], nmurow[0:1, ti * 128:(ti + 1) * 128],
                        ucrow[0:1, (f0 + fc) * 512:(f0 + fc + 1) * 512], start=False, stop=False))
                    fns.append(lambda e, fc=fc, f0=f0, ti=ti, pb_=pb_: e.matmul(
                        ps[0:128, pb_ + fc * 512:pb_ + (fc + 1) * 512], rvirow[0:1, ti * 128:(ti + 1) * 128],
                        ccrow[0:1, (f0 + fc) * 512:(f0 + fc + 1) * 512], start=False, stop=(True)))
                mmw = w1tk + (lduc0, lduc, trc)
                if e1_last[par] is not None:
                    mmw += (e1_last[par],)
                mm = p.mmg(fns, waits=mmw)
                hb_ = hhtile[par]
                hev = []
                dbgT = []
                for fc in range(nfc):
                    e1w = (mm,) + (tuple(hev[-1:]) if hev else ()) + tuple(dbgT[-1:])
                    if fc == 0 and st_last[par] is not None:
                        e1w += (st_last[par],)
                    if fc == 0 and e3_prev is not None:
                        e1w += (e3_prev,)
                    e1 = p.ve(lambda e, fc=fc, ti=ti, pb_=pb_: e.tensor_scalar(h1p[:, 0:512],
                              ps[0:128, pb_ + fc * 512:pb_ + (fc + 1) * 512], rvcol[:, ti:ti + 1], None,
                              op0=ALU.mult), waits=e1w)
                    if A0_DEBUG and half == 0 and ti == 0:
                        dbgT.append(p.dma("sync", hb["dbgf1_h"][8 + fc * 128: 8 + (fc + 1) * 128, 0:512],
                                          h1p[:, 0:512], waits=(e1,)))
                    e2w = (e1,)
                    if fc == 0 and e3_prev is not None:
                        e2w += (e3_prev,)
                    e2 = p.ve(lambda e: e.activation(ggs[:, 0:256], h1p[:, 1:512:2],
                              AF.Gelu), eng="scalar", waits=e2w)
                    if A0_DEBUG and half == 0 and ti == 0:
                        dbgT.append(p.dma("sync", hb["dbgf1_h"][520 + fc * 128: 520 + (fc + 1) * 128, 0:256],
                                          ggs[:, 0:256], waits=(e2,)))
                    e3 = p.ve(lambda e, fc=fc, hb_=hb_: e.tensor_tensor(hb_[:, fc * 256:(fc + 1) * 256],
                              h1p[:, 0:512:2], ggs[:, 0:256], op=ALU.mult), waits=(e2,) + tuple(dbgT[-1:]))
                    hev.append(e3)
                    e1_last[par] = e1
                e3_prev = hev[-1]
                if A0_DEBUG and half == 0 and ti == 0:
                    dbgT.append(p.dma("sync", hb["dbgf1_h"][1032:1160, 0:512],
                                      hb_[:, 0:1024].bitcast(F32), waits=tuple(hev)))
                st = p.dma("sync", hb["hh_h"][ti * 128:(ti + 1) * 128, f0 * 256:(f0 + nfc) * 256],
                           hb_[:, 0:nfc * 256], waits=tuple(hev) + tuple(dbgT[-1:]))
                st_last[par] = st
                hh_done.append(st)
                gt += 1
            mm_prev_half = mm
        ar.pop()
        ar.push()
        hhT = [ar.alloc(128, 512, BF16) for _ in range(4)]
        w2s = [ar.alloc(128, 1024, BF16) for _ in range(2)]
        b2r = ar.alloc(1, 1024, BF16)
        F1_REFS['hhT'] = hhT; F1_REFS['w2s'] = w2s; F1_REFS['b2r'] = b2r
        ldb2 = p.dma("gpsimd", b2r, hb[f"f1b2{l}"][:, :], waits=(hh_done[-1],))
        prev2 = None
        for tg in range(4):
            tis = [tg * 4 + j for j in range(4)]
            mm_l = None
            w2g = []
            for hc in range(22):
                wb = w2s[hc % 2]
                hcb = hhT[hc % 4]
                gw = (w2g[hc - 2],) if hc >= 2 else ((prev2,) if prev2 else (hh_done[-1],))
                wtk = p.dma("sync", wb, hb[f"w2b_h{l}"][hc * 128:(hc + 1) * 128, :], waits=gw)
                gh = (w2g[hc - 4],) if hc >= 4 else ((prev2,) if prev2 else (hh_done[-1],))
                htk = p.dma("scalar", hcb, hb["hh_h"][tg * 512:(tg + 1) * 512, hc * 128:(hc + 1) * 128],
                            waits=gh + (hh_done[-1],), transpose=True)
                fns = []
                for j in range(4):
                    for vc2 in range(2):
                        fns.append(lambda e, j=j, vc2=vc2, hc=hc, hcb=hcb, wb=wb: e.matmul(
                            ps[0:128, j * 1024 + vc2 * 512: j * 1024 + (vc2 + 1) * 512],
                            hcb[:, j * 128:(j + 1) * 128], wb[:, vc2 * 512:(vc2 + 1) * 512],
                            start=(hc == 0), stop=False))
                mm_l = p.mmg(fns, waits=(wtk, htk) + ((prev2,) if prev2 and hc == 0 else ()))
                w2g.append(mm_l)
            fns = []
            for j in range(4):
                for vc2 in range(2):
                    fns.append(lambda e, j=j, vc2=vc2: e.matmul(
                        ps[0:128, j * 1024 + vc2 * 512: j * 1024 + (vc2 + 1) * 512],
                        orb[0:1, 0:128], b2r[0:1, vc2 * 512:(vc2 + 1) * 512],
                        start=False, stop=True))
            mm_l = p.mmg(fns, waits=(ldb2, mo2))
            evl = []
            for j in range(4):
                evl.append(p.ve(lambda e, j=j, ti=tis[j]: e.tensor_add(
                    xt[ti], xt[ti], ps[0:128, j * 1024:(j + 1) * 1024]), waits=(mm_l,)))
            prev2 = evl[-1]
        ar.pop()
        ar.pop()
        x_ready = (prev2,)
        if F1_TRUNC <= 2 + 2 * l:
            ar.pop()
            return prev2

    ar.push()
    xo = [ar.alloc(128, 1024, F32) for _ in range(2)]
    fin = None
    for i in range(NT):
        ld = p.dma("sync", xo[i % 2], hb["xsp_h"][i * 128:(i + 1) * 128, :],
                   waits=x_ready + ((fin,) if fin and i >= 2 else ()))
        fin = p.ve(lambda e, i=i, b=xo[i % 2]: e.scalar_tensor_tensor(
            xt[i], xt[i], 0.5, b, op0=ALU.mult, op1=ALU.add), waits=(ld,))
    ar.pop()
    ar.pop()
    return fin
